# version 01
# baseline (speedup 1.0000x reference)
import numpy as np
import ml_dtypes
import concourse.bass as bass
import concourse.mybir as mybir
from concourse.bass_utils import run_bass_kernel_spmd
from contextlib import ExitStack

F32 = mybir.dt.float32
BF16 = mybir.dt.bfloat16
AF = mybir.ActivationFunctionType
ALU = mybir.AluOpType
AX = mybir.AxisListType
BF = ml_dtypes.bfloat16

NCORE = 8
TPC = 2048
NTILE = TPC // 128
DM = 1024
H = 4
DK = 256
CA = 128
NCH_A = TPC // CA
CB = 64
NCH_B = TPC // CB
DI = 2048
SEG = 512
NSEG = TPC // SEG
NE = 64
EPC = NE // NCORE

RG = [list(range(NCORE))]

CONSTS = [1e-5, -0.5, -8.0, 1.0, -1.0, 0.125, 1.0 / 256, 1.0 / 2048]
CIDX = {v: i for i, v in enumerate(CONSTS)}


class Sem:
    __slots__ = ("h", "v")

    def __init__(self, h):
        self.h = h
        self.v = 0


class Ctx:
    def __init__(self, nc, dbg=()):
        self.nc = nc
        self.dbg = set(dbg)
        self.dbg_outs = {}
        self.top = ExitStack()
        self._semn = 0

    def sem(self, name=None):
        self._semn += 1
        h = self.top.enter_context(self.nc.semaphore(name or f"s{self._semn}"))
        return Sem(h)

    def init_arena(self):
        import concourse.mybir as _mb
        avail = self.nc.sbuf_bytes_remaining - 8192
        base, end = self.nc.bump_sbuf(avail)
        self.abase, self.aend, self.acur = base, end, base
        self.amarks = []
        self.apeak = base

    def salloc(self, name, shape, dtype):
        nb = int(np.prod(shape[1:])) * mybir.dt.size(dtype)
        nb = (nb + 31) & ~31
        t = self.nc.alloc_sbuf_tensor_at(name, shape, dtype, offset=self.acur)
        self.acur += nb
        self.apeak = max(self.apeak, self.acur)
        assert self.acur <= self.aend, (name, self.acur, self.aend)
        return t

    def uname(self, n):
        self._semn += 1
        return f"{n}_{self._semn}"

    def smark(self):
        self.amarks.append(self.acur)

    def srel(self):
        self.acur = self.amarks.pop()

    def cst(self, v):
        return self.cvec[:, CIDX[v]:CIDX[v] + 1]

    def cst64(self, v):
        return self.cvec[0:64, CIDX[v]:CIDX[v] + 1]

    def dump_dram(self, name, dram):
        if name not in self.dbg:
            return
        nc = self.nc
        o = nc.dram_tensor("dbg_" + name, list(dram.shape), dram.dtype,
                           kind="ExternalOutput")
        s = self.sem()
        with nc.Block() as blk:
            @blk.sync
            def _(e):
                e.dma_start(out=o.ap(), in_=dram.ap()).then_inc(s.h, 16)
                e.wait_ge(s.h, 16)
        self.dbg_outs[name] = list(dram.shape)

    def dump(self, name, ap):
        if name not in self.dbg:
            return
        nc = self.nc
        shape = list(ap.shape)
        o = nc.dram_tensor("dbg_" + name, shape, ap.dtype, kind="ExternalOutput")
        s = self.sem()
        with nc.Block() as blk:
            @blk.sync
            def _(e):
                e.dma_start(out=o.ap(), in_=ap).then_inc(s.h, 16)
                e.wait_ge(s.h, 16)
        self.dbg_outs[name] = shape


def inc(instr, sem, n=1):
    instr.then_inc(sem.h, n)
    sem.v += n


class SerialPhase:

    def __init__(self, ctx):
        self.ctx = ctx
        self.steps = []

    def add(self, eng, fn, amt=1):
        self.steps.append((eng, fn, amt))

    def t(self, fn):
        self.add("tensor", fn)

    def v(self, fn):
        self.add("vector", fn)

    def s(self, fn):
        self.add("scalar", fn)

    def g(self, fn):
        self.add("gpsimd", fn)

    def dma(self, eng, fn):
        self.add(eng, fn, 16)

    def run(self):
        nc = self.ctx.nc
        sem = self.ctx.sem()
        cum = 0
        pre = []
        for (eng, fn, amt) in self.steps:
            pre.append(cum)
            cum += amt
        by_eng = {}
        for i, (eng, fn, amt) in enumerate(self.steps):
            by_eng.setdefault(eng, []).append((i, fn, amt))
        with nc.Block(no_gpsimd_drain=True) as blk:
            for eng, lst in by_eng.items():
                def mk(lst):
                    def body(e):
                        for (i, fn, amt) in lst:
                            if pre[i] > 0:
                                e.wait_ge(sem.h, pre[i])
                            o = fn(e)
                            o.then_inc(sem.h, amt)
                        return
                    return body
                getattr(blk, eng)(mk(lst))

            @blk.sync
            def _(e):
                e.wait_ge(sem.h, cum)


def emit_ln_transpose(ctx, est, xnT, h_src=None, x_dram=None, store_stats=None):
    nc = ctx.nc
    from_dram = x_dram is not None
    s_ld = ctx.sem()
    s_var = ctx.sem()
    s_rstd = ctx.sem()
    s_xn = ctx.sem()
    s_tp = ctx.sem()
    s_sc = ctx.sem()

    ctx.smark()
    xt = ctx.salloc(ctx.uname("ln_x"), [128, 2 * DM], F32) if from_dram else None
    sq = ctx.salloc(ctx.uname("ln_sq"), [128, 16], F32)
    st = ctx.salloc(ctx.uname("ln_st"), [128, 16], F32)
    xnb = ctx.salloc(ctx.uname("ln_xn"), [128, 2 * DM], BF16)
    pst = est.enter_context(nc.psum_tensor(ctx.uname("ln_ps"), [128, 2 * DM], BF16))

    def src(t):
        return (xt[:, (t % 2) * DM:(t % 2 + 1) * DM] if from_dram
                else h_src[:, t * DM:(t + 1) * DM])

    with nc.Block(no_gpsimd_drain=True) as blk:
        if from_dram:
            @blk.sync
            def _(e):
                for t in range(NTILE):
                    if t >= 2:
                        e.wait_ge(s_xn.h, t - 1)
                    o = e.dma_start(out=xt[:, (t % 2) * DM:(t % 2 + 1) * DM],
                                    in_=x_dram[t * 128:(t + 1) * 128, :])
                    inc(o, s_ld, 16)

        @blk.vector
        def _(e):
            for t in range(NTILE):
                p = t % 2
                if from_dram:
                    e.wait_ge(s_ld.h, 16 * (t + 1))
                e.bn_stats(sq[:, 0:6], src(t)[:, 0:512] if False else
                           bass.AP(src(t).tensor, src(t).offset, [src(t).ap[0], [1, 512]]))
                e.bn_stats(sq[:, 6:12],
                           bass.AP(src(t).tensor, src(t).offset + 512, [src(t).ap[0], [1, 512]]))
                e.drain()
                tt = e.bn_aggr(st[:, p * 8:p * 8 + 2], sq[:, 0:12])
                e.drain()
                inc(tt, s_var)
                e.wait_ge(s_rstd.h, t + 1)
                if t >= 2:
                    e.wait_ge(s_tp.h, t - 1)
                o = e.tensor_scalar(xnb[:, p * DM:(p + 1) * DM], src(t),
                                    st[:, p * 8 + 0:p * 8 + 1], st[:, p * 8 + 3:p * 8 + 4],
                                    op0=ALU.subtract, op1=ALU.mult)
                inc(o, s_xn)
                if store_stats is not None:
                    e.tensor_copy(store_stats[:, 2 * t:2 * t + 1], st[:, p * 8:p * 8 + 1])
                    e.tensor_copy(store_stats[:, 2 * t + 1:2 * t + 2],
                                  st[:, p * 8 + 3:p * 8 + 4])
                if t >= 1:
                    e.wait_ge(s_tp.h, t)
                    sc = e.tensor_copy(
                        bass.AP(xnT, (t - 1) * 128,
                                [[8 * TPC, 128], [TPC, 8], [1, 128]]),
                        bass.AP(pst, ((t - 1) % 2) * DM, [[2 * DM, 128], [128, 8], [1, 128]]))
                    inc(sc, s_sc)
            e.wait_ge(s_tp.h, NTILE)
            sc = e.tensor_copy(
                bass.AP(xnT, (NTILE - 1) * 128, [[8 * TPC, 128], [TPC, 8], [1, 128]]),
                bass.AP(pst, ((NTILE - 1) % 2) * DM, [[2 * DM, 128], [128, 8], [1, 128]]))
            inc(sc, s_sc)

        @blk.scalar
        def _(e):
            for t in range(NTILE):
                p = t % 2
                e.wait_ge(s_var.h, t + 1)
                if t >= 1:
                    e.wait_ge(s_xn.h, t)
                e.activation(st[:, p * 8 + 3:p * 8 + 4], st[:, p * 8 + 1:p * 8 + 2],
                             AF.Ln, bias=ctx.cst(1e-5))
                o = e.activation(st[:, p * 8 + 3:p * 8 + 4], st[:, p * 8 + 3:p * 8 + 4],
                                 AF.Exp, scale=ctx.cst(-0.5))
                inc(o, s_rstd)

        @blk.tensor
        def _(e):
            for t in range(NTILE):
                p = t % 2
                e.wait_ge(s_xn.h, t + 1)
                if t >= 2:
                    e.wait_ge(s_sc.h, t - 1)
                for i in range(8):
                    o = e.transpose(pst[:, p * DM + i * 128:p * DM + (i + 1) * 128],
                                    xnb[:, p * DM + i * 128:p * DM + (i + 1) * 128],
                                    ctx.ident[:, :])
                inc(o, s_tp)
    ctx.srel()


def emit_proj_qkvb(ctx, est, xnT, wq, wk, wv, wbeta, q_tm, k_tm, v_tm, beta_sb):
    nc = ctx.nc
    s_mm = ctx.sem()
    s_ev = ctx.sem()
    s_ss = ctx.sem()
    s_rs = ctx.sem()
    s_ap = ctx.sem()

    ctx.smark()
    ps = est.enter_context(nc.psum_tensor("pj_ps", [128, 3 * DM], F32))
    psb = est.enter_context(nc.psum_tensor("pj_psb", [128, 2 * H], F32))
    qf = ctx.salloc("pj_qf", [128, 2 * DM], BF16)
    kf = ctx.salloc("pj_kf", [128, 2 * DM], BF16)
    sq = ctx.salloc("pj_sq", [128, DM], BF16)
    rs = ctx.salloc("pj_rs", [128, 16], F32)

    with nc.Block(no_gpsimd_drain=True) as blk:
        @blk.tensor
        def _(e):
            for t in range(NTILE):
                if t >= 1:
                    e.wait_ge(s_ev.h, 2 * t)
                for w, off in ((wq, 0), (wk, DM), (wv, 2 * DM)):
                    for half in range(2):
                        for dt in range(8):
                            o = e.matmul(ps[:, off + half * 512:off + (half + 1) * 512],
                                         xnT[:, dt * TPC + t * 128:dt * TPC + (t + 1) * 128],
                                         w[:, dt * DM + half * 512:dt * DM + (half + 1) * 512],
                                         start=(dt == 0), stop=(dt == 7))
                inc(o, s_mm)
                for dt in range(8):
                    o = e.matmul(psb[:, (t % 2) * H:(t % 2 + 1) * H],
                                 xnT[:, dt * TPC + t * 128:dt * TPC + (t + 1) * 128],
                                 wbeta[:, dt * H:(dt + 1) * H],
                                 start=(dt == 0), stop=(dt == 7))
                inc(o, s_mm)

        @blk.scalar
        def _(e):
            for t in range(NTILE):
                p = t % 2
                e.wait_ge(s_mm.h, 2 * t + 1)
                if t >= 2:
                    e.wait_ge(s_ap.h, t - 1)
                e.activation(qf[:, p * DM:(p + 1) * DM], ps[:, 0:DM], AF.Copy, bias=0.0)
                e.activation(kf[:, p * DM:(p + 1) * DM], ps[:, DM:2 * DM], AF.Copy, bias=0.0)
                o = e.activation(v_tm[:, t * DM:(t + 1) * DM], ps[:, 2 * DM:3 * DM],
                                 AF.Copy, bias=0.0)
                inc(o, s_ev)
                e.wait_ge(s_mm.h, 2 * t + 2)
                o = e.activation(beta_sb[:, t * H:(t + 1) * H], psb[:, p * H:(p + 1) * H],
                                 AF.Sigmoid)
                inc(o, s_ev)
                e.wait_ge(s_ss.h, t + 1)
                e.activation(rs[:, p * 8:p * 8 + 8], rs[:, p * 8:p * 8 + 8], AF.Ln)
                o = e.activation(rs[:, p * 8:p * 8 + 8], rs[:, p * 8:p * 8 + 8],
                                 AF.Exp, scale=ctx.cst(-0.5))
                inc(o, s_rs)

        @blk.vector
        def _(e):
            for t in range(NTILE):
                p = t % 2
                e.wait_ge(s_ev.h, 2 * t + 1)
                e.tensor_tensor(sq[:, :], qf[:, p * DM:(p + 1) * DM],
                                qf[:, p * DM:(p + 1) * DM], op=ALU.mult)
                e.tensor_reduce(rs[:, p * 8:p * 8 + 4],
                                sq[:, :].rearrange("a (h d) -> a h d", d=DK),
                                axis=AX.X, op=ALU.add)
                e.tensor_tensor(sq[:, :], kf[:, p * DM:(p + 1) * DM],
                                kf[:, p * DM:(p + 1) * DM], op=ALU.mult)
                o = e.tensor_reduce(rs[:, p * 8 + 4:p * 8 + 8],
                                    sq[:, :].rearrange("a (h d) -> a h d", d=DK),
                                    axis=AX.X, op=ALU.add)
                inc(o, s_ss)
                e.wait_ge(s_rs.h, t + 1)
                e.tensor_tensor(q_tm[:, t * DM:(t + 1) * DM].rearrange("a (h d) -> a h d", d=DK),
                                qf[:, p * DM:(p + 1) * DM].rearrange("a (h d) -> a h d", d=DK),
                                bass.AP(rs, p * 8, [[16, 128], [1, H], [0, DK]]),
                                op=ALU.mult)
                o = e.tensor_tensor(k_tm[:, t * DM:(t + 1) * DM].rearrange("a (h d) -> a h d", d=DK),
                                    kf[:, p * DM:(p + 1) * DM].rearrange("a (h d) -> a h d", d=DK),
                                    bass.AP(rs, p * 8 + 4, [[16, 128], [1, H], [0, DK]]),
                                    op=ALU.mult)
                inc(o, s_ap)
    ctx.srel()


def emit_attn_a_delta(ctx, est, q_tm, k_tm, v_tm, beta_sb, o_tm,
                      gin_bp, gout_bp, mask_slo, mask_trilq, idpat, id4,
                      pmask, pimask):
    nc = ctx.nc
    ctx.smark()
    sbf = lambda n, s, d: ctx.salloc(n, s, d)
    psf = lambda n, s, d: est.enter_context(nc.psum_tensor(n, s, d))

    w_fm = sbf("w_fm", [128, NCH_A * H * 256], BF16)
    u_tm = sbf("u_tm", [128, NCH_A * H * 256], BF16)
    s_sb = sbf("s_sb", [128, 2048], BF16)
    kf = sbf("kf", [128, 1024], BF16)
    e_sb = sbf("e_sb", [128, 1024], BF16)
    ctx.smark()
    bk = sbf("bk", [128, 1024], BF16)
    bv = sbf("bv", [128, 1024], BF16)
    a_sb = sbf("a_sb", [128, 512], BF16)
    at_sb = sbf("at_sb", [128, 512], BF16)
    apow = sbf("apow", [128, 7 * 512], BF16)
    m_sb = sbf("m_sb", [128, 2 * 512], BF16)
    tt_sb = sbf("tt_sb", [128, 512], BF16)

    ps_t = psf("ps_t", [128, 1024], BF16)
    ps_a = psf("ps_a", [128, 512], F32)
    ps_w = psf("ps_w", [128, 1024], F32)
    est2 = ExitStack()
    ps_b = est2.enter_context(nc.psum_tensor("ps_b", [128, 512], F32))
    ps_c = est2.enter_context(nc.psum_tensor("ps_c", [128, 512], F32))
    ps_s = None

    def wfm(n, h, dkt):
        return w_fm[:, ((n * H + h) * 2 + dkt) * 128:((n * H + h) * 2 + dkt) * 128 + 128]

    def utm(n, h):
        return u_tm[:, (n * H + h) * 256:(n * H + h) * 256 + 256]

    def ktm(n, h, dkt):
        return k_tm[:, n * DM + h * 256 + dkt * 128:n * DM + h * 256 + dkt * 128 + 128]

    def ssb(dkt, h):
        return s_sb[:, dkt * 1024 + h * 256:dkt * 1024 + h * 256 + 256]

    def ssb_t(dkt):
        return s_sb[:, dkt * 1024:(dkt + 1) * 1024]

    bcast_beta = lambda n: bass.AP(beta_sb, n * H, [[NTILE * H, 128], [1, H], [0, 256]])

    for n in range(NCH_A):
        ph = SerialPhase(ctx)
        ph.v(lambda e, n=n: e.tensor_tensor(
            bk[:, :].rearrange("a (h d) -> a h d", d=256),
            k_tm[:, n * DM:(n + 1) * DM].rearrange("a (h d) -> a h d", d=256),
            bcast_beta(n), op=ALU.mult))
        ph.v(lambda e, n=n: e.tensor_tensor(
            bv[:, :].rearrange("a (h d) -> a h d", d=256),
            v_tm[:, n * DM:(n + 1) * DM].rearrange("a (h d) -> a h d", d=256),
            bcast_beta(n), op=ALU.mult))

        def tp_k(e, n=n):
            for h in range(H):
                for dkt in range(2):
                    o = e.transpose(ps_t[:, (h * 2 + dkt) * 128:(h * 2 + dkt) * 128 + 128],
                                    ktm(n, h, dkt), ctx.ident[:, :])
            return o
        ph.t(tp_k)
        ph.v(lambda e: e.tensor_copy(kf[:, :], ps_t[:, :]))

        def mm_s(e, n=n):
            for h in range(H):
                for dkt in range(2):
                    o = e.matmul(ps_a[:, h * 128:(h + 1) * 128],
                                 kf[:, (h * 2 + dkt) * 128:(h * 2 + dkt) * 128 + 128],
                                 kf[:, (h * 2 + dkt) * 128:(h * 2 + dkt) * 128 + 128],
                                 start=(dkt == 0), stop=(dkt == 1))
            return o
        ph.t(mm_s)

        def ev_a(e, n=n):
            for h in range(H):
                o = e.scalar_tensor_tensor(
                    a_sb[:, h * 128:(h + 1) * 128], ps_a[:, h * 128:(h + 1) * 128],
                    beta_sb[:, n * H + h:n * H + h + 1], mask_slo[:, :],
                    op0=ALU.mult, op1=ALU.mult)
            return o
        ph.v(ev_a)
        ph.v(lambda e: e.tensor_copy(apow[:, 0:512], a_sb[:, :]))

        def tp_a(e):
            for h in range(H):
                o = e.transpose(ps_t[:, h * 128:(h + 1) * 128],
                                a_sb[:, h * 128:(h + 1) * 128], ctx.ident[:, :])
            return o
        ph.t(tp_a)
        ph.v(lambda e: e.tensor_copy(at_sb[:, :], ps_t[:, 0:512]))

        for j in range(1, 7):
            def mm_pow(e, j=j):
                for h in range(H):
                    sl = slice(h * 128, (h + 1) * 128)
                    e.matmul(ps_b[:, sl], at_sb[:, sl],
                             apow[:, (j - 1) * 512:(j - 1) * 512 + 512][:, sl],
                             start=True, stop=True)
                    o = e.matmul(ps_c[:, sl],
                                 apow[:, (j - 1) * 512:(j - 1) * 512 + 512][:, sl],
                                 at_sb[:, sl], start=True, stop=True)
                return o
            ph.t(mm_pow)
            ph.v(lambda e, j=j: e.tensor_copy(apow[:, j * 512:(j + 1) * 512], ps_b[:, :]))
            if j < 6:
                ph.s(lambda e: e.activation(at_sb[:, :], ps_c[:, :], AF.Copy, bias=0.0))
            else:
                ph.v(lambda e: e.tensor_tensor(m_sb[:, 0:512], ps_c[:, :],
                                               id4[:, :], op=ALU.add))

        mcur = 0
        for j in (5, 4, 3, 2, 1):
            def mm_ch(e, j=j, mcur=mcur):
                for h in range(H):
                    sl = slice(h * 128, (h + 1) * 128)
                    e.matmul(ps_a[:, sl], apow[:, j * 512:(j + 1) * 512][:, sl],
                             m_sb[:, mcur * 512:mcur * 512 + 512][:, sl],
                             start=True, stop=False)
                    o = e.matmul(ps_a[:, sl], ctx.ident[:, :],
                                 m_sb[:, mcur * 512:mcur * 512 + 512][:, sl],
                                 start=False, stop=True)
                return o
            ph.t(mm_ch)
            ph.v(lambda e, mcur=mcur: e.tensor_copy(
                m_sb[:, (1 - mcur) * 512:(1 - mcur) * 512 + 512], ps_a[:, :]))
            mcur = 1 - mcur

        def mm_last(e, mcur=mcur):
            for h in range(H):
                sl = slice(h * 128, (h + 1) * 128)
                o = e.matmul(ps_a[:, sl], apow[:, 0:512][:, sl],
                             m_sb[:, mcur * 512:mcur * 512 + 512][:, sl],
                             start=True, stop=True)
            return o
        ph.t(mm_last)
        ph.v(lambda e, mcur=mcur: e.tensor_tensor(
            tt_sb[:, :], m_sb[:, mcur * 512:mcur * 512 + 512], ps_a[:, :],
            op=ALU.subtract))

        def mm_w(e, n=n):
            for h in range(H):
                for dkt in range(2):
                    o = e.matmul(ps_w[:, (h * 2 + dkt) * 128:(h * 2 + dkt) * 128 + 128],
                                 bk[:, h * 256 + dkt * 128:h * 256 + dkt * 128 + 128],
                                 tt_sb[:, h * 128:(h + 1) * 128],
                                 start=True, stop=True)
            return o
        ph.t(mm_w)
        ph.v(lambda e, n=n: e.tensor_copy(
            w_fm[:, n * H * 256:(n + 1) * H * 256], ps_w[:, :]))

        def mm_u(e, n=n):
            for h in range(H):
                o = e.matmul(ps_w[:, h * 256:(h + 1) * 256],
                             tt_sb[:, h * 128:(h + 1) * 128],
                             bv[:, h * 256:(h + 1) * 256],
                             start=True, stop=True)
            return o
        ph.t(mm_u)
        ph.v(lambda e, n=n: e.tensor_copy(
            u_tm[:, n * H * 256:(n + 1) * H * 256], ps_w[:, :]))
        ph.run()

    est2.close()
    ps_s = psf("ps_s", [128, 2048], F32)
    ctx.srel()
    ctx.smark()
    pt_sb = sbf("pt_sb", [128, 2048], BF16)
    ptm = sbf("ptm", [128, 2048], BF16)
    bpin = sbf("bpin", [128, 2048], BF16)
    qf = sbf("qf", [128, 1024], BF16)
    mt_sb = sbf("mt_sb", [128, 512], BF16)
    ctx.dump("w_fm", w_fm[:, :])
    ctx.dump("u_tm", u_tm[:, :])

    btrace = None
    if "btrace" in ctx.dbg:
        btrace = nc.dram_tensor("dbg_btrace", [NCH_A, 128, 2048], BF16,
                                kind="ExternalOutput")
        ctx.dbg_outs["btrace"] = [NCH_A, 128, 2048]
    for pas in ("B", "P"):
        ph = SerialPhase(ctx)
        if pas == "B":
            ph.v(lambda e: e.memset(s_sb[:, :], 0.0))
        else:
            ph.v(lambda e: e.tensor_copy(s_sb[:, :], idpat[:, :]))
        for n in range(NCH_A):
            def mm_e(e, n=n):
                for h in range(H):
                    for dkt in range(2):
                        o = e.matmul(ps_w[:, h * 256:(h + 1) * 256],
                                     wfm(n, h, dkt), ssb(dkt, h),
                                     start=(dkt == 0), stop=(dkt == 1))
                return o
            ph.t(mm_e)
            if pas == "B":
                ph.v(lambda e, n=n: e.tensor_tensor(
                    e_sb[:, :], u_tm[:, n * H * 256:(n + 1) * H * 256],
                    ps_w[:, :], op=ALU.subtract))
            else:
                ph.s(lambda e: e.activation(e_sb[:, :], ps_w[:, :], AF.Copy,
                                            bias=0.0, scale=ctx.cst(-1.0)))

            def mm_su(e, n=n):
                for dkt in range(2):
                    for h in range(H):
                        o = e.matmul(ps_s[:, dkt * 1024 + h * 256:dkt * 1024 + (h) * 256 + 256],
                                     ktm(n, h, dkt), e_sb[:, h * 256:(h + 1) * 256],
                                     start=(n == 0 and h % 2 == 0),
                                     stop=(n == NCH_A - 1))
                return o
            ph.t(mm_su)
            if pas == "B":
                ph.v(lambda e: e.tensor_copy(ssb_t(0), ps_s[:, 0:1024]))
                ph.v(lambda e: e.tensor_copy(ssb_t(1), ps_s[:, 1024:2048]))
            else:
                ph.v(lambda e: e.tensor_tensor(ssb_t(0), ps_s[:, 0:1024],
                                               idpat[:, 0:1024], op=ALU.add))
                ph.v(lambda e: e.tensor_tensor(ssb_t(1), ps_s[:, 1024:2048],
                                               idpat[:, 1024:2048], op=ALU.add))
            if pas == "B" and btrace is not None:
                ph.dma("sync", lambda e, n=n: e.dma_start(out=btrace[n, :, :],
                                                          in_=s_sb[:, :]))
        if pas == "B":
            ph.dma("sync", lambda e: e.dma_start(out=gin_bp[:, 0:2048], in_=s_sb[:, :]))
        else:
            def tp_p(e):
                for dkt in range(2):
                    for h in range(H):
                        for d2 in range(2):
                            o = e.transpose(
                                ps_t[:, ((dkt * H + h) * 2 + d2) % 8 * 128:
                                     ((dkt * H + h) * 2 + d2) % 8 * 128 + 128],
                                s_sb[:, dkt * 1024 + h * 256 + d2 * 128:
                                     dkt * 1024 + h * 256 + d2 * 128 + 128],
                                ctx.ident[:, :])
                return o
            def tp_p1(e):
                for idx, (dkt, h, d2) in enumerate(
                        [(a, b, c) for a in range(2) for b in range(H) for c in range(2)][:8]):
                    o = e.transpose(ps_t[:, idx * 128:idx * 128 + 128],
                                    s_sb[:, dkt * 1024 + h * 256 + d2 * 128:
                                         dkt * 1024 + h * 256 + d2 * 128 + 128],
                                    ctx.ident[:, :])
                return o

            def ev_p1(e):
                for idx, (dkt, h, d2) in enumerate(
                        [(a, b, c) for a in range(2) for b in range(H) for c in range(2)][:8]):
                    o = e.tensor_copy(pt_sb[:, d2 * 1024 + h * 256 + dkt * 128:
                                            d2 * 1024 + h * 256 + dkt * 128 + 128],
                                      ps_t[:, idx * 128:idx * 128 + 128])
                return o

            def tp_p2(e):
                for idx, (dkt, h, d2) in enumerate(
                        [(a, b, c) for a in range(2) for b in range(H) for c in range(2)][8:]):
                    o = e.transpose(ps_t[:, idx * 128:idx * 128 + 128],
                                    s_sb[:, dkt * 1024 + h * 256 + d2 * 128:
                                         dkt * 1024 + h * 256 + d2 * 128 + 128],
                                    ctx.ident[:, :])
                return o

            def ev_p2(e):
                for idx, (dkt, h, d2) in enumerate(
                        [(a, b, c) for a in range(2) for b in range(H) for c in range(2)][8:]):
                    o = e.tensor_copy(pt_sb[:, d2 * 1024 + h * 256 + dkt * 128:
                                            d2 * 1024 + h * 256 + dkt * 128 + 128],
                                      ps_t[:, idx * 128:idx * 128 + 128])
                return o
            ph.t(tp_p1)
            ph.v(ev_p1)
            ph.t(tp_p2)
            ph.v(ev_p2)
            ph.dma("sync", lambda e: e.dma_start(out=gin_bp[:, 2048:4096], in_=pt_sb[:, :]))
        ph.run()

    ph = SerialPhase(ctx)
    ph.g(lambda e: e.collective_compute(
        "AllGather", ALU.bypass, replica_groups=RG,
        ins=[gin_bp.ap().opt()], outs=[gout_bp.ap().opt()]))
    ph.v(lambda e: e.memset(s_sb[:, :], 0.0))
    for i in range(NCORE - 1):
        ph.dma("sync", lambda e, i=i: e.dma_start(out=bpin[:, :],
                                                  in_=gout_bp[i, :, 2048:4096]))
        ph.v(lambda e, i=i: e.tensor_scalar(
            ptm[:, :], bpin[:, :], pmask[:, i:i + 1], None, op0=ALU.mult))
        ph.v(lambda e, i=i: e.scalar_tensor_tensor(
            ptm[:, :], idpat[:, :], pimask[:, i:i + 1], ptm[:, :],
            op0=ALU.mult, op1=ALU.add))
        ph.dma("sync", lambda e, i=i: e.dma_start(out=bpin[:, :],
                                                  in_=gout_bp[i, :, 0:2048]))

        def mm_s0(e, i=i):
            for dkt in range(2):
                for h in range(H):
                    for dkt2 in range(2):
                        o = e.matmul(
                            ps_s[:, dkt * 1024 + h * 256:dkt * 1024 + h * 256 + 256],
                            ptm[:, dkt2 * 1024 + h * 256 + dkt * 128:
                                dkt2 * 1024 + h * 256 + dkt * 128 + 128],
                            ssb(dkt2, h), start=(dkt2 == 0), stop=(dkt2 == 1))
            return o
        ph.t(mm_s0)
        ph.v(lambda e, i=i: e.scalar_tensor_tensor(
            s_sb[:, :], bpin[:, :], pmask[:, i:i + 1], ps_s[:, :],
            op0=ALU.mult, op1=ALU.add))
    ph.run()
    ctx.dump("s0", s_sb[:, :])

    for n in range(NCH_A):
        ph = SerialPhase(ctx)

        def tp_q(e, n=n):
            for h in range(H):
                for dkt in range(2):
                    o = e.transpose(ps_t[:, (h * 2 + dkt) * 128:(h * 2 + dkt) * 128 + 128],
                                    q_tm[:, n * DM + h * 256 + dkt * 128:
                                         n * DM + h * 256 + dkt * 128 + 128],
                                    ctx.ident[:, :])
            return o
        ph.t(tp_q)
        ph.v(lambda e: e.tensor_copy(qf[:, :], ps_t[:, :]))

        def tp_k2(e, n=n):
            for h in range(H):
                for dkt in range(2):
                    o = e.transpose(ps_t[:, (h * 2 + dkt) * 128:(h * 2 + dkt) * 128 + 128],
                                    ktm(n, h, dkt), ctx.ident[:, :])
            return o
        ph.t(tp_k2)
        ph.v(lambda e: e.tensor_copy(kf[:, :], ps_t[:, :]))

        def mm_e(e, n=n):
            for h in range(H):
                for dkt in range(2):
                    o = e.matmul(ps_w[:, h * 256:(h + 1) * 256],
                                 wfm(n, h, dkt), ssb(dkt, h),
                                 start=(dkt == 0), stop=(dkt == 1))
            return o
        ph.t(mm_e)
        ph.v(lambda e, n=n: e.tensor_tensor(
            e_sb[:, :], u_tm[:, n * H * 256:(n + 1) * H * 256], ps_w[:, :],
            op=ALU.subtract))

        def mm_mt(e):
            for h in range(H):
                for dkt in range(2):
                    o = e.matmul(ps_a[:, h * 128:(h + 1) * 128],
                                 kf[:, (h * 2 + dkt) * 128:(h * 2 + dkt) * 128 + 128],
                                 qf[:, (h * 2 + dkt) * 128:(h * 2 + dkt) * 128 + 128],
                                 start=(dkt == 0), stop=(dkt == 1))
            return o
        ph.t(mm_mt)
        ph.v(lambda e: e.tensor_tensor(
            mt_sb[:, :].rearrange("a (h c) -> a h c", c=128),
            ps_a[:, :].rearrange("a (h c) -> a h c", c=128),
            bass.AP(mask_trilq, 0, [[128, 128], [0, H], [1, 128]]),
            op=ALU.mult))

        def mm_o(e, n=n):
            for h in range(H):
                for dkt in range(2):
                    e.matmul(ps_w[:, h * 256:(h + 1) * 256],
                             qf[:, (h * 2 + dkt) * 128:(h * 2 + dkt) * 128 + 128],
                             ssb(dkt, h), start=(dkt == 0), stop=False)
                o = e.matmul(ps_w[:, h * 256:(h + 1) * 256],
                             mt_sb[:, h * 128:(h + 1) * 128],
                             e_sb[:, h * 256:(h + 1) * 256], start=False, stop=True)
            return o
        ph.t(mm_o)
        ph.v(lambda e, n=n: e.tensor_copy(o_tm[:, n * DM:(n + 1) * DM], ps_w[:, :]))

        def mm_su(e, n=n):
            for dkt in range(2):
                for h in range(H):
                    if n == 0:
                        e.matmul(ps_s[:, dkt * 1024 + h * 256:dkt * 1024 + h * 256 + 256],
                                 ctx.ident[:, :], ssb(dkt, h),
                                 start=(h % 2 == 0), stop=False)
                    o = e.matmul(ps_s[:, dkt * 1024 + h * 256:dkt * 1024 + h * 256 + 256],
                                 ktm(n, h, dkt), e_sb[:, h * 256:(h + 1) * 256],
                                 start=False, stop=(n == NCH_A - 1))
            return o
        ph.t(mm_su)
        ph.s(lambda e: e.activation(ssb_t(0), ps_s[:, 0:1024], AF.Copy, bias=0.0))
        ph.s(lambda e: e.activation(ssb_t(1), ps_s[:, 1024:2048], AF.Copy, bias=0.0))
        ph.run()
    ctx.srel()
    ctx.srel()
    ctx.dump("o_tm", o_tm[:, :])
    ctx.dump_dram("gin", gin_bp)


def emit_attn_b(ctx, est, h_in_d, h_out_d, w, dr):
    nc = ctx.nc
    ctx.smark()
    SW = 512
    NSG = TPC // SW
    CPS = SW // CB

    s_f = ctx.salloc("b_sf", [64, DI], F32)
    s_bf = ctx.salloc("b_sbf", [64, DI], BF16)
    qb_all = ctx.salloc("b_qball", [64, TPC], BF16)
    btot_all = ctx.salloc("b_btot", [64, NCH_B], F32)
    stats = ctx.salloc("b_stats", [128, 2 * NTILE], F32)
    carry = ctx.salloc("b_carry", [128, 16 * 3], BF16)
    cumg = ctx.salloc("b_cumg", [64, NCH_B + 8], F32)

    xn3 = ctx.salloc("b_xn3", [128, 2 * DM], BF16)

    ctx.smark()
    ln_tmp = ctx.salloc("b_lntmp", [128, DM], F32)
    bnst = ctx.salloc("b_bnst", [128, 16], F32)
    hg = ctx.salloc("b_hg", [128, 8 * 48], F32)

    est_ps = ExitStack()

    def ln_tile_emit(ph, t, dst_fm_tile_fn):
        p = t % 2
        ph.dma("sync", lambda e, t=t, p=p: e.dma_start(
            out=ln_tmp[:, 0:DM],
            in_=h_in_d[t * 128:(t + 1) * 128, :]))

        def stats_f(e, t=t, p=p):
            e.bn_stats(bnst[:, 0:6],
                       bass.AP(ln_tmp, 0, [[DM, 128], [1, 512]]))
            e.bn_stats(bnst[:, 6:12],
                       bass.AP(ln_tmp, 512, [[DM, 128], [1, 512]]))
            e.drain()
            o = e.bn_aggr(stats[:, 2 * t:2 * t + 2], bnst[:, 0:12])
            e.drain()
            return o
        ph.v(stats_f)
        ph.s(lambda e, t=t: e.activation(stats[:, 2 * t + 1:2 * t + 2],
                                         stats[:, 2 * t + 1:2 * t + 2],
                                         AF.Ln, bias=ctx.cst(1e-5)))
        ph.s(lambda e, t=t: e.activation(stats[:, 2 * t + 1:2 * t + 2],
                                         stats[:, 2 * t + 1:2 * t + 2],
                                         AF.Exp, scale=ctx.cst(-0.5)))
        ph.v(lambda e, t=t, p=p: e.tensor_scalar(
            xn3[:, p * DM:(p + 1) * DM], ln_tmp[:, 0:DM],
            stats[:, 2 * t:2 * t + 1], stats[:, 2 * t + 1:2 * t + 2],
            op0=ALU.subtract, op1=ALU.mult))

    ps_h = est_ps.enter_context(nc.psum_tensor("b_psh", [128, 2 * DM], BF16))
    ps_hf = est_ps.enter_context(nc.psum_tensor("b_pshf", [128, 64], F32))
    ph = SerialPhase(ctx)
    ph.dma("sync", lambda e: e.dma_start(out=ln_tmp[0:3, 0:DM],
                                         in_=h_in_d[TPC - 3:TPC, :]))

    def h_stats(e):
        e.bn_stats(bnst[0:3, 0:6], bass.AP(ln_tmp, 0, [[DM, 3], [1, 512]]))
        e.bn_stats(bnst[0:3, 6:12], bass.AP(ln_tmp, 512, [[DM, 3], [1, 512]]))
        e.drain()
        o = e.bn_aggr(bnst[0:3, 12:14], bnst[0:3, 0:12])
        e.drain()
        return o
    ph.v(h_stats)
    ph.s(lambda e: e.activation(bnst[:, 13:14], bnst[:, 13:14], AF.Ln,
                                bias=ctx.cst(1e-5)))
    ph.s(lambda e: e.activation(bnst[:, 13:14], bnst[:, 13:14], AF.Exp,
                                scale=ctx.cst(-0.5)))
    ph.v(lambda e: e.tensor_scalar(
        xn3[0:3, 0:DM], ln_tmp[0:3, 0:DM], bnst[0:3, 12:13], bnst[0:3, 13:14],
        op0=ALU.subtract, op1=ALU.mult))

    def h_tp(e):
        for i in range(8):
            o = e.transpose(ps_h[:, i * 128:i * 128 + 3],
                            xn3[0:3, i * 128:(i + 1) * 128], ctx.ident[0:3, 0:3])
        return o
    ph.t(h_tp)
    ph.v(lambda e: e.tensor_copy(xn3[:, DM:DM + 8 * 3].rearrange(
        "a (d c) -> a d c", c=3),
        bass.AP(ps_h, 0, [[2 * DM, 128], [128, 8], [1, 3]])))

    def h_mm(e):
        for cht in range(16):
            for dt in range(8):
                o = e.matmul(ps_hf[:, cht * 3:cht * 3 + 3],
                             w["upT"][:, dt * DI + cht * 128:dt * DI + cht * 128 + 128],
                             xn3[:, DM + dt * 3:DM + dt * 3 + 3],
                             start=(dt == 0), stop=(dt == 7))
        return o
    ph.t(h_mm)
    ph.v(lambda e: e.tensor_copy(hg[:, 0:48], ps_hf[:, 0:48]))
    ph.dma("sync", lambda e: e.dma_start(out=dr["gin_h"].ap(), in_=hg[:, 0:48]))
    ph.g(lambda e: e.collective_compute(
        "AllGather", ALU.bypass, replica_groups=RG,
        ins=[dr["gin_h"].ap().opt()], outs=[dr["gout_h"].ap().opt()]))
    ph.dma("sync", lambda e: e.dma_start(
        out=hg[:, :], in_=bass.AP(dr["gout_h"], 0, [[48, 128], [128 * 48, 8], [1, 48]])))

    def h_sel(e):
        e.memset(carry[:, :], 0.0)
        e.drain()
        for i in range(8):
            o = e.scalar_tensor_tensor(carry[:, :], hg[:, i * 48:(i + 1) * 48],
                                       w["hmask"][:, i:i + 1], carry[:, :],
                                       op0=ALU.mult, op1=ALU.add)
            e.drain()
        return o
    ph.v(h_sel)
    ph.run()
    est_ps.close()
    ctx.dump("carry", carry[:, :])

    xnT_s = ctx.salloc("b_xnTs", [128, 8 * SW], BF16)
    xup = ctx.salloc("b_xup", [128, 16 * SW], BF16)
    shf = ctx.salloc("b_shf", [128, 16 * SW], BF16)
    vfm = ctx.salloc("b_vfm", [128, 16 * SW], BF16)
    sh_tm = ctx.salloc("b_shtm", [128, DI], BF16)
    xcat = ctx.salloc("b_xcat", [128, SW + 3], BF16)
    acc_a = ctx.salloc("b_acca", [128, SW], F32)
    acc_b = ctx.salloc("b_accb", [128, SW], F32)
    qk_t = ctx.salloc("b_qkt", [128, SW], BF16)
    gt_t = ctx.salloc("b_gtt", [128, SW], F32)
    k_sb = ctx.salloc("b_ksb", [64, SW], F32)
    tau_s = ctx.salloc("b_taus", [64, SW], F32)
    gl_s = ctx.salloc("b_gls", [64, SW], F32)
    it_s = ctx.salloc("b_its", [64, SW], F32)
    rt_s = ctx.salloc("b_rts", [64, SW], F32)
    b_s = ctx.salloc("b_bs", [64, SW], F32)
    eb_s = ctx.salloc("b_ebs", [64, SW], BF16)
    sc_a = ctx.salloc("b_sca", [64, SW], F32)
    qs_s = ctx.salloc("b_qss", [64, SW], BF16)
    ks_s = ctx.salloc("b_kss", [64, SW], BF16)
    kd_s = ctx.salloc("b_kds", [64, SW], BF16)
    kit = ctx.salloc("b_kit", [64, SW], BF16)
    r16 = ctx.salloc("b_r16", [16, SW], BF16)
    seg_sc = ctx.salloc("b_segsc", [64, 24], F32)
    rr_f = ctx.salloc("b_rrf", [1, SW], F32)
    r_row = ctx.salloc("b_rrow", [1, SW], BF16)
    k2b = ctx.salloc("b_k2b", [64, SW], BF16)
    r_col = ctx.salloc("b_rcol", [64, 8], F32)
    v_tm = ctx.salloc("b_vtm", [64, DI], BF16)
    kdt = ctx.salloc("b_kdt", [64, 64], BF16)
    a_p = ctx.salloc("b_ap", [64, 64], BF16)
    o_ch = ctx.salloc("b_och", [64, DI], BF16)
    onescol = ctx.salloc("b_ones", [64, 1], BF16)

    est_p1 = ExitStack()
    ps_q = est_p1.enter_context(nc.psum_tensor("b_psq", [128, 1024], F32))
    ps_t = est_p1.enter_context(nc.psum_tensor("b_pst", [128, 2048], BF16))
    ps_o = est_p1.enter_context(nc.psum_tensor("b_pso", [64, 1024], F32))
    ps_ds = est_p1.enter_context(nc.psum_tensor("b_psds", [64, 1024], F32))
    ps_x = ps_q

    for sg in range(NSG):
        ph = SerialPhase(ctx)
        ph.v(lambda e: e.memset(onescol[:, :], 1.0))
        for tt_ in range(4):
            t = sg * 4 + tt_
            ln_tile_emit(ph, t, None)

            def tp_x(e, tt_=tt_, t=t):
                for i in range(8):
                    o = e.transpose(ps_t[:, i * 128:(i + 1) * 128],
                                    xn3[:, (t % 2) * DM + i * 128:(t % 2) * DM + (i + 1) * 128],
                                    ctx.ident[:, :])
                return o
            ph.t(tp_x)
            ph.v(lambda e, tt_=tt_: e.tensor_copy(
                bass.AP(xnT_s, tt_ * 128, [[8 * SW, 128], [SW, 8], [1, 128]]),
                bass.AP(ps_t, 0, [[2048, 128], [128, 8], [1, 128]])))

        for cht in range(16):
            def mm_up(e, cht=cht):
                for dt in range(8):
                    o = e.matmul(ps_x[:, 0:512],
                                 w["upT"][:, dt * DI + cht * 128:dt * DI + cht * 128 + 128],
                                 xnT_s[:, dt * SW:(dt + 1) * SW],
                                 start=(dt == 0), stop=(dt == 7))
                return o
            ph.t(mm_up)
            ph.s(lambda e, cht=cht: e.activation(
                xup[:, cht * SW:(cht + 1) * SW], ps_x[:, 0:512], AF.Copy, bias=0.0))

        for cht in range(16):
            ph.v(lambda e, cht=cht: e.tensor_copy(xcat[:, 0:3],
                                                  carry[:, cht * 3:(cht + 1) * 3]))
            ph.v(lambda e, cht=cht: e.tensor_copy(xcat[:, 3:SW + 3],
                                                  xup[:, cht * SW:(cht + 1) * SW]))
            ph.v(lambda e, cht=cht: e.tensor_scalar(
                acc_a[:, :], xcat[:, 0:SW], w["cwcol"][:, cht * 4:cht * 4 + 1],
                None, op0=ALU.mult))
            ph.v(lambda e, cht=cht: e.scalar_tensor_tensor(
                acc_b[:, :], xcat[:, 1:SW + 1], w["cwcol"][:, cht * 4 + 1:cht * 4 + 2],
                acc_a[:, :], op0=ALU.mult, op1=ALU.add))
            ph.v(lambda e, cht=cht: e.scalar_tensor_tensor(
                acc_a[:, :], xcat[:, 2:SW + 2], w["cwcol"][:, cht * 4 + 2:cht * 4 + 3],
                acc_b[:, :], op0=ALU.mult, op1=ALU.add))
            ph.v(lambda e, cht=cht: e.scalar_tensor_tensor(
                acc_b[:, :], xcat[:, 3:SW + 3], w["cwcol"][:, cht * 4 + 3:cht * 4 + 4],
                acc_a[:, :], op0=ALU.mult, op1=ALU.add))
            ph.s(lambda e, cht=cht: e.activation(
                shf[:, cht * SW:(cht + 1) * SW], acc_b[:, :], AF.Silu,
                bias=w["cbcol"][:, cht:cht + 1]))
            if sg < NSG - 1:
                ph.v(lambda e, cht=cht: e.tensor_copy(
                    carry[:, cht * 3:(cht + 1) * 3],
                    xup[:, (cht + 1) * SW - 3:(cht + 1) * SW]))

        for tt_ in range(4):
            def tp_s(e, tt_=tt_):
                for i in range(16):
                    o = e.transpose(ps_t[:, i * 128:(i + 1) * 128],
                                    shf[:, i * SW + tt_ * 128:i * SW + tt_ * 128 + 128],
                                    ctx.ident[:, :])
                return o
            ph.t(tp_s)
            ph.v(lambda e, tt_=tt_: e.tensor_copy(sh_tm[:, :], ps_t[:, :]))
            ph.dma("sync", lambda e, tt_=tt_, sg=sg: e.dma_start(
                out=dr["sh_d"][(sg * 4 + tt_) * 128:(sg * 4 + tt_ + 1) * 128, :],
                in_=sh_tm[:, :]))

        for cht in range(16):
            def mm_r1(e, cht=cht):
                o = e.matmul(ps_q[0:16, 0:512],
                             w["ig1T"][:, cht * 16:(cht + 1) * 16],
                             xup[:, cht * SW:(cht + 1) * SW],
                             start=(cht == 0), stop=(cht == 15))
                return o
            ph.t(mm_r1)
        ph.v(lambda e: e.tensor_copy(r16[:, :], ps_q[0:16, 0:512]))
        for cht in range(16):
            def mm_r2(e, cht=cht):
                o = e.matmul(ps_x[:, 0:512],
                             w["ig2T"][:, cht * 128:(cht + 1) * 128],
                             r16[:, :], start=True, stop=True)
                return o
            ph.t(mm_r2)
            ph.s(lambda e, cht=cht: e.activation(
                acc_a[:, :], ps_x[:, 0:512], AF.Sigmoid, bias=w["igb2col"][:, cht:cht + 1]))
            ph.v(lambda e, cht=cht: e.tensor_tensor(
                vfm[:, cht * SW:(cht + 1) * SW], acc_a[:, :],
                xup[:, cht * SW:(cht + 1) * SW], op=ALU.mult))

        for cht in range(16):
            def mm_qk(e, cht=cht):
                for of in range(2):
                    o = e.matmul(ps_q[:, of * 512:(of + 1) * 512],
                                 w["qkgtT"][:, cht * 256 + of * 128:cht * 256 + of * 128 + 128],
                                 shf[:, cht * SW:(cht + 1) * SW],
                                 start=(cht == 0), stop=(cht == 15))
                return o
            ph.t(mm_qk)
        ph.v(lambda e: e.tensor_copy(qk_t[:, :], ps_q[:, 0:512]))
        ph.s(lambda e: e.activation(gt_t[:, :], ps_q[:, 512:1024], AF.Copy, bias=0.0))
        ph.dma("gpsimd", lambda e: e.dma_start(out=k_sb[:, :], in_=qk_t[64:128, :]))
        ph.s(lambda e: e.activation(sc_a[:, :], gt_t[0:64, :], AF.Abs,
                                    bias=w["gbc"][0:64, 0:1]))
        ph.s(lambda e: e.activation(sc_a[:, :], sc_a[:, :], AF.Exp,
                                    scale=ctx.cst64(-1.0)))
        ph.s(lambda e: e.activation(sc_a[:, :], sc_a[:, :], AF.Ln,
                                    bias=ctx.cst64(1.0)))
        ph.s(lambda e: e.activation(gl_s[:, :], gt_t[0:64, :], AF.Relu,
                                    bias=w["gbc"][0:64, 0:1]))
        ph.v(lambda e: e.tensor_tensor(gl_s[:, :], gl_s[:, :], sc_a[:, :],
                                       op=ALU.add))
        ph.dma("gpsimd", lambda e: e.dma_start(out=tau_s[:, :], in_=gt_t[64:128, :]))
        ph.s(lambda e: e.activation(tau_s[:, :], tau_s[:, :], AF.Sigmoid,
                                    bias=w["tbc"][0:64, 0:1]))
        ph.s(lambda e: e.activation(it_s[:, :], gl_s[:, :], AF.Ln))
        ph.v(lambda e: e.tensor_tensor(it_s[:, :], it_s[:, :], tau_s[:, :], op=ALU.mult))
        ph.s(lambda e: e.activation(it_s[:, :], it_s[:, :], AF.Exp))
        ph.v(lambda e: e.scalar_tensor_tensor(rt_s[:, :], gl_s[:, :], ctx.cst64(-1.0),
                                              tau_s[:, :], op0=ALU.mult, op1=ALU.mult))
        ph.v(lambda e: e.tensor_tensor_scan(b_s[:, :], w["smask"][0:64, :], rt_s[:, :],
                                            0.0, op0=ALU.mult, op1=ALU.add))
        ph.v(lambda e: e.tensor_copy(seg_sc[:, 0:8],
                                     bass.AP(b_s, 63, [[SW, 64], [64, 8]])))
        ph.v(lambda e: e.tensor_copy(seg_sc[:, 8:16],
                                     bass.AP(b_s, 31, [[SW, 64], [64, 8]])))
        ph.s(lambda e: e.activation(seg_sc[:, 16:24], seg_sc[:, 0:8], AF.Exp))
        ph.v(lambda e, sg=sg: e.tensor_copy(btot_all[:, sg * 8:(sg + 1) * 8],
                                            seg_sc[:, 0:8]))
        ph.s(lambda e: e.activation(eb_s[:, :], b_s[:, :], AF.Exp))
        ph.v(lambda e, sg=sg: e.scalar_tensor_tensor(
            qb_all[:, sg * SW:(sg + 1) * SW], qk_t[0:64, :], ctx.cst64(0.125),
            eb_s[:, :], op0=ALU.mult, op1=ALU.mult))
        ph.v(lambda e: e.tensor_tensor(
            sc_a[:, :].rearrange("a (c x) -> a c x", x=CB),
            b_s[:, :].rearrange("a (c x) -> a c x", x=CB),
            bass.AP(seg_sc, 8, [[24, 64], [1, 8], [0, CB]]), op=ALU.subtract))
        ph.s(lambda e: e.activation(sc_a[:, :], sc_a[:, :], AF.Exp))
        ph.v(lambda e: e.scalar_tensor_tensor(
            qs_s[:, :], qk_t[0:64, :], ctx.cst64(0.125), sc_a[:, :],
            op0=ALU.mult, op1=ALU.mult))
        ph.v(lambda e: e.tensor_tensor(kit[:, :], k_sb[:, :], it_s[:, :], op=ALU.mult))
        ph.v(lambda e: e.tensor_tensor(
            sc_a[:, :].rearrange("a (c x) -> a c x", x=CB),
            bass.AP(seg_sc, 8, [[24, 64], [1, 8], [0, CB]]),
            b_s[:, :].rearrange("a (c x) -> a c x", x=CB), op=ALU.subtract))
        ph.s(lambda e: e.activation(sc_a[:, :], sc_a[:, :], AF.Exp))
        ph.v(lambda e: e.tensor_tensor(ks_s[:, :], kit[:, :], sc_a[:, :], op=ALU.mult))
        ph.v(lambda e: e.tensor_tensor(
            sc_a[:, :].rearrange("a (c x) -> a c x", x=CB),
            bass.AP(seg_sc, 0, [[24, 64], [1, 8], [0, CB]]),
            b_s[:, :].rearrange("a (c x) -> a c x", x=CB), op=ALU.subtract))
        ph.s(lambda e: e.activation(sc_a[:, :], sc_a[:, :], AF.Exp))
        ph.v(lambda e: e.tensor_tensor(kd_s[:, :], kit[:, :], sc_a[:, :], op=ALU.mult))
        ph.v(lambda e: e.tensor_tensor(k2b[:, :], k_sb[:, :], k_sb[:, :], op=ALU.mult))
        ph.t(lambda e: e.matmul(ps_o[0:1, 0:512], onescol[:, :], k2b[:, :],
                                start=True, stop=True))
        ph.v(lambda e: e.tensor_scalar(rr_f[:, :], ps_o[0:1, 0:512], 1e-24, None,
                                       op0=ALU.max))
        ph.s(lambda e: e.activation(rr_f[:, :], rr_f[:, :], AF.Ln))
        ph.s(lambda e: e.activation(r_row[:, :], rr_f[:, :], AF.Exp,
                                    scale=ctx.cvec[0:1, CIDX[-0.5]:CIDX[-0.5] + 1]))

        def tp_r(e):
            for cc in range(8):
                o = e.transpose(ps_t[0:64, 2 * cc:2 * cc + 1],
                                r_row[:, cc * 64:(cc + 1) * 64], ctx.ident[0:1, 0:1])
            return o
        ph.t(tp_r)
        ph.v(lambda e: e.tensor_copy(r_col[:, :],
                                     bass.AP(ps_t, 0, [[2048, 64], [2, 8]])))

        for cc in range(CPS):
            n = sg * CPS + cc
            def tp_v(e, cc=cc):
                for i in range(16):
                    o = e.transpose(ps_t[0:64, i * 128:(i + 1) * 128],
                                    vfm[:, i * SW + cc * CB:i * SW + (cc + 1) * CB],
                                    ctx.ident[:, :])
                return o
            ph.t(tp_v)
            ph.v(lambda e: e.tensor_copy(v_tm[:, :], ps_t[0:64, :]))
            ph.t(lambda e, cc=cc: e.transpose(
                ps_o[0:64, 0:64], kd_s[:, cc * CB:(cc + 1) * CB], ctx.ident[:, :])
                if False else
                e.transpose(ps_t[0:64, 0:64], kd_s[:, cc * CB:(cc + 1) * CB],
                            ctx.ident[0:64, 0:64]))
            ph.v(lambda e, cc=cc: e.tensor_scalar(
                kdt[:, :], ps_t[0:64, 0:64], r_col[:, cc:cc + 1], None, op0=ALU.mult))
            ph.t(lambda e, cc=cc: e.matmul(
                ps_ds[:, 0:64], ks_s[:, cc * CB:(cc + 1) * CB],
                qs_s[:, cc * CB:(cc + 1) * CB], start=True, stop=True))
            ph.v(lambda e, cc=cc: e.scalar_tensor_tensor(
                a_p[:, :], ps_ds[:, 0:64], r_col[:, cc:cc + 1],
                w["m64"], op0=ALU.mult, op1=ALU.mult))
            for vh in range(2):
                def mm_o(e, cc=cc, vh=vh):
                    for j in range(2):
                        e.matmul(ps_o[:, j * 512:(j + 1) * 512],
                                 qb_all[:, sg * SW + cc * CB:sg * SW + (cc + 1) * CB],
                                 s_bf[:, vh * 1024 + j * 512:vh * 1024 + (j + 1) * 512],
                                 start=True, stop=False)
                        o = e.matmul(ps_o[:, j * 512:(j + 1) * 512],
                                     a_p[:, :],
                                     v_tm[:, vh * 1024 + j * 512:vh * 1024 + (j + 1) * 512],
                                     start=False, stop=True)
                    return o
                ph.t(mm_o)

                def mm_ds(e, cc=cc, vh=vh):
                    for j in range(2):
                        o = e.matmul(ps_ds[:, j * 512:(j + 1) * 512],
                                     kdt[:, :],
                                     v_tm[:, vh * 1024 + j * 512:vh * 1024 + (j + 1) * 512],
                                     start=True, stop=True)
                    return o
                ph.t(mm_ds)
                ph.v(lambda e, vh=vh: e.tensor_copy(
                    o_ch[:, vh * 1024:(vh + 1) * 1024], ps_o[:, :]))
                ph.v(lambda e, cc=cc, vh=vh: e.scalar_tensor_tensor(
                    s_f[:, vh * 1024:(vh + 1) * 1024],
                    s_f[:, vh * 1024:(vh + 1) * 1024],
                    seg_sc[:, 16 + cc:17 + cc], ps_ds[:, :],
                    op0=ALU.mult, op1=ALU.add))
                ph.s(lambda e, vh=vh: e.activation(
                    s_bf[:, vh * 1024:(vh + 1) * 1024],
                    s_f[:, vh * 1024:(vh + 1) * 1024], AF.Copy, bias=0.0))
            ph.dma("sync", lambda e, n=n: e.dma_start(
                out=dr["o_d"][n * CB:(n + 1) * CB, :], in_=o_ch[:, :]))
        ph.run()

    ph = SerialPhase(ctx)
    ph.v(lambda e: e.tensor_tensor_scan(cumg[:, 0:NCH_B],
                                        w["ones64"][0:64, 0:NCH_B],
                                        btot_all[:, :], 0.0,
                                        op0=ALU.mult, op1=ALU.add))
    ph.v(lambda e: e.memset(cumg[:, NCH_B:NCH_B + 8], 0.0))
    ph.s(lambda e: e.activation(cumg[:, NCH_B + 1:NCH_B + 2],
                                cumg[:, NCH_B - 1:NCH_B], AF.Exp))
    ph.dma("sync", lambda e: e.dma_start(out=dr["gin_g"][:, 0:DI], in_=s_f[:, :]))
    ph.dma("sync", lambda e: e.dma_start(out=dr["gin_g"][:, DI:DI + 1],
                                         in_=cumg[:, NCH_B + 1:NCH_B + 2],
                                         allow_slow_non_contiguous=True))
    ph.g(lambda e: e.collective_compute(
        "AllGather", ALU.bypass, replica_groups=RG,
        ins=[dr["gin_g"].ap().opt()], outs=[dr["gout_g"].ap().opt()]))
    ph.run()
    ctx.srel()

    ctx.smark()
    gg = ctx.salloc("b_gg", [64, DI + 32], F32)
    gsc = ctx.salloc("b_gsc", [64, 8], F32)
    ph = SerialPhase(ctx)
    ph.v(lambda e: e.memset(s_f[:, :], 0.0))
    for i in range(NCORE - 1):
        ph.dma("sync", lambda e, i=i: e.dma_start(out=gg[:, 0:DI + 1],
                                                  in_=dr["gout_g"][i, :, 0:DI + 1]))
        ph.v(lambda e, i=i: e.tensor_scalar(
            gsc[:, 0:1], gg[:, DI:DI + 1], w["pm"][0:64, i:i + 1],
            w["pim"][0:64, i:i + 1], op0=ALU.mult, op1=ALU.add))
        ph.v(lambda e, i=i: e.tensor_scalar(
            gg[:, 0:DI], gg[:, 0:DI], w["pm"][0:64, i:i + 1], None, op0=ALU.mult))
        ph.v(lambda e, i=i: e.scalar_tensor_tensor(
            s_f[:, :], s_f[:, :], gsc[:, 0:1], gg[:, 0:DI],
            op0=ALU.mult, op1=ALU.add))
    ph.s(lambda e: e.activation(s_bf[:, :], s_f[:, :], AF.Copy, bias=0.0))
    ph.run()
    ctx.dump("s0b", s_f[:, :])

    ctx.smark()
    w["gateT"] = ctx.salloc("b_gateT", [128, 8 * DI], BF16)
    w["downT"] = ctx.salloc("b_downT", [128, 16 * DM], BF16)
    ecg = ctx.salloc("b_ecg", [64, NCH_B], F32)
    qbd = ctx.salloc("b_qbd", [64, 2 * CB], BF16)
    o_t = ctx.salloc("b_ot", [128, DI], BF16)
    sh_t = ctx.salloc("b_sht", [128, DI], BF16)
    g2_t = ctx.salloc("b_g2t", [128, DI], BF16)
    m1_t = ctx.salloc("b_m1t", [128, DI], BF16)
    xg2_t = ctx.salloc("b_xg2t", [128, DI], BF16)
    xgf_t = ctx.salloc("b_xgft", [128, DI], BF16)
    xgT_t = ctx.salloc("b_xgTt", [128, DI], BF16)
    ht2 = ctx.salloc("b_ht2", [128, DM], F32)
    rms_c = ctx.salloc("b_rmsc", [128, 4], F32)
    sqs = ctx.salloc("b_sqs", [128, DI], BF16)

    ph = SerialPhase(ctx)
    ph.dma("gpsimd", lambda e: e.dma_start(
        out=w["gateT"][:, :],
        in_=bass.AP(dr["gateT_d"], 0, [[DI, 128], [128 * DI, 8], [1, DI]])))
    ph.dma("gpsimd", lambda e: e.dma_start(
        out=w["downT"][:, :],
        in_=bass.AP(dr["downT_d"], 0, [[DM, 128], [128 * DM, 16], [1, DM]])))
    ph.v(lambda e: e.memset(ecg[:, 0:1], 0.0))
    ph.v(lambda e: e.tensor_copy(ecg[:, 1:NCH_B], cumg[:, 0:NCH_B - 1]))
    ph.s(lambda e: e.activation(ecg[:, :], ecg[:, :], AF.Exp))
    ph.run()

    est_p1.close()
    ps_big = est.enter_context(nc.psum_tensor("b_psbig", [128, DI], F32))
    ps_tp2 = est.enter_context(nc.psum_tensor("b_pstp2", [128, DI], BF16))
    ps_dn = est.enter_context(nc.psum_tensor("b_psdn", [128, DM], F32))

    for t in range(NTILE):
        ph = SerialPhase(ctx)
        p = t % 2
        ph.dma("sync", lambda e, t=t: e.dma_start(
            out=ht2[:, :], in_=h_in_d[t * 128:(t + 1) * 128, :]))
        ph.v(lambda e, t=t: e.tensor_scalar(
            xn3[:, 0:DM], ht2[:, :],
            stats[:, 2 * t:2 * t + 1], stats[:, 2 * t + 1:2 * t + 2],
            op0=ALU.subtract, op1=ALU.mult))

        def tp_xn(e):
            for i in range(8):
                o = e.transpose(ps_tp2[:, i * 128:(i + 1) * 128],
                                xn3[:, i * 128:(i + 1) * 128],
                                ctx.ident[:, :])
            return o
        ph.t(tp_xn)
        ph.v(lambda e: e.tensor_copy(xgT_t[:, 0:1024], ps_tp2[:, 0:1024]))

        def mm_fg(e):
            for q4 in range(4):
                for dt in range(8):
                    o = e.matmul(ps_big[:, q4 * 512:(q4 + 1) * 512],
                                 xgT_t[:, dt * 128:(dt + 1) * 128],
                                 w["gateT"][:, dt * DI + q4 * 512:dt * DI + (q4 + 1) * 512],
                                 start=(dt == 0), stop=(dt == 7))
            return o
        ph.t(mm_fg)
        ph.s(lambda e: e.activation(g2_t[:, :], ps_big[:, :], AF.Silu))

        ph.dma("sync", lambda e, t=t: e.dma_start(
            out=o_t[:, :], in_=dr["o_d"][t * 128:(t + 1) * 128, :]))
        for half in range(2):
            def qbd_f(e, t=t, half=half):
                o = e.tensor_scalar(
                    qbd[:, half * CB:(half + 1) * CB],
                    qb_all[:, t * 128 + half * CB:t * 128 + (half + 1) * CB],
                    ecg[:, 2 * t + half:2 * t + half + 1], None, op0=ALU.mult)
                return o
            ph.v(qbd_f)

        def mm_oc(e, t=t):
            for half in range(2):
                for j in range(4):
                    o = e.matmul(ps_big[0:64, j * 512:(j + 1) * 512] if half == 0
                                 else ps_big[64:128, j * 512:(j + 1) * 512],
                                 qbd[:, half * CB:(half + 1) * CB],
                                 s_bf[:, j * 512:(j + 1) * 512],
                                 start=True, stop=True,
                                 tile_position=(0, 64) if half == 1 else None)
            return o
        ph.t(mm_oc)
        ph.v(lambda e: e.tensor_tensor(o_t[:, :], o_t[:, :],
                                       ps_big[:, :], op=ALU.add))

        ph.dma("sync", lambda e, t=t: e.dma_start(
            out=sh_t[:, :], in_=dr["sh_d"][t * 128:(t + 1) * 128, :]))
        ph.v(lambda e: e.tensor_tensor(m1_t[:, :], sh_t[:, :], w["rwrow"][:, :],
                                       op=ALU.mult))
        ph.v(lambda e: e.tensor_tensor(m1_t[:, :], m1_t[:, :], o_t[:, :],
                                       op=ALU.add))
        ph.v(lambda e: e.tensor_tensor(xg2_t[:, :], m1_t[:, :], g2_t[:, :],
                                       op=ALU.mult))
        ph.s(lambda e: e.activation(sqs[:, :], xg2_t[:, :], AF.Square,
                                    accum_out=rms_c[:, 0:1]))
        ph.s(lambda e: e.activation(rms_c[:, 1:2], rms_c[:, 0:1], AF.Ln,
                                    scale=ctx.cst(1.0 / DI), bias=ctx.cst(1e-5)))
        ph.s(lambda e: e.activation(rms_c[:, 1:2], rms_c[:, 1:2], AF.Exp,
                                    scale=ctx.cst(-0.5)))
        ph.v(lambda e: e.tensor_tensor(xgf_t[:, :], xg2_t[:, :], w["nwrow"][:, :],
                                       op=ALU.mult))
        ph.v(lambda e: e.tensor_scalar(xgf_t[:, :], xgf_t[:, :], rms_c[:, 1:2],
                                       None, op0=ALU.mult))

        def tp_xg1(e):
            for i in range(8):
                o = e.transpose(ps_tp2[:, i * 128:(i + 1) * 128],
                                xgf_t[:, i * 128:(i + 1) * 128], ctx.ident[:, :])
            return o

        def tp_xg2_(e):
            for i in range(8, 16):
                o = e.transpose(ps_tp2[:, (i - 8) * 128:(i - 7) * 128],
                                xgf_t[:, i * 128:(i + 1) * 128], ctx.ident[:, :])
            return o
        ph.t(tp_xg1)
        ph.v(lambda e: e.tensor_copy(xgT_t[:, 0:1024], ps_tp2[:, 0:1024]))
        ph.t(tp_xg2_)
        ph.v(lambda e: e.tensor_copy(xgT_t[:, 1024:2048], ps_tp2[:, 0:1024]))

        def mm_dn(e):
            for half in range(2):
                for cht in range(16):
                    o = e.matmul(ps_dn[:, half * 512:(half + 1) * 512],
                                 xgT_t[:, cht * 128:(cht + 1) * 128],
                                 w["downT"][:, cht * DM + half * 512:cht * DM + (half + 1) * 512],
                                 start=(cht == 0), stop=(cht == 15))
            return o
        ph.t(mm_dn)
        ph.v(lambda e: e.tensor_tensor(ht2[:, :], ht2[:, :],
                                       ps_dn[:, :], op=ALU.add))
        ph.dma("sync", lambda e, t=t: e.dma_start(
            out=h_out_d[t * 128:(t + 1) * 128, :], in_=ht2[:, :]))
        ph.run()
    ctx.srel()
    ctx.srel()
    ctx.srel()


def emit_ffn(ctx, est, tag, h_in_d, h_out_d, phi_sb, wsh_d, bsh_d,
             gin_s, gout_s, gin_y, gout_y):
    nc = ctx.nc
    ctx.smark()
    xnT = ctx.salloc(f"f{tag}_xnT", [128, 8 * TPC], BF16)
    s_col = ctx.salloc(f"f{tag}_scol", [128, 8], F32)
    s_bf = ctx.salloc(f"f{tag}_sbf", [128, 8], BF16)
    sg = ctx.salloc(f"f{tag}_sg", [128, 64], F32)
    w_e = ctx.salloc(f"f{tag}_we", [128, 2 * 8 * DM], BF16)
    y_loc = ctx.salloc(f"f{tag}_yloc", [1, EPC * DM], F32)
    b_loc = ctx.salloc(f"f{tag}_bloc", [1, EPC * DM], F32)
    y_bf = ctx.salloc(f"f{tag}_ybf", [64, DM], BF16)
    el = ctx.salloc(f"f{tag}_el", [128, 2048], BF16)
    zc = ctx.salloc(f"f{tag}_zc", [128, 4], F32)
    craw = ctx.salloc(f"f{tag}_craw", [128, 64], F32)
    c_bf = ctx.salloc(f"f{tag}_cbf", [128, 64], BF16)
    cT = ctx.salloc(f"f{tag}_cT", [64, NTILE * 128], BF16)
    ht = ctx.salloc(f"f{tag}_ht", [128, 2 * DM], F32)

    est_p = ExitStack()
    emit_ln_transpose(ctx, est_p, xnT, x_dram=h_in_d)
    est_p.close()

    est_y = ExitStack()
    ps_y = est_y.enter_context(nc.psum_tensor(f"f{tag}_psy", [1, DM], F32))

    ph = SerialPhase(ctx)
    ph.v(lambda e: e.tensor_reduce(
        s_col[:, :], xnT[:, :].rearrange("a (d t) -> a d t", t=TPC),
        axis=AX.X, op=ALU.add))
    ph.dma("sync", lambda e: e.dma_start(out=gin_s.ap(), in_=s_col[:, :]))
    ph.g(lambda e: e.collective_compute(
        "AllGather", ALU.bypass, replica_groups=RG,
        ins=[gin_s.ap().opt()], outs=[gout_s.ap().opt()]))
    ph.dma("sync", lambda e: e.dma_start(
        out=sg[:, :], in_=bass.AP(gout_s, 0, [[8, 128], [1024, 8], [1, 8]])))
    ph.dma("sync", lambda e: e.dma_start(out=b_loc[:, :], in_=bass.AP(bsh_d, 0, [[EPC * DM, 1], [1, EPC * DM]])))

    def sum_s(e):
        for i in range(1, 8):
            o = e.tensor_tensor(s_col[:, :], sg[:, 0:8] if i == 1 else s_col[:, :],
                                sg[:, i * 8:(i + 1) * 8], op=ALU.add)
            e.drain()
        return o
    ph.v(sum_s)
    ph.v(lambda e: e.tensor_copy(s_bf[:, :], s_col[:, :]))
    ph.run()
    ctx.dump("s_tot" + tag, s_col[:, :])

    ph = SerialPhase(ctx)
    for e_i in range(EPC):
        ph.dma("gpsimd", lambda e, e_i=e_i: e.dma_start(
            out=w_e[:, (e_i % 2) * 8 * DM:(e_i % 2 + 1) * 8 * DM],
            in_=bass.AP(wsh_d, e_i * DM * DM, [[DM, 128], [128 * DM, 8], [1, DM]])))

        def mm_y(e, e_i=e_i):
            for half in range(2):
                for dt in range(8):
                    o = e.matmul(ps_y[:, half * 512:(half + 1) * 512],
                                 s_bf[:, dt:dt + 1],
                                 w_e[:, (e_i % 2) * 8 * DM + dt * DM + half * 512:
                                     (e_i % 2) * 8 * DM + dt * DM + (half + 1) * 512],
                                 start=(dt == 0), stop=(dt == 7))
            return o
        ph.t(mm_y)
        ph.v(lambda e, e_i=e_i: e.tensor_copy(y_loc[:, e_i * DM:(e_i + 1) * DM], ps_y[:, :]))
    ph.v(lambda e: e.tensor_tensor(y_loc[:, :], y_loc[:, :], b_loc[:, :], op=ALU.add))
    ph.dma("sync", lambda e: e.dma_start(out=bass.AP(gin_y, 0, [[EPC * DM, 1], [1, EPC * DM]]), in_=y_loc[:, :]))
    ph.g(lambda e: e.collective_compute(
        "AllGather", ALU.bypass, replica_groups=RG,
        ins=[gin_y.ap().opt()], outs=[gout_y.ap().opt()]))
    ph.dma("gpsimd", lambda e: e.dma_start(
        out=y_bf[:, :], in_=bass.AP(gout_y, 0, [[DM, 64], [1, DM]])))
    ph.run()
    est_y.close()
    ps_l = est.enter_context(nc.psum_tensor(f"f{tag}_psl", [128, 2048], F32))
    ps_c = est.enter_context(nc.psum_tensor(f"f{tag}_psc", [64, 128], BF16))
    ps_o = est.enter_context(nc.psum_tensor(f"f{tag}_pso", [128, DM], F32))
    ctx.dump("y" + tag, y_bf[:, :])

    ph = SerialPhase(ctx)
    for t in range(NTILE):
        def mm_l(e, t=t):
            for q4 in range(4):
                for dt in range(8):
                    o = e.matmul(ps_l[:, q4 * 512:(q4 + 1) * 512],
                                 xnT[:, dt * TPC + t * 128:dt * TPC + (t + 1) * 128],
                                 phi_sb[:, dt * 2048 + q4 * 512:dt * 2048 + (q4 + 1) * 512],
                                 start=(dt == 0), stop=(dt == 7))
            return o
        ph.t(mm_l)
        ph.s(lambda e: e.activation(el[:, :], ps_l[:, :], AF.Exp,
                                    bias=ctx.cst(-8.0), accum_out=zc[:, 0:1]))
        ph.v(lambda e: e.tensor_reduce(
            craw[:, :], el[:, :].rearrange("a (n p) -> a n p", p=32),
            axis=AX.X, op=ALU.add))
        ph.v(lambda e: e.reciprocal(zc[:, 1:2], zc[:, 0:1]))
        ph.v(lambda e: e.tensor_scalar(c_bf[:, :], craw[:, :], zc[:, 1:2], None,
                                       op0=ALU.mult))

        def tp_c(e, t=t):
            return e.transpose(ps_c[:, :], c_bf[:, :], ctx.ident[:, :])
        ph.t(tp_c)
        ph.v(lambda e, t=t: e.tensor_copy(cT[:, t * 128:(t + 1) * 128], ps_c[:, :]))

        ph.dma("sync", lambda e, t=t: e.dma_start(
            out=ht[:, (t % 2) * DM:(t % 2 + 1) * DM],
            in_=h_in_d[t * 128:(t + 1) * 128, :]))

        def mm_o(e, t=t):
            for half in range(2):
                o = e.matmul(ps_o[:, half * 512:(half + 1) * 512],
                             cT[:, t * 128:(t + 1) * 128],
                             y_bf[:, half * 512:(half + 1) * 512],
                             start=True, stop=True)
            return o
        ph.t(mm_o)
        ph.v(lambda e, t=t: e.tensor_tensor(
            ht[:, (t % 2) * DM:(t % 2 + 1) * DM],
            ht[:, (t % 2) * DM:(t % 2 + 1) * DM], ps_o[:, :], op=ALU.add))
        ph.dma("sync", lambda e, t=t: e.dma_start(
            out=h_out_d[t * 128:(t + 1) * 128, :],
            in_=ht[:, (t % 2) * DM:(t % 2 + 1) * DM]))
    ph.run()
    ctx.srel()


def build_program(dbg=()):
    nc = bass.Bass("TRN2", target_bir_lowering=False)
    ctx = Ctx(nc, dbg)

    x_d = nc.dram_tensor("x", [TPC, DM], F32, kind="ExternalInput")
    wq_d = nc.dram_tensor("wq", [DM, DM], F32, kind="ExternalInput")
    wk_d = nc.dram_tensor("wk", [DM, DM], F32, kind="ExternalInput")
    wv_d = nc.dram_tensor("wv", [DM, DM], F32, kind="ExternalInput")
    wb_d = nc.dram_tensor("wbeta", [DM, H], F32, kind="ExternalInput")
    wo_d = nc.dram_tensor("wo", [DM, DM], F32, kind="ExternalInput")
    ident_d = nc.dram_tensor("ident", [128, 128], BF16, kind="ExternalInput")
    cvec_d = nc.dram_tensor("cvec", [128, len(CONSTS)], F32, kind="ExternalInput")
    m_slo_d = nc.dram_tensor("m_slo", [128, 128], BF16, kind="ExternalInput")
    m_trq_d = nc.dram_tensor("m_trq", [128, 128], BF16, kind="ExternalInput")
    idpat_d = nc.dram_tensor("idpat", [128, 2048], BF16, kind="ExternalInput")
    id4_d = nc.dram_tensor("id4", [128, 512], BF16, kind="ExternalInput")
    pmask_d = nc.dram_tensor("pmask", [128, 8], F32, kind="ExternalInput")
    pimask_d = nc.dram_tensor("pimask", [128, 8], F32, kind="ExternalInput")
    gn_col_d = nc.dram_tensor("gn_col", [128, 8], F32, kind="ExternalInput")
    out_d = nc.dram_tensor("out", [TPC, DM], F32, kind="ExternalOutput")

    phi_d = nc.dram_tensor("phi2", [DM, 2048], F32, kind="ExternalInput")
    wsh_d = nc.dram_tensor("wsh", [EPC, DM, DM], F32, kind="ExternalInput")
    bsh_d = nc.dram_tensor("bsh", [EPC, DM], F32, kind="ExternalInput")

    gin_bp = nc.dram_tensor("gin_bp", [128, 4096], BF16)
    gout_bp = nc.dram_tensor("gout_bp", [8, 128, 4096], BF16)
    h1_d = nc.dram_tensor("h1_d", [TPC, DM], F32)
    h2_d = nc.dram_tensor("h2_d", [TPC, DM], F32)
    gin_s1 = nc.dram_tensor("gin_s1", [128, 8], F32)
    gout_s1 = nc.dram_tensor("gout_s1", [8, 128, 8], F32)
    gin_y1 = nc.dram_tensor("gin_y1", [EPC, DM], F32)
    gout_y1 = nc.dram_tensor("gout_y1", [8, EPC, DM], F32)
    gin_s2 = nc.dram_tensor("gin_s2", [128, 8], F32)
    gout_s2 = nc.dram_tensor("gout_s2", [8, 128, 8], F32)
    gin_y2 = nc.dram_tensor("gin_y2", [EPC, DM], F32)
    gout_y2 = nc.dram_tensor("gout_y2", [8, EPC, DM], F32)

    h3_d = nc.dram_tensor("h3_d", [TPC, DM], F32)
    sh_d = nc.dram_tensor("sh_d", [TPC, DI], BF16)
    o_d = nc.dram_tensor("o_d", [TPC, DI], BF16)
    gin_h = nc.dram_tensor("gin_h", [128, 48], F32)
    gout_h = nc.dram_tensor("gout_h", [8, 128, 48], F32)
    gin_g = nc.dram_tensor("gin_g", [64, DI + 32], F32)
    gout_g = nc.dram_tensor("gout_g", [8, 64, DI + 32], F32)
    upT_d = nc.dram_tensor("upT", [DM, DI], F32, kind="ExternalInput")
    gateT_d = nc.dram_tensor("gateT", [DM, DI], F32, kind="ExternalInput")
    downT_d = nc.dram_tensor("downT", [DI, DM], F32, kind="ExternalInput")
    qkgtT_d = nc.dram_tensor("qkgtT", [DI, 256], F32, kind="ExternalInput")
    ig1T_d = nc.dram_tensor("ig1T", [DI, 16], F32, kind="ExternalInput")
    ig2T_d = nc.dram_tensor("ig2T", [16, DI], F32, kind="ExternalInput")
    cwcol_d = nc.dram_tensor("cwcol", [128, 64], F32, kind="ExternalInput")
    cbcol_d = nc.dram_tensor("cbcol", [128, 16], F32, kind="ExternalInput")
    igb2_d = nc.dram_tensor("igb2col", [128, 16], F32, kind="ExternalInput")
    gbc_d = nc.dram_tensor("gbc", [128, 1], F32, kind="ExternalInput")
    tbc_d = nc.dram_tensor("tbc", [128, 1], F32, kind="ExternalInput")
    rwrow_d = nc.dram_tensor("rwrow", [128, DI], F32, kind="ExternalInput")
    nwrow_d = nc.dram_tensor("nwrow", [128, DI], F32, kind="ExternalInput")
    smask_d = nc.dram_tensor("smask", [128, 512], F32, kind="ExternalInput")
    ones64_d = nc.dram_tensor("ones64", [128, 64], F32, kind="ExternalInput")
    hmask_d = nc.dram_tensor("hmask", [128, 8], F32, kind="ExternalInput")

    top = ctx.top
    ctx.ident = top.enter_context(nc.sbuf_tensor("ident_sb", [128, 128], BF16))
    ctx.cvec = top.enter_context(nc.sbuf_tensor("cvec_sb", [128, len(CONSTS)], F32))
    mask_slo = top.enter_context(nc.sbuf_tensor("m_slo_sb", [128, 128], BF16))
    mask_trq = top.enter_context(nc.sbuf_tensor("m_trq_sb", [128, 128], BF16))
    idpat = top.enter_context(nc.sbuf_tensor("idpat_sb", [128, 2048], BF16))
    id4 = top.enter_context(nc.sbuf_tensor("id4_sb", [128, 512], BF16))
    pmask = top.enter_context(nc.sbuf_tensor("pmask_sb", [128, 8], F32))
    pimask = top.enter_context(nc.sbuf_tensor("pimask_sb", [128, 8], F32))
    gn_col = top.enter_context(nc.sbuf_tensor("gn_col_sb", [128, 8], F32))
    ctx.init_arena()

    with top:
        s0 = ctx.sem()
        with nc.Block(no_gpsimd_drain=True) as blk:
            @blk.sync
            def _(e):
                for dst, srcd in ((ctx.ident, ident_d), (ctx.cvec, cvec_d),
                                  (mask_slo, m_slo_d), (mask_trq, m_trq_d),
                                  (idpat, idpat_d), (id4, id4_d),
                                  (pmask, pmask_d), (pimask, pimask_d),
                                  (gn_col, gn_col_d)):
                    o = e.dma_start(out=dst[:, :], in_=srcd.ap())
                    inc(o, s0, 16)
                e.wait_ge(s0.h, s0.v)

        ctx.smark()
        q_tm = ctx.salloc("q_tm", [128, NTILE * DM], BF16)
        k_tm = ctx.salloc("k_tm", [128, NTILE * DM], BF16)
        beta_sb = ctx.salloc("beta", [128, NTILE * H], F32)
        v_tm = ctx.salloc("v_tm", [128, NTILE * DM], BF16)

        ctx.smark()
        xnT = ctx.salloc("xnT", [128, 8 * TPC], BF16)
        wq_sb = ctx.salloc("wq_sb", [128, 8 * DM], BF16)
        wk_sb = ctx.salloc("wk_sb", [128, 8 * DM], BF16)
        wv_sb = ctx.salloc("wv_sb", [128, 8 * DM], BF16)
        wbe_sb = ctx.salloc("wbe_sb", [128, 8 * H], BF16)
        sw = ctx.sem()
        with nc.Block(no_gpsimd_drain=True) as blk:
            @blk.gpsimd
            def _(e):
                for w_d, w_sb, cols in ((wq_d, wq_sb, DM), (wk_d, wk_sb, DM),
                                        (wv_d, wv_sb, DM), (wb_d, wbe_sb, H)):
                    o = e.dma_start(
                        out=w_sb[:, :],
                        in_=bass.AP(w_d, 0, [[cols, 128], [128 * cols, 8], [1, cols]]))
                    inc(o, sw, 16)
                e.wait_ge(sw.h, sw.v)

        est_p = ExitStack()
        emit_ln_transpose(ctx, est_p, xnT, x_dram=x_d)
        est_p.close()
        ctx.dump("xnT", xnT[:, :])

        est_p = ExitStack()
        emit_proj_qkvb(ctx, est_p, xnT, wq_sb, wk_sb, wv_sb, wbe_sb,
                       q_tm, k_tm, v_tm, beta_sb)
        est_p.close()
        ctx.srel()
        ctx.dump("q_tm", q_tm[:, :])
        ctx.dump("k_tm", k_tm[:, :])
        ctx.dump("v_tm", v_tm[:, :])
        ctx.dump("beta", beta_sb[:, :])

        o_tm = v_tm
        est_d = ExitStack()
        emit_attn_a_delta(ctx, est_d, q_tm, k_tm, v_tm, beta_sb, o_tm,
                          gin_bp, gout_bp, mask_slo, mask_trq, idpat, id4,
                          pmask, pimask)
        est_d.close()

        ctx.smark()
        est_p6 = ExitStack()
        h_sb = ctx.salloc("h_sb", [128, 2 * DM], F32)
        wo_sb = ctx.salloc("wo_sb", [128, 8 * DM], BF16)
        xt6 = ctx.salloc("xt6", [128, 2 * DM], F32)
        onr = ctx.salloc("onr", [128, DM], BF16)
        sq6 = ctx.salloc("sq6", [128, DM], F32)
        rs6 = ctx.salloc("rs6", [128, 4], F32)
        onT = ctx.salloc("onT", [128, DM], BF16)
        pt6 = est_p6.enter_context(nc.psum_tensor("pt6", [128, DM], BF16))
        pr6 = est_p6.enter_context(nc.psum_tensor("pr6", [128, DM], F32))

        ph = SerialPhase(ctx)
        ph.dma("gpsimd", lambda e: e.dma_start(
            out=wo_sb[:, :],
            in_=bass.AP(wo_d, 0, [[DM, 128], [128 * DM, 8], [1, DM]])))

        def scale_wo(e):
            for dt in range(8):
                o = e.tensor_scalar(wo_sb[:, dt * DM:(dt + 1) * DM],
                                    wo_sb[:, dt * DM:(dt + 1) * DM],
                                    gn_col[:, dt:dt + 1], None, op0=ALU.mult)
            return o
        ph.v(scale_wo)
        for t in range(NTILE):
            ph.dma("sync", lambda e, t=t: e.dma_start(
                out=xt6[:, (t % 2) * DM:(t % 2 + 1) * DM],
                in_=x_d[t * 128:(t + 1) * 128, :]))
            ph.v(lambda e, t=t: e.tensor_tensor(
                sq6[:, :], o_tm[:, t * DM:(t + 1) * DM],
                o_tm[:, t * DM:(t + 1) * DM], op=ALU.mult))
            ph.v(lambda e, t=t: e.tensor_reduce(
                rs6[:, :], sq6[:, :].rearrange("a (h d) -> a h d", d=DK),
                axis=AX.X, op=ALU.add))
            ph.s(lambda e: e.activation(rs6[:, :], rs6[:, :], AF.Ln,
                                        scale=ctx.cst(1.0 / 256), bias=ctx.cst(1e-5)))
            ph.s(lambda e: e.activation(rs6[:, :], rs6[:, :], AF.Exp,
                                        scale=ctx.cst(-0.5)))
            ph.v(lambda e, t=t: e.tensor_tensor(
                onr[:, :].rearrange("a (h d) -> a h d", d=DK),
                o_tm[:, t * DM:(t + 1) * DM].rearrange("a (h d) -> a h d", d=DK),
                bass.AP(rs6, 0, [[4, 128], [1, H], [0, DK]]), op=ALU.mult))

            def tp_on(e, t=t):
                for i in range(8):
                    o = e.transpose(pt6[:, i * 128:(i + 1) * 128],
                                    onr[:, i * 128:(i + 1) * 128], ctx.ident[:, :])
                return o
            ph.t(tp_on)
            ph.v(lambda e: e.tensor_copy(onT[:, :], pt6[:, :]))

            def mm_wo(e, t=t):
                for half in range(2):
                    for dt in range(8):
                        o = e.matmul(pr6[:, half * 512:(half + 1) * 512],
                                     onT[:, dt * 128:(dt + 1) * 128],
                                     wo_sb[:, dt * DM + half * 512:dt * DM + (half + 1) * 512],
                                     start=(dt == 0), stop=(dt == 7))
                return o
            ph.t(mm_wo)
            ph.v(lambda e, t=t: e.tensor_tensor(
                h_sb[:, (t % 2) * DM:(t % 2 + 1) * DM],
                xt6[:, (t % 2) * DM:(t % 2 + 1) * DM],
                pr6[:, :], op=ALU.add))
            ph.dma("sync", lambda e, t=t: e.dma_start(
                out=h1_d[t * 128:(t + 1) * 128, :],
                in_=h_sb[:, (t % 2) * DM:(t % 2 + 1) * DM]))
        ph.run()
        est_p6.close()
        ctx.srel()
        ctx.srel()
        ctx.dump_dram("h1", h1_d)

        ctx.smark()
        phi_sb = ctx.salloc("phi_sb", [128, 8 * 2048], BF16)
        sph = ctx.sem()
        with nc.Block(no_gpsimd_drain=True) as blk:
            @blk.gpsimd
            def _(e):
                o = e.dma_start(out=phi_sb[:, :],
                                in_=bass.AP(phi_d, 0, [[2048, 128],
                                                       [128 * 2048, 8], [1, 2048]]))
                inc(o, sph, 16)
                e.wait_ge(sph.h, sph.v)
        est_f1 = ExitStack()
        emit_ffn(ctx, est_f1, "1", h1_d, h2_d, phi_sb, wsh_d, bsh_d,
                 gin_s1, gout_s1, gin_y1, gout_y1)
        est_f1.close()
        ctx.srel()
        ctx.dump_dram("h2", h2_d)

        ctx.smark()
        w = {
            "upT": ctx.salloc("b_upT", [128, 8 * DI], BF16),
            "qkgtT": ctx.salloc("b_qkgtT", [128, 16 * 256], BF16),
            "ig1T": ctx.salloc("b_ig1T", [128, 16 * 16], BF16),
            "ig2T": ctx.salloc("b_ig2T", [16, DI], BF16),
            "cwcol": ctx.salloc("b_cw", [128, 64], F32),
            "cbcol": ctx.salloc("b_cb", [128, 16], F32),
            "igb2col": ctx.salloc("b_igb", [128, 16], F32),
            "gbc": ctx.salloc("b_gbc", [128, 1], F32),
            "tbc": ctx.salloc("b_tbc", [128, 1], F32),
            "rwrow": ctx.salloc("b_rw", [128, DI], F32),
            "nwrow": ctx.salloc("b_nw", [128, DI], F32),
            "smask": ctx.salloc("b_sm", [128, 512], F32),
            "ones64": ctx.salloc("b_o64", [128, 64], F32),
            "hmask": ctx.salloc("b_hm", [128, 8], F32),
            "m64": mask_trq[0:64, 0:64],
            "pm": pmask,
            "pim": pimask,
        }
        swb = ctx.sem()
        with nc.Block(no_gpsimd_drain=True) as blk:
            @blk.gpsimd
            def _(e):
                o = e.dma_start(out=w["upT"][:, :],
                                in_=bass.AP(upT_d, 0, [[DI, 128], [128 * DI, 8], [1, DI]]))
                inc(o, swb, 16)
                o = e.dma_start(out=w["qkgtT"][:, :],
                                in_=bass.AP(qkgtT_d, 0, [[256, 128], [128 * 256, 16], [1, 256]]))
                inc(o, swb, 16)
                o = e.dma_start(out=w["ig1T"][:, :],
                                in_=bass.AP(ig1T_d, 0, [[16, 128], [128 * 16, 16], [1, 16]]))
                inc(o, swb, 16)
                o = e.dma_start(out=w["ig2T"][:, :], in_=ig2T_d.ap())
                inc(o, swb, 16)
            @blk.sync
            def _(e):
                for dst, srcd in ((w["cwcol"], cwcol_d), (w["cbcol"], cbcol_d),
                                  (w["igb2col"], igb2_d), (w["gbc"], gbc_d),
                                  (w["tbc"], tbc_d), (w["rwrow"], rwrow_d),
                                  (w["nwrow"], nwrow_d), (w["smask"], smask_d),
                                  (w["ones64"], ones64_d), (w["hmask"], hmask_d)):
                    o = e.dma_start(out=dst[:, :], in_=srcd.ap())
                    inc(o, swb, 16)
                e.wait_ge(swb.h, swb.v)
            @blk.gpsimd
            def _(e):
                e.wait_ge(swb.h, swb.v)
        dr = {"sh_d": sh_d, "o_d": o_d, "gin_h": gin_h, "gout_h": gout_h,
              "gin_g": gin_g, "gout_g": gout_g, "gateT_d": gateT_d,
              "downT_d": downT_d}
        est_b = ExitStack()
        emit_attn_b(ctx, est_b, h2_d, h3_d, w, dr)
        est_b.close()
        ctx.srel()
        ctx.dump_dram("h3", h3_d)

        ctx.smark()
        phi_sb2 = ctx.salloc("phi_sb2", [128, 8 * 2048], BF16)
        sph2 = ctx.sem()
        with nc.Block(no_gpsimd_drain=True) as blk:
            @blk.gpsimd
            def _(e):
                o = e.dma_start(out=phi_sb2[:, :],
                                in_=bass.AP(phi_d, 0, [[2048, 128],
                                                       [128 * 2048, 8], [1, 2048]]))
                inc(o, sph2, 16)
                e.wait_ge(sph2.h, sph2.v)
        est_f2 = ExitStack()
        emit_ffn(ctx, est_f2, "2", h3_d, out_d, phi_sb2, wsh_d, bsh_d,
                 gin_s2, gout_s2, gin_y2, gout_y2)
        est_f2.close()
        ctx.srel()
        top.pop_all()
    return nc, ctx


def _prep_inputs(inputs):
    g = lambda n: np.asarray(inputs[n], np.float32)
    x = g("L0_attn_a_x").reshape(16384, DM)
    ident = np.eye(128, dtype=BF)
    ii, jj = np.indices((128, 128))
    m_slo = (jj < ii).astype(BF)
    m_trq = (ii <= jj).astype(BF)
    idpat = np.zeros((128, 2048), BF)
    for dkt in range(2):
        for h in range(H):
            for p in range(128):
                idpat[p, dkt * 1024 + h * 256 + dkt * 128 + p] = 1
    id4 = np.tile(np.eye(128, dtype=BF), (1, 4)).reshape(128, 512)
    id4 = np.concatenate([np.eye(128, dtype=BF)] * 4, axis=1)
    gn4 = np.tile(g("L0_attn_a_g_norm"), H)
    gn_col = gn4.reshape(8, 128).T.astype(np.float32).copy()
    cvec = np.tile(np.asarray(CONSTS, np.float32)[None, :], (128, 1))
    phi2 = np.ascontiguousarray(g("L0_ffn_phi").reshape(DM, 2048))
    Wff = g("L0_ffn_W")
    bff = g("L0_ffn_b")
    upT = np.ascontiguousarray(g("L0_attn_b_up_w").T)
    gateT = np.ascontiguousarray(g("L0_attn_b_gate_w").T)
    downT = np.ascontiguousarray(g("L0_attn_b_down_w").T)
    qkgtT = np.ascontiguousarray(np.concatenate(
        [g("L0_attn_b_q_w"), g("L0_attn_b_k_w"), g("L0_attn_b_g_w"),
         g("L0_attn_b_tau_w")], 0).T)
    ig1T = np.ascontiguousarray(g("L0_attn_b_ig_w1").T)
    ig2T = np.ascontiguousarray(g("L0_attn_b_ig_w2").T)
    cwcol = np.ascontiguousarray(
        g("L0_attn_b_conv_w").reshape(16, 128, 4).transpose(1, 0, 2).reshape(128, 64))
    cbcol = np.ascontiguousarray(g("L0_attn_b_conv_b").reshape(16, 128).T)
    igb2col = np.ascontiguousarray(g("L0_attn_b_ig_b2").reshape(16, 128).T)
    gbc = np.zeros((128, 1), np.float32)
    gbc[0:64, 0] = g("L0_attn_b_g_b")
    tbc = np.zeros((128, 1), np.float32)
    tbc[0:64, 0] = g("L0_attn_b_tau_b")
    rwrow = np.tile(g("L0_attn_b_residual_weight")[None, :], (128, 1))
    nwrow = np.tile(g("L0_attn_b_norm_weight")[None, :], (128, 1))
    smask = np.ones((128, 512), np.float32)
    smask[:, ::CB] = 0.0
    ones64 = np.ones((128, 64), np.float32)

    def hmask_c(c):
        hm = np.zeros((128, 8), np.float32)
        if c > 0:
            hm[:, c - 1] = 1.0
        return hm
    per_core = []
    for c in range(NCORE):
        pm = np.zeros((128, 8), np.float32)
        pm[:, :c] = 1.0
        per_core.append({
            "x": np.ascontiguousarray(x[c * TPC:(c + 1) * TPC]),
            "wq": g("L0_attn_a_Wq"),
            "wk": g("L0_attn_a_Wk"),
            "wv": g("L0_attn_a_Wv"),
            "wbeta": g("L0_attn_a_Wbeta"),
            "wo": g("L0_attn_a_Wo"),
            "ident": ident,
            "cvec": cvec,
            "m_slo": m_slo,
            "m_trq": m_trq,
            "idpat": idpat,
            "id4": id4,
            "pmask": pm,
            "pimask": 1.0 - pm,
            "gn_col": gn_col,
            "phi2": phi2,
            "wsh": np.ascontiguousarray(Wff[c * EPC:(c + 1) * EPC]),
            "bsh": np.ascontiguousarray(bff[c * EPC:(c + 1) * EPC]),
            "upT": upT, "gateT": gateT, "downT": downT, "qkgtT": qkgtT,
            "ig1T": ig1T, "ig2T": ig2T, "cwcol": cwcol, "cbcol": cbcol,
            "igb2col": igb2col, "gbc": gbc, "tbc": tbc, "rwrow": rwrow,
            "nwrow": nwrow, "smask": smask, "ones64": ones64, "hmask": hmask_c(c),
        })
    return per_core


def kernel(**inputs):
    per_core = _prep_inputs(inputs)
    nc, _ = build_program()
    res = run_bass_kernel_spmd(nc, per_core, core_ids=list(range(NCORE)))
    out = np.concatenate([np.asarray(r["out"], np.float32) for r in res.results], 0)
    return out.reshape(1, 16384, DM)


# version 03
# speedup vs baseline: 1.0652x; 1.0652x over previous
import numpy as np
import ml_dtypes
import concourse.bass as bass
import concourse.mybir as mybir
from concourse.bass_utils import run_bass_kernel_spmd
from contextlib import ExitStack

F32 = mybir.dt.float32
BF16 = mybir.dt.bfloat16
AF = mybir.ActivationFunctionType
ALU = mybir.AluOpType
AX = mybir.AxisListType
BF = ml_dtypes.bfloat16

NCORE = 8
TPC = 2048
NTILE = TPC // 128
DM = 1024
H = 4
DK = 256
CA = 128
NCH_A = TPC // CA
CB = 64
NCH_B = TPC // CB
DI = 2048
SEG = 512
NSEG = TPC // SEG
NE = 64
EPC = NE // NCORE

RG = [list(range(NCORE))]

CONSTS = [1e-5, -0.5, -8.0, 1.0, -1.0, 0.125, 1.0 / 256, 1.0 / 2048]
CIDX = {v: i for i, v in enumerate(CONSTS)}


class Sem:
    __slots__ = ("h", "v")

    def __init__(self, h):
        self.h = h
        self.v = 0


class Ctx:
    def __init__(self, nc, dbg=()):
        self.nc = nc
        self.dbg = set(dbg)
        self.dbg_outs = {}
        self.top = ExitStack()
        self._semn = 0

    def sem(self, name=None):
        self._semn += 1
        h = self.top.enter_context(self.nc.semaphore(name or f"s{self._semn}"))
        return Sem(h)

    def init_arena(self):
        import concourse.mybir as _mb
        avail = self.nc.sbuf_bytes_remaining - 8192
        base, end = self.nc.bump_sbuf(avail)
        self.abase, self.aend, self.acur = base, end, base
        self.amarks = []
        self.apeak = base

    def salloc(self, name, shape, dtype):
        nb = int(np.prod(shape[1:])) * mybir.dt.size(dtype)
        nb = (nb + 31) & ~31
        t = self.nc.alloc_sbuf_tensor_at(name, shape, dtype, offset=self.acur)
        self.acur += nb
        self.apeak = max(self.apeak, self.acur)
        assert self.acur <= self.aend, (name, self.acur, self.aend)
        return t

    def uname(self, n):
        self._semn += 1
        return f"{n}_{self._semn}"

    def smark(self):
        self.amarks.append(self.acur)

    def srel(self):
        self.acur = self.amarks.pop()

    def cst(self, v):
        return self.cvec[:, CIDX[v]:CIDX[v] + 1]

    def cst64(self, v):
        return self.cvec[0:64, CIDX[v]:CIDX[v] + 1]

    def dump_dram(self, name, dram):
        if name not in self.dbg:
            return
        nc = self.nc
        o = nc.dram_tensor("dbg_" + name, list(dram.shape), dram.dtype,
                           kind="ExternalOutput")
        s = self.sem()
        with nc.Block() as blk:
            @blk.sync
            def _(e):
                e.dma_start(out=o.ap(), in_=dram.ap()).then_inc(s.h, 16)
                e.wait_ge(s.h, 16)
        self.dbg_outs[name] = list(dram.shape)

    def dump(self, name, ap):
        if name not in self.dbg:
            return
        nc = self.nc
        shape = list(ap.shape)
        o = nc.dram_tensor("dbg_" + name, shape, ap.dtype, kind="ExternalOutput")
        s = self.sem()
        with nc.Block() as blk:
            @blk.sync
            def _(e):
                e.dma_start(out=o.ap(), in_=ap).then_inc(s.h, 16)
                e.wait_ge(s.h, 16)
        self.dbg_outs[name] = shape


def inc(instr, sem, n=1):
    instr.then_inc(sem.h, n)
    sem.v += n


class Pipe:

    ENGS = ("tensor", "vector", "scalar", "gpsimd", "sync")

    def __init__(self, ctx):
        self.ctx = ctx
        if not hasattr(ctx, "esems"):
            ctx.esems = {E: ctx.sem() for E in self.ENGS}
            ctx.dsems = {E: ctx.sem() for E in self.ENGS}
        self.steps = []

    def _cur(self):
        return [(E, self.ctx.esems[E].v, self.ctx.dsems[E].v) for E in self.ENGS]

    def add(self, eng, fn, amt=1, is_dma=False, deps="all"):
        ctx = self.ctx
        waits = []
        if deps == "all":
            for (E, cv, dv) in self._cur():
                if cv > 0:
                    waits.append((ctx.esems[E], cv))
                if dv > 0:
                    waits.append((ctx.dsems[E], dv))
        elif deps:
            for d in deps:
                if d is None:
                    continue
                if isinstance(d, list):
                    for (E, cv, dv) in d:
                        if cv > 0:
                            waits.append((ctx.esems[E], cv))
                        if dv > 0:
                            waits.append((ctx.dsems[E], dv))
                else:
                    waits.append(d)
        sem = ctx.dsems[eng] if is_dma else ctx.esems[eng]
        sem.v += amt
        handle = (sem, sem.v)
        self.steps.append((eng, fn, amt, sem, waits))
        return handle

    def t(self, fn, deps="all"):
        return self.add("tensor", fn, deps=deps)

    def v(self, fn, deps="all", drain=True):
        if drain:
            orig = fn

            def fn2(e, orig=orig):
                e.drain()
                return orig(e)
            fn = fn2
        return self.add("vector", fn, deps=deps)

    def s(self, fn, deps="all"):
        return self.add("scalar", fn, deps=deps)

    def g(self, fn, deps="all"):
        return self.add("gpsimd", fn, deps=deps)

    def dma(self, eng, fn, deps="all"):
        return self.add(eng, fn, amt=16, is_dma=True, deps=deps)

    def barrier(self):
        return self._cur()

    def run(self):
        nc = self.ctx.nc
        by_eng = {}
        for (eng, fn, amt, sem, waits) in self.steps:
            by_eng.setdefault(eng, []).append((fn, amt, sem, waits))
        final = self._cur()
        with nc.Block(no_gpsimd_drain=True) as blk:
            for eng, lst in by_eng.items():
                def mk(lst):
                    def body(e):
                        seen = {}
                        for (fn, amt, sem, waits) in lst:
                            for (ws, wv) in waits:
                                if seen.get(id(ws), -1) < wv:
                                    e.wait_ge(ws.h, wv)
                                    seen[id(ws)] = wv
                            o = fn(e)
                            o.then_inc(sem.h, amt)
                    return body
                getattr(blk, eng)(mk(lst))

            @blk.sync
            def _(e):
                for (E, cv, dv) in final:
                    if cv > 0:
                        e.wait_ge(self.ctx.esems[E].h, cv)
                    if dv > 0:
                        e.wait_ge(self.ctx.dsems[E].h, dv)


class SerialPhase:

    def __init__(self, ctx):
        self.ctx = ctx
        self.steps = []

    def add(self, eng, fn, amt=1):
        self.steps.append((eng, fn, amt))

    def t(self, fn):
        self.add("tensor", fn)

    def v(self, fn):
        self.add("vector", fn)

    def s(self, fn):
        self.add("scalar", fn)

    def g(self, fn):
        self.add("gpsimd", fn)

    def dma(self, eng, fn):
        self.add(eng, fn, 16)

    def run(self):
        nc = self.ctx.nc
        sem = self.ctx.sem()
        cum = 0
        pre = []
        for (eng, fn, amt) in self.steps:
            pre.append(cum)
            cum += amt
        by_eng = {}
        for i, (eng, fn, amt) in enumerate(self.steps):
            by_eng.setdefault(eng, []).append((i, fn, amt))
        with nc.Block(no_gpsimd_drain=True) as blk:
            for eng, lst in by_eng.items():
                def mk(lst):
                    def body(e):
                        for (i, fn, amt) in lst:
                            if pre[i] > 0:
                                e.wait_ge(sem.h, pre[i])
                            o = fn(e)
                            o.then_inc(sem.h, amt)
                        return
                    return body
                getattr(blk, eng)(mk(lst))

            @blk.sync
            def _(e):
                e.wait_ge(sem.h, cum)


def emit_ln_transpose(ctx, est, xnT, h_src=None, x_dram=None, store_stats=None):
    nc = ctx.nc
    from_dram = x_dram is not None
    s_ld = ctx.sem()
    s_var = ctx.sem()
    s_rstd = ctx.sem()
    s_xn = ctx.sem()
    s_tp = ctx.sem()
    s_sc = ctx.sem()

    ctx.smark()
    xt = ctx.salloc(ctx.uname("ln_x"), [128, 2 * DM], F32) if from_dram else None
    sq = ctx.salloc(ctx.uname("ln_sq"), [128, 16], F32)
    st = ctx.salloc(ctx.uname("ln_st"), [128, 16], F32)
    xnb = ctx.salloc(ctx.uname("ln_xn"), [128, 2 * DM], BF16)
    pst = est.enter_context(nc.psum_tensor(ctx.uname("ln_ps"), [128, 2 * DM], BF16))

    def src(t):
        return (xt[:, (t % 2) * DM:(t % 2 + 1) * DM] if from_dram
                else h_src[:, t * DM:(t + 1) * DM])

    with nc.Block(no_gpsimd_drain=True) as blk:
        if from_dram:
            @blk.sync
            def _(e):
                for t in range(NTILE):
                    if t >= 2:
                        e.wait_ge(s_xn.h, t - 1)
                    o = e.dma_start(out=xt[:, (t % 2) * DM:(t % 2 + 1) * DM],
                                    in_=x_dram[t * 128:(t + 1) * 128, :])
                    inc(o, s_ld, 16)

        @blk.vector
        def _(e):
            for t in range(NTILE):
                p = t % 2
                if from_dram:
                    e.wait_ge(s_ld.h, 16 * (t + 1))
                e.bn_stats(sq[:, 0:6], src(t)[:, 0:512] if False else
                           bass.AP(src(t).tensor, src(t).offset, [src(t).ap[0], [1, 512]]))
                e.bn_stats(sq[:, 6:12],
                           bass.AP(src(t).tensor, src(t).offset + 512, [src(t).ap[0], [1, 512]]))
                e.drain()
                tt = e.bn_aggr(st[:, p * 8:p * 8 + 2], sq[:, 0:12])
                e.drain()
                inc(tt, s_var)
                e.wait_ge(s_rstd.h, t + 1)
                if t >= 2:
                    e.wait_ge(s_tp.h, t - 1)
                o = e.tensor_scalar(xnb[:, p * DM:(p + 1) * DM], src(t),
                                    st[:, p * 8 + 0:p * 8 + 1], st[:, p * 8 + 3:p * 8 + 4],
                                    op0=ALU.subtract, op1=ALU.mult)
                inc(o, s_xn)
                if store_stats is not None:
                    e.tensor_copy(store_stats[:, 2 * t:2 * t + 1], st[:, p * 8:p * 8 + 1])
                    e.tensor_copy(store_stats[:, 2 * t + 1:2 * t + 2],
                                  st[:, p * 8 + 3:p * 8 + 4])
                if t >= 1:
                    e.wait_ge(s_tp.h, t)
                    sc = e.tensor_copy(
                        bass.AP(xnT, (t - 1) * 128,
                                [[8 * TPC, 128], [TPC, 8], [1, 128]]),
                        bass.AP(pst, ((t - 1) % 2) * DM, [[2 * DM, 128], [128, 8], [1, 128]]))
                    inc(sc, s_sc)
            e.wait_ge(s_tp.h, NTILE)
            sc = e.tensor_copy(
                bass.AP(xnT, (NTILE - 1) * 128, [[8 * TPC, 128], [TPC, 8], [1, 128]]),
                bass.AP(pst, ((NTILE - 1) % 2) * DM, [[2 * DM, 128], [128, 8], [1, 128]]))
            inc(sc, s_sc)

        @blk.scalar
        def _(e):
            for t in range(NTILE):
                p = t % 2
                e.wait_ge(s_var.h, t + 1)
                if t >= 1:
                    e.wait_ge(s_xn.h, t)
                e.activation(st[:, p * 8 + 3:p * 8 + 4], st[:, p * 8 + 1:p * 8 + 2],
                             AF.Ln, bias=ctx.cst(1e-5))
                o = e.activation(st[:, p * 8 + 3:p * 8 + 4], st[:, p * 8 + 3:p * 8 + 4],
                                 AF.Exp, scale=ctx.cst(-0.5))
                inc(o, s_rstd)

        @blk.tensor
        def _(e):
            for t in range(NTILE):
                p = t % 2
                e.wait_ge(s_xn.h, t + 1)
                if t >= 2:
                    e.wait_ge(s_sc.h, t - 1)
                for i in range(8):
                    o = e.transpose(pst[:, p * DM + i * 128:p * DM + (i + 1) * 128],
                                    xnb[:, p * DM + i * 128:p * DM + (i + 1) * 128],
                                    ctx.ident[:, :])
                inc(o, s_tp)
    ctx.srel()


def emit_proj_qkvb(ctx, est, xnT, wq, wk, wv, wbeta, q_tm, k_tm, v_tm, beta_sb):
    nc = ctx.nc
    s_mm = ctx.sem()
    s_ev = ctx.sem()
    s_ss = ctx.sem()
    s_rs = ctx.sem()
    s_ap = ctx.sem()

    ctx.smark()
    ps = est.enter_context(nc.psum_tensor("pj_ps", [128, 3 * DM], F32))
    psb = est.enter_context(nc.psum_tensor("pj_psb", [128, 2 * H], F32))
    qf = ctx.salloc("pj_qf", [128, 2 * DM], BF16)
    kf = ctx.salloc("pj_kf", [128, 2 * DM], BF16)
    sq = ctx.salloc("pj_sq", [128, DM], BF16)
    rs = ctx.salloc("pj_rs", [128, 16], F32)

    with nc.Block(no_gpsimd_drain=True) as blk:
        @blk.tensor
        def _(e):
            for t in range(NTILE):
                if t >= 1:
                    e.wait_ge(s_ev.h, 2 * t)
                for w, off in ((wq, 0), (wk, DM), (wv, 2 * DM)):
                    for half in range(2):
                        for dt in range(8):
                            o = e.matmul(ps[:, off + half * 512:off + (half + 1) * 512],
                                         xnT[:, dt * TPC + t * 128:dt * TPC + (t + 1) * 128],
                                         w[:, dt * DM + half * 512:dt * DM + (half + 1) * 512],
                                         start=(dt == 0), stop=(dt == 7))
                inc(o, s_mm)
                for dt in range(8):
                    o = e.matmul(psb[:, (t % 2) * H:(t % 2 + 1) * H],
                                 xnT[:, dt * TPC + t * 128:dt * TPC + (t + 1) * 128],
                                 wbeta[:, dt * H:(dt + 1) * H],
                                 start=(dt == 0), stop=(dt == 7))
                inc(o, s_mm)

        @blk.scalar
        def _(e):
            for t in range(NTILE):
                p = t % 2
                e.wait_ge(s_mm.h, 2 * t + 1)
                if t >= 2:
                    e.wait_ge(s_ap.h, t - 1)
                e.activation(qf[:, p * DM:(p + 1) * DM], ps[:, 0:DM], AF.Copy, bias=0.0)
                e.activation(kf[:, p * DM:(p + 1) * DM], ps[:, DM:2 * DM], AF.Copy, bias=0.0)
                o = e.activation(v_tm[:, t * DM:(t + 1) * DM], ps[:, 2 * DM:3 * DM],
                                 AF.Copy, bias=0.0)
                inc(o, s_ev)
                e.wait_ge(s_mm.h, 2 * t + 2)
                o = e.activation(beta_sb[:, t * H:(t + 1) * H], psb[:, p * H:(p + 1) * H],
                                 AF.Sigmoid)
                inc(o, s_ev)
                e.wait_ge(s_ss.h, t + 1)
                e.activation(rs[:, p * 8:p * 8 + 8], rs[:, p * 8:p * 8 + 8], AF.Ln)
                o = e.activation(rs[:, p * 8:p * 8 + 8], rs[:, p * 8:p * 8 + 8],
                                 AF.Exp, scale=ctx.cst(-0.5))
                inc(o, s_rs)

        @blk.vector
        def _(e):
            for t in range(NTILE):
                p = t % 2
                e.wait_ge(s_ev.h, 2 * t + 1)
                e.tensor_tensor(sq[:, :], qf[:, p * DM:(p + 1) * DM],
                                qf[:, p * DM:(p + 1) * DM], op=ALU.mult)
                e.tensor_reduce(rs[:, p * 8:p * 8 + 4],
                                sq[:, :].rearrange("a (h d) -> a h d", d=DK),
                                axis=AX.X, op=ALU.add)
                e.tensor_tensor(sq[:, :], kf[:, p * DM:(p + 1) * DM],
                                kf[:, p * DM:(p + 1) * DM], op=ALU.mult)
                o = e.tensor_reduce(rs[:, p * 8 + 4:p * 8 + 8],
                                    sq[:, :].rearrange("a (h d) -> a h d", d=DK),
                                    axis=AX.X, op=ALU.add)
                inc(o, s_ss)
                e.wait_ge(s_rs.h, t + 1)
                e.tensor_tensor(q_tm[:, t * DM:(t + 1) * DM].rearrange("a (h d) -> a h d", d=DK),
                                qf[:, p * DM:(p + 1) * DM].rearrange("a (h d) -> a h d", d=DK),
                                bass.AP(rs, p * 8, [[16, 128], [1, H], [0, DK]]),
                                op=ALU.mult)
                o = e.tensor_tensor(k_tm[:, t * DM:(t + 1) * DM].rearrange("a (h d) -> a h d", d=DK),
                                    kf[:, p * DM:(p + 1) * DM].rearrange("a (h d) -> a h d", d=DK),
                                    bass.AP(rs, p * 8 + 4, [[16, 128], [1, H], [0, DK]]),
                                    op=ALU.mult)
                inc(o, s_ap)
    ctx.srel()


def emit_attn_a_delta(ctx, est, q_tm, k_tm, v_tm, beta_sb, o_tm,
                      gin_bp, gout_bp, mask_slo, mask_trilq, idpat, id4,
                      pmask, pimask):
    nc = ctx.nc
    ctx.smark()
    sbf = lambda n, s, d: ctx.salloc(n, s, d)
    psf = lambda n, s, d: est.enter_context(nc.psum_tensor(n, s, d))

    w_fm = sbf("w_fm", [128, NCH_A * H * 256], BF16)
    u_tm = sbf("u_tm", [128, NCH_A * H * 256], BF16)
    s_sb = sbf("s_sb", [128, 2048], BF16)
    kf = sbf("kf", [128, 1024], BF16)
    e_sb = sbf("e_sb", [128, 1024], BF16)
    ctx.smark()
    bk = sbf("bk", [128, 1024], BF16)
    bv = sbf("bv", [128, 1024], BF16)
    a_sb = sbf("a_sb", [128, 512], BF16)
    at_sb = sbf("at_sb", [128, 512], BF16)
    apow = sbf("apow", [128, 7 * 512], BF16)
    m_sb = sbf("m_sb", [128, 2 * 512], BF16)
    tt_sb = sbf("tt_sb", [128, 512], BF16)

    ps_t = psf("ps_t", [128, 1024], BF16)
    ps_a = psf("ps_a", [128, 512], F32)
    ps_w = psf("ps_w", [128, 1024], F32)
    est2 = ExitStack()
    ps_b = est2.enter_context(nc.psum_tensor("ps_b", [128, 512], F32))
    ps_c = est2.enter_context(nc.psum_tensor("ps_c", [128, 512], F32))
    ps_s = None

    def wfm(n, h, dkt):
        return w_fm[:, ((n * H + h) * 2 + dkt) * 128:((n * H + h) * 2 + dkt) * 128 + 128]

    def utm(n, h):
        return u_tm[:, (n * H + h) * 256:(n * H + h) * 256 + 256]

    def ktm(n, h, dkt):
        return k_tm[:, n * DM + h * 256 + dkt * 128:n * DM + h * 256 + dkt * 128 + 128]

    def ssb(dkt, h):
        return s_sb[:, dkt * 1024 + h * 256:dkt * 1024 + h * 256 + 256]

    def ssb_t(dkt):
        return s_sb[:, dkt * 1024:(dkt + 1) * 1024]

    bcast_beta = lambda n: bass.AP(beta_sb, n * H, [[NTILE * H, 128], [1, H], [0, 256]])

    for n in range(NCH_A):
        ph = SerialPhase(ctx)
        ph.v(lambda e, n=n: e.tensor_tensor(
            bk[:, :].rearrange("a (h d) -> a h d", d=256),
            k_tm[:, n * DM:(n + 1) * DM].rearrange("a (h d) -> a h d", d=256),
            bcast_beta(n), op=ALU.mult))
        ph.v(lambda e, n=n: e.tensor_tensor(
            bv[:, :].rearrange("a (h d) -> a h d", d=256),
            v_tm[:, n * DM:(n + 1) * DM].rearrange("a (h d) -> a h d", d=256),
            bcast_beta(n), op=ALU.mult))

        def tp_k(e, n=n):
            for h in range(H):
                for dkt in range(2):
                    o = e.transpose(ps_t[:, (h * 2 + dkt) * 128:(h * 2 + dkt) * 128 + 128],
                                    ktm(n, h, dkt), ctx.ident[:, :])
            return o
        ph.t(tp_k)
        ph.v(lambda e: e.tensor_copy(kf[:, :], ps_t[:, :]))

        def mm_s(e, n=n):
            for h in range(H):
                for dkt in range(2):
                    o = e.matmul(ps_a[:, h * 128:(h + 1) * 128],
                                 kf[:, (h * 2 + dkt) * 128:(h * 2 + dkt) * 128 + 128],
                                 kf[:, (h * 2 + dkt) * 128:(h * 2 + dkt) * 128 + 128],
                                 start=(dkt == 0), stop=(dkt == 1))
            return o
        ph.t(mm_s)

        def ev_a(e, n=n):
            for h in range(H):
                o = e.scalar_tensor_tensor(
                    a_sb[:, h * 128:(h + 1) * 128], ps_a[:, h * 128:(h + 1) * 128],
                    beta_sb[:, n * H + h:n * H + h + 1], mask_slo[:, :],
                    op0=ALU.mult, op1=ALU.mult)
            return o
        ph.v(ev_a)
        ph.v(lambda e: e.tensor_copy(apow[:, 0:512], a_sb[:, :]))

        def tp_a(e):
            for h in range(H):
                o = e.transpose(ps_t[:, h * 128:(h + 1) * 128],
                                a_sb[:, h * 128:(h + 1) * 128], ctx.ident[:, :])
            return o
        ph.t(tp_a)
        ph.v(lambda e: e.tensor_copy(at_sb[:, :], ps_t[:, 0:512]))

        for j in range(1, 7):
            def mm_pow(e, j=j):
                for h in range(H):
                    sl = slice(h * 128, (h + 1) * 128)
                    e.matmul(ps_b[:, sl], at_sb[:, sl],
                             apow[:, (j - 1) * 512:(j - 1) * 512 + 512][:, sl],
                             start=True, stop=True)
                    o = e.matmul(ps_c[:, sl],
                                 apow[:, (j - 1) * 512:(j - 1) * 512 + 512][:, sl],
                                 at_sb[:, sl], start=True, stop=True)
                return o
            ph.t(mm_pow)
            ph.v(lambda e, j=j: e.tensor_copy(apow[:, j * 512:(j + 1) * 512], ps_b[:, :]))
            if j < 6:
                ph.s(lambda e: e.activation(at_sb[:, :], ps_c[:, :], AF.Copy, bias=0.0))
            else:
                ph.v(lambda e: e.tensor_tensor(m_sb[:, 0:512], ps_c[:, :],
                                               id4[:, :], op=ALU.add))

        mcur = 0
        for j in (5, 4, 3, 2, 1):
            def mm_ch(e, j=j, mcur=mcur):
                for h in range(H):
                    sl = slice(h * 128, (h + 1) * 128)
                    e.matmul(ps_a[:, sl], apow[:, j * 512:(j + 1) * 512][:, sl],
                             m_sb[:, mcur * 512:mcur * 512 + 512][:, sl],
                             start=True, stop=False)
                    o = e.matmul(ps_a[:, sl], ctx.ident[:, :],
                                 m_sb[:, mcur * 512:mcur * 512 + 512][:, sl],
                                 start=False, stop=True)
                return o
            ph.t(mm_ch)
            ph.v(lambda e, mcur=mcur: e.tensor_copy(
                m_sb[:, (1 - mcur) * 512:(1 - mcur) * 512 + 512], ps_a[:, :]))
            mcur = 1 - mcur

        def mm_last(e, mcur=mcur):
            for h in range(H):
                sl = slice(h * 128, (h + 1) * 128)
                o = e.matmul(ps_a[:, sl], apow[:, 0:512][:, sl],
                             m_sb[:, mcur * 512:mcur * 512 + 512][:, sl],
                             start=True, stop=True)
            return o
        ph.t(mm_last)
        ph.v(lambda e, mcur=mcur: e.tensor_tensor(
            tt_sb[:, :], m_sb[:, mcur * 512:mcur * 512 + 512], ps_a[:, :],
            op=ALU.subtract))

        def mm_w(e, n=n):
            for h in range(H):
                for dkt in range(2):
                    o = e.matmul(ps_w[:, (h * 2 + dkt) * 128:(h * 2 + dkt) * 128 + 128],
                                 bk[:, h * 256 + dkt * 128:h * 256 + dkt * 128 + 128],
                                 tt_sb[:, h * 128:(h + 1) * 128],
                                 start=True, stop=True)
            return o
        ph.t(mm_w)
        ph.v(lambda e, n=n: e.tensor_copy(
            w_fm[:, n * H * 256:(n + 1) * H * 256], ps_w[:, :]))

        def mm_u(e, n=n):
            for h in range(H):
                o = e.matmul(ps_w[:, h * 256:(h + 1) * 256],
                             tt_sb[:, h * 128:(h + 1) * 128],
                             bv[:, h * 256:(h + 1) * 256],
                             start=True, stop=True)
            return o
        ph.t(mm_u)
        ph.v(lambda e, n=n: e.tensor_copy(
            u_tm[:, n * H * 256:(n + 1) * H * 256], ps_w[:, :]))
        ph.run()

    est2.close()
    ps_s = psf("ps_s", [128, 2048], F32)
    ctx.srel()
    ctx.smark()
    pt_sb = sbf("pt_sb", [128, 2048], BF16)
    ptm = sbf("ptm", [128, 2048], BF16)
    bpin = sbf("bpin", [128, 2048], BF16)
    qf = sbf("qf", [128, 1024], BF16)
    mt_sb = sbf("mt_sb", [128, 512], BF16)
    ctx.dump("w_fm", w_fm[:, :])
    ctx.dump("u_tm", u_tm[:, :])

    btrace = None
    if "btrace" in ctx.dbg:
        btrace = nc.dram_tensor("dbg_btrace", [NCH_A, 128, 2048], BF16,
                                kind="ExternalOutput")
        ctx.dbg_outs["btrace"] = [NCH_A, 128, 2048]
    for pas in ("B", "P"):
        ph = SerialPhase(ctx)
        if pas == "B":
            ph.v(lambda e: e.memset(s_sb[:, :], 0.0))
        else:
            ph.v(lambda e: e.tensor_copy(s_sb[:, :], idpat[:, :]))
        for n in range(NCH_A):
            def mm_e(e, n=n):
                for h in range(H):
                    for dkt in range(2):
                        o = e.matmul(ps_w[:, h * 256:(h + 1) * 256],
                                     wfm(n, h, dkt), ssb(dkt, h),
                                     start=(dkt == 0), stop=(dkt == 1))
                return o
            ph.t(mm_e)
            if pas == "B":
                ph.v(lambda e, n=n: e.tensor_tensor(
                    e_sb[:, :], u_tm[:, n * H * 256:(n + 1) * H * 256],
                    ps_w[:, :], op=ALU.subtract))
            else:
                ph.s(lambda e: e.activation(e_sb[:, :], ps_w[:, :], AF.Copy,
                                            bias=0.0, scale=ctx.cst(-1.0)))

            def mm_su(e, n=n):
                for dkt in range(2):
                    for h in range(H):
                        o = e.matmul(ps_s[:, dkt * 1024 + h * 256:dkt * 1024 + (h) * 256 + 256],
                                     ktm(n, h, dkt), e_sb[:, h * 256:(h + 1) * 256],
                                     start=(n == 0 and h % 2 == 0),
                                     stop=(n == NCH_A - 1))
                return o
            ph.t(mm_su)
            if pas == "B":
                ph.v(lambda e: e.tensor_copy(ssb_t(0), ps_s[:, 0:1024]))
                ph.v(lambda e: e.tensor_copy(ssb_t(1), ps_s[:, 1024:2048]))
            else:
                ph.v(lambda e: e.tensor_tensor(ssb_t(0), ps_s[:, 0:1024],
                                               idpat[:, 0:1024], op=ALU.add))
                ph.v(lambda e: e.tensor_tensor(ssb_t(1), ps_s[:, 1024:2048],
                                               idpat[:, 1024:2048], op=ALU.add))
            if pas == "B" and btrace is not None:
                ph.dma("sync", lambda e, n=n: e.dma_start(out=btrace[n, :, :],
                                                          in_=s_sb[:, :]))
        if pas == "B":
            ph.dma("sync", lambda e: e.dma_start(out=gin_bp[:, 0:2048], in_=s_sb[:, :]))
        else:
            def tp_p(e):
                for dkt in range(2):
                    for h in range(H):
                        for d2 in range(2):
                            o = e.transpose(
                                ps_t[:, ((dkt * H + h) * 2 + d2) % 8 * 128:
                                     ((dkt * H + h) * 2 + d2) % 8 * 128 + 128],
                                s_sb[:, dkt * 1024 + h * 256 + d2 * 128:
                                     dkt * 1024 + h * 256 + d2 * 128 + 128],
                                ctx.ident[:, :])
                return o
            def tp_p1(e):
                for idx, (dkt, h, d2) in enumerate(
                        [(a, b, c) for a in range(2) for b in range(H) for c in range(2)][:8]):
                    o = e.transpose(ps_t[:, idx * 128:idx * 128 + 128],
                                    s_sb[:, dkt * 1024 + h * 256 + d2 * 128:
                                         dkt * 1024 + h * 256 + d2 * 128 + 128],
                                    ctx.ident[:, :])
                return o

            def ev_p1(e):
                for idx, (dkt, h, d2) in enumerate(
                        [(a, b, c) for a in range(2) for b in range(H) for c in range(2)][:8]):
                    o = e.tensor_copy(pt_sb[:, d2 * 1024 + h * 256 + dkt * 128:
                                            d2 * 1024 + h * 256 + dkt * 128 + 128],
                                      ps_t[:, idx * 128:idx * 128 + 128])
                return o

            def tp_p2(e):
                for idx, (dkt, h, d2) in enumerate(
                        [(a, b, c) for a in range(2) for b in range(H) for c in range(2)][8:]):
                    o = e.transpose(ps_t[:, idx * 128:idx * 128 + 128],
                                    s_sb[:, dkt * 1024 + h * 256 + d2 * 128:
                                         dkt * 1024 + h * 256 + d2 * 128 + 128],
                                    ctx.ident[:, :])
                return o

            def ev_p2(e):
                for idx, (dkt, h, d2) in enumerate(
                        [(a, b, c) for a in range(2) for b in range(H) for c in range(2)][8:]):
                    o = e.tensor_copy(pt_sb[:, d2 * 1024 + h * 256 + dkt * 128:
                                            d2 * 1024 + h * 256 + dkt * 128 + 128],
                                      ps_t[:, idx * 128:idx * 128 + 128])
                return o
            ph.t(tp_p1)
            ph.v(ev_p1)
            ph.t(tp_p2)
            ph.v(ev_p2)
            ph.dma("sync", lambda e: e.dma_start(out=gin_bp[:, 2048:4096], in_=pt_sb[:, :]))
        ph.run()

    ph = SerialPhase(ctx)
    ph.g(lambda e: e.collective_compute(
        "AllGather", ALU.bypass, replica_groups=RG,
        ins=[gin_bp.ap().opt()], outs=[gout_bp.ap().opt()]))
    ph.v(lambda e: e.memset(s_sb[:, :], 0.0))
    for i in range(NCORE - 1):
        ph.dma("sync", lambda e, i=i: e.dma_start(out=bpin[:, :],
                                                  in_=gout_bp[i, :, 2048:4096]))
        ph.v(lambda e, i=i: e.tensor_scalar(
            ptm[:, :], bpin[:, :], pmask[:, i:i + 1], None, op0=ALU.mult))
        ph.v(lambda e, i=i: e.scalar_tensor_tensor(
            ptm[:, :], idpat[:, :], pimask[:, i:i + 1], ptm[:, :],
            op0=ALU.mult, op1=ALU.add))
        ph.dma("sync", lambda e, i=i: e.dma_start(out=bpin[:, :],
                                                  in_=gout_bp[i, :, 0:2048]))

        def mm_s0(e, i=i):
            for dkt in range(2):
                for h in range(H):
                    for dkt2 in range(2):
                        o = e.matmul(
                            ps_s[:, dkt * 1024 + h * 256:dkt * 1024 + h * 256 + 256],
                            ptm[:, dkt2 * 1024 + h * 256 + dkt * 128:
                                dkt2 * 1024 + h * 256 + dkt * 128 + 128],
                            ssb(dkt2, h), start=(dkt2 == 0), stop=(dkt2 == 1))
            return o
        ph.t(mm_s0)
        ph.v(lambda e, i=i: e.scalar_tensor_tensor(
            s_sb[:, :], bpin[:, :], pmask[:, i:i + 1], ps_s[:, :],
            op0=ALU.mult, op1=ALU.add))
    ph.run()
    ctx.dump("s0", s_sb[:, :])

    for n in range(NCH_A):
        ph = SerialPhase(ctx)

        def tp_q(e, n=n):
            for h in range(H):
                for dkt in range(2):
                    o = e.transpose(ps_t[:, (h * 2 + dkt) * 128:(h * 2 + dkt) * 128 + 128],
                                    q_tm[:, n * DM + h * 256 + dkt * 128:
                                         n * DM + h * 256 + dkt * 128 + 128],
                                    ctx.ident[:, :])
            return o
        ph.t(tp_q)
        ph.v(lambda e: e.tensor_copy(qf[:, :], ps_t[:, :]))

        def tp_k2(e, n=n):
            for h in range(H):
                for dkt in range(2):
                    o = e.transpose(ps_t[:, (h * 2 + dkt) * 128:(h * 2 + dkt) * 128 + 128],
                                    ktm(n, h, dkt), ctx.ident[:, :])
            return o
        ph.t(tp_k2)
        ph.v(lambda e: e.tensor_copy(kf[:, :], ps_t[:, :]))

        def mm_e(e, n=n):
            for h in range(H):
                for dkt in range(2):
                    o = e.matmul(ps_w[:, h * 256:(h + 1) * 256],
                                 wfm(n, h, dkt), ssb(dkt, h),
                                 start=(dkt == 0), stop=(dkt == 1))
            return o
        ph.t(mm_e)
        ph.v(lambda e, n=n: e.tensor_tensor(
            e_sb[:, :], u_tm[:, n * H * 256:(n + 1) * H * 256], ps_w[:, :],
            op=ALU.subtract))

        def mm_mt(e):
            for h in range(H):
                for dkt in range(2):
                    o = e.matmul(ps_a[:, h * 128:(h + 1) * 128],
                                 kf[:, (h * 2 + dkt) * 128:(h * 2 + dkt) * 128 + 128],
                                 qf[:, (h * 2 + dkt) * 128:(h * 2 + dkt) * 128 + 128],
                                 start=(dkt == 0), stop=(dkt == 1))
            return o
        ph.t(mm_mt)
        ph.v(lambda e: e.tensor_tensor(
            mt_sb[:, :].rearrange("a (h c) -> a h c", c=128),
            ps_a[:, :].rearrange("a (h c) -> a h c", c=128),
            bass.AP(mask_trilq, 0, [[128, 128], [0, H], [1, 128]]),
            op=ALU.mult))

        def mm_o(e, n=n):
            for h in range(H):
                for dkt in range(2):
                    e.matmul(ps_w[:, h * 256:(h + 1) * 256],
                             qf[:, (h * 2 + dkt) * 128:(h * 2 + dkt) * 128 + 128],
                             ssb(dkt, h), start=(dkt == 0), stop=False)
                o = e.matmul(ps_w[:, h * 256:(h + 1) * 256],
                             mt_sb[:, h * 128:(h + 1) * 128],
                             e_sb[:, h * 256:(h + 1) * 256], start=False, stop=True)
            return o
        ph.t(mm_o)
        ph.v(lambda e, n=n: e.tensor_copy(o_tm[:, n * DM:(n + 1) * DM], ps_w[:, :]))

        def mm_su(e, n=n):
            for dkt in range(2):
                for h in range(H):
                    if n == 0:
                        e.matmul(ps_s[:, dkt * 1024 + h * 256:dkt * 1024 + h * 256 + 256],
                                 ctx.ident[:, :], ssb(dkt, h),
                                 start=(h % 2 == 0), stop=False)
                    o = e.matmul(ps_s[:, dkt * 1024 + h * 256:dkt * 1024 + h * 256 + 256],
                                 ktm(n, h, dkt), e_sb[:, h * 256:(h + 1) * 256],
                                 start=False, stop=(n == NCH_A - 1))
            return o
        ph.t(mm_su)
        ph.s(lambda e: e.activation(ssb_t(0), ps_s[:, 0:1024], AF.Copy, bias=0.0))
        ph.s(lambda e: e.activation(ssb_t(1), ps_s[:, 1024:2048], AF.Copy, bias=0.0))
        ph.run()
    ctx.srel()
    ctx.srel()
    ctx.dump("o_tm", o_tm[:, :])
    ctx.dump_dram("gin", gin_bp)


def emit_attn_b(ctx, est, h_in_d, h_out_d, w, dr):
    nc = ctx.nc
    ctx.smark()
    SW = 512
    NSG = TPC // SW
    CPS = SW // CB

    s_f = ctx.salloc("b_sf", [64, DI], F32)
    s_bf = ctx.salloc("b_sbf", [64, DI], BF16)
    qb_all = ctx.salloc("b_qball", [64, TPC], BF16)
    btot_all = ctx.salloc("b_btot", [64, NCH_B], F32)
    stats = ctx.salloc("b_stats", [128, 2 * NTILE], F32)
    carry = ctx.salloc("b_carry", [128, 16 * 3], BF16)
    cumg = ctx.salloc("b_cumg", [64, NCH_B + 8], F32)

    xn3 = ctx.salloc("b_xn3", [128, 2 * DM], BF16)

    ctx.smark()
    ln_tmp = ctx.salloc("b_lntmp", [128, DM], F32)
    bnst = ctx.salloc("b_bnst", [128, 16], F32)
    hg = ctx.salloc("b_hg", [128, 8 * 48], F32)

    est_ps = ExitStack()

    def ln_tile_emit(ph, t, dst_fm_tile_fn):
        p = t % 2
        ph.dma("sync", lambda e, t=t, p=p: e.dma_start(
            out=ln_tmp[:, 0:DM],
            in_=h_in_d[t * 128:(t + 1) * 128, :]))

        def stats_f(e, t=t, p=p):
            e.bn_stats(bnst[:, 0:6],
                       bass.AP(ln_tmp, 0, [[DM, 128], [1, 512]]))
            e.bn_stats(bnst[:, 6:12],
                       bass.AP(ln_tmp, 512, [[DM, 128], [1, 512]]))
            e.drain()
            o = e.bn_aggr(stats[:, 2 * t:2 * t + 2], bnst[:, 0:12])
            e.drain()
            return o
        ph.v(stats_f)
        ph.s(lambda e, t=t: e.activation(stats[:, 2 * t + 1:2 * t + 2],
                                         stats[:, 2 * t + 1:2 * t + 2],
                                         AF.Ln, bias=ctx.cst(1e-5)))
        ph.s(lambda e, t=t: e.activation(stats[:, 2 * t + 1:2 * t + 2],
                                         stats[:, 2 * t + 1:2 * t + 2],
                                         AF.Exp, scale=ctx.cst(-0.5)))
        ph.v(lambda e, t=t, p=p: e.tensor_scalar(
            xn3[:, p * DM:(p + 1) * DM], ln_tmp[:, 0:DM],
            stats[:, 2 * t:2 * t + 1], stats[:, 2 * t + 1:2 * t + 2],
            op0=ALU.subtract, op1=ALU.mult))

    ps_h = est_ps.enter_context(nc.psum_tensor("b_psh", [128, 2 * DM], BF16))
    ps_hf = est_ps.enter_context(nc.psum_tensor("b_pshf", [128, 64], F32))
    ph = SerialPhase(ctx)
    ph.dma("sync", lambda e: e.dma_start(out=ln_tmp[0:3, 0:DM],
                                         in_=h_in_d[TPC - 3:TPC, :]))

    def h_stats(e):
        e.bn_stats(bnst[0:3, 0:6], bass.AP(ln_tmp, 0, [[DM, 3], [1, 512]]))
        e.bn_stats(bnst[0:3, 6:12], bass.AP(ln_tmp, 512, [[DM, 3], [1, 512]]))
        e.drain()
        o = e.bn_aggr(bnst[0:3, 12:14], bnst[0:3, 0:12])
        e.drain()
        return o
    ph.v(h_stats)
    ph.s(lambda e: e.activation(bnst[:, 13:14], bnst[:, 13:14], AF.Ln,
                                bias=ctx.cst(1e-5)))
    ph.s(lambda e: e.activation(bnst[:, 13:14], bnst[:, 13:14], AF.Exp,
                                scale=ctx.cst(-0.5)))
    ph.v(lambda e: e.tensor_scalar(
        xn3[0:3, 0:DM], ln_tmp[0:3, 0:DM], bnst[0:3, 12:13], bnst[0:3, 13:14],
        op0=ALU.subtract, op1=ALU.mult))

    def h_tp(e):
        for i in range(8):
            o = e.transpose(ps_h[:, i * 128:i * 128 + 3],
                            xn3[0:3, i * 128:(i + 1) * 128], ctx.ident[0:3, 0:3])
        return o
    ph.t(h_tp)
    ph.v(lambda e: e.tensor_copy(xn3[:, DM:DM + 8 * 3].rearrange(
        "a (d c) -> a d c", c=3),
        bass.AP(ps_h, 0, [[2 * DM, 128], [128, 8], [1, 3]])))

    def h_mm(e):
        for cht in range(16):
            for dt in range(8):
                o = e.matmul(ps_hf[:, cht * 3:cht * 3 + 3],
                             w["upT"][:, dt * DI + cht * 128:dt * DI + cht * 128 + 128],
                             xn3[:, DM + dt * 3:DM + dt * 3 + 3],
                             start=(dt == 0), stop=(dt == 7))
        return o
    ph.t(h_mm)
    ph.v(lambda e: e.tensor_copy(hg[:, 0:48], ps_hf[:, 0:48]))
    ph.dma("sync", lambda e: e.dma_start(out=dr["gin_h"].ap(), in_=hg[:, 0:48]))
    ph.g(lambda e: e.collective_compute(
        "AllGather", ALU.bypass, replica_groups=RG,
        ins=[dr["gin_h"].ap().opt()], outs=[dr["gout_h"].ap().opt()]))
    ph.dma("sync", lambda e: e.dma_start(
        out=hg[:, :], in_=bass.AP(dr["gout_h"], 0, [[48, 128], [128 * 48, 8], [1, 48]])))

    def h_sel(e):
        e.memset(carry[:, :], 0.0)
        e.memset(s_f[:, :], 0.0)
        e.memset(s_bf[:, :], 0.0)
        e.drain()
        for i in range(8):
            o = e.scalar_tensor_tensor(carry[:, :], hg[:, i * 48:(i + 1) * 48],
                                       w["hmask"][:, i:i + 1], carry[:, :],
                                       op0=ALU.mult, op1=ALU.add)
            e.drain()
        return o
    ph.v(h_sel)
    ph.run()
    est_ps.close()
    ctx.dump("carry", carry[:, :])

    xnT_s = ctx.salloc("b_xnTs", [128, 8 * SW], BF16)
    xup = ctx.salloc("b_xup", [128, 16 * SW], BF16)
    shf = ctx.salloc("b_shf", [128, 16 * SW], BF16)
    vfm = ctx.salloc("b_vfm", [128, 16 * SW], BF16)
    sh_tm = ctx.salloc("b_shtm", [128, DI], BF16)
    xcat = ctx.salloc("b_xcat", [128, SW + 3], BF16)
    acc_a = ctx.salloc("b_acca", [128, SW], F32)
    acc_b = ctx.salloc("b_accb", [128, SW], F32)
    qk_t = ctx.salloc("b_qkt", [128, SW], BF16)
    gt_t = ctx.salloc("b_gtt", [128, SW], F32)
    k_sb = ctx.salloc("b_ksb", [64, SW], F32)
    tau_s = ctx.salloc("b_taus", [64, SW], F32)
    gl_s = ctx.salloc("b_gls", [64, SW], F32)
    it_s = ctx.salloc("b_its", [64, SW], F32)
    rt_s = ctx.salloc("b_rts", [64, SW], F32)
    b_s = ctx.salloc("b_bs", [64, SW], F32)
    eb_s = ctx.salloc("b_ebs", [64, SW], BF16)
    sc_a = ctx.salloc("b_sca", [64, SW], F32)
    qs_s = ctx.salloc("b_qss", [64, SW], BF16)
    ks_s = ctx.salloc("b_kss", [64, SW], BF16)
    kd_s = ctx.salloc("b_kds", [64, SW], BF16)
    kit = ctx.salloc("b_kit", [64, SW], BF16)
    r16 = ctx.salloc("b_r16", [16, SW], BF16)
    seg_sc = ctx.salloc("b_segsc", [64, 24], F32)
    rr_f = ctx.salloc("b_rrf", [1, SW], F32)
    r_row = ctx.salloc("b_rrow", [1, SW], BF16)
    k2b = ctx.salloc("b_k2b", [64, SW], BF16)
    r_col = ctx.salloc("b_rcol", [64, 8], F32)
    v_tm = ctx.salloc("b_vtm", [64, DI], BF16)
    kdt = ctx.salloc("b_kdt", [64, 64], BF16)
    a_p = ctx.salloc("b_ap", [64, 64], BF16)
    o_ch = ctx.salloc("b_och", [64, DI], BF16)
    onescol = ctx.salloc("b_ones", [64, 1], BF16)

    est_p1 = ExitStack()
    ps_q = est_p1.enter_context(nc.psum_tensor("b_psq", [128, 1024], F32))
    ps_t = est_p1.enter_context(nc.psum_tensor("b_pst", [128, 2048], BF16))
    ps_o = est_p1.enter_context(nc.psum_tensor("b_pso", [64, 1024], F32))
    ps_ds = est_p1.enter_context(nc.psum_tensor("b_psds", [64, 1024], F32))
    ps_x = ps_q

    for sg in range(NSG):
        ph = SerialPhase(ctx)
        ph.v(lambda e: e.memset(onescol[:, :], 1.0))
        for tt_ in range(4):
            t = sg * 4 + tt_
            ln_tile_emit(ph, t, None)

            def tp_x(e, tt_=tt_, t=t):
                for i in range(8):
                    o = e.transpose(ps_t[:, i * 128:(i + 1) * 128],
                                    xn3[:, (t % 2) * DM + i * 128:(t % 2) * DM + (i + 1) * 128],
                                    ctx.ident[:, :])
                return o
            ph.t(tp_x)
            ph.v(lambda e, tt_=tt_: e.tensor_copy(
                bass.AP(xnT_s, tt_ * 128, [[8 * SW, 128], [SW, 8], [1, 128]]),
                bass.AP(ps_t, 0, [[2048, 128], [128, 8], [1, 128]])))

        for cht in range(16):
            def mm_up(e, cht=cht):
                for dt in range(8):
                    o = e.matmul(ps_x[:, 0:512],
                                 w["upT"][:, dt * DI + cht * 128:dt * DI + cht * 128 + 128],
                                 xnT_s[:, dt * SW:(dt + 1) * SW],
                                 start=(dt == 0), stop=(dt == 7))
                return o
            ph.t(mm_up)
            ph.s(lambda e, cht=cht: e.activation(
                xup[:, cht * SW:(cht + 1) * SW], ps_x[:, 0:512], AF.Copy, bias=0.0))

        for cht in range(16):
            ph.v(lambda e, cht=cht: e.tensor_copy(xcat[:, 0:3],
                                                  carry[:, cht * 3:(cht + 1) * 3]))
            ph.v(lambda e, cht=cht: e.tensor_copy(xcat[:, 3:SW + 3],
                                                  xup[:, cht * SW:(cht + 1) * SW]))
            ph.v(lambda e, cht=cht: e.tensor_scalar(
                acc_a[:, :], xcat[:, 0:SW], w["cwcol"][:, cht * 4:cht * 4 + 1],
                None, op0=ALU.mult))
            ph.v(lambda e, cht=cht: e.scalar_tensor_tensor(
                acc_b[:, :], xcat[:, 1:SW + 1], w["cwcol"][:, cht * 4 + 1:cht * 4 + 2],
                acc_a[:, :], op0=ALU.mult, op1=ALU.add))
            ph.v(lambda e, cht=cht: e.scalar_tensor_tensor(
                acc_a[:, :], xcat[:, 2:SW + 2], w["cwcol"][:, cht * 4 + 2:cht * 4 + 3],
                acc_b[:, :], op0=ALU.mult, op1=ALU.add))
            ph.v(lambda e, cht=cht: e.scalar_tensor_tensor(
                acc_b[:, :], xcat[:, 3:SW + 3], w["cwcol"][:, cht * 4 + 3:cht * 4 + 4],
                acc_a[:, :], op0=ALU.mult, op1=ALU.add))
            ph.s(lambda e, cht=cht: e.activation(
                shf[:, cht * SW:(cht + 1) * SW], acc_b[:, :], AF.Silu,
                bias=w["cbcol"][:, cht:cht + 1]))
            if sg < NSG - 1:
                ph.v(lambda e, cht=cht: e.tensor_copy(
                    carry[:, cht * 3:(cht + 1) * 3],
                    xup[:, (cht + 1) * SW - 3:(cht + 1) * SW]))

        for tt_ in range(4):
            def tp_s(e, tt_=tt_):
                for i in range(16):
                    o = e.transpose(ps_t[:, i * 128:(i + 1) * 128],
                                    shf[:, i * SW + tt_ * 128:i * SW + tt_ * 128 + 128],
                                    ctx.ident[:, :])
                return o
            ph.t(tp_s)
            ph.v(lambda e, tt_=tt_: e.tensor_copy(sh_tm[:, :], ps_t[:, :]))
            ph.dma("sync", lambda e, tt_=tt_, sg=sg: e.dma_start(
                out=dr["sh_d"][(sg * 4 + tt_) * 128:(sg * 4 + tt_ + 1) * 128, :],
                in_=sh_tm[:, :]))

        for cht in range(16):
            def mm_r1(e, cht=cht):
                o = e.matmul(ps_q[0:16, 0:512],
                             w["ig1T"][:, cht * 16:(cht + 1) * 16],
                             xup[:, cht * SW:(cht + 1) * SW],
                             start=(cht == 0), stop=(cht == 15))
                return o
            ph.t(mm_r1)
        ph.v(lambda e: e.tensor_copy(r16[:, :], ps_q[0:16, 0:512]))
        for cht in range(16):
            def mm_r2(e, cht=cht):
                o = e.matmul(ps_x[:, 0:512],
                             w["ig2T"][:, cht * 128:(cht + 1) * 128],
                             r16[:, :], start=True, stop=True)
                return o
            ph.t(mm_r2)
            ph.s(lambda e, cht=cht: e.activation(
                acc_a[:, :], ps_x[:, 0:512], AF.Sigmoid, bias=w["igb2col"][:, cht:cht + 1]))
            ph.v(lambda e, cht=cht: e.tensor_tensor(
                vfm[:, cht * SW:(cht + 1) * SW], acc_a[:, :],
                xup[:, cht * SW:(cht + 1) * SW], op=ALU.mult))

        for cht in range(16):
            def mm_qk(e, cht=cht):
                for of in range(2):
                    o = e.matmul(ps_q[:, of * 512:(of + 1) * 512],
                                 w["qkgtT"][:, cht * 256 + of * 128:cht * 256 + of * 128 + 128],
                                 shf[:, cht * SW:(cht + 1) * SW],
                                 start=(cht == 0), stop=(cht == 15))
                return o
            ph.t(mm_qk)
        ph.v(lambda e: e.tensor_copy(qk_t[:, :], ps_q[:, 0:512]))
        ph.s(lambda e: e.activation(gt_t[:, :], ps_q[:, 512:1024], AF.Copy, bias=0.0))
        ph.dma("gpsimd", lambda e: e.dma_start(out=k_sb[:, :], in_=qk_t[64:128, :]))
        ph.s(lambda e: e.activation(sc_a[:, :], gt_t[0:64, :], AF.Abs,
                                    bias=w["gbc"][0:64, 0:1]))
        ph.s(lambda e: e.activation(sc_a[:, :], sc_a[:, :], AF.Exp,
                                    scale=ctx.cst64(-1.0)))
        ph.s(lambda e: e.activation(sc_a[:, :], sc_a[:, :], AF.Ln,
                                    bias=ctx.cst64(1.0)))
        ph.s(lambda e: e.activation(gl_s[:, :], gt_t[0:64, :], AF.Relu,
                                    bias=w["gbc"][0:64, 0:1]))
        ph.v(lambda e: e.tensor_tensor(gl_s[:, :], gl_s[:, :], sc_a[:, :],
                                       op=ALU.add))
        ph.dma("gpsimd", lambda e: e.dma_start(out=tau_s[:, :], in_=gt_t[64:128, :]))
        ph.s(lambda e: e.activation(tau_s[:, :], tau_s[:, :], AF.Sigmoid,
                                    bias=w["tbc"][0:64, 0:1]))
        ph.s(lambda e: e.activation(it_s[:, :], gl_s[:, :], AF.Ln))
        ph.v(lambda e: e.tensor_tensor(it_s[:, :], it_s[:, :], tau_s[:, :], op=ALU.mult))
        ph.s(lambda e: e.activation(it_s[:, :], it_s[:, :], AF.Exp))
        ph.v(lambda e: e.scalar_tensor_tensor(rt_s[:, :], gl_s[:, :], ctx.cst64(-1.0),
                                              tau_s[:, :], op0=ALU.mult, op1=ALU.mult))
        ph.v(lambda e: e.tensor_tensor_scan(b_s[:, :], w["smask"][0:64, :], rt_s[:, :],
                                            0.0, op0=ALU.mult, op1=ALU.add))
        ph.v(lambda e: e.tensor_copy(seg_sc[:, 0:8],
                                     bass.AP(b_s, 63, [[SW, 64], [64, 8]])))
        ph.v(lambda e: e.tensor_copy(seg_sc[:, 8:16],
                                     bass.AP(b_s, 31, [[SW, 64], [64, 8]])))
        ph.s(lambda e: e.activation(seg_sc[:, 16:24], seg_sc[:, 0:8], AF.Exp))
        ph.v(lambda e, sg=sg: e.tensor_copy(btot_all[:, sg * 8:(sg + 1) * 8],
                                            seg_sc[:, 0:8]))
        ph.s(lambda e: e.activation(eb_s[:, :], b_s[:, :], AF.Exp))
        ph.v(lambda e, sg=sg: e.scalar_tensor_tensor(
            qb_all[:, sg * SW:(sg + 1) * SW], qk_t[0:64, :], ctx.cst64(0.125),
            eb_s[:, :], op0=ALU.mult, op1=ALU.mult))
        ph.v(lambda e: e.tensor_tensor(
            sc_a[:, :].rearrange("a (c x) -> a c x", x=CB),
            b_s[:, :].rearrange("a (c x) -> a c x", x=CB),
            bass.AP(seg_sc, 8, [[24, 64], [1, 8], [0, CB]]), op=ALU.subtract))
        ph.s(lambda e: e.activation(sc_a[:, :], sc_a[:, :], AF.Exp))
        ph.v(lambda e: e.scalar_tensor_tensor(
            qs_s[:, :], qk_t[0:64, :], ctx.cst64(0.125), sc_a[:, :],
            op0=ALU.mult, op1=ALU.mult))
        ph.v(lambda e: e.tensor_tensor(kit[:, :], k_sb[:, :], it_s[:, :], op=ALU.mult))
        ph.v(lambda e: e.tensor_tensor(
            sc_a[:, :].rearrange("a (c x) -> a c x", x=CB),
            bass.AP(seg_sc, 8, [[24, 64], [1, 8], [0, CB]]),
            b_s[:, :].rearrange("a (c x) -> a c x", x=CB), op=ALU.subtract))
        ph.s(lambda e: e.activation(sc_a[:, :], sc_a[:, :], AF.Exp))
        ph.v(lambda e: e.tensor_tensor(ks_s[:, :], kit[:, :], sc_a[:, :], op=ALU.mult))
        ph.v(lambda e: e.tensor_tensor(
            sc_a[:, :].rearrange("a (c x) -> a c x", x=CB),
            bass.AP(seg_sc, 0, [[24, 64], [1, 8], [0, CB]]),
            b_s[:, :].rearrange("a (c x) -> a c x", x=CB), op=ALU.subtract))
        ph.s(lambda e: e.activation(sc_a[:, :], sc_a[:, :], AF.Exp))
        ph.v(lambda e: e.tensor_tensor(kd_s[:, :], kit[:, :], sc_a[:, :], op=ALU.mult))
        ph.v(lambda e: e.tensor_tensor(k2b[:, :], k_sb[:, :], k_sb[:, :], op=ALU.mult))
        ph.t(lambda e: e.matmul(ps_o[0:1, 0:512], onescol[:, :], k2b[:, :],
                                start=True, stop=True))
        ph.v(lambda e: e.tensor_scalar(rr_f[:, :], ps_o[0:1, 0:512], 1e-24, None,
                                       op0=ALU.max))
        ph.s(lambda e: e.activation(rr_f[:, :], rr_f[:, :], AF.Ln))
        ph.s(lambda e: e.activation(r_row[:, :], rr_f[:, :], AF.Exp,
                                    scale=ctx.cvec[0:1, CIDX[-0.5]:CIDX[-0.5] + 1]))

        def tp_r(e):
            for cc in range(8):
                o = e.transpose(ps_t[0:64, 2 * cc:2 * cc + 1],
                                r_row[:, cc * 64:(cc + 1) * 64], ctx.ident[0:1, 0:1])
            return o
        ph.t(tp_r)
        ph.v(lambda e: e.tensor_copy(r_col[:, :],
                                     bass.AP(ps_t, 0, [[2048, 64], [2, 8]])))

        for cc in range(CPS):
            n = sg * CPS + cc
            def tp_v(e, cc=cc):
                for i in range(16):
                    o = e.transpose(ps_t[0:64, i * 128:(i + 1) * 128],
                                    vfm[:, i * SW + cc * CB:i * SW + (cc + 1) * CB],
                                    ctx.ident[:, :])
                return o
            ph.t(tp_v)
            ph.v(lambda e: e.tensor_copy(v_tm[:, :], ps_t[0:64, :]))
            ph.t(lambda e, cc=cc: e.transpose(
                ps_o[0:64, 0:64], kd_s[:, cc * CB:(cc + 1) * CB], ctx.ident[:, :])
                if False else
                e.transpose(ps_t[0:64, 0:64], kd_s[:, cc * CB:(cc + 1) * CB],
                            ctx.ident[0:64, 0:64]))
            ph.v(lambda e, cc=cc: e.tensor_scalar(
                kdt[:, :], ps_t[0:64, 0:64], r_col[:, cc:cc + 1], None, op0=ALU.mult))
            ph.t(lambda e, cc=cc: e.matmul(
                ps_ds[:, 0:64], ks_s[:, cc * CB:(cc + 1) * CB],
                qs_s[:, cc * CB:(cc + 1) * CB], start=True, stop=True))
            ph.v(lambda e, cc=cc: e.scalar_tensor_tensor(
                a_p[:, :], ps_ds[:, 0:64], r_col[:, cc:cc + 1],
                w["m64"], op0=ALU.mult, op1=ALU.mult))
            for vh in range(2):
                def mm_o(e, cc=cc, vh=vh):
                    for j in range(2):
                        e.matmul(ps_o[:, j * 512:(j + 1) * 512],
                                 qb_all[:, sg * SW + cc * CB:sg * SW + (cc + 1) * CB],
                                 s_bf[:, vh * 1024 + j * 512:vh * 1024 + (j + 1) * 512],
                                 start=True, stop=False)
                        o = e.matmul(ps_o[:, j * 512:(j + 1) * 512],
                                     a_p[:, :],
                                     v_tm[:, vh * 1024 + j * 512:vh * 1024 + (j + 1) * 512],
                                     start=False, stop=True)
                    return o
                ph.t(mm_o)

                def mm_ds(e, cc=cc, vh=vh):
                    for j in range(2):
                        o = e.matmul(ps_ds[:, j * 512:(j + 1) * 512],
                                     kdt[:, :],
                                     v_tm[:, vh * 1024 + j * 512:vh * 1024 + (j + 1) * 512],
                                     start=True, stop=True)
                    return o
                ph.t(mm_ds)
                ph.v(lambda e, vh=vh: e.tensor_copy(
                    o_ch[:, vh * 1024:(vh + 1) * 1024], ps_o[:, :]))
                ph.v(lambda e, cc=cc, vh=vh: e.scalar_tensor_tensor(
                    s_f[:, vh * 1024:(vh + 1) * 1024],
                    s_f[:, vh * 1024:(vh + 1) * 1024],
                    seg_sc[:, 16 + cc:17 + cc], ps_ds[:, :],
                    op0=ALU.mult, op1=ALU.add))
                ph.s(lambda e, vh=vh: e.activation(
                    s_bf[:, vh * 1024:(vh + 1) * 1024],
                    s_f[:, vh * 1024:(vh + 1) * 1024], AF.Copy, bias=0.0))
            ph.dma("sync", lambda e, n=n: e.dma_start(
                out=dr["o_d"][n * CB:(n + 1) * CB, :], in_=o_ch[:, :]))
        ph.run()

    ph = SerialPhase(ctx)
    ph.v(lambda e: e.tensor_tensor_scan(cumg[:, 0:NCH_B],
                                        w["ones64"][0:64, 0:NCH_B],
                                        btot_all[:, :], 0.0,
                                        op0=ALU.mult, op1=ALU.add))
    ph.v(lambda e: e.memset(cumg[:, NCH_B:NCH_B + 8], 0.0))
    ph.s(lambda e: e.activation(cumg[:, NCH_B + 1:NCH_B + 2],
                                cumg[:, NCH_B - 1:NCH_B], AF.Exp))
    ph.dma("sync", lambda e: e.dma_start(out=dr["gin_g"][:, 0:DI], in_=s_f[:, :]))
    ph.dma("sync", lambda e: e.dma_start(out=dr["gin_g"][:, DI:DI + 1],
                                         in_=cumg[:, NCH_B + 1:NCH_B + 2],
                                         allow_slow_non_contiguous=True))
    ph.g(lambda e: e.collective_compute(
        "AllGather", ALU.bypass, replica_groups=RG,
        ins=[dr["gin_g"].ap().opt()], outs=[dr["gout_g"].ap().opt()]))
    ph.run()
    ctx.srel()

    ctx.smark()
    gg = ctx.salloc("b_gg", [64, DI + 32], F32)
    gsc = ctx.salloc("b_gsc", [64, 8], F32)
    ph = SerialPhase(ctx)
    ph.v(lambda e: e.memset(s_f[:, :], 0.0))
    for i in range(NCORE - 1):
        ph.dma("sync", lambda e, i=i: e.dma_start(out=gg[:, 0:DI + 1],
                                                  in_=dr["gout_g"][i, :, 0:DI + 1]))
        ph.v(lambda e, i=i: e.tensor_scalar(
            gsc[:, 0:1], gg[:, DI:DI + 1], w["pm"][0:64, i:i + 1],
            w["pim"][0:64, i:i + 1], op0=ALU.mult, op1=ALU.add))
        ph.v(lambda e, i=i: e.tensor_scalar(
            gg[:, 0:DI], gg[:, 0:DI], w["pm"][0:64, i:i + 1], None, op0=ALU.mult))
        ph.v(lambda e, i=i: e.scalar_tensor_tensor(
            s_f[:, :], s_f[:, :], gsc[:, 0:1], gg[:, 0:DI],
            op0=ALU.mult, op1=ALU.add))
    ph.s(lambda e: e.activation(s_bf[:, :], s_f[:, :], AF.Copy, bias=0.0))
    ph.run()
    ctx.dump("s0b", s_f[:, :])

    ctx.smark()
    w["gateT"] = ctx.salloc("b_gateT", [128, 8 * DI], BF16)
    w["downT"] = ctx.salloc("b_downT", [128, 16 * DM], BF16)
    ecg = ctx.salloc("b_ecg", [64, NCH_B], F32)
    qbd = ctx.salloc("b_qbd", [64, 2 * CB], BF16)
    o_t = ctx.salloc("b_ot", [128, DI], BF16)
    sh_t = ctx.salloc("b_sht", [128, DI], BF16)
    g2_t = ctx.salloc("b_g2t", [128, DI], BF16)
    m1_t = ctx.salloc("b_m1t", [128, DI], BF16)
    xg2_t = ctx.salloc("b_xg2t", [128, DI], BF16)
    xgf_t = ctx.salloc("b_xgft", [128, DI], BF16)
    xgT_t = ctx.salloc("b_xgTt", [128, DI], BF16)
    ht2 = ctx.salloc("b_ht2", [128, DM], F32)
    rms_c = ctx.salloc("b_rmsc", [128, 4], F32)
    sqs = ctx.salloc("b_sqs", [128, DI], BF16)

    ph = SerialPhase(ctx)
    ph.dma("gpsimd", lambda e: e.dma_start(
        out=w["gateT"][:, :],
        in_=bass.AP(dr["gateT_d"], 0, [[DI, 128], [128 * DI, 8], [1, DI]])))
    ph.dma("gpsimd", lambda e: e.dma_start(
        out=w["downT"][:, :],
        in_=bass.AP(dr["downT_d"], 0, [[DM, 128], [128 * DM, 16], [1, DM]])))
    ph.v(lambda e: e.memset(ecg[:, 0:1], 0.0))
    ph.v(lambda e: e.tensor_copy(ecg[:, 1:NCH_B], cumg[:, 0:NCH_B - 1]))
    ph.s(lambda e: e.activation(ecg[:, :], ecg[:, :], AF.Exp))
    ph.run()

    est_p1.close()
    ps_big = est.enter_context(nc.psum_tensor("b_psbig", [128, DI], F32))
    ps_tp2 = est.enter_context(nc.psum_tensor("b_pstp2", [128, DI], BF16))
    ps_dn = est.enter_context(nc.psum_tensor("b_psdn", [128, DM], F32))

    for t in range(NTILE):
        ph = SerialPhase(ctx)
        p = t % 2
        ph.dma("sync", lambda e, t=t: e.dma_start(
            out=ht2[:, :], in_=h_in_d[t * 128:(t + 1) * 128, :]))
        ph.v(lambda e, t=t: e.tensor_scalar(
            xn3[:, 0:DM], ht2[:, :],
            stats[:, 2 * t:2 * t + 1], stats[:, 2 * t + 1:2 * t + 2],
            op0=ALU.subtract, op1=ALU.mult))

        def tp_xn(e):
            for i in range(8):
                o = e.transpose(ps_tp2[:, i * 128:(i + 1) * 128],
                                xn3[:, i * 128:(i + 1) * 128],
                                ctx.ident[:, :])
            return o
        ph.t(tp_xn)
        ph.v(lambda e: e.tensor_copy(xgT_t[:, 0:1024], ps_tp2[:, 0:1024]))

        def mm_fg(e):
            for q4 in range(4):
                for dt in range(8):
                    o = e.matmul(ps_big[:, q4 * 512:(q4 + 1) * 512],
                                 xgT_t[:, dt * 128:(dt + 1) * 128],
                                 w["gateT"][:, dt * DI + q4 * 512:dt * DI + (q4 + 1) * 512],
                                 start=(dt == 0), stop=(dt == 7))
            return o
        ph.t(mm_fg)
        ph.s(lambda e: e.activation(g2_t[:, :], ps_big[:, :], AF.Silu))

        ph.dma("sync", lambda e, t=t: e.dma_start(
            out=o_t[:, :], in_=dr["o_d"][t * 128:(t + 1) * 128, :]))
        for half in range(2):
            def qbd_f(e, t=t, half=half):
                o = e.tensor_scalar(
                    qbd[:, half * CB:(half + 1) * CB],
                    qb_all[:, t * 128 + half * CB:t * 128 + (half + 1) * CB],
                    ecg[:, 2 * t + half:2 * t + half + 1], None, op0=ALU.mult)
                return o
            ph.v(qbd_f)

        def mm_oc(e, t=t):
            for half in range(2):
                for j in range(4):
                    o = e.matmul(ps_big[0:64, j * 512:(j + 1) * 512] if half == 0
                                 else ps_big[64:128, j * 512:(j + 1) * 512],
                                 qbd[:, half * CB:(half + 1) * CB],
                                 s_bf[:, j * 512:(j + 1) * 512],
                                 start=True, stop=True,
                                 tile_position=(0, 64) if half == 1 else None)
            return o
        ph.t(mm_oc)
        ph.v(lambda e: e.tensor_tensor(o_t[:, :], o_t[:, :],
                                       ps_big[:, :], op=ALU.add))
        if t == 0 and "p2t0" in ctx.dbg:
            dbgt = nc.dram_tensor("dbg_p2t0", [128, 3 * DI + 128], BF16,
                                  kind="ExternalOutput")
            ctx.dbg_outs["p2t0"] = [128, 3 * DI + 128]
            ph.dma("sync", lambda e: e.dma_start(out=dbgt[:, 0:DI], in_=o_t[:, :]))
            ph.dma("sync", lambda e: e.dma_start(out=dbgt[:, DI:2 * DI], in_=g2_t[:, :]))
            ph.dma("sync", lambda e: e.dma_start(out=dbgt[0:64, 3 * DI:3 * DI + 128],
                                                 in_=qbd[:, :]))

        ph.dma("sync", lambda e, t=t: e.dma_start(
            out=sh_t[:, :], in_=dr["sh_d"][t * 128:(t + 1) * 128, :]))
        ph.v(lambda e: e.tensor_tensor(m1_t[:, :], sh_t[:, :], w["rwrow"][:, :],
                                       op=ALU.mult))
        ph.v(lambda e: e.tensor_tensor(m1_t[:, :], m1_t[:, :], o_t[:, :],
                                       op=ALU.add))
        ph.v(lambda e: e.tensor_tensor(xg2_t[:, :], m1_t[:, :], g2_t[:, :],
                                       op=ALU.mult))
        ph.s(lambda e: e.activation(sqs[:, :], xg2_t[:, :], AF.Square,
                                    accum_out=rms_c[:, 0:1]))
        ph.s(lambda e: e.activation(rms_c[:, 1:2], rms_c[:, 0:1], AF.Ln,
                                    scale=ctx.cst(1.0 / DI), bias=ctx.cst(1e-5)))
        ph.s(lambda e: e.activation(rms_c[:, 1:2], rms_c[:, 1:2], AF.Exp,
                                    scale=ctx.cst(-0.5)))
        ph.v(lambda e: e.tensor_tensor(xgf_t[:, :], xg2_t[:, :], w["nwrow"][:, :],
                                       op=ALU.mult))
        ph.v(lambda e: e.tensor_scalar(xgf_t[:, :], xgf_t[:, :], rms_c[:, 1:2],
                                       None, op0=ALU.mult))

        def tp_xg1(e):
            for i in range(8):
                o = e.transpose(ps_tp2[:, i * 128:(i + 1) * 128],
                                xgf_t[:, i * 128:(i + 1) * 128], ctx.ident[:, :])
            return o

        def tp_xg2_(e):
            for i in range(8, 16):
                o = e.transpose(ps_tp2[:, (i - 8) * 128:(i - 7) * 128],
                                xgf_t[:, i * 128:(i + 1) * 128], ctx.ident[:, :])
            return o
        ph.t(tp_xg1)
        ph.v(lambda e: e.tensor_copy(xgT_t[:, 0:1024], ps_tp2[:, 0:1024]))
        ph.t(tp_xg2_)
        ph.v(lambda e: e.tensor_copy(xgT_t[:, 1024:2048], ps_tp2[:, 0:1024]))

        def mm_dn(e):
            for half in range(2):
                for cht in range(16):
                    o = e.matmul(ps_dn[:, half * 512:(half + 1) * 512],
                                 xgT_t[:, cht * 128:(cht + 1) * 128],
                                 w["downT"][:, cht * DM + half * 512:cht * DM + (half + 1) * 512],
                                 start=(cht == 0), stop=(cht == 15))
            return o
        ph.t(mm_dn)
        ph.v(lambda e: e.tensor_tensor(ht2[:, :], ht2[:, :],
                                       ps_dn[:, :], op=ALU.add))
        ph.dma("sync", lambda e, t=t: e.dma_start(
            out=h_out_d[t * 128:(t + 1) * 128, :], in_=ht2[:, :]))
        ph.run()
    ctx.srel()
    ctx.srel()
    ctx.srel()


def emit_ffn(ctx, est, tag, h_in_d, h_out_d, phi_sb, wsh_d, bsh_d,
             gin_s, gout_s, gin_y, gout_y):
    nc = ctx.nc
    ctx.smark()
    xnT = ctx.salloc(f"f{tag}_xnT", [128, 8 * TPC], BF16)
    s_col = ctx.salloc(f"f{tag}_scol", [128, 8], F32)
    s_bf = ctx.salloc(f"f{tag}_sbf", [128, 8], BF16)
    sg = ctx.salloc(f"f{tag}_sg", [128, 64], F32)
    y_bf = ctx.salloc(f"f{tag}_ybf", [64, DM], BF16)
    est_p = ExitStack()
    emit_ln_transpose(ctx, est_p, xnT, x_dram=h_in_d)
    est_p.close()

    ctx.smark()
    w_e = ctx.salloc(f"f{tag}_we", [128, 2 * 8 * DM], BF16)
    y_loc = ctx.salloc(f"f{tag}_yloc", [1, EPC * DM], F32)
    b_loc = ctx.salloc(f"f{tag}_bloc", [1, EPC * DM], F32)

    est_y = ExitStack()
    ps_y = est_y.enter_context(nc.psum_tensor(f"f{tag}_psy", [1, DM], F32))

    ph = Pipe(ctx)
    ph.v(lambda e: e.tensor_reduce(
        s_col[:, :], xnT[:, :].rearrange("a (d t) -> a d t", t=TPC),
        axis=AX.X, op=ALU.add))
    ph.dma("sync", lambda e: e.dma_start(out=gin_s.ap(), in_=s_col[:, :]))
    ph.g(lambda e: e.collective_compute(
        "AllGather", ALU.bypass, replica_groups=RG,
        ins=[gin_s.ap().opt()], outs=[gout_s.ap().opt()]))
    ph.dma("sync", lambda e: e.dma_start(
        out=sg[:, :], in_=bass.AP(gout_s, 0, [[8, 128], [1024, 8], [1, 8]])))
    ph.dma("sync", lambda e: e.dma_start(
        out=b_loc[:, :], in_=bass.AP(bsh_d, 0, [[EPC * DM, 1], [1, EPC * DM]])),
        deps="all")

    def sum_s(e):
        for i in range(1, 8):
            o = e.tensor_tensor(s_col[:, :], sg[:, 0:8] if i == 1 else s_col[:, :],
                                sg[:, i * 8:(i + 1) * 8], op=ALU.add)
            e.drain()
        return o
    ph.v(sum_s)
    ph.v(lambda e: e.tensor_copy(s_bf[:, :], s_col[:, :]))
    ph.run()
    ctx.dump("s_tot" + tag, s_col[:, :])

    ph = Pipe(ctx)
    h_mm = {}
    h_ev = {}
    h_dw = {}
    for e_i in range(EPC):
        h_dw[e_i] = ph.dma("gpsimd", lambda e, e_i=e_i: e.dma_start(
            out=w_e[:, (e_i % 2) * 8 * DM:(e_i % 2 + 1) * 8 * DM],
            in_=bass.AP(wsh_d, e_i * DM * DM, [[DM, 128], [128 * DM, 8], [1, DM]])),
            deps=[h_mm.get(e_i - 2)])

        def mm_y(e, e_i=e_i):
            for half in range(2):
                for dt in range(8):
                    o = e.matmul(ps_y[:, half * 512:(half + 1) * 512],
                                 s_bf[:, dt:dt + 1],
                                 w_e[:, (e_i % 2) * 8 * DM + dt * DM + half * 512:
                                     (e_i % 2) * 8 * DM + dt * DM + (half + 1) * 512],
                                 start=(dt == 0), stop=(dt == 7))
            return o
        h_mm[e_i] = ph.t(mm_y, deps=[h_dw[e_i], h_ev.get(e_i - 1)])
        h_ev[e_i] = ph.v(lambda e, e_i=e_i: e.tensor_copy(
            y_loc[:, e_i * DM:(e_i + 1) * DM], ps_y[:, :]), deps=[h_mm[e_i]])
    ph.v(lambda e: e.tensor_tensor(y_loc[:, :], y_loc[:, :], b_loc[:, :],
                                   op=ALU.add), deps="all")
    ph.dma("sync", lambda e: e.dma_start(
        out=bass.AP(gin_y, 0, [[EPC * DM, 1], [1, EPC * DM]]), in_=y_loc[:, :]),
        deps="all")
    ph.g(lambda e: e.collective_compute(
        "AllGather", ALU.bypass, replica_groups=RG,
        ins=[gin_y.ap().opt()], outs=[gout_y.ap().opt()]), deps="all")
    ph.dma("gpsimd", lambda e: e.dma_start(
        out=y_bf[:, :], in_=bass.AP(gout_y, 0, [[DM, 64], [1, DM]])), deps="all")
    ph.run()
    est_y.close()
    ctx.srel()
    el = ctx.salloc(f"f{tag}_el", [128, 2 * 2048], BF16)
    zc = ctx.salloc(f"f{tag}_zc", [128, 8], F32)
    craw = ctx.salloc(f"f{tag}_craw", [128, 2 * 64], F32)
    c_bf = ctx.salloc(f"f{tag}_cbf", [128, 2 * 64], BF16)
    cT = ctx.salloc(f"f{tag}_cT", [64, NTILE * 128], BF16)
    ht = ctx.salloc(f"f{tag}_ht", [128, 2 * DM], F32)

    ps_l = est.enter_context(nc.psum_tensor(f"f{tag}_psl", [128, 2048], F32))
    ps_c = est.enter_context(nc.psum_tensor(f"f{tag}_psc", [64, 128], BF16))
    ps_o = est.enter_context(nc.psum_tensor(f"f{tag}_pso", [128, DM], F32))
    ctx.dump("y" + tag, y_bf[:, :])

    ph = Pipe(ctx)
    H_exp = {}
    H_red = {}
    H_rcp = {}
    H_scl = {}
    H_tpc = {}
    H_ecT = {}
    H_add = {}
    H_out = {}
    H_h = {}
    for t in range(NTILE):
        p = t % 2
        H_h[t] = ph.dma("sync", lambda e, t=t, p=p: e.dma_start(
            out=ht[:, p * DM:(p + 1) * DM], in_=h_in_d[t * 128:(t + 1) * 128, :]),
            deps=[H_out.get(t - 2)])

        def mm_l(e, t=t):
            for q4 in range(4):
                for dt in range(8):
                    o = e.matmul(ps_l[:, q4 * 512:(q4 + 1) * 512],
                                 xnT[:, dt * TPC + t * 128:dt * TPC + (t + 1) * 128],
                                 phi_sb[:, dt * 2048 + q4 * 512:dt * 2048 + (q4 + 1) * 512],
                                 start=(dt == 0), stop=(dt == 7))
            return o
        l1 = ph.t(mm_l, deps=[H_exp.get(t - 1)])
        H_exp[t] = ph.s(lambda e, p=p: e.activation(
            el[:, p * 2048:(p + 1) * 2048], ps_l[:, :], AF.Exp,
            bias=ctx.cst(-8.0), accum_out=zc[:, p * 4:p * 4 + 1]),
            deps=[l1, H_scl.get(t - 2)])
        H_red[t] = ph.v(lambda e, p=p: e.tensor_reduce(
            craw[:, p * 64:(p + 1) * 64],
            el[:, p * 2048:(p + 1) * 2048].rearrange("a (n q) -> a n q", q=32),
            axis=AX.X, op=ALU.add), deps=[H_exp[t]])
        H_rcp[t] = ph.v(lambda e, p=p: e.reciprocal(
            zc[:, p * 4 + 1:p * 4 + 2], zc[:, p * 4:p * 4 + 1]), deps=[H_exp[t]])

        def scl(e, p=p):
            e.drain()
            return e.tensor_scalar(c_bf[:, p * 64:(p + 1) * 64],
                                   craw[:, p * 64:(p + 1) * 64],
                                   zc[:, p * 4 + 1:p * 4 + 2], None, op0=ALU.mult)
        H_scl[t] = ph.v(scl, deps=[H_tpc.get(t - 2)])
        H_tpc[t] = ph.t(lambda e, p=p: e.transpose(
            ps_c[:, :], c_bf[:, p * 64:(p + 1) * 64], ctx.ident[:, :]),
            deps=[H_scl[t], H_ecT.get(t - 1)])
        H_ecT[t] = ph.v(lambda e, t=t: e.tensor_copy(
            cT[:, t * 128:(t + 1) * 128], ps_c[:, :]), deps=[H_tpc[t]])

        def mm_o(e, t=t):
            for half in range(2):
                o = e.matmul(ps_o[:, half * 512:(half + 1) * 512],
                             cT[:, t * 128:(t + 1) * 128],
                             y_bf[:, half * 512:(half + 1) * 512],
                             start=True, stop=True)
            return o
        oo = ph.t(mm_o, deps=[H_ecT[t], H_add.get(t - 1)])
        H_add[t] = ph.v(lambda e, p=p: e.tensor_tensor(
            ht[:, p * DM:(p + 1) * DM], ht[:, p * DM:(p + 1) * DM],
            ps_o[:, :], op=ALU.add), deps=[oo, H_h[t]])
        H_out[t] = ph.dma("sync", lambda e, t=t, p=p: e.dma_start(
            out=h_out_d[t * 128:(t + 1) * 128, :], in_=ht[:, p * DM:(p + 1) * DM]),
            deps=[H_add[t]])
    ph.run()
    ctx.srel()


def build_program(dbg=()):
    nc = bass.Bass("TRN2", target_bir_lowering=False)
    ctx = Ctx(nc, dbg)

    x_d = nc.dram_tensor("x", [TPC, DM], F32, kind="ExternalInput")
    wq_d = nc.dram_tensor("wq", [DM, DM], F32, kind="ExternalInput")
    wk_d = nc.dram_tensor("wk", [DM, DM], F32, kind="ExternalInput")
    wv_d = nc.dram_tensor("wv", [DM, DM], F32, kind="ExternalInput")
    wb_d = nc.dram_tensor("wbeta", [DM, H], F32, kind="ExternalInput")
    wo_d = nc.dram_tensor("wo", [DM, DM], F32, kind="ExternalInput")
    ident_d = nc.dram_tensor("ident", [128, 128], BF16, kind="ExternalInput")
    cvec_d = nc.dram_tensor("cvec", [128, len(CONSTS)], F32, kind="ExternalInput")
    m_slo_d = nc.dram_tensor("m_slo", [128, 128], BF16, kind="ExternalInput")
    m_trq_d = nc.dram_tensor("m_trq", [128, 128], BF16, kind="ExternalInput")
    idpat_d = nc.dram_tensor("idpat", [128, 2048], BF16, kind="ExternalInput")
    id4_d = nc.dram_tensor("id4", [128, 512], BF16, kind="ExternalInput")
    pmask_d = nc.dram_tensor("pmask", [128, 8], F32, kind="ExternalInput")
    pimask_d = nc.dram_tensor("pimask", [128, 8], F32, kind="ExternalInput")
    gn_col_d = nc.dram_tensor("gn_col", [128, 8], F32, kind="ExternalInput")
    out_d = nc.dram_tensor("out", [TPC, DM], F32, kind="ExternalOutput")

    phi_d = nc.dram_tensor("phi2", [DM, 2048], F32, kind="ExternalInput")
    wsh_d = nc.dram_tensor("wsh", [EPC, DM, DM], F32, kind="ExternalInput")
    bsh_d = nc.dram_tensor("bsh", [EPC, DM], F32, kind="ExternalInput")

    gin_bp = nc.dram_tensor("gin_bp", [128, 4096], BF16)
    gout_bp = nc.dram_tensor("gout_bp", [8, 128, 4096], BF16)
    h1_d = nc.dram_tensor("h1_d", [TPC, DM], F32)
    h2_d = nc.dram_tensor("h2_d", [TPC, DM], F32)
    gin_s1 = nc.dram_tensor("gin_s1", [128, 8], F32)
    gout_s1 = nc.dram_tensor("gout_s1", [8, 128, 8], F32)
    gin_y1 = nc.dram_tensor("gin_y1", [EPC, DM], F32)
    gout_y1 = nc.dram_tensor("gout_y1", [8, EPC, DM], F32)
    gin_s2 = nc.dram_tensor("gin_s2", [128, 8], F32)
    gout_s2 = nc.dram_tensor("gout_s2", [8, 128, 8], F32)
    gin_y2 = nc.dram_tensor("gin_y2", [EPC, DM], F32)
    gout_y2 = nc.dram_tensor("gout_y2", [8, EPC, DM], F32)

    h3_d = nc.dram_tensor("h3_d", [TPC, DM], F32)
    sh_d = nc.dram_tensor("sh_d", [TPC, DI], BF16)
    o_d = nc.dram_tensor("o_d", [TPC, DI], BF16)
    gin_h = nc.dram_tensor("gin_h", [128, 48], F32)
    gout_h = nc.dram_tensor("gout_h", [8, 128, 48], F32)
    gin_g = nc.dram_tensor("gin_g", [64, DI + 32], F32)
    gout_g = nc.dram_tensor("gout_g", [8, 64, DI + 32], F32)
    upT_d = nc.dram_tensor("upT", [DM, DI], F32, kind="ExternalInput")
    gateT_d = nc.dram_tensor("gateT", [DM, DI], F32, kind="ExternalInput")
    downT_d = nc.dram_tensor("downT", [DI, DM], F32, kind="ExternalInput")
    qkgtT_d = nc.dram_tensor("qkgtT", [DI, 256], F32, kind="ExternalInput")
    ig1T_d = nc.dram_tensor("ig1T", [DI, 16], F32, kind="ExternalInput")
    ig2T_d = nc.dram_tensor("ig2T", [16, DI], F32, kind="ExternalInput")
    cwcol_d = nc.dram_tensor("cwcol", [128, 64], F32, kind="ExternalInput")
    cbcol_d = nc.dram_tensor("cbcol", [128, 16], F32, kind="ExternalInput")
    igb2_d = nc.dram_tensor("igb2col", [128, 16], F32, kind="ExternalInput")
    gbc_d = nc.dram_tensor("gbc", [128, 1], F32, kind="ExternalInput")
    tbc_d = nc.dram_tensor("tbc", [128, 1], F32, kind="ExternalInput")
    rwrow_d = nc.dram_tensor("rwrow", [128, DI], F32, kind="ExternalInput")
    nwrow_d = nc.dram_tensor("nwrow", [128, DI], F32, kind="ExternalInput")
    smask_d = nc.dram_tensor("smask", [128, 512], F32, kind="ExternalInput")
    ones64_d = nc.dram_tensor("ones64", [128, 64], F32, kind="ExternalInput")
    hmask_d = nc.dram_tensor("hmask", [128, 8], F32, kind="ExternalInput")

    top = ctx.top
    ctx.ident = top.enter_context(nc.sbuf_tensor("ident_sb", [128, 128], BF16))
    ctx.cvec = top.enter_context(nc.sbuf_tensor("cvec_sb", [128, len(CONSTS)], F32))
    mask_slo = top.enter_context(nc.sbuf_tensor("m_slo_sb", [128, 128], BF16))
    mask_trq = top.enter_context(nc.sbuf_tensor("m_trq_sb", [128, 128], BF16))
    idpat = top.enter_context(nc.sbuf_tensor("idpat_sb", [128, 2048], BF16))
    id4 = top.enter_context(nc.sbuf_tensor("id4_sb", [128, 512], BF16))
    pmask = top.enter_context(nc.sbuf_tensor("pmask_sb", [128, 8], F32))
    pimask = top.enter_context(nc.sbuf_tensor("pimask_sb", [128, 8], F32))
    gn_col = top.enter_context(nc.sbuf_tensor("gn_col_sb", [128, 8], F32))
    ctx.init_arena()

    with top:
        s0 = ctx.sem()
        with nc.Block(no_gpsimd_drain=True) as blk:
            @blk.sync
            def _(e):
                for dst, srcd in ((ctx.ident, ident_d), (ctx.cvec, cvec_d),
                                  (mask_slo, m_slo_d), (mask_trq, m_trq_d),
                                  (idpat, idpat_d), (id4, id4_d),
                                  (pmask, pmask_d), (pimask, pimask_d),
                                  (gn_col, gn_col_d)):
                    o = e.dma_start(out=dst[:, :], in_=srcd.ap())
                    inc(o, s0, 16)
                e.wait_ge(s0.h, s0.v)

        ctx.smark()
        q_tm = ctx.salloc("q_tm", [128, NTILE * DM], BF16)
        k_tm = ctx.salloc("k_tm", [128, NTILE * DM], BF16)
        beta_sb = ctx.salloc("beta", [128, NTILE * H], F32)
        v_tm = ctx.salloc("v_tm", [128, NTILE * DM], BF16)

        ctx.smark()
        xnT = ctx.salloc("xnT", [128, 8 * TPC], BF16)
        wq_sb = ctx.salloc("wq_sb", [128, 8 * DM], BF16)
        wk_sb = ctx.salloc("wk_sb", [128, 8 * DM], BF16)
        wv_sb = ctx.salloc("wv_sb", [128, 8 * DM], BF16)
        wbe_sb = ctx.salloc("wbe_sb", [128, 8 * H], BF16)
        sw = ctx.sem()
        with nc.Block(no_gpsimd_drain=True) as blk:
            @blk.gpsimd
            def _(e):
                for w_d, w_sb, cols in ((wq_d, wq_sb, DM), (wk_d, wk_sb, DM),
                                        (wv_d, wv_sb, DM), (wb_d, wbe_sb, H)):
                    o = e.dma_start(
                        out=w_sb[:, :],
                        in_=bass.AP(w_d, 0, [[cols, 128], [128 * cols, 8], [1, cols]]))
                    inc(o, sw, 16)
                e.wait_ge(sw.h, sw.v)

        est_p = ExitStack()
        emit_ln_transpose(ctx, est_p, xnT, x_dram=x_d)
        est_p.close()
        ctx.dump("xnT", xnT[:, :])

        est_p = ExitStack()
        emit_proj_qkvb(ctx, est_p, xnT, wq_sb, wk_sb, wv_sb, wbe_sb,
                       q_tm, k_tm, v_tm, beta_sb)
        est_p.close()
        ctx.srel()
        ctx.dump("q_tm", q_tm[:, :])
        ctx.dump("k_tm", k_tm[:, :])
        ctx.dump("v_tm", v_tm[:, :])
        ctx.dump("beta", beta_sb[:, :])

        o_tm = v_tm
        est_d = ExitStack()
        emit_attn_a_delta(ctx, est_d, q_tm, k_tm, v_tm, beta_sb, o_tm,
                          gin_bp, gout_bp, mask_slo, mask_trq, idpat, id4,
                          pmask, pimask)
        est_d.close()

        ctx.smark()
        est_p6 = ExitStack()
        h_sb = ctx.salloc("h_sb", [128, 2 * DM], F32)
        wo_sb = ctx.salloc("wo_sb", [128, 8 * DM], BF16)
        xt6 = ctx.salloc("xt6", [128, 2 * DM], F32)
        onr = ctx.salloc("onr", [128, DM], BF16)
        sq6 = ctx.salloc("sq6", [128, DM], F32)
        rs6 = ctx.salloc("rs6", [128, 4], F32)
        onT = ctx.salloc("onT", [128, DM], BF16)
        pt6 = est_p6.enter_context(nc.psum_tensor("pt6", [128, DM], BF16))
        pr6 = est_p6.enter_context(nc.psum_tensor("pr6", [128, DM], F32))

        ph = SerialPhase(ctx)
        ph.dma("gpsimd", lambda e: e.dma_start(
            out=wo_sb[:, :],
            in_=bass.AP(wo_d, 0, [[DM, 128], [128 * DM, 8], [1, DM]])))

        def scale_wo(e):
            for dt in range(8):
                o = e.tensor_scalar(wo_sb[:, dt * DM:(dt + 1) * DM],
                                    wo_sb[:, dt * DM:(dt + 1) * DM],
                                    gn_col[:, dt:dt + 1], None, op0=ALU.mult)
            return o
        ph.v(scale_wo)
        for t in range(NTILE):
            ph.dma("sync", lambda e, t=t: e.dma_start(
                out=xt6[:, (t % 2) * DM:(t % 2 + 1) * DM],
                in_=x_d[t * 128:(t + 1) * 128, :]))
            ph.v(lambda e, t=t: e.tensor_tensor(
                sq6[:, :], o_tm[:, t * DM:(t + 1) * DM],
                o_tm[:, t * DM:(t + 1) * DM], op=ALU.mult))
            ph.v(lambda e, t=t: e.tensor_reduce(
                rs6[:, :], sq6[:, :].rearrange("a (h d) -> a h d", d=DK),
                axis=AX.X, op=ALU.add))
            ph.s(lambda e: e.activation(rs6[:, :], rs6[:, :], AF.Ln,
                                        scale=ctx.cst(1.0 / 256), bias=ctx.cst(1e-5)))
            ph.s(lambda e: e.activation(rs6[:, :], rs6[:, :], AF.Exp,
                                        scale=ctx.cst(-0.5)))
            ph.v(lambda e, t=t: e.tensor_tensor(
                onr[:, :].rearrange("a (h d) -> a h d", d=DK),
                o_tm[:, t * DM:(t + 1) * DM].rearrange("a (h d) -> a h d", d=DK),
                bass.AP(rs6, 0, [[4, 128], [1, H], [0, DK]]), op=ALU.mult))

            def tp_on(e, t=t):
                for i in range(8):
                    o = e.transpose(pt6[:, i * 128:(i + 1) * 128],
                                    onr[:, i * 128:(i + 1) * 128], ctx.ident[:, :])
                return o
            ph.t(tp_on)
            ph.v(lambda e: e.tensor_copy(onT[:, :], pt6[:, :]))

            def mm_wo(e, t=t):
                for half in range(2):
                    for dt in range(8):
                        o = e.matmul(pr6[:, half * 512:(half + 1) * 512],
                                     onT[:, dt * 128:(dt + 1) * 128],
                                     wo_sb[:, dt * DM + half * 512:dt * DM + (half + 1) * 512],
                                     start=(dt == 0), stop=(dt == 7))
                return o
            ph.t(mm_wo)
            ph.v(lambda e, t=t: e.tensor_tensor(
                h_sb[:, (t % 2) * DM:(t % 2 + 1) * DM],
                xt6[:, (t % 2) * DM:(t % 2 + 1) * DM],
                pr6[:, :], op=ALU.add))
            ph.dma("sync", lambda e, t=t: e.dma_start(
                out=h1_d[t * 128:(t + 1) * 128, :],
                in_=h_sb[:, (t % 2) * DM:(t % 2 + 1) * DM]))
        ph.run()
        est_p6.close()
        ctx.srel()
        ctx.srel()
        ctx.dump_dram("h1", h1_d)

        ctx.smark()
        phi_sb = ctx.salloc("phi_sb", [128, 8 * 2048], BF16)
        sph = ctx.sem()
        with nc.Block(no_gpsimd_drain=True) as blk:
            @blk.gpsimd
            def _(e):
                o = e.dma_start(out=phi_sb[:, :],
                                in_=bass.AP(phi_d, 0, [[2048, 128],
                                                       [128 * 2048, 8], [1, 2048]]))
                inc(o, sph, 16)
                e.wait_ge(sph.h, sph.v)
        est_f1 = ExitStack()
        emit_ffn(ctx, est_f1, "1", h1_d, h2_d, phi_sb, wsh_d, bsh_d,
                 gin_s1, gout_s1, gin_y1, gout_y1)
        est_f1.close()
        ctx.srel()
        ctx.dump_dram("h2", h2_d)

        ctx.smark()
        w = {
            "upT": ctx.salloc("b_upT", [128, 8 * DI], BF16),
            "qkgtT": ctx.salloc("b_qkgtT", [128, 16 * 256], BF16),
            "ig1T": ctx.salloc("b_ig1T", [128, 16 * 16], BF16),
            "ig2T": ctx.salloc("b_ig2T", [16, DI], BF16),
            "cwcol": ctx.salloc("b_cw", [128, 64], F32),
            "cbcol": ctx.salloc("b_cb", [128, 16], F32),
            "igb2col": ctx.salloc("b_igb", [128, 16], F32),
            "gbc": ctx.salloc("b_gbc", [128, 1], F32),
            "tbc": ctx.salloc("b_tbc", [128, 1], F32),
            "rwrow": ctx.salloc("b_rw", [128, DI], F32),
            "nwrow": ctx.salloc("b_nw", [128, DI], F32),
            "smask": ctx.salloc("b_sm", [128, 512], F32),
            "ones64": ctx.salloc("b_o64", [128, 64], F32),
            "hmask": ctx.salloc("b_hm", [128, 8], F32),
            "m64": mask_trq[0:64, 0:64],
            "pm": pmask,
            "pim": pimask,
        }
        swb = ctx.sem()
        with nc.Block(no_gpsimd_drain=True) as blk:
            @blk.gpsimd
            def _(e):
                o = e.dma_start(out=w["upT"][:, :],
                                in_=bass.AP(upT_d, 0, [[DI, 128], [128 * DI, 8], [1, DI]]))
                inc(o, swb, 16)
                o = e.dma_start(out=w["qkgtT"][:, :],
                                in_=bass.AP(qkgtT_d, 0, [[256, 128], [128 * 256, 16], [1, 256]]))
                inc(o, swb, 16)
                o = e.dma_start(out=w["ig1T"][:, :],
                                in_=bass.AP(ig1T_d, 0, [[16, 128], [128 * 16, 16], [1, 16]]))
                inc(o, swb, 16)
                o = e.dma_start(out=w["ig2T"][:, :], in_=ig2T_d.ap())
                inc(o, swb, 16)
            @blk.sync
            def _(e):
                for dst, srcd in ((w["cwcol"], cwcol_d), (w["cbcol"], cbcol_d),
                                  (w["igb2col"], igb2_d), (w["gbc"], gbc_d),
                                  (w["tbc"], tbc_d), (w["rwrow"], rwrow_d),
                                  (w["nwrow"], nwrow_d), (w["smask"], smask_d),
                                  (w["ones64"], ones64_d), (w["hmask"], hmask_d)):
                    o = e.dma_start(out=dst[:, :], in_=srcd.ap())
                    inc(o, swb, 16)
                e.wait_ge(swb.h, swb.v)
            @blk.gpsimd
            def _(e):
                e.wait_ge(swb.h, swb.v)
        dr = {"sh_d": sh_d, "o_d": o_d, "gin_h": gin_h, "gout_h": gout_h,
              "gin_g": gin_g, "gout_g": gout_g, "gateT_d": gateT_d,
              "downT_d": downT_d}
        est_b = ExitStack()
        emit_attn_b(ctx, est_b, h2_d, h3_d, w, dr)
        est_b.close()
        ctx.srel()
        ctx.dump_dram("h3", h3_d)

        ctx.smark()
        phi_sb2 = ctx.salloc("phi_sb2", [128, 8 * 2048], BF16)
        sph2 = ctx.sem()
        with nc.Block(no_gpsimd_drain=True) as blk:
            @blk.gpsimd
            def _(e):
                o = e.dma_start(out=phi_sb2[:, :],
                                in_=bass.AP(phi_d, 0, [[2048, 128],
                                                       [128 * 2048, 8], [1, 2048]]))
                inc(o, sph2, 16)
                e.wait_ge(sph2.h, sph2.v)
        est_f2 = ExitStack()
        emit_ffn(ctx, est_f2, "2", h3_d, out_d, phi_sb2, wsh_d, bsh_d,
                 gin_s2, gout_s2, gin_y2, gout_y2)
        est_f2.close()
        ctx.srel()
        top.pop_all()
    return nc, ctx


def _prep_inputs(inputs):
    g = lambda n: np.asarray(inputs[n], np.float32)
    x = g("L0_attn_a_x").reshape(16384, DM)
    ident = np.eye(128, dtype=BF)
    ii, jj = np.indices((128, 128))
    m_slo = (jj < ii).astype(BF)
    m_trq = (ii <= jj).astype(BF)
    idpat = np.zeros((128, 2048), BF)
    for dkt in range(2):
        for h in range(H):
            for p in range(128):
                idpat[p, dkt * 1024 + h * 256 + dkt * 128 + p] = 1
    id4 = np.tile(np.eye(128, dtype=BF), (1, 4)).reshape(128, 512)
    id4 = np.concatenate([np.eye(128, dtype=BF)] * 4, axis=1)
    gn4 = np.tile(g("L0_attn_a_g_norm"), H)
    gn_col = gn4.reshape(8, 128).T.astype(np.float32).copy()
    cvec = np.tile(np.asarray(CONSTS, np.float32)[None, :], (128, 1))
    phi2 = np.ascontiguousarray(g("L0_ffn_phi").reshape(DM, 2048))
    Wff = g("L0_ffn_W")
    bff = g("L0_ffn_b")
    upT = np.ascontiguousarray(g("L0_attn_b_up_w").T)
    gateT = np.ascontiguousarray(g("L0_attn_b_gate_w").T)
    downT = np.ascontiguousarray(g("L0_attn_b_down_w").T)
    qkgtT = np.ascontiguousarray(np.concatenate(
        [g("L0_attn_b_q_w"), g("L0_attn_b_k_w"), g("L0_attn_b_g_w"),
         g("L0_attn_b_tau_w")], 0).T)
    ig1T = np.ascontiguousarray(g("L0_attn_b_ig_w1").T)
    ig2T = np.ascontiguousarray(g("L0_attn_b_ig_w2").T)
    cwcol = np.ascontiguousarray(
        g("L0_attn_b_conv_w").reshape(16, 128, 4).transpose(1, 0, 2).reshape(128, 64))
    cbcol = np.ascontiguousarray(g("L0_attn_b_conv_b").reshape(16, 128).T)
    igb2col = np.ascontiguousarray(g("L0_attn_b_ig_b2").reshape(16, 128).T)
    gbc = np.zeros((128, 1), np.float32)
    gbc[0:64, 0] = g("L0_attn_b_g_b")
    tbc = np.zeros((128, 1), np.float32)
    tbc[0:64, 0] = g("L0_attn_b_tau_b")
    rwrow = np.tile(g("L0_attn_b_residual_weight")[None, :], (128, 1))
    nwrow = np.tile(g("L0_attn_b_norm_weight")[None, :], (128, 1))
    smask = np.ones((128, 512), np.float32)
    smask[:, ::CB] = 0.0
    ones64 = np.ones((128, 64), np.float32)

    def hmask_c(c):
        hm = np.zeros((128, 8), np.float32)
        if c > 0:
            hm[:, c - 1] = 1.0
        return hm
    per_core = []
    for c in range(NCORE):
        pm = np.zeros((128, 8), np.float32)
        pm[:, :c] = 1.0
        per_core.append({
            "x": np.ascontiguousarray(x[c * TPC:(c + 1) * TPC]),
            "wq": g("L0_attn_a_Wq"),
            "wk": g("L0_attn_a_Wk"),
            "wv": g("L0_attn_a_Wv"),
            "wbeta": g("L0_attn_a_Wbeta"),
            "wo": g("L0_attn_a_Wo"),
            "ident": ident,
            "cvec": cvec,
            "m_slo": m_slo,
            "m_trq": m_trq,
            "idpat": idpat,
            "id4": id4,
            "pmask": pm,
            "pimask": 1.0 - pm,
            "gn_col": gn_col,
            "phi2": phi2,
            "wsh": np.ascontiguousarray(Wff[c * EPC:(c + 1) * EPC]),
            "bsh": np.ascontiguousarray(bff[c * EPC:(c + 1) * EPC]),
            "upT": upT, "gateT": gateT, "downT": downT, "qkgtT": qkgtT,
            "ig1T": ig1T, "ig2T": ig2T, "cwcol": cwcol, "cbcol": cbcol,
            "igb2col": igb2col, "gbc": gbc, "tbc": tbc, "rwrow": rwrow,
            "nwrow": nwrow, "smask": smask, "ones64": ones64, "hmask": hmask_c(c),
        })
    return per_core


def kernel(**inputs):
    per_core = _prep_inputs(inputs)
    nc, _ = build_program()
    res = run_bass_kernel_spmd(nc, per_core, core_ids=list(range(NCORE)))
    out = np.concatenate([np.asarray(r["out"], np.float32) for r in res.results], 0)
    return out.reshape(1, 16384, DM)


# version 04
# speedup vs baseline: 1.0945x; 1.0275x over previous
import numpy as np
import ml_dtypes
import concourse.bass as bass
import concourse.mybir as mybir
from concourse.bass_utils import run_bass_kernel_spmd
from contextlib import ExitStack

F32 = mybir.dt.float32
BF16 = mybir.dt.bfloat16
AF = mybir.ActivationFunctionType
ALU = mybir.AluOpType
AX = mybir.AxisListType
BF = ml_dtypes.bfloat16

NCORE = 8
TPC = 2048
NTILE = TPC // 128
DM = 1024
H = 4
DK = 256
CA = 128
NCH_A = TPC // CA
CB = 64
NCH_B = TPC // CB
DI = 2048
SEG = 512
NSEG = TPC // SEG
NE = 64
EPC = NE // NCORE

RG = [list(range(NCORE))]

CONSTS = [1e-5, -0.5, -8.0, 1.0, -1.0, 0.125, 1.0 / 256, 1.0 / 2048]
CIDX = {v: i for i, v in enumerate(CONSTS)}


class Sem:
    __slots__ = ("h", "v")

    def __init__(self, h):
        self.h = h
        self.v = 0


class Ctx:
    def __init__(self, nc, dbg=()):
        self.nc = nc
        self.dbg = set(dbg)
        self.dbg_outs = {}
        self.top = ExitStack()
        self._semn = 0

    def sem(self, name=None):
        self._semn += 1
        h = self.top.enter_context(self.nc.semaphore(name or f"s{self._semn}"))
        return Sem(h)

    def init_arena(self):
        import concourse.mybir as _mb
        avail = self.nc.sbuf_bytes_remaining - 8192
        base, end = self.nc.bump_sbuf(avail)
        self.abase, self.aend, self.acur = base, end, base
        self.amarks = []
        self.apeak = base

    def salloc(self, name, shape, dtype):
        nb = int(np.prod(shape[1:])) * mybir.dt.size(dtype)
        nb = (nb + 31) & ~31
        t = self.nc.alloc_sbuf_tensor_at(name, shape, dtype, offset=self.acur)
        self.acur += nb
        self.apeak = max(self.apeak, self.acur)
        assert self.acur <= self.aend, (name, self.acur, self.aend)
        return t

    def uname(self, n):
        self._semn += 1
        return f"{n}_{self._semn}"

    def smark(self):
        self.amarks.append(self.acur)

    def srel(self):
        self.acur = self.amarks.pop()

    def cst(self, v):
        return self.cvec[:, CIDX[v]:CIDX[v] + 1]

    def cst64(self, v):
        return self.cvec[0:64, CIDX[v]:CIDX[v] + 1]

    def dump_dram(self, name, dram):
        if name not in self.dbg:
            return
        nc = self.nc
        o = nc.dram_tensor("dbg_" + name, list(dram.shape), dram.dtype,
                           kind="ExternalOutput")
        s = self.sem()
        with nc.Block() as blk:
            @blk.sync
            def _(e):
                e.dma_start(out=o.ap(), in_=dram.ap()).then_inc(s.h, 16)
                e.wait_ge(s.h, 16)
        self.dbg_outs[name] = list(dram.shape)

    def dump(self, name, ap):
        if name not in self.dbg:
            return
        nc = self.nc
        shape = list(ap.shape)
        o = nc.dram_tensor("dbg_" + name, shape, ap.dtype, kind="ExternalOutput")
        s = self.sem()
        with nc.Block() as blk:
            @blk.sync
            def _(e):
                e.dma_start(out=o.ap(), in_=ap).then_inc(s.h, 16)
                e.wait_ge(s.h, 16)
        self.dbg_outs[name] = shape


def inc(instr, sem, n=1):
    instr.then_inc(sem.h, n)
    sem.v += n


class Pipe:

    ENGS = ("tensor", "vector", "scalar", "gpsimd", "sync")

    def __init__(self, ctx):
        self.ctx = ctx
        if not hasattr(ctx, "esems"):
            ctx.esems = {E: ctx.sem() for E in self.ENGS}
            ctx.dsems = {E: ctx.sem() for E in self.ENGS}
        self.steps = []

    def _cur(self):
        return [(E, self.ctx.esems[E].v, self.ctx.dsems[E].v) for E in self.ENGS]

    def add(self, eng, fn, amt=1, is_dma=False, deps="all"):
        ctx = self.ctx
        waits = []
        if deps == "all":
            for (E, cv, dv) in self._cur():
                if cv > 0:
                    waits.append((ctx.esems[E], cv))
                if dv > 0:
                    waits.append((ctx.dsems[E], dv))
        elif deps:
            for d in deps:
                if d is None:
                    continue
                if isinstance(d, list):
                    for (E, cv, dv) in d:
                        if cv > 0:
                            waits.append((ctx.esems[E], cv))
                        if dv > 0:
                            waits.append((ctx.dsems[E], dv))
                else:
                    waits.append(d)
        sem = ctx.dsems[eng] if is_dma else ctx.esems[eng]
        sem.v += amt
        handle = (sem, sem.v)
        self.steps.append((eng, fn, amt, sem, waits))
        return handle

    def t(self, fn, deps="all"):
        return self.add("tensor", fn, deps=deps)

    def v(self, fn, deps="all", drain=True):
        if drain:
            orig = fn

            def fn2(e, orig=orig):
                e.drain()
                return orig(e)
            fn = fn2
        return self.add("vector", fn, deps=deps)

    def s(self, fn, deps="all"):
        return self.add("scalar", fn, deps=deps)

    def g(self, fn, deps="all"):
        return self.add("gpsimd", fn, deps=deps)

    def dma(self, eng, fn, deps="all"):
        return self.add(eng, fn, amt=16, is_dma=True, deps=deps)

    def barrier(self):
        return self._cur()

    def run(self):
        nc = self.ctx.nc
        by_eng = {}
        for (eng, fn, amt, sem, waits) in self.steps:
            by_eng.setdefault(eng, []).append((fn, amt, sem, waits))
        final = self._cur()
        with nc.Block(no_gpsimd_drain=True) as blk:
            for eng, lst in by_eng.items():
                def mk(lst):
                    def body(e):
                        seen = {}
                        for (fn, amt, sem, waits) in lst:
                            for (ws, wv) in waits:
                                if seen.get(id(ws), -1) < wv:
                                    e.wait_ge(ws.h, wv)
                                    seen[id(ws)] = wv
                            o = fn(e)
                            o.then_inc(sem.h, amt)
                    return body
                getattr(blk, eng)(mk(lst))

            @blk.sync
            def _(e):
                for (E, cv, dv) in final:
                    if cv > 0:
                        e.wait_ge(self.ctx.esems[E].h, cv)
                    if dv > 0:
                        e.wait_ge(self.ctx.dsems[E].h, dv)


class SerialPhase:

    def __init__(self, ctx):
        self.ctx = ctx
        self.steps = []

    def add(self, eng, fn, amt=1):
        self.steps.append((eng, fn, amt))

    def t(self, fn):
        self.add("tensor", fn)

    def v(self, fn):
        self.add("vector", fn)

    def s(self, fn):
        self.add("scalar", fn)

    def g(self, fn):
        self.add("gpsimd", fn)

    def dma(self, eng, fn):
        self.add(eng, fn, 16)

    def run(self):
        nc = self.ctx.nc
        sem = self.ctx.sem()
        cum = 0
        pre = []
        for (eng, fn, amt) in self.steps:
            pre.append(cum)
            cum += amt
        by_eng = {}
        for i, (eng, fn, amt) in enumerate(self.steps):
            by_eng.setdefault(eng, []).append((i, fn, amt))
        with nc.Block(no_gpsimd_drain=True) as blk:
            for eng, lst in by_eng.items():
                def mk(lst):
                    def body(e):
                        for (i, fn, amt) in lst:
                            if pre[i] > 0:
                                e.wait_ge(sem.h, pre[i])
                            o = fn(e)
                            o.then_inc(sem.h, amt)
                        return
                    return body
                getattr(blk, eng)(mk(lst))

            @blk.sync
            def _(e):
                e.wait_ge(sem.h, cum)


def emit_ln_transpose(ctx, est, xnT, h_src=None, x_dram=None, store_stats=None):
    nc = ctx.nc
    from_dram = x_dram is not None
    s_ld = ctx.sem()
    s_var = ctx.sem()
    s_rstd = ctx.sem()
    s_xn = ctx.sem()
    s_tp = ctx.sem()
    s_sc = ctx.sem()

    ctx.smark()
    xt = ctx.salloc(ctx.uname("ln_x"), [128, 2 * DM], F32) if from_dram else None
    sq = ctx.salloc(ctx.uname("ln_sq"), [128, 16], F32)
    st = ctx.salloc(ctx.uname("ln_st"), [128, 16], F32)
    xnb = ctx.salloc(ctx.uname("ln_xn"), [128, 2 * DM], BF16)
    pst = est.enter_context(nc.psum_tensor(ctx.uname("ln_ps"), [128, 2 * DM], BF16))

    def src(t):
        return (xt[:, (t % 2) * DM:(t % 2 + 1) * DM] if from_dram
                else h_src[:, t * DM:(t + 1) * DM])

    with nc.Block(no_gpsimd_drain=True) as blk:
        if from_dram:
            @blk.sync
            def _(e):
                for t in range(NTILE):
                    if t >= 2:
                        e.wait_ge(s_xn.h, t - 1)
                    o = e.dma_start(out=xt[:, (t % 2) * DM:(t % 2 + 1) * DM],
                                    in_=x_dram[t * 128:(t + 1) * 128, :])
                    inc(o, s_ld, 16)

        @blk.vector
        def _(e):
            for t in range(NTILE):
                p = t % 2
                if from_dram:
                    e.wait_ge(s_ld.h, 16 * (t + 1))
                e.bn_stats(sq[:, 0:6], src(t)[:, 0:512] if False else
                           bass.AP(src(t).tensor, src(t).offset, [src(t).ap[0], [1, 512]]))
                e.bn_stats(sq[:, 6:12],
                           bass.AP(src(t).tensor, src(t).offset + 512, [src(t).ap[0], [1, 512]]))
                e.drain()
                tt = e.bn_aggr(st[:, p * 8:p * 8 + 2], sq[:, 0:12])
                e.drain()
                inc(tt, s_var)
                e.wait_ge(s_rstd.h, t + 1)
                if t >= 2:
                    e.wait_ge(s_tp.h, t - 1)
                o = e.tensor_scalar(xnb[:, p * DM:(p + 1) * DM], src(t),
                                    st[:, p * 8 + 0:p * 8 + 1], st[:, p * 8 + 3:p * 8 + 4],
                                    op0=ALU.subtract, op1=ALU.mult)
                inc(o, s_xn)
                if store_stats is not None:
                    e.tensor_copy(store_stats[:, 2 * t:2 * t + 1], st[:, p * 8:p * 8 + 1])
                    e.tensor_copy(store_stats[:, 2 * t + 1:2 * t + 2],
                                  st[:, p * 8 + 3:p * 8 + 4])
                if t >= 1:
                    e.wait_ge(s_tp.h, t)
                    sc = e.tensor_copy(
                        bass.AP(xnT, (t - 1) * 128,
                                [[8 * TPC, 128], [TPC, 8], [1, 128]]),
                        bass.AP(pst, ((t - 1) % 2) * DM, [[2 * DM, 128], [128, 8], [1, 128]]))
                    inc(sc, s_sc)
            e.wait_ge(s_tp.h, NTILE)
            sc = e.tensor_copy(
                bass.AP(xnT, (NTILE - 1) * 128, [[8 * TPC, 128], [TPC, 8], [1, 128]]),
                bass.AP(pst, ((NTILE - 1) % 2) * DM, [[2 * DM, 128], [128, 8], [1, 128]]))
            inc(sc, s_sc)

        @blk.scalar
        def _(e):
            for t in range(NTILE):
                p = t % 2
                e.wait_ge(s_var.h, t + 1)
                if t >= 1:
                    e.wait_ge(s_xn.h, t)
                e.activation(st[:, p * 8 + 3:p * 8 + 4], st[:, p * 8 + 1:p * 8 + 2],
                             AF.Ln, bias=ctx.cst(1e-5))
                o = e.activation(st[:, p * 8 + 3:p * 8 + 4], st[:, p * 8 + 3:p * 8 + 4],
                                 AF.Exp, scale=ctx.cst(-0.5))
                inc(o, s_rstd)

        @blk.tensor
        def _(e):
            for t in range(NTILE):
                p = t % 2
                e.wait_ge(s_xn.h, t + 1)
                if t >= 2:
                    e.wait_ge(s_sc.h, t - 1)
                for i in range(8):
                    o = e.transpose(pst[:, p * DM + i * 128:p * DM + (i + 1) * 128],
                                    xnb[:, p * DM + i * 128:p * DM + (i + 1) * 128],
                                    ctx.ident[:, :])
                inc(o, s_tp)
    ctx.srel()


def emit_proj_qkvb(ctx, est, xnT, wq, wk, wv, wbeta, q_tm, k_tm, v_tm, beta_sb):
    nc = ctx.nc
    s_mm = ctx.sem()
    s_ev = ctx.sem()
    s_ss = ctx.sem()
    s_rs = ctx.sem()
    s_ap = ctx.sem()

    ctx.smark()
    ps = est.enter_context(nc.psum_tensor("pj_ps", [128, 3 * DM], F32))
    psb = est.enter_context(nc.psum_tensor("pj_psb", [128, 2 * H], F32))
    qf = ctx.salloc("pj_qf", [128, 2 * DM], BF16)
    kf = ctx.salloc("pj_kf", [128, 2 * DM], BF16)
    sq = ctx.salloc("pj_sq", [128, DM], BF16)
    rs = ctx.salloc("pj_rs", [128, 16], F32)

    with nc.Block(no_gpsimd_drain=True) as blk:
        @blk.tensor
        def _(e):
            for t in range(NTILE):
                if t >= 1:
                    e.wait_ge(s_ev.h, 2 * t)
                for w, off in ((wq, 0), (wk, DM), (wv, 2 * DM)):
                    for half in range(2):
                        for dt in range(8):
                            o = e.matmul(ps[:, off + half * 512:off + (half + 1) * 512],
                                         xnT[:, dt * TPC + t * 128:dt * TPC + (t + 1) * 128],
                                         w[:, dt * DM + half * 512:dt * DM + (half + 1) * 512],
                                         start=(dt == 0), stop=(dt == 7))
                inc(o, s_mm)
                for dt in range(8):
                    o = e.matmul(psb[:, (t % 2) * H:(t % 2 + 1) * H],
                                 xnT[:, dt * TPC + t * 128:dt * TPC + (t + 1) * 128],
                                 wbeta[:, dt * H:(dt + 1) * H],
                                 start=(dt == 0), stop=(dt == 7))
                inc(o, s_mm)

        @blk.scalar
        def _(e):
            for t in range(NTILE):
                p = t % 2
                e.wait_ge(s_mm.h, 2 * t + 1)
                if t >= 2:
                    e.wait_ge(s_ap.h, t - 1)
                e.activation(qf[:, p * DM:(p + 1) * DM], ps[:, 0:DM], AF.Copy, bias=0.0)
                e.activation(kf[:, p * DM:(p + 1) * DM], ps[:, DM:2 * DM], AF.Copy, bias=0.0)
                o = e.activation(v_tm[:, t * DM:(t + 1) * DM], ps[:, 2 * DM:3 * DM],
                                 AF.Copy, bias=0.0)
                inc(o, s_ev)
                e.wait_ge(s_mm.h, 2 * t + 2)
                o = e.activation(beta_sb[:, t * H:(t + 1) * H], psb[:, p * H:(p + 1) * H],
                                 AF.Sigmoid)
                inc(o, s_ev)
                e.wait_ge(s_ss.h, t + 1)
                e.activation(rs[:, p * 8:p * 8 + 8], rs[:, p * 8:p * 8 + 8], AF.Ln)
                o = e.activation(rs[:, p * 8:p * 8 + 8], rs[:, p * 8:p * 8 + 8],
                                 AF.Exp, scale=ctx.cst(-0.5))
                inc(o, s_rs)

        @blk.vector
        def _(e):
            for t in range(NTILE):
                p = t % 2
                e.wait_ge(s_ev.h, 2 * t + 1)
                e.tensor_tensor(sq[:, :], qf[:, p * DM:(p + 1) * DM],
                                qf[:, p * DM:(p + 1) * DM], op=ALU.mult)
                e.tensor_reduce(rs[:, p * 8:p * 8 + 4],
                                sq[:, :].rearrange("a (h d) -> a h d", d=DK),
                                axis=AX.X, op=ALU.add)
                e.tensor_tensor(sq[:, :], kf[:, p * DM:(p + 1) * DM],
                                kf[:, p * DM:(p + 1) * DM], op=ALU.mult)
                o = e.tensor_reduce(rs[:, p * 8 + 4:p * 8 + 8],
                                    sq[:, :].rearrange("a (h d) -> a h d", d=DK),
                                    axis=AX.X, op=ALU.add)
                inc(o, s_ss)
                e.wait_ge(s_rs.h, t + 1)
                e.tensor_tensor(q_tm[:, t * DM:(t + 1) * DM].rearrange("a (h d) -> a h d", d=DK),
                                qf[:, p * DM:(p + 1) * DM].rearrange("a (h d) -> a h d", d=DK),
                                bass.AP(rs, p * 8, [[16, 128], [1, H], [0, DK]]),
                                op=ALU.mult)
                o = e.tensor_tensor(k_tm[:, t * DM:(t + 1) * DM].rearrange("a (h d) -> a h d", d=DK),
                                    kf[:, p * DM:(p + 1) * DM].rearrange("a (h d) -> a h d", d=DK),
                                    bass.AP(rs, p * 8 + 4, [[16, 128], [1, H], [0, DK]]),
                                    op=ALU.mult)
                inc(o, s_ap)
    ctx.srel()


def emit_attn_a_delta(ctx, est, q_tm, k_tm, v_tm, beta_sb, o_tm,
                      gin_bp, gout_bp, mask_slo, mask_trilq, idpat, id4,
                      pmask, pimask):
    nc = ctx.nc
    ctx.smark()
    sbf = lambda n, s, d: ctx.salloc(n, s, d)
    psf = lambda n, s, d: est.enter_context(nc.psum_tensor(n, s, d))

    w_fm = sbf("w_fm", [128, NCH_A * H * 256], BF16)
    u_tm = sbf("u_tm", [128, NCH_A * H * 256], BF16)
    s_sb = sbf("s_sb", [128, 2048], BF16)
    kf = sbf("kf", [128, 1024], BF16)
    e_sb = sbf("e_sb", [128, 1024], BF16)
    ctx.smark()
    bk = sbf("bk", [128, 1024], BF16)
    bv = sbf("bv", [128, 1024], BF16)
    a_sb = sbf("a_sb", [128, 512], BF16)
    at_sb = sbf("at_sb", [128, 512], BF16)
    apow = sbf("apow", [128, 7 * 512], BF16)
    m_sb = sbf("m_sb", [128, 2 * 512], BF16)
    tt_sb = sbf("tt_sb", [128, 512], BF16)

    ps_t = psf("ps_t", [128, 1024], BF16)
    ps_a = psf("ps_a", [128, 512], F32)
    ps_w = psf("ps_w", [128, 1024], F32)
    est2 = ExitStack()
    ps_b = est2.enter_context(nc.psum_tensor("ps_b", [128, 512], F32))
    ps_c = est2.enter_context(nc.psum_tensor("ps_c", [128, 512], F32))
    ps_s = None

    def wfm(n, h, dkt):
        return w_fm[:, ((n * H + h) * 2 + dkt) * 128:((n * H + h) * 2 + dkt) * 128 + 128]

    def utm(n, h):
        return u_tm[:, (n * H + h) * 256:(n * H + h) * 256 + 256]

    def ktm(n, h, dkt):
        return k_tm[:, n * DM + h * 256 + dkt * 128:n * DM + h * 256 + dkt * 128 + 128]

    def ssb(dkt, h):
        return s_sb[:, dkt * 1024 + h * 256:dkt * 1024 + h * 256 + 256]

    def ssb_t(dkt):
        return s_sb[:, dkt * 1024:(dkt + 1) * 1024]

    bcast_beta = lambda n: bass.AP(beta_sb, n * H, [[NTILE * H, 128], [1, H], [0, 256]])

    for n in range(NCH_A):
        ph = SerialPhase(ctx)
        ph.v(lambda e, n=n: e.tensor_tensor(
            bk[:, :].rearrange("a (h d) -> a h d", d=256),
            k_tm[:, n * DM:(n + 1) * DM].rearrange("a (h d) -> a h d", d=256),
            bcast_beta(n), op=ALU.mult))
        ph.v(lambda e, n=n: e.tensor_tensor(
            bv[:, :].rearrange("a (h d) -> a h d", d=256),
            v_tm[:, n * DM:(n + 1) * DM].rearrange("a (h d) -> a h d", d=256),
            bcast_beta(n), op=ALU.mult))

        def tp_k(e, n=n):
            for h in range(H):
                for dkt in range(2):
                    o = e.transpose(ps_t[:, (h * 2 + dkt) * 128:(h * 2 + dkt) * 128 + 128],
                                    ktm(n, h, dkt), ctx.ident[:, :])
            return o
        ph.t(tp_k)
        ph.v(lambda e: e.tensor_copy(kf[:, :], ps_t[:, :]))

        def mm_s(e, n=n):
            for h in range(H):
                for dkt in range(2):
                    o = e.matmul(ps_a[:, h * 128:(h + 1) * 128],
                                 kf[:, (h * 2 + dkt) * 128:(h * 2 + dkt) * 128 + 128],
                                 kf[:, (h * 2 + dkt) * 128:(h * 2 + dkt) * 128 + 128],
                                 start=(dkt == 0), stop=(dkt == 1))
            return o
        ph.t(mm_s)

        def ev_a(e, n=n):
            for h in range(H):
                o = e.scalar_tensor_tensor(
                    a_sb[:, h * 128:(h + 1) * 128], ps_a[:, h * 128:(h + 1) * 128],
                    beta_sb[:, n * H + h:n * H + h + 1], mask_slo[:, :],
                    op0=ALU.mult, op1=ALU.mult)
            return o
        ph.v(ev_a)
        ph.v(lambda e: e.tensor_copy(apow[:, 0:512], a_sb[:, :]))

        def tp_a(e):
            for h in range(H):
                o = e.transpose(ps_t[:, h * 128:(h + 1) * 128],
                                a_sb[:, h * 128:(h + 1) * 128], ctx.ident[:, :])
            return o
        ph.t(tp_a)
        ph.v(lambda e: e.tensor_copy(at_sb[:, :], ps_t[:, 0:512]))

        for j in range(1, 7):
            def mm_pow(e, j=j):
                for h in range(H):
                    sl = slice(h * 128, (h + 1) * 128)
                    e.matmul(ps_b[:, sl], at_sb[:, sl],
                             apow[:, (j - 1) * 512:(j - 1) * 512 + 512][:, sl],
                             start=True, stop=True)
                    o = e.matmul(ps_c[:, sl],
                                 apow[:, (j - 1) * 512:(j - 1) * 512 + 512][:, sl],
                                 at_sb[:, sl], start=True, stop=True)
                return o
            ph.t(mm_pow)
            ph.v(lambda e, j=j: e.tensor_copy(apow[:, j * 512:(j + 1) * 512], ps_b[:, :]))
            if j < 6:
                ph.s(lambda e: e.activation(at_sb[:, :], ps_c[:, :], AF.Copy, bias=0.0))
            else:
                ph.v(lambda e: e.tensor_tensor(m_sb[:, 0:512], ps_c[:, :],
                                               id4[:, :], op=ALU.add))

        mcur = 0
        for j in (5, 4, 3, 2, 1):
            def mm_ch(e, j=j, mcur=mcur):
                for h in range(H):
                    sl = slice(h * 128, (h + 1) * 128)
                    e.matmul(ps_a[:, sl], apow[:, j * 512:(j + 1) * 512][:, sl],
                             m_sb[:, mcur * 512:mcur * 512 + 512][:, sl],
                             start=True, stop=False)
                    o = e.matmul(ps_a[:, sl], ctx.ident[:, :],
                                 m_sb[:, mcur * 512:mcur * 512 + 512][:, sl],
                                 start=False, stop=True)
                return o
            ph.t(mm_ch)
            ph.v(lambda e, mcur=mcur: e.tensor_copy(
                m_sb[:, (1 - mcur) * 512:(1 - mcur) * 512 + 512], ps_a[:, :]))
            mcur = 1 - mcur

        def mm_last(e, mcur=mcur):
            for h in range(H):
                sl = slice(h * 128, (h + 1) * 128)
                o = e.matmul(ps_a[:, sl], apow[:, 0:512][:, sl],
                             m_sb[:, mcur * 512:mcur * 512 + 512][:, sl],
                             start=True, stop=True)
            return o
        ph.t(mm_last)
        ph.v(lambda e, mcur=mcur: e.tensor_tensor(
            tt_sb[:, :], m_sb[:, mcur * 512:mcur * 512 + 512], ps_a[:, :],
            op=ALU.subtract))

        def mm_w(e, n=n):
            for h in range(H):
                for dkt in range(2):
                    o = e.matmul(ps_w[:, (h * 2 + dkt) * 128:(h * 2 + dkt) * 128 + 128],
                                 bk[:, h * 256 + dkt * 128:h * 256 + dkt * 128 + 128],
                                 tt_sb[:, h * 128:(h + 1) * 128],
                                 start=True, stop=True)
            return o
        ph.t(mm_w)
        ph.v(lambda e, n=n: e.tensor_copy(
            w_fm[:, n * H * 256:(n + 1) * H * 256], ps_w[:, :]))

        def mm_u(e, n=n):
            for h in range(H):
                o = e.matmul(ps_w[:, h * 256:(h + 1) * 256],
                             tt_sb[:, h * 128:(h + 1) * 128],
                             bv[:, h * 256:(h + 1) * 256],
                             start=True, stop=True)
            return o
        ph.t(mm_u)
        ph.v(lambda e, n=n: e.tensor_copy(
            u_tm[:, n * H * 256:(n + 1) * H * 256], ps_w[:, :]))
        ph.run()

    est2.close()
    ps_s = psf("ps_s", [128, 2048], F32)
    ctx.srel()
    ctx.smark()
    pt_sb = sbf("pt_sb", [128, 2048], BF16)
    ptm = sbf("ptm", [128, 2048], BF16)
    bpin = sbf("bpin", [128, 2048], BF16)
    qf = sbf("qf", [128, 1024], BF16)
    mt_sb = sbf("mt_sb", [128, 512], BF16)
    ctx.dump("w_fm", w_fm[:, :])
    ctx.dump("u_tm", u_tm[:, :])

    btrace = None
    if "btrace" in ctx.dbg:
        btrace = nc.dram_tensor("dbg_btrace", [NCH_A, 128, 2048], BF16,
                                kind="ExternalOutput")
        ctx.dbg_outs["btrace"] = [NCH_A, 128, 2048]
    for pas in ("B", "P"):
        ph = SerialPhase(ctx)
        if pas == "B":
            ph.v(lambda e: e.memset(s_sb[:, :], 0.0))
        else:
            ph.v(lambda e: e.tensor_copy(s_sb[:, :], idpat[:, :]))
        for n in range(NCH_A):
            def mm_e(e, n=n):
                for h in range(H):
                    for dkt in range(2):
                        o = e.matmul(ps_w[:, h * 256:(h + 1) * 256],
                                     wfm(n, h, dkt), ssb(dkt, h),
                                     start=(dkt == 0), stop=(dkt == 1))
                return o
            ph.t(mm_e)
            if pas == "B":
                ph.v(lambda e, n=n: e.tensor_tensor(
                    e_sb[:, :], u_tm[:, n * H * 256:(n + 1) * H * 256],
                    ps_w[:, :], op=ALU.subtract))
            else:
                ph.s(lambda e: e.activation(e_sb[:, :], ps_w[:, :], AF.Copy,
                                            bias=0.0, scale=ctx.cst(-1.0)))

            def mm_su(e, n=n):
                for dkt in range(2):
                    for h in range(H):
                        o = e.matmul(ps_s[:, dkt * 1024 + h * 256:dkt * 1024 + (h) * 256 + 256],
                                     ktm(n, h, dkt), e_sb[:, h * 256:(h + 1) * 256],
                                     start=(n == 0 and h % 2 == 0),
                                     stop=(n == NCH_A - 1))
                return o
            ph.t(mm_su)
            if pas == "B":
                ph.v(lambda e: e.tensor_copy(ssb_t(0), ps_s[:, 0:1024]))
                ph.v(lambda e: e.tensor_copy(ssb_t(1), ps_s[:, 1024:2048]))
            else:
                ph.v(lambda e: e.tensor_tensor(ssb_t(0), ps_s[:, 0:1024],
                                               idpat[:, 0:1024], op=ALU.add))
                ph.v(lambda e: e.tensor_tensor(ssb_t(1), ps_s[:, 1024:2048],
                                               idpat[:, 1024:2048], op=ALU.add))
            if pas == "B" and btrace is not None:
                ph.dma("sync", lambda e, n=n: e.dma_start(out=btrace[n, :, :],
                                                          in_=s_sb[:, :]))
        if pas == "B":
            ph.dma("sync", lambda e: e.dma_start(out=gin_bp[:, 0:2048], in_=s_sb[:, :]))
        else:
            def tp_p(e):
                for dkt in range(2):
                    for h in range(H):
                        for d2 in range(2):
                            o = e.transpose(
                                ps_t[:, ((dkt * H + h) * 2 + d2) % 8 * 128:
                                     ((dkt * H + h) * 2 + d2) % 8 * 128 + 128],
                                s_sb[:, dkt * 1024 + h * 256 + d2 * 128:
                                     dkt * 1024 + h * 256 + d2 * 128 + 128],
                                ctx.ident[:, :])
                return o
            def tp_p1(e):
                for idx, (dkt, h, d2) in enumerate(
                        [(a, b, c) for a in range(2) for b in range(H) for c in range(2)][:8]):
                    o = e.transpose(ps_t[:, idx * 128:idx * 128 + 128],
                                    s_sb[:, dkt * 1024 + h * 256 + d2 * 128:
                                         dkt * 1024 + h * 256 + d2 * 128 + 128],
                                    ctx.ident[:, :])
                return o

            def ev_p1(e):
                for idx, (dkt, h, d2) in enumerate(
                        [(a, b, c) for a in range(2) for b in range(H) for c in range(2)][:8]):
                    o = e.tensor_copy(pt_sb[:, d2 * 1024 + h * 256 + dkt * 128:
                                            d2 * 1024 + h * 256 + dkt * 128 + 128],
                                      ps_t[:, idx * 128:idx * 128 + 128])
                return o

            def tp_p2(e):
                for idx, (dkt, h, d2) in enumerate(
                        [(a, b, c) for a in range(2) for b in range(H) for c in range(2)][8:]):
                    o = e.transpose(ps_t[:, idx * 128:idx * 128 + 128],
                                    s_sb[:, dkt * 1024 + h * 256 + d2 * 128:
                                         dkt * 1024 + h * 256 + d2 * 128 + 128],
                                    ctx.ident[:, :])
                return o

            def ev_p2(e):
                for idx, (dkt, h, d2) in enumerate(
                        [(a, b, c) for a in range(2) for b in range(H) for c in range(2)][8:]):
                    o = e.tensor_copy(pt_sb[:, d2 * 1024 + h * 256 + dkt * 128:
                                            d2 * 1024 + h * 256 + dkt * 128 + 128],
                                      ps_t[:, idx * 128:idx * 128 + 128])
                return o
            ph.t(tp_p1)
            ph.v(ev_p1)
            ph.t(tp_p2)
            ph.v(ev_p2)
            ph.dma("sync", lambda e: e.dma_start(out=gin_bp[:, 2048:4096], in_=pt_sb[:, :]))
        ph.run()

    ph = SerialPhase(ctx)
    ph.g(lambda e: e.collective_compute(
        "AllGather", ALU.bypass, replica_groups=RG,
        ins=[gin_bp.ap().opt()], outs=[gout_bp.ap().opt()]))
    ph.v(lambda e: e.memset(s_sb[:, :], 0.0))
    for i in range(NCORE - 1):
        ph.dma("sync", lambda e, i=i: e.dma_start(out=bpin[:, :],
                                                  in_=gout_bp[i, :, 2048:4096]))
        ph.v(lambda e, i=i: e.tensor_scalar(
            ptm[:, :], bpin[:, :], pmask[:, i:i + 1], None, op0=ALU.mult))
        ph.v(lambda e, i=i: e.scalar_tensor_tensor(
            ptm[:, :], idpat[:, :], pimask[:, i:i + 1], ptm[:, :],
            op0=ALU.mult, op1=ALU.add))
        ph.dma("sync", lambda e, i=i: e.dma_start(out=bpin[:, :],
                                                  in_=gout_bp[i, :, 0:2048]))

        def mm_s0(e, i=i):
            for dkt in range(2):
                for h in range(H):
                    for dkt2 in range(2):
                        o = e.matmul(
                            ps_s[:, dkt * 1024 + h * 256:dkt * 1024 + h * 256 + 256],
                            ptm[:, dkt2 * 1024 + h * 256 + dkt * 128:
                                dkt2 * 1024 + h * 256 + dkt * 128 + 128],
                            ssb(dkt2, h), start=(dkt2 == 0), stop=(dkt2 == 1))
            return o
        ph.t(mm_s0)
        ph.v(lambda e, i=i: e.scalar_tensor_tensor(
            s_sb[:, :], bpin[:, :], pmask[:, i:i + 1], ps_s[:, :],
            op0=ALU.mult, op1=ALU.add))
    ph.run()
    ctx.dump("s0", s_sb[:, :])

    for n in range(NCH_A):
        ph = SerialPhase(ctx)

        def tp_q(e, n=n):
            for h in range(H):
                for dkt in range(2):
                    o = e.transpose(ps_t[:, (h * 2 + dkt) * 128:(h * 2 + dkt) * 128 + 128],
                                    q_tm[:, n * DM + h * 256 + dkt * 128:
                                         n * DM + h * 256 + dkt * 128 + 128],
                                    ctx.ident[:, :])
            return o
        ph.t(tp_q)
        ph.v(lambda e: e.tensor_copy(qf[:, :], ps_t[:, :]))

        def tp_k2(e, n=n):
            for h in range(H):
                for dkt in range(2):
                    o = e.transpose(ps_t[:, (h * 2 + dkt) * 128:(h * 2 + dkt) * 128 + 128],
                                    ktm(n, h, dkt), ctx.ident[:, :])
            return o
        ph.t(tp_k2)
        ph.v(lambda e: e.tensor_copy(kf[:, :], ps_t[:, :]))

        def mm_e(e, n=n):
            for h in range(H):
                for dkt in range(2):
                    o = e.matmul(ps_w[:, h * 256:(h + 1) * 256],
                                 wfm(n, h, dkt), ssb(dkt, h),
                                 start=(dkt == 0), stop=(dkt == 1))
            return o
        ph.t(mm_e)
        ph.v(lambda e, n=n: e.tensor_tensor(
            e_sb[:, :], u_tm[:, n * H * 256:(n + 1) * H * 256], ps_w[:, :],
            op=ALU.subtract))

        def mm_mt(e):
            for h in range(H):
                for dkt in range(2):
                    o = e.matmul(ps_a[:, h * 128:(h + 1) * 128],
                                 kf[:, (h * 2 + dkt) * 128:(h * 2 + dkt) * 128 + 128],
                                 qf[:, (h * 2 + dkt) * 128:(h * 2 + dkt) * 128 + 128],
                                 start=(dkt == 0), stop=(dkt == 1))
            return o
        ph.t(mm_mt)
        ph.v(lambda e: e.tensor_tensor(
            mt_sb[:, :].rearrange("a (h c) -> a h c", c=128),
            ps_a[:, :].rearrange("a (h c) -> a h c", c=128),
            bass.AP(mask_trilq, 0, [[128, 128], [0, H], [1, 128]]),
            op=ALU.mult))

        def mm_o(e, n=n):
            for h in range(H):
                for dkt in range(2):
                    e.matmul(ps_w[:, h * 256:(h + 1) * 256],
                             qf[:, (h * 2 + dkt) * 128:(h * 2 + dkt) * 128 + 128],
                             ssb(dkt, h), start=(dkt == 0), stop=False)
                o = e.matmul(ps_w[:, h * 256:(h + 1) * 256],
                             mt_sb[:, h * 128:(h + 1) * 128],
                             e_sb[:, h * 256:(h + 1) * 256], start=False, stop=True)
            return o
        ph.t(mm_o)
        ph.v(lambda e, n=n: e.tensor_copy(o_tm[:, n * DM:(n + 1) * DM], ps_w[:, :]))

        def mm_su(e, n=n):
            for dkt in range(2):
                for h in range(H):
                    if n == 0:
                        e.matmul(ps_s[:, dkt * 1024 + h * 256:dkt * 1024 + h * 256 + 256],
                                 ctx.ident[:, :], ssb(dkt, h),
                                 start=(h % 2 == 0), stop=False)
                    o = e.matmul(ps_s[:, dkt * 1024 + h * 256:dkt * 1024 + h * 256 + 256],
                                 ktm(n, h, dkt), e_sb[:, h * 256:(h + 1) * 256],
                                 start=False, stop=(n == NCH_A - 1))
            return o
        ph.t(mm_su)
        ph.s(lambda e: e.activation(ssb_t(0), ps_s[:, 0:1024], AF.Copy, bias=0.0))
        ph.s(lambda e: e.activation(ssb_t(1), ps_s[:, 1024:2048], AF.Copy, bias=0.0))
        ph.run()
    ctx.srel()
    ctx.srel()
    ctx.dump("o_tm", o_tm[:, :])
    ctx.dump_dram("gin", gin_bp)


def emit_attn_b(ctx, est, h_in_d, h_out_d, w, dr):
    nc = ctx.nc
    ctx.smark()
    SW = 512
    NSG = TPC // SW
    CPS = SW // CB

    s_f = ctx.salloc("b_sf", [64, DI], F32)
    s_bf = ctx.salloc("b_sbf", [64, DI], BF16)
    qb_all = ctx.salloc("b_qball", [64, TPC], BF16)
    btot_all = ctx.salloc("b_btot", [64, NCH_B], F32)
    stats = ctx.salloc("b_stats", [128, 2 * NTILE], F32)
    carry = ctx.salloc("b_carry", [128, 16 * 3], BF16)
    cumg = ctx.salloc("b_cumg", [64, NCH_B + 8], F32)

    xn3 = ctx.salloc("b_xn3", [128, 2 * DM], BF16)

    ctx.smark()
    w["upT"] = ctx.salloc("b_upT", [128, 8 * DI], BF16)
    sup = ctx.sem()
    with nc.Block(no_gpsimd_drain=True) as blk:
        @blk.gpsimd
        def _(e):
            o = e.dma_start(out=w["upT"][:, :],
                            in_=bass.AP(dr["upT_d"], 0,
                                        [[DI, 128], [128 * DI, 8], [1, DI]]))
            inc(o, sup, 16)
            e.wait_ge(sup.h, sup.v)
    ln_tmp = ctx.salloc("b_lntmp", [128, DM], F32)
    bnst = ctx.salloc("b_bnst", [128, 16], F32)
    hg = ctx.salloc("b_hg", [128, 8 * 48], F32)

    est_ps = ExitStack()

    def ln_tile_emit(ph, t, dst_fm_tile_fn):
        p = t % 2
        ph.dma("sync", lambda e, t=t, p=p: e.dma_start(
            out=ln_tmp[:, 0:DM],
            in_=h_in_d[t * 128:(t + 1) * 128, :]))

        def stats_f(e, t=t, p=p):
            e.bn_stats(bnst[:, 0:6],
                       bass.AP(ln_tmp, 0, [[DM, 128], [1, 512]]))
            e.bn_stats(bnst[:, 6:12],
                       bass.AP(ln_tmp, 512, [[DM, 128], [1, 512]]))
            e.drain()
            o = e.bn_aggr(stats[:, 2 * t:2 * t + 2], bnst[:, 0:12])
            e.drain()
            return o
        ph.v(stats_f)
        ph.s(lambda e, t=t: e.activation(stats[:, 2 * t + 1:2 * t + 2],
                                         stats[:, 2 * t + 1:2 * t + 2],
                                         AF.Ln, bias=ctx.cst(1e-5)))
        ph.s(lambda e, t=t: e.activation(stats[:, 2 * t + 1:2 * t + 2],
                                         stats[:, 2 * t + 1:2 * t + 2],
                                         AF.Exp, scale=ctx.cst(-0.5)))
        ph.v(lambda e, t=t, p=p: e.tensor_scalar(
            xn3[:, p * DM:(p + 1) * DM], ln_tmp[:, 0:DM],
            stats[:, 2 * t:2 * t + 1], stats[:, 2 * t + 1:2 * t + 2],
            op0=ALU.subtract, op1=ALU.mult))

    ps_h = est_ps.enter_context(nc.psum_tensor("b_psh", [128, 2 * DM], BF16))
    ps_hf = est_ps.enter_context(nc.psum_tensor("b_pshf", [128, 64], F32))
    ph = SerialPhase(ctx)
    ph.dma("sync", lambda e: e.dma_start(out=ln_tmp[0:3, 0:DM],
                                         in_=h_in_d[TPC - 3:TPC, :]))

    def h_stats(e):
        e.bn_stats(bnst[0:3, 0:6], bass.AP(ln_tmp, 0, [[DM, 3], [1, 512]]))
        e.bn_stats(bnst[0:3, 6:12], bass.AP(ln_tmp, 512, [[DM, 3], [1, 512]]))
        e.drain()
        o = e.bn_aggr(bnst[0:3, 12:14], bnst[0:3, 0:12])
        e.drain()
        return o
    ph.v(h_stats)
    ph.s(lambda e: e.activation(bnst[:, 13:14], bnst[:, 13:14], AF.Ln,
                                bias=ctx.cst(1e-5)))
    ph.s(lambda e: e.activation(bnst[:, 13:14], bnst[:, 13:14], AF.Exp,
                                scale=ctx.cst(-0.5)))
    ph.v(lambda e: e.tensor_scalar(
        xn3[0:3, 0:DM], ln_tmp[0:3, 0:DM], bnst[0:3, 12:13], bnst[0:3, 13:14],
        op0=ALU.subtract, op1=ALU.mult))

    def h_tp(e):
        for i in range(8):
            o = e.transpose(ps_h[:, i * 128:i * 128 + 3],
                            xn3[0:3, i * 128:(i + 1) * 128], ctx.ident[0:3, 0:3])
        return o
    ph.t(h_tp)
    ph.v(lambda e: e.tensor_copy(xn3[:, DM:DM + 8 * 3].rearrange(
        "a (d c) -> a d c", c=3),
        bass.AP(ps_h, 0, [[2 * DM, 128], [128, 8], [1, 3]])))

    def h_mm(e):
        for cht in range(16):
            for dt in range(8):
                o = e.matmul(ps_hf[:, cht * 3:cht * 3 + 3],
                             w["upT"][:, dt * DI + cht * 128:dt * DI + cht * 128 + 128],
                             xn3[:, DM + dt * 3:DM + dt * 3 + 3],
                             start=(dt == 0), stop=(dt == 7))
        return o
    ph.t(h_mm)
    ph.v(lambda e: e.tensor_copy(hg[:, 0:48], ps_hf[:, 0:48]))
    ph.dma("sync", lambda e: e.dma_start(out=dr["gin_h"].ap(), in_=hg[:, 0:48]))
    ph.g(lambda e: e.collective_compute(
        "AllGather", ALU.bypass, replica_groups=RG,
        ins=[dr["gin_h"].ap().opt()], outs=[dr["gout_h"].ap().opt()]))
    ph.dma("sync", lambda e: e.dma_start(
        out=hg[:, :], in_=bass.AP(dr["gout_h"], 0, [[48, 128], [128 * 48, 8], [1, 48]])))

    def h_sel(e):
        e.memset(carry[:, :], 0.0)
        e.memset(s_f[:, :], 0.0)
        e.memset(s_bf[:, :], 0.0)
        e.drain()
        for i in range(8):
            o = e.scalar_tensor_tensor(carry[:, :], hg[:, i * 48:(i + 1) * 48],
                                       w["hmask"][:, i:i + 1], carry[:, :],
                                       op0=ALU.mult, op1=ALU.add)
            e.drain()
        return o
    ph.v(h_sel)
    ph.run()
    est_ps.close()
    ctx.dump("carry", carry[:, :])

    xnT_s = ctx.salloc("b_xnTs", [128, 8 * SW], BF16)
    xup = ctx.salloc("b_xup", [128, 16 * SW], BF16)
    shf = ctx.salloc("b_shf", [128, 16 * SW], BF16)
    vfm = ctx.salloc("b_vfm", [128, 16 * SW], BF16)
    sh_tm = ctx.salloc("b_shtm", [128, DI], BF16)
    xcat = ctx.salloc("b_xcat", [128, SW + 3], BF16)
    acc_a = ctx.salloc("b_acca", [128, SW], F32)
    acc_b = ctx.salloc("b_accb", [128, SW], F32)
    qk_t = ctx.salloc("b_qkt", [128, SW], BF16)
    gt_t = ctx.salloc("b_gtt", [128, SW], F32)
    k_sb = ctx.salloc("b_ksb", [64, SW], F32)
    tau_s = ctx.salloc("b_taus", [64, SW], F32)
    gl_s = ctx.salloc("b_gls", [64, SW], F32)
    it_s = ctx.salloc("b_its", [64, SW], F32)
    rt_s = ctx.salloc("b_rts", [64, SW], F32)
    b_s = ctx.salloc("b_bs", [64, SW], F32)
    eb_s = ctx.salloc("b_ebs", [64, SW], BF16)
    sc_a = ctx.salloc("b_sca", [64, SW], F32)
    qs_s = ctx.salloc("b_qss", [64, SW], BF16)
    ks_s = ctx.salloc("b_kss", [64, SW], BF16)
    kd_s = ctx.salloc("b_kds", [64, SW], BF16)
    kit = ctx.salloc("b_kit", [64, SW], BF16)
    r16 = ctx.salloc("b_r16", [16, SW], BF16)
    seg_sc = ctx.salloc("b_segsc", [64, 24], F32)
    rr_f = ctx.salloc("b_rrf", [1, SW], F32)
    r_row = ctx.salloc("b_rrow", [1, SW], BF16)
    k2b = ctx.salloc("b_k2b", [64, SW], BF16)
    r_col = ctx.salloc("b_rcol", [64, 8], F32)
    v_tm = ctx.salloc("b_vtm", [64, DI], BF16)
    kdt = ctx.salloc("b_kdt", [64, 64], BF16)
    a_p = ctx.salloc("b_ap", [64, 64], BF16)
    o_ch = ctx.salloc("b_och", [64, DI], BF16)
    onescol = ctx.salloc("b_ones", [64, 1], BF16)

    est_p1 = ExitStack()
    ps_q = est_p1.enter_context(nc.psum_tensor("b_psq", [128, 1024], F32))
    ps_t = est_p1.enter_context(nc.psum_tensor("b_pst", [128, 2048], BF16))
    ps_o = est_p1.enter_context(nc.psum_tensor("b_pso", [64, 1024], F32))
    ps_ds = est_p1.enter_context(nc.psum_tensor("b_psds", [64, 1024], F32))
    ps_x = ps_q

    for sg in range(NSG):
        ph = SerialPhase(ctx)
        ph.v(lambda e: e.memset(onescol[:, :], 1.0))
        for tt_ in range(4):
            t = sg * 4 + tt_
            ln_tile_emit(ph, t, None)

            def tp_x(e, tt_=tt_, t=t):
                for i in range(8):
                    o = e.transpose(ps_t[:, i * 128:(i + 1) * 128],
                                    xn3[:, (t % 2) * DM + i * 128:(t % 2) * DM + (i + 1) * 128],
                                    ctx.ident[:, :])
                return o
            ph.t(tp_x)
            ph.v(lambda e, tt_=tt_: e.tensor_copy(
                bass.AP(xnT_s, tt_ * 128, [[8 * SW, 128], [SW, 8], [1, 128]]),
                bass.AP(ps_t, 0, [[2048, 128], [128, 8], [1, 128]])))

        for cht in range(16):
            def mm_up(e, cht=cht):
                for dt in range(8):
                    o = e.matmul(ps_x[:, 0:512],
                                 w["upT"][:, dt * DI + cht * 128:dt * DI + cht * 128 + 128],
                                 xnT_s[:, dt * SW:(dt + 1) * SW],
                                 start=(dt == 0), stop=(dt == 7))
                return o
            ph.t(mm_up)
            ph.s(lambda e, cht=cht: e.activation(
                xup[:, cht * SW:(cht + 1) * SW], ps_x[:, 0:512], AF.Copy, bias=0.0))

        for cht in range(16):
            ph.v(lambda e, cht=cht: e.tensor_copy(xcat[:, 0:3],
                                                  carry[:, cht * 3:(cht + 1) * 3]))
            ph.v(lambda e, cht=cht: e.tensor_copy(xcat[:, 3:SW + 3],
                                                  xup[:, cht * SW:(cht + 1) * SW]))
            ph.v(lambda e, cht=cht: e.tensor_scalar(
                acc_a[:, :], xcat[:, 0:SW], w["cwcol"][:, cht * 4:cht * 4 + 1],
                None, op0=ALU.mult))
            ph.v(lambda e, cht=cht: e.scalar_tensor_tensor(
                acc_b[:, :], xcat[:, 1:SW + 1], w["cwcol"][:, cht * 4 + 1:cht * 4 + 2],
                acc_a[:, :], op0=ALU.mult, op1=ALU.add))
            ph.v(lambda e, cht=cht: e.scalar_tensor_tensor(
                acc_a[:, :], xcat[:, 2:SW + 2], w["cwcol"][:, cht * 4 + 2:cht * 4 + 3],
                acc_b[:, :], op0=ALU.mult, op1=ALU.add))
            ph.v(lambda e, cht=cht: e.scalar_tensor_tensor(
                acc_b[:, :], xcat[:, 3:SW + 3], w["cwcol"][:, cht * 4 + 3:cht * 4 + 4],
                acc_a[:, :], op0=ALU.mult, op1=ALU.add))
            ph.s(lambda e, cht=cht: e.activation(
                shf[:, cht * SW:(cht + 1) * SW], acc_b[:, :], AF.Silu,
                bias=w["cbcol"][:, cht:cht + 1]))
            if sg < NSG - 1:
                ph.v(lambda e, cht=cht: e.tensor_copy(
                    carry[:, cht * 3:(cht + 1) * 3],
                    xup[:, (cht + 1) * SW - 3:(cht + 1) * SW]))

        for tt_ in range(4):
            def tp_s(e, tt_=tt_):
                for i in range(16):
                    o = e.transpose(ps_t[:, i * 128:(i + 1) * 128],
                                    shf[:, i * SW + tt_ * 128:i * SW + tt_ * 128 + 128],
                                    ctx.ident[:, :])
                return o
            ph.t(tp_s)
            ph.v(lambda e, tt_=tt_: e.tensor_copy(sh_tm[:, :], ps_t[:, :]))
            ph.dma("sync", lambda e, tt_=tt_, sg=sg: e.dma_start(
                out=dr["sh_d"][(sg * 4 + tt_) * 128:(sg * 4 + tt_ + 1) * 128, :],
                in_=sh_tm[:, :]))

        for cht in range(16):
            def mm_r1(e, cht=cht):
                o = e.matmul(ps_q[0:16, 0:512],
                             w["ig1T"][:, cht * 16:(cht + 1) * 16],
                             xup[:, cht * SW:(cht + 1) * SW],
                             start=(cht == 0), stop=(cht == 15))
                return o
            ph.t(mm_r1)
        ph.v(lambda e: e.tensor_copy(r16[:, :], ps_q[0:16, 0:512]))
        for cht in range(16):
            def mm_r2(e, cht=cht):
                o = e.matmul(ps_x[:, 0:512],
                             w["ig2T"][:, cht * 128:(cht + 1) * 128],
                             r16[:, :], start=True, stop=True)
                return o
            ph.t(mm_r2)
            ph.s(lambda e, cht=cht: e.activation(
                acc_a[:, :], ps_x[:, 0:512], AF.Sigmoid, bias=w["igb2col"][:, cht:cht + 1]))
            ph.v(lambda e, cht=cht: e.tensor_tensor(
                vfm[:, cht * SW:(cht + 1) * SW], acc_a[:, :],
                xup[:, cht * SW:(cht + 1) * SW], op=ALU.mult))

        for cht in range(16):
            def mm_qk(e, cht=cht):
                for of in range(2):
                    o = e.matmul(ps_q[:, of * 512:(of + 1) * 512],
                                 w["qkgtT"][:, cht * 256 + of * 128:cht * 256 + of * 128 + 128],
                                 shf[:, cht * SW:(cht + 1) * SW],
                                 start=(cht == 0), stop=(cht == 15))
                return o
            ph.t(mm_qk)
        ph.v(lambda e: e.tensor_copy(qk_t[:, :], ps_q[:, 0:512]))
        ph.s(lambda e: e.activation(gt_t[:, :], ps_q[:, 512:1024], AF.Copy, bias=0.0))
        ph.dma("gpsimd", lambda e: e.dma_start(out=k_sb[:, :], in_=qk_t[64:128, :]))
        ph.s(lambda e: e.activation(sc_a[:, :], gt_t[0:64, :], AF.Abs,
                                    bias=w["gbc"][0:64, 0:1]))
        ph.s(lambda e: e.activation(sc_a[:, :], sc_a[:, :], AF.Exp,
                                    scale=ctx.cst64(-1.0)))
        ph.s(lambda e: e.activation(sc_a[:, :], sc_a[:, :], AF.Ln,
                                    bias=ctx.cst64(1.0)))
        ph.s(lambda e: e.activation(gl_s[:, :], gt_t[0:64, :], AF.Relu,
                                    bias=w["gbc"][0:64, 0:1]))
        ph.v(lambda e: e.tensor_tensor(gl_s[:, :], gl_s[:, :], sc_a[:, :],
                                       op=ALU.add))
        ph.dma("gpsimd", lambda e: e.dma_start(out=tau_s[:, :], in_=gt_t[64:128, :]))
        ph.s(lambda e: e.activation(tau_s[:, :], tau_s[:, :], AF.Sigmoid,
                                    bias=w["tbc"][0:64, 0:1]))
        ph.s(lambda e: e.activation(it_s[:, :], gl_s[:, :], AF.Ln))
        ph.v(lambda e: e.tensor_tensor(it_s[:, :], it_s[:, :], tau_s[:, :], op=ALU.mult))
        ph.s(lambda e: e.activation(it_s[:, :], it_s[:, :], AF.Exp))
        ph.v(lambda e: e.scalar_tensor_tensor(rt_s[:, :], gl_s[:, :], ctx.cst64(-1.0),
                                              tau_s[:, :], op0=ALU.mult, op1=ALU.mult))
        ph.v(lambda e: e.tensor_tensor_scan(b_s[:, :], w["smask"][0:64, :], rt_s[:, :],
                                            0.0, op0=ALU.mult, op1=ALU.add))
        ph.v(lambda e: e.tensor_copy(seg_sc[:, 0:8],
                                     bass.AP(b_s, 63, [[SW, 64], [64, 8]])))
        ph.v(lambda e: e.tensor_copy(seg_sc[:, 8:16],
                                     bass.AP(b_s, 31, [[SW, 64], [64, 8]])))
        ph.s(lambda e: e.activation(seg_sc[:, 16:24], seg_sc[:, 0:8], AF.Exp))
        ph.v(lambda e, sg=sg: e.tensor_copy(btot_all[:, sg * 8:(sg + 1) * 8],
                                            seg_sc[:, 0:8]))
        ph.s(lambda e: e.activation(eb_s[:, :], b_s[:, :], AF.Exp))
        ph.v(lambda e, sg=sg: e.scalar_tensor_tensor(
            qb_all[:, sg * SW:(sg + 1) * SW], qk_t[0:64, :], ctx.cst64(0.125),
            eb_s[:, :], op0=ALU.mult, op1=ALU.mult))
        ph.v(lambda e: e.tensor_tensor(
            sc_a[:, :].rearrange("a (c x) -> a c x", x=CB),
            b_s[:, :].rearrange("a (c x) -> a c x", x=CB),
            bass.AP(seg_sc, 8, [[24, 64], [1, 8], [0, CB]]), op=ALU.subtract))
        ph.s(lambda e: e.activation(sc_a[:, :], sc_a[:, :], AF.Exp))
        ph.v(lambda e: e.scalar_tensor_tensor(
            qs_s[:, :], qk_t[0:64, :], ctx.cst64(0.125), sc_a[:, :],
            op0=ALU.mult, op1=ALU.mult))
        ph.v(lambda e: e.tensor_tensor(kit[:, :], k_sb[:, :], it_s[:, :], op=ALU.mult))
        ph.v(lambda e: e.tensor_tensor(
            sc_a[:, :].rearrange("a (c x) -> a c x", x=CB),
            bass.AP(seg_sc, 8, [[24, 64], [1, 8], [0, CB]]),
            b_s[:, :].rearrange("a (c x) -> a c x", x=CB), op=ALU.subtract))
        ph.s(lambda e: e.activation(sc_a[:, :], sc_a[:, :], AF.Exp))
        ph.v(lambda e: e.tensor_tensor(ks_s[:, :], kit[:, :], sc_a[:, :], op=ALU.mult))
        ph.v(lambda e: e.tensor_tensor(
            sc_a[:, :].rearrange("a (c x) -> a c x", x=CB),
            bass.AP(seg_sc, 0, [[24, 64], [1, 8], [0, CB]]),
            b_s[:, :].rearrange("a (c x) -> a c x", x=CB), op=ALU.subtract))
        ph.s(lambda e: e.activation(sc_a[:, :], sc_a[:, :], AF.Exp))
        ph.v(lambda e: e.tensor_tensor(kd_s[:, :], kit[:, :], sc_a[:, :], op=ALU.mult))
        ph.v(lambda e: e.tensor_tensor(k2b[:, :], k_sb[:, :], k_sb[:, :], op=ALU.mult))
        ph.t(lambda e: e.matmul(ps_o[0:1, 0:512], onescol[:, :], k2b[:, :],
                                start=True, stop=True))
        ph.v(lambda e: e.tensor_scalar(rr_f[:, :], ps_o[0:1, 0:512], 1e-24, None,
                                       op0=ALU.max))
        ph.s(lambda e: e.activation(rr_f[:, :], rr_f[:, :], AF.Ln))
        ph.s(lambda e: e.activation(r_row[:, :], rr_f[:, :], AF.Exp,
                                    scale=ctx.cvec[0:1, CIDX[-0.5]:CIDX[-0.5] + 1]))

        def tp_r(e):
            for cc in range(8):
                o = e.transpose(ps_t[0:64, 2 * cc:2 * cc + 1],
                                r_row[:, cc * 64:(cc + 1) * 64], ctx.ident[0:1, 0:1])
            return o
        ph.t(tp_r)
        ph.v(lambda e: e.tensor_copy(r_col[:, :],
                                     bass.AP(ps_t, 0, [[2048, 64], [2, 8]])))

        for cc in range(CPS):
            n = sg * CPS + cc
            def tp_v(e, cc=cc):
                for i in range(16):
                    o = e.transpose(ps_t[0:64, i * 128:(i + 1) * 128],
                                    vfm[:, i * SW + cc * CB:i * SW + (cc + 1) * CB],
                                    ctx.ident[:, :])
                return o
            ph.t(tp_v)
            ph.v(lambda e: e.tensor_copy(v_tm[:, :], ps_t[0:64, :]))
            ph.t(lambda e, cc=cc: e.transpose(
                ps_o[0:64, 0:64], kd_s[:, cc * CB:(cc + 1) * CB], ctx.ident[:, :])
                if False else
                e.transpose(ps_t[0:64, 0:64], kd_s[:, cc * CB:(cc + 1) * CB],
                            ctx.ident[0:64, 0:64]))
            ph.v(lambda e, cc=cc: e.tensor_scalar(
                kdt[:, :], ps_t[0:64, 0:64], r_col[:, cc:cc + 1], None, op0=ALU.mult))
            ph.t(lambda e, cc=cc: e.matmul(
                ps_ds[:, 0:64], ks_s[:, cc * CB:(cc + 1) * CB],
                qs_s[:, cc * CB:(cc + 1) * CB], start=True, stop=True))
            ph.v(lambda e, cc=cc: e.scalar_tensor_tensor(
                a_p[:, :], ps_ds[:, 0:64], r_col[:, cc:cc + 1],
                w["m64"], op0=ALU.mult, op1=ALU.mult))
            for vh in range(2):
                def mm_o(e, cc=cc, vh=vh):
                    for j in range(2):
                        e.matmul(ps_o[:, j * 512:(j + 1) * 512],
                                 qb_all[:, sg * SW + cc * CB:sg * SW + (cc + 1) * CB],
                                 s_bf[:, vh * 1024 + j * 512:vh * 1024 + (j + 1) * 512],
                                 start=True, stop=False)
                        o = e.matmul(ps_o[:, j * 512:(j + 1) * 512],
                                     a_p[:, :],
                                     v_tm[:, vh * 1024 + j * 512:vh * 1024 + (j + 1) * 512],
                                     start=False, stop=True)
                    return o
                ph.t(mm_o)

                def mm_ds(e, cc=cc, vh=vh):
                    for j in range(2):
                        o = e.matmul(ps_ds[:, j * 512:(j + 1) * 512],
                                     kdt[:, :],
                                     v_tm[:, vh * 1024 + j * 512:vh * 1024 + (j + 1) * 512],
                                     start=True, stop=True)
                    return o
                ph.t(mm_ds)
                ph.v(lambda e, vh=vh: e.tensor_copy(
                    o_ch[:, vh * 1024:(vh + 1) * 1024], ps_o[:, :]))
                ph.v(lambda e, cc=cc, vh=vh: e.scalar_tensor_tensor(
                    s_f[:, vh * 1024:(vh + 1) * 1024],
                    s_f[:, vh * 1024:(vh + 1) * 1024],
                    seg_sc[:, 16 + cc:17 + cc], ps_ds[:, :],
                    op0=ALU.mult, op1=ALU.add))
                ph.s(lambda e, vh=vh: e.activation(
                    s_bf[:, vh * 1024:(vh + 1) * 1024],
                    s_f[:, vh * 1024:(vh + 1) * 1024], AF.Copy, bias=0.0))
            ph.dma("sync", lambda e, n=n: e.dma_start(
                out=dr["o_d"][n * CB:(n + 1) * CB, :], in_=o_ch[:, :]))
        ph.run()

    ph = SerialPhase(ctx)
    ph.v(lambda e: e.tensor_tensor_scan(cumg[:, 0:NCH_B],
                                        w["ones64"][0:64, 0:NCH_B],
                                        btot_all[:, :], 0.0,
                                        op0=ALU.mult, op1=ALU.add))
    ph.v(lambda e: e.memset(cumg[:, NCH_B:NCH_B + 8], 0.0))
    ph.s(lambda e: e.activation(cumg[:, NCH_B + 1:NCH_B + 2],
                                cumg[:, NCH_B - 1:NCH_B], AF.Exp))
    ph.dma("sync", lambda e: e.dma_start(out=dr["gin_g"][:, 0:DI], in_=s_f[:, :]))
    ph.dma("sync", lambda e: e.dma_start(out=dr["gin_g"][:, DI:DI + 1],
                                         in_=cumg[:, NCH_B + 1:NCH_B + 2],
                                         allow_slow_non_contiguous=True))
    ph.g(lambda e: e.collective_compute(
        "AllGather", ALU.bypass, replica_groups=RG,
        ins=[dr["gin_g"].ap().opt()], outs=[dr["gout_g"].ap().opt()]))
    ph.run()
    ctx.srel()

    ctx.smark()
    gg = ctx.salloc("b_gg", [64, DI + 32], F32)
    gsc = ctx.salloc("b_gsc", [64, 8], F32)
    ph = SerialPhase(ctx)
    ph.v(lambda e: e.memset(s_f[:, :], 0.0))
    for i in range(NCORE - 1):
        ph.dma("sync", lambda e, i=i: e.dma_start(out=gg[:, 0:DI + 1],
                                                  in_=dr["gout_g"][i, :, 0:DI + 1]))
        ph.v(lambda e, i=i: e.tensor_scalar(
            gsc[:, 0:1], gg[:, DI:DI + 1], w["pm"][0:64, i:i + 1],
            w["pim"][0:64, i:i + 1], op0=ALU.mult, op1=ALU.add))
        ph.v(lambda e, i=i: e.tensor_scalar(
            gg[:, 0:DI], gg[:, 0:DI], w["pm"][0:64, i:i + 1], None, op0=ALU.mult))
        ph.v(lambda e, i=i: e.scalar_tensor_tensor(
            s_f[:, :], s_f[:, :], gsc[:, 0:1], gg[:, 0:DI],
            op0=ALU.mult, op1=ALU.add))
    ph.s(lambda e: e.activation(s_bf[:, :], s_f[:, :], AF.Copy, bias=0.0))
    ph.run()
    ctx.dump("s0b", s_f[:, :])

    ctx.smark()
    w["gateT"] = ctx.salloc("b_gateT", [128, 8 * DI], BF16)
    w["downT"] = ctx.salloc("b_downT", [128, 16 * DM], BF16)
    ecg = ctx.salloc("b_ecg", [64, NCH_B], F32)
    qbd = ctx.salloc("b_qbd", [64, 2 * 2 * CB], BF16)
    o_t = ctx.salloc("b_ot", [128, DI], BF16)
    sh_t = ctx.salloc("b_sht", [128, DI], BF16)
    g2_t = ctx.salloc("b_g2t", [128, 2 * DI], BF16)
    m1_t = ctx.salloc("b_m1t", [128, DI], BF16)
    xg2_t = ctx.salloc("b_xg2t", [128, DI], BF16)
    xgf_t = ctx.salloc("b_xgft", [128, DI], BF16)
    xnf_t = ctx.salloc("b_xnft", [128, DM], BF16)
    xgT_t = ctx.salloc("b_xgTt", [128, DI], BF16)
    ht2 = ctx.salloc("b_ht2", [128, 2 * DM], F32)
    rms_c = ctx.salloc("b_rmsc", [128, 8], F32)

    ph = SerialPhase(ctx)
    ph.dma("gpsimd", lambda e: e.dma_start(
        out=w["gateT"][:, :],
        in_=bass.AP(dr["gateT_d"], 0, [[DI, 128], [128 * DI, 8], [1, DI]])))
    ph.dma("gpsimd", lambda e: e.dma_start(
        out=w["downT"][:, :],
        in_=bass.AP(dr["downT_d"], 0, [[DM, 128], [128 * DM, 16], [1, DM]])))
    ph.v(lambda e: e.memset(ecg[:, 0:1], 0.0))
    ph.v(lambda e: e.tensor_copy(ecg[:, 1:NCH_B], cumg[:, 0:NCH_B - 1]))
    ph.s(lambda e: e.activation(ecg[:, :], ecg[:, :], AF.Exp))
    ph.run()

    est_p1.close()
    ps_big = est.enter_context(nc.psum_tensor("b_psbig", [128, DI], F32))
    ps_tp2 = est.enter_context(nc.psum_tensor("b_pstp2", [128, DI], BF16))
    ps_dn = est.enter_context(nc.psum_tensor("b_psdn", [128, DM], F32))

    ph = Pipe(ctx)
    Hht = {}; Hxn = {}; Htpa = {}; Heva = {}; Hfg = {}; Hsil = {}
    Hqbd = {}; Hoc = {}; Hdo = {}; Hao = {}; Hdsh = {}; Hm1 = {}
    Hxg2 = {}; Hsq = {}; Hexp = {}; Hxgf = {}; Htpd = {}; Hevd = {}
    Htpe = {}; Heve = {}; Hdn = {}; Hra = {}; Hout = {}
    for t in range(NTILE):
        p = t % 2
        Hht[t] = ph.dma("sync", lambda e, t=t, p=p: e.dma_start(
            out=ht2[:, p * DM:(p + 1) * DM], in_=h_in_d[t * 128:(t + 1) * 128, :]),
            deps=[Hout.get(t - 2)])
        Hxn[t] = ph.v(lambda e, t=t, p=p: e.tensor_scalar(
            xn3[:, p * DM:(p + 1) * DM], ht2[:, p * DM:(p + 1) * DM],
            stats[:, 2 * t:2 * t + 1], stats[:, 2 * t + 1:2 * t + 2],
            op0=ALU.subtract, op1=ALU.mult),
            deps=[Hht[t], Htpa.get(t - 2)])

        def tp_xn(e, p=p):
            for i in range(8):
                o = e.transpose(ps_tp2[:, i * 128:(i + 1) * 128],
                                xn3[:, p * DM + i * 128:p * DM + (i + 1) * 128],
                                ctx.ident[:, :])
            return o
        Htpa[t] = ph.t(tp_xn, deps=[Hxn[t], Heve.get(t - 1)])
        Heva[t] = ph.v(lambda e: e.tensor_copy(xnf_t[:, :], ps_tp2[:, 0:1024]),
                       deps=[Htpa[t]])

        def mm_fg(e):
            for q4 in range(4):
                for dt in range(8):
                    o = e.matmul(ps_big[:, q4 * 512:(q4 + 1) * 512],
                                 xnf_t[:, dt * 128:(dt + 1) * 128],
                                 w["gateT"][:, dt * DI + q4 * 512:dt * DI + (q4 + 1) * 512],
                                 start=(dt == 0), stop=(dt == 7))
            return o
        Hfg[t] = ph.t(mm_fg, deps=[Heva[t], Hao.get(t - 1)])
        Hsil[t] = ph.s(lambda e, p=p: e.activation(
            g2_t[:, p * DI:(p + 1) * DI], ps_big[:, :], AF.Silu),
            deps=[Hfg[t], Hxg2.get(t - 2)])

        for half in range(2):
            Hqbd[(t, half)] = ph.v(lambda e, t=t, p=p, half=half: e.tensor_scalar(
                qbd[:, p * 128 + half * CB:p * 128 + (half + 1) * CB],
                qb_all[:, t * 128 + half * CB:t * 128 + (half + 1) * CB],
                ecg[:, 2 * t + half:2 * t + half + 1], None, op0=ALU.mult),
                deps=[Hoc.get(t - 2)])

        def mm_oc(e, t=t, p=p):
            for half in range(2):
                for j in range(4):
                    o = e.matmul(ps_big[0:64, j * 512:(j + 1) * 512] if half == 0
                                 else ps_big[64:128, j * 512:(j + 1) * 512],
                                 qbd[:, p * 128 + half * CB:p * 128 + (half + 1) * CB],
                                 s_bf[:, j * 512:(j + 1) * 512],
                                 start=True, stop=True,
                                 tile_position=(0, 64) if half == 1 else None)
            return o
        Hoc[t] = ph.t(mm_oc, deps=[Hsil[t], Hqbd[(t, 0)], Hqbd[(t, 1)]])
        Hdo[t] = ph.dma("sync", lambda e, t=t: e.dma_start(
            out=o_t[:, :], in_=dr["o_d"][t * 128:(t + 1) * 128, :]),
            deps=[Hm1.get(t - 1)])
        Hao[t] = ph.v(lambda e: e.tensor_tensor(
            o_t[:, :], o_t[:, :],
            ps_big[:, :], op=ALU.add), deps=[Hoc[t], Hdo[t]])

        Hdsh[t] = ph.dma("sync", lambda e, t=t: e.dma_start(
            out=sh_t[:, :], in_=dr["sh_d"][t * 128:(t + 1) * 128, :]),
            deps=[Hm1.get(t - 1)])
        Hm1[t] = ph.v(lambda e: e.tensor_tensor(
            m1_t[:, :], sh_t[:, :], w["rwrow"][:, :],
            op=ALU.mult), deps=[Hdsh[t], Hsq.get(t - 1)])
        ph.v(lambda e: e.tensor_tensor(m1_t[:, :], m1_t[:, :],
                                       o_t[:, :],
                                       op=ALU.add), deps=[Hao[t]])
        Hxg2[t] = ph.v(lambda e, p=p: e.tensor_tensor(
            xg2_t[:, :], m1_t[:, :], g2_t[:, p * DI:(p + 1) * DI],
            op=ALU.mult), deps=[Hsil[t]])
        Hsq[t] = ph.s(lambda e, p=p: e.activation(
            m1_t[:, :], xg2_t[:, :], AF.Square, accum_out=rms_c[:, p * 4:p * 4 + 1]),
            deps=[Hxg2[t], Hxgf.get(t - 2)])
        ph.s(lambda e, p=p: e.activation(rms_c[:, p * 4 + 1:p * 4 + 2],
                                         rms_c[:, p * 4:p * 4 + 1], AF.Ln,
                                         scale=ctx.cst(1.0 / DI), bias=ctx.cst(1e-5)))
        Hexp[t] = ph.s(lambda e, p=p: e.activation(
            rms_c[:, p * 4 + 1:p * 4 + 2], rms_c[:, p * 4 + 1:p * 4 + 2], AF.Exp,
            scale=ctx.cst(-0.5)))
        ph.v(lambda e: e.tensor_tensor(xgf_t[:, :], xg2_t[:, :], w["nwrow"][:, :],
                                       op=ALU.mult), deps=[Hxg2[t], Htpe.get(t - 1)])
        Hxgf[t] = ph.v(lambda e, p=p: e.tensor_scalar(
            xgf_t[:, :], xgf_t[:, :], rms_c[:, p * 4 + 1:p * 4 + 2], None,
            op0=ALU.mult), deps=[Hexp[t]])

        def tp_xg1(e):
            for i in range(8):
                o = e.transpose(ps_tp2[:, i * 128:(i + 1) * 128],
                                xgf_t[:, i * 128:(i + 1) * 128], ctx.ident[:, :])
            return o

        def tp_xg2_(e):
            for i in range(8, 16):
                o = e.transpose(ps_tp2[:, (i - 8) * 128:(i - 7) * 128],
                                xgf_t[:, i * 128:(i + 1) * 128], ctx.ident[:, :])
            return o
        Htpd[t] = ph.t(tp_xg1, deps=[Hxgf[t], Heva[t]])
        Hevd[t] = ph.v(lambda e: e.tensor_copy(xgT_t[:, 0:1024], ps_tp2[:, 0:1024]),
                       deps=[Htpd[t], Hdn.get(t - 1)])
        Htpe[t] = ph.t(tp_xg2_, deps=[Hevd[t]])
        Heve[t] = ph.v(lambda e: e.tensor_copy(xgT_t[:, 1024:2048], ps_tp2[:, 0:1024]),
                       deps=[Htpe[t]])

        def mm_dn(e):
            for half in range(2):
                for cht in range(16):
                    o = e.matmul(ps_dn[:, half * 512:(half + 1) * 512],
                                 xgT_t[:, cht * 128:(cht + 1) * 128],
                                 w["downT"][:, cht * DM + half * 512:cht * DM + (half + 1) * 512],
                                 start=(cht == 0), stop=(cht == 15))
            return o
        Hdn[t] = ph.t(mm_dn, deps=[Heve[t], Hra.get(t - 1)])
        Hra[t] = ph.v(lambda e, p=p: e.tensor_tensor(
            ht2[:, p * DM:(p + 1) * DM], ht2[:, p * DM:(p + 1) * DM],
            ps_dn[:, :], op=ALU.add), deps=[Hdn[t]])
        Hout[t] = ph.dma("sync", lambda e, t=t, p=p: e.dma_start(
            out=h_out_d[t * 128:(t + 1) * 128, :], in_=ht2[:, p * DM:(p + 1) * DM]),
            deps=[Hra[t]])
    ph.run()
    ctx.srel()
    ctx.srel()
    ctx.srel()


def emit_ffn(ctx, est, tag, h_in_d, h_out_d, phi_sb, wsh_d, bsh_d,
             gin_s, gout_s, gin_y, gout_y):
    nc = ctx.nc
    ctx.smark()
    xnT = ctx.salloc(f"f{tag}_xnT", [128, 8 * TPC], BF16)
    s_col = ctx.salloc(f"f{tag}_scol", [128, 8], F32)
    s_bf = ctx.salloc(f"f{tag}_sbf", [128, 8], BF16)
    sg = ctx.salloc(f"f{tag}_sg", [128, 64], F32)
    y_bf = ctx.salloc(f"f{tag}_ybf", [64, DM], BF16)
    est_p = ExitStack()
    emit_ln_transpose(ctx, est_p, xnT, x_dram=h_in_d)
    est_p.close()

    ctx.smark()
    w_e = ctx.salloc(f"f{tag}_we", [128, 2 * 8 * DM], BF16)
    y_loc = ctx.salloc(f"f{tag}_yloc", [1, EPC * DM], F32)
    b_loc = ctx.salloc(f"f{tag}_bloc", [1, EPC * DM], F32)

    est_y = ExitStack()
    ps_y = est_y.enter_context(nc.psum_tensor(f"f{tag}_psy", [1, DM], F32))

    ph = Pipe(ctx)
    ph.v(lambda e: e.tensor_reduce(
        s_col[:, :], xnT[:, :].rearrange("a (d t) -> a d t", t=TPC),
        axis=AX.X, op=ALU.add))
    ph.dma("sync", lambda e: e.dma_start(out=gin_s.ap(), in_=s_col[:, :]))
    ph.g(lambda e: e.collective_compute(
        "AllGather", ALU.bypass, replica_groups=RG,
        ins=[gin_s.ap().opt()], outs=[gout_s.ap().opt()]))
    ph.dma("sync", lambda e: e.dma_start(
        out=sg[:, :], in_=bass.AP(gout_s, 0, [[8, 128], [1024, 8], [1, 8]])))
    ph.dma("sync", lambda e: e.dma_start(
        out=b_loc[:, :], in_=bass.AP(bsh_d, 0, [[EPC * DM, 1], [1, EPC * DM]])),
        deps="all")

    def sum_s(e):
        for i in range(1, 8):
            o = e.tensor_tensor(s_col[:, :], sg[:, 0:8] if i == 1 else s_col[:, :],
                                sg[:, i * 8:(i + 1) * 8], op=ALU.add)
            e.drain()
        return o
    ph.v(sum_s)
    ph.v(lambda e: e.tensor_copy(s_bf[:, :], s_col[:, :]))
    ph.run()
    ctx.dump("s_tot" + tag, s_col[:, :])

    ph = Pipe(ctx)
    h_mm = {}
    h_ev = {}
    h_dw = {}
    for e_i in range(EPC):
        h_dw[e_i] = ph.dma("gpsimd", lambda e, e_i=e_i: e.dma_start(
            out=w_e[:, (e_i % 2) * 8 * DM:(e_i % 2 + 1) * 8 * DM],
            in_=bass.AP(wsh_d, e_i * DM * DM, [[DM, 128], [128 * DM, 8], [1, DM]])),
            deps=[h_mm.get(e_i - 2)])

        def mm_y(e, e_i=e_i):
            for half in range(2):
                for dt in range(8):
                    o = e.matmul(ps_y[:, half * 512:(half + 1) * 512],
                                 s_bf[:, dt:dt + 1],
                                 w_e[:, (e_i % 2) * 8 * DM + dt * DM + half * 512:
                                     (e_i % 2) * 8 * DM + dt * DM + (half + 1) * 512],
                                 start=(dt == 0), stop=(dt == 7))
            return o
        h_mm[e_i] = ph.t(mm_y, deps=[h_dw[e_i], h_ev.get(e_i - 1)])
        h_ev[e_i] = ph.v(lambda e, e_i=e_i: e.tensor_copy(
            y_loc[:, e_i * DM:(e_i + 1) * DM], ps_y[:, :]), deps=[h_mm[e_i]])
    ph.v(lambda e: e.tensor_tensor(y_loc[:, :], y_loc[:, :], b_loc[:, :],
                                   op=ALU.add), deps="all")
    ph.dma("sync", lambda e: e.dma_start(
        out=bass.AP(gin_y, 0, [[EPC * DM, 1], [1, EPC * DM]]), in_=y_loc[:, :]),
        deps="all")
    ph.g(lambda e: e.collective_compute(
        "AllGather", ALU.bypass, replica_groups=RG,
        ins=[gin_y.ap().opt()], outs=[gout_y.ap().opt()]), deps="all")
    ph.dma("gpsimd", lambda e: e.dma_start(
        out=y_bf[:, :], in_=bass.AP(gout_y, 0, [[DM, 64], [1, DM]])), deps="all")
    ph.run()
    est_y.close()
    ctx.srel()
    el = ctx.salloc(f"f{tag}_el", [128, 2 * 2048], BF16)
    zc = ctx.salloc(f"f{tag}_zc", [128, 8], F32)
    craw = ctx.salloc(f"f{tag}_craw", [128, 2 * 64], F32)
    c_bf = ctx.salloc(f"f{tag}_cbf", [128, 2 * 64], BF16)
    cT = ctx.salloc(f"f{tag}_cT", [64, NTILE * 128], BF16)
    ht = ctx.salloc(f"f{tag}_ht", [128, 2 * DM], F32)

    ps_l = est.enter_context(nc.psum_tensor(f"f{tag}_psl", [128, 2048], F32))
    ps_c = est.enter_context(nc.psum_tensor(f"f{tag}_psc", [64, 128], BF16))
    ps_o = est.enter_context(nc.psum_tensor(f"f{tag}_pso", [128, DM], F32))
    ctx.dump("y" + tag, y_bf[:, :])

    ph = Pipe(ctx)
    H_exp = {}
    H_red = {}
    H_rcp = {}
    H_scl = {}
    H_tpc = {}
    H_ecT = {}
    H_add = {}
    H_out = {}
    H_h = {}
    for t in range(NTILE):
        p = t % 2
        H_h[t] = ph.dma("sync", lambda e, t=t, p=p: e.dma_start(
            out=ht[:, p * DM:(p + 1) * DM], in_=h_in_d[t * 128:(t + 1) * 128, :]),
            deps=[H_out.get(t - 2)])

        def mm_l(e, t=t):
            for q4 in range(4):
                for dt in range(8):
                    o = e.matmul(ps_l[:, q4 * 512:(q4 + 1) * 512],
                                 xnT[:, dt * TPC + t * 128:dt * TPC + (t + 1) * 128],
                                 phi_sb[:, dt * 2048 + q4 * 512:dt * 2048 + (q4 + 1) * 512],
                                 start=(dt == 0), stop=(dt == 7))
            return o
        l1 = ph.t(mm_l, deps=[H_exp.get(t - 1)])
        H_exp[t] = ph.s(lambda e, p=p: e.activation(
            el[:, p * 2048:(p + 1) * 2048], ps_l[:, :], AF.Exp,
            bias=ctx.cst(-8.0), accum_out=zc[:, p * 4:p * 4 + 1]),
            deps=[l1, H_scl.get(t - 2)])
        H_red[t] = ph.v(lambda e, p=p: e.tensor_reduce(
            craw[:, p * 64:(p + 1) * 64],
            el[:, p * 2048:(p + 1) * 2048].rearrange("a (n q) -> a n q", q=32),
            axis=AX.X, op=ALU.add), deps=[H_exp[t]])
        H_rcp[t] = ph.v(lambda e, p=p: e.reciprocal(
            zc[:, p * 4 + 1:p * 4 + 2], zc[:, p * 4:p * 4 + 1]), deps=[H_exp[t]])

        def scl(e, p=p):
            e.drain()
            return e.tensor_scalar(c_bf[:, p * 64:(p + 1) * 64],
                                   craw[:, p * 64:(p + 1) * 64],
                                   zc[:, p * 4 + 1:p * 4 + 2], None, op0=ALU.mult)
        H_scl[t] = ph.v(scl, deps=[H_tpc.get(t - 2)])
        H_tpc[t] = ph.t(lambda e, p=p: e.transpose(
            ps_c[:, :], c_bf[:, p * 64:(p + 1) * 64], ctx.ident[:, :]),
            deps=[H_scl[t], H_ecT.get(t - 1)])
        H_ecT[t] = ph.v(lambda e, t=t: e.tensor_copy(
            cT[:, t * 128:(t + 1) * 128], ps_c[:, :]), deps=[H_tpc[t]])

        def mm_o(e, t=t):
            for half in range(2):
                o = e.matmul(ps_o[:, half * 512:(half + 1) * 512],
                             cT[:, t * 128:(t + 1) * 128],
                             y_bf[:, half * 512:(half + 1) * 512],
                             start=True, stop=True)
            return o
        oo = ph.t(mm_o, deps=[H_ecT[t], H_add.get(t - 1)])
        H_add[t] = ph.v(lambda e, p=p: e.tensor_tensor(
            ht[:, p * DM:(p + 1) * DM], ht[:, p * DM:(p + 1) * DM],
            ps_o[:, :], op=ALU.add), deps=[oo, H_h[t]])
        H_out[t] = ph.dma("sync", lambda e, t=t, p=p: e.dma_start(
            out=h_out_d[t * 128:(t + 1) * 128, :], in_=ht[:, p * DM:(p + 1) * DM]),
            deps=[H_add[t]])
    ph.run()
    ctx.srel()


def build_program(dbg=()):
    nc = bass.Bass("TRN2", target_bir_lowering=False)
    ctx = Ctx(nc, dbg)

    x_d = nc.dram_tensor("x", [TPC, DM], F32, kind="ExternalInput")
    wq_d = nc.dram_tensor("wq", [DM, DM], F32, kind="ExternalInput")
    wk_d = nc.dram_tensor("wk", [DM, DM], F32, kind="ExternalInput")
    wv_d = nc.dram_tensor("wv", [DM, DM], F32, kind="ExternalInput")
    wb_d = nc.dram_tensor("wbeta", [DM, H], F32, kind="ExternalInput")
    wo_d = nc.dram_tensor("wo", [DM, DM], F32, kind="ExternalInput")
    ident_d = nc.dram_tensor("ident", [128, 128], BF16, kind="ExternalInput")
    cvec_d = nc.dram_tensor("cvec", [128, len(CONSTS)], F32, kind="ExternalInput")
    m_slo_d = nc.dram_tensor("m_slo", [128, 128], BF16, kind="ExternalInput")
    m_trq_d = nc.dram_tensor("m_trq", [128, 128], BF16, kind="ExternalInput")
    idpat_d = nc.dram_tensor("idpat", [128, 2048], BF16, kind="ExternalInput")
    id4_d = nc.dram_tensor("id4", [128, 512], BF16, kind="ExternalInput")
    pmask_d = nc.dram_tensor("pmask", [128, 8], F32, kind="ExternalInput")
    pimask_d = nc.dram_tensor("pimask", [128, 8], F32, kind="ExternalInput")
    gn_col_d = nc.dram_tensor("gn_col", [128, 8], F32, kind="ExternalInput")
    out_d = nc.dram_tensor("out", [TPC, DM], F32, kind="ExternalOutput")

    phi_d = nc.dram_tensor("phi2", [DM, 2048], F32, kind="ExternalInput")
    wsh_d = nc.dram_tensor("wsh", [EPC, DM, DM], F32, kind="ExternalInput")
    bsh_d = nc.dram_tensor("bsh", [EPC, DM], F32, kind="ExternalInput")

    gin_bp = nc.dram_tensor("gin_bp", [128, 4096], BF16)
    gout_bp = nc.dram_tensor("gout_bp", [8, 128, 4096], BF16)
    h1_d = nc.dram_tensor("h1_d", [TPC, DM], F32)
    h2_d = nc.dram_tensor("h2_d", [TPC, DM], F32)
    gin_s1 = nc.dram_tensor("gin_s1", [128, 8], F32)
    gout_s1 = nc.dram_tensor("gout_s1", [8, 128, 8], F32)
    gin_y1 = nc.dram_tensor("gin_y1", [EPC, DM], F32)
    gout_y1 = nc.dram_tensor("gout_y1", [8, EPC, DM], F32)
    gin_s2 = nc.dram_tensor("gin_s2", [128, 8], F32)
    gout_s2 = nc.dram_tensor("gout_s2", [8, 128, 8], F32)
    gin_y2 = nc.dram_tensor("gin_y2", [EPC, DM], F32)
    gout_y2 = nc.dram_tensor("gout_y2", [8, EPC, DM], F32)

    h3_d = nc.dram_tensor("h3_d", [TPC, DM], F32)
    sh_d = nc.dram_tensor("sh_d", [TPC, DI], BF16)
    o_d = nc.dram_tensor("o_d", [TPC, DI], BF16)
    gin_h = nc.dram_tensor("gin_h", [128, 48], F32)
    gout_h = nc.dram_tensor("gout_h", [8, 128, 48], F32)
    gin_g = nc.dram_tensor("gin_g", [64, DI + 32], F32)
    gout_g = nc.dram_tensor("gout_g", [8, 64, DI + 32], F32)
    upT_d = nc.dram_tensor("upT", [DM, DI], F32, kind="ExternalInput")
    gateT_d = nc.dram_tensor("gateT", [DM, DI], F32, kind="ExternalInput")
    downT_d = nc.dram_tensor("downT", [DI, DM], F32, kind="ExternalInput")
    qkgtT_d = nc.dram_tensor("qkgtT", [DI, 256], F32, kind="ExternalInput")
    ig1T_d = nc.dram_tensor("ig1T", [DI, 16], F32, kind="ExternalInput")
    ig2T_d = nc.dram_tensor("ig2T", [16, DI], F32, kind="ExternalInput")
    cwcol_d = nc.dram_tensor("cwcol", [128, 64], F32, kind="ExternalInput")
    cbcol_d = nc.dram_tensor("cbcol", [128, 16], F32, kind="ExternalInput")
    igb2_d = nc.dram_tensor("igb2col", [128, 16], F32, kind="ExternalInput")
    gbc_d = nc.dram_tensor("gbc", [128, 1], F32, kind="ExternalInput")
    tbc_d = nc.dram_tensor("tbc", [128, 1], F32, kind="ExternalInput")
    rwrow_d = nc.dram_tensor("rwrow", [128, DI], F32, kind="ExternalInput")
    nwrow_d = nc.dram_tensor("nwrow", [128, DI], F32, kind="ExternalInput")
    smask_d = nc.dram_tensor("smask", [128, 512], F32, kind="ExternalInput")
    ones64_d = nc.dram_tensor("ones64", [128, 64], F32, kind="ExternalInput")
    hmask_d = nc.dram_tensor("hmask", [128, 8], F32, kind="ExternalInput")

    top = ctx.top
    ctx.ident = top.enter_context(nc.sbuf_tensor("ident_sb", [128, 128], BF16))
    ctx.cvec = top.enter_context(nc.sbuf_tensor("cvec_sb", [128, len(CONSTS)], F32))
    mask_slo = top.enter_context(nc.sbuf_tensor("m_slo_sb", [128, 128], BF16))
    mask_trq = top.enter_context(nc.sbuf_tensor("m_trq_sb", [128, 128], BF16))
    idpat = top.enter_context(nc.sbuf_tensor("idpat_sb", [128, 2048], BF16))
    id4 = top.enter_context(nc.sbuf_tensor("id4_sb", [128, 512], BF16))
    pmask = top.enter_context(nc.sbuf_tensor("pmask_sb", [128, 8], F32))
    pimask = top.enter_context(nc.sbuf_tensor("pimask_sb", [128, 8], F32))
    gn_col = top.enter_context(nc.sbuf_tensor("gn_col_sb", [128, 8], F32))
    ctx.init_arena()

    with top:
        s0 = ctx.sem()
        with nc.Block(no_gpsimd_drain=True) as blk:
            @blk.sync
            def _(e):
                for dst, srcd in ((ctx.ident, ident_d), (ctx.cvec, cvec_d),
                                  (mask_slo, m_slo_d), (mask_trq, m_trq_d),
                                  (idpat, idpat_d), (id4, id4_d),
                                  (pmask, pmask_d), (pimask, pimask_d),
                                  (gn_col, gn_col_d)):
                    o = e.dma_start(out=dst[:, :], in_=srcd.ap())
                    inc(o, s0, 16)
                e.wait_ge(s0.h, s0.v)

        ctx.smark()
        q_tm = ctx.salloc("q_tm", [128, NTILE * DM], BF16)
        k_tm = ctx.salloc("k_tm", [128, NTILE * DM], BF16)
        beta_sb = ctx.salloc("beta", [128, NTILE * H], F32)
        v_tm = ctx.salloc("v_tm", [128, NTILE * DM], BF16)

        ctx.smark()
        xnT = ctx.salloc("xnT", [128, 8 * TPC], BF16)
        wq_sb = ctx.salloc("wq_sb", [128, 8 * DM], BF16)
        wk_sb = ctx.salloc("wk_sb", [128, 8 * DM], BF16)
        wv_sb = ctx.salloc("wv_sb", [128, 8 * DM], BF16)
        wbe_sb = ctx.salloc("wbe_sb", [128, 8 * H], BF16)
        sw = ctx.sem()
        with nc.Block(no_gpsimd_drain=True) as blk:
            @blk.gpsimd
            def _(e):
                for w_d, w_sb, cols in ((wq_d, wq_sb, DM), (wk_d, wk_sb, DM),
                                        (wv_d, wv_sb, DM), (wb_d, wbe_sb, H)):
                    o = e.dma_start(
                        out=w_sb[:, :],
                        in_=bass.AP(w_d, 0, [[cols, 128], [128 * cols, 8], [1, cols]]))
                    inc(o, sw, 16)
                e.wait_ge(sw.h, sw.v)

        est_p = ExitStack()
        emit_ln_transpose(ctx, est_p, xnT, x_dram=x_d)
        est_p.close()
        ctx.dump("xnT", xnT[:, :])

        est_p = ExitStack()
        emit_proj_qkvb(ctx, est_p, xnT, wq_sb, wk_sb, wv_sb, wbe_sb,
                       q_tm, k_tm, v_tm, beta_sb)
        est_p.close()
        ctx.srel()
        ctx.dump("q_tm", q_tm[:, :])
        ctx.dump("k_tm", k_tm[:, :])
        ctx.dump("v_tm", v_tm[:, :])
        ctx.dump("beta", beta_sb[:, :])

        o_tm = v_tm
        est_d = ExitStack()
        emit_attn_a_delta(ctx, est_d, q_tm, k_tm, v_tm, beta_sb, o_tm,
                          gin_bp, gout_bp, mask_slo, mask_trq, idpat, id4,
                          pmask, pimask)
        est_d.close()

        ctx.smark()
        est_p6 = ExitStack()
        h_sb = ctx.salloc("h_sb", [128, 2 * DM], F32)
        wo_sb = ctx.salloc("wo_sb", [128, 8 * DM], BF16)
        xt6 = ctx.salloc("xt6", [128, 2 * DM], F32)
        onr = ctx.salloc("onr", [128, DM], BF16)
        sq6 = ctx.salloc("sq6", [128, DM], F32)
        rs6 = ctx.salloc("rs6", [128, 4], F32)
        onT = ctx.salloc("onT", [128, DM], BF16)
        pt6 = est_p6.enter_context(nc.psum_tensor("pt6", [128, DM], BF16))
        pr6 = est_p6.enter_context(nc.psum_tensor("pr6", [128, DM], F32))

        ph = SerialPhase(ctx)
        ph.dma("gpsimd", lambda e: e.dma_start(
            out=wo_sb[:, :],
            in_=bass.AP(wo_d, 0, [[DM, 128], [128 * DM, 8], [1, DM]])))

        def scale_wo(e):
            for dt in range(8):
                o = e.tensor_scalar(wo_sb[:, dt * DM:(dt + 1) * DM],
                                    wo_sb[:, dt * DM:(dt + 1) * DM],
                                    gn_col[:, dt:dt + 1], None, op0=ALU.mult)
            return o
        ph.v(scale_wo)
        for t in range(NTILE):
            ph.dma("sync", lambda e, t=t: e.dma_start(
                out=xt6[:, (t % 2) * DM:(t % 2 + 1) * DM],
                in_=x_d[t * 128:(t + 1) * 128, :]))
            ph.v(lambda e, t=t: e.tensor_tensor(
                sq6[:, :], o_tm[:, t * DM:(t + 1) * DM],
                o_tm[:, t * DM:(t + 1) * DM], op=ALU.mult))
            ph.v(lambda e, t=t: e.tensor_reduce(
                rs6[:, :], sq6[:, :].rearrange("a (h d) -> a h d", d=DK),
                axis=AX.X, op=ALU.add))
            ph.s(lambda e: e.activation(rs6[:, :], rs6[:, :], AF.Ln,
                                        scale=ctx.cst(1.0 / 256), bias=ctx.cst(1e-5)))
            ph.s(lambda e: e.activation(rs6[:, :], rs6[:, :], AF.Exp,
                                        scale=ctx.cst(-0.5)))
            ph.v(lambda e, t=t: e.tensor_tensor(
                onr[:, :].rearrange("a (h d) -> a h d", d=DK),
                o_tm[:, t * DM:(t + 1) * DM].rearrange("a (h d) -> a h d", d=DK),
                bass.AP(rs6, 0, [[4, 128], [1, H], [0, DK]]), op=ALU.mult))

            def tp_on(e, t=t):
                for i in range(8):
                    o = e.transpose(pt6[:, i * 128:(i + 1) * 128],
                                    onr[:, i * 128:(i + 1) * 128], ctx.ident[:, :])
                return o
            ph.t(tp_on)
            ph.v(lambda e: e.tensor_copy(onT[:, :], pt6[:, :]))

            def mm_wo(e, t=t):
                for half in range(2):
                    for dt in range(8):
                        o = e.matmul(pr6[:, half * 512:(half + 1) * 512],
                                     onT[:, dt * 128:(dt + 1) * 128],
                                     wo_sb[:, dt * DM + half * 512:dt * DM + (half + 1) * 512],
                                     start=(dt == 0), stop=(dt == 7))
                return o
            ph.t(mm_wo)
            ph.v(lambda e, t=t: e.tensor_tensor(
                h_sb[:, (t % 2) * DM:(t % 2 + 1) * DM],
                xt6[:, (t % 2) * DM:(t % 2 + 1) * DM],
                pr6[:, :], op=ALU.add))
            ph.dma("sync", lambda e, t=t: e.dma_start(
                out=h1_d[t * 128:(t + 1) * 128, :],
                in_=h_sb[:, (t % 2) * DM:(t % 2 + 1) * DM]))
        ph.run()
        est_p6.close()
        ctx.srel()
        ctx.srel()
        ctx.dump_dram("h1", h1_d)

        ctx.smark()
        phi_sb = ctx.salloc("phi_sb", [128, 8 * 2048], BF16)
        sph = ctx.sem()
        with nc.Block(no_gpsimd_drain=True) as blk:
            @blk.gpsimd
            def _(e):
                o = e.dma_start(out=phi_sb[:, :],
                                in_=bass.AP(phi_d, 0, [[2048, 128],
                                                       [128 * 2048, 8], [1, 2048]]))
                inc(o, sph, 16)
                e.wait_ge(sph.h, sph.v)
        est_f1 = ExitStack()
        emit_ffn(ctx, est_f1, "1", h1_d, h2_d, phi_sb, wsh_d, bsh_d,
                 gin_s1, gout_s1, gin_y1, gout_y1)
        est_f1.close()
        ctx.srel()
        ctx.dump_dram("h2", h2_d)

        ctx.smark()
        w = {
            "qkgtT": ctx.salloc("b_qkgtT", [128, 16 * 256], BF16),
            "ig1T": ctx.salloc("b_ig1T", [128, 16 * 16], BF16),
            "ig2T": ctx.salloc("b_ig2T", [16, DI], BF16),
            "cwcol": ctx.salloc("b_cw", [128, 64], F32),
            "cbcol": ctx.salloc("b_cb", [128, 16], F32),
            "igb2col": ctx.salloc("b_igb", [128, 16], F32),
            "gbc": ctx.salloc("b_gbc", [128, 1], F32),
            "tbc": ctx.salloc("b_tbc", [128, 1], F32),
            "rwrow": ctx.salloc("b_rw", [128, DI], BF16),
            "nwrow": ctx.salloc("b_nw", [128, DI], BF16),
            "smask": ctx.salloc("b_sm", [128, 512], F32),
            "ones64": ctx.salloc("b_o64", [128, 64], F32),
            "hmask": ctx.salloc("b_hm", [128, 8], F32),
            "m64": mask_trq[0:64, 0:64],
            "pm": pmask,
            "pim": pimask,
        }
        swb = ctx.sem()
        with nc.Block(no_gpsimd_drain=True) as blk:
            @blk.gpsimd
            def _(e):
                o = e.dma_start(out=w["qkgtT"][:, :],
                                in_=bass.AP(qkgtT_d, 0, [[256, 128], [128 * 256, 16], [1, 256]]))
                inc(o, swb, 16)
                o = e.dma_start(out=w["ig1T"][:, :],
                                in_=bass.AP(ig1T_d, 0, [[16, 128], [128 * 16, 16], [1, 16]]))
                inc(o, swb, 16)
                o = e.dma_start(out=w["ig2T"][:, :], in_=ig2T_d.ap())
                inc(o, swb, 16)
                o = e.dma_start(out=w["rwrow"][:, :], in_=rwrow_d.ap())
                inc(o, swb, 16)
                o = e.dma_start(out=w["nwrow"][:, :], in_=nwrow_d.ap())
                inc(o, swb, 16)
            @blk.sync
            def _(e):
                for dst, srcd in ((w["cwcol"], cwcol_d), (w["cbcol"], cbcol_d),
                                  (w["igb2col"], igb2_d), (w["gbc"], gbc_d),
                                  (w["tbc"], tbc_d), (w["smask"], smask_d),
                                  (w["ones64"], ones64_d), (w["hmask"], hmask_d)):
                    o = e.dma_start(out=dst[:, :], in_=srcd.ap())
                    inc(o, swb, 16)
                e.wait_ge(swb.h, swb.v)
            @blk.gpsimd
            def _(e):
                e.wait_ge(swb.h, swb.v)
        dr = {"sh_d": sh_d, "o_d": o_d, "gin_h": gin_h, "gout_h": gout_h,
              "gin_g": gin_g, "gout_g": gout_g, "gateT_d": gateT_d,
              "downT_d": downT_d, "upT_d": upT_d}
        est_b = ExitStack()
        emit_attn_b(ctx, est_b, h2_d, h3_d, w, dr)
        est_b.close()
        ctx.srel()
        ctx.dump_dram("h3", h3_d)

        ctx.smark()
        phi_sb2 = ctx.salloc("phi_sb2", [128, 8 * 2048], BF16)
        sph2 = ctx.sem()
        with nc.Block(no_gpsimd_drain=True) as blk:
            @blk.gpsimd
            def _(e):
                o = e.dma_start(out=phi_sb2[:, :],
                                in_=bass.AP(phi_d, 0, [[2048, 128],
                                                       [128 * 2048, 8], [1, 2048]]))
                inc(o, sph2, 16)
                e.wait_ge(sph2.h, sph2.v)
        est_f2 = ExitStack()
        emit_ffn(ctx, est_f2, "2", h3_d, out_d, phi_sb2, wsh_d, bsh_d,
                 gin_s2, gout_s2, gin_y2, gout_y2)
        est_f2.close()
        ctx.srel()
        top.pop_all()
    return nc, ctx


def _prep_inputs(inputs):
    g = lambda n: np.asarray(inputs[n], np.float32)
    x = g("L0_attn_a_x").reshape(16384, DM)
    ident = np.eye(128, dtype=BF)
    ii, jj = np.indices((128, 128))
    m_slo = (jj < ii).astype(BF)
    m_trq = (ii <= jj).astype(BF)
    idpat = np.zeros((128, 2048), BF)
    for dkt in range(2):
        for h in range(H):
            for p in range(128):
                idpat[p, dkt * 1024 + h * 256 + dkt * 128 + p] = 1
    id4 = np.tile(np.eye(128, dtype=BF), (1, 4)).reshape(128, 512)
    id4 = np.concatenate([np.eye(128, dtype=BF)] * 4, axis=1)
    gn4 = np.tile(g("L0_attn_a_g_norm"), H)
    gn_col = gn4.reshape(8, 128).T.astype(np.float32).copy()
    cvec = np.tile(np.asarray(CONSTS, np.float32)[None, :], (128, 1))
    phi2 = np.ascontiguousarray(g("L0_ffn_phi").reshape(DM, 2048))
    Wff = g("L0_ffn_W")
    bff = g("L0_ffn_b")
    upT = np.ascontiguousarray(g("L0_attn_b_up_w").T)
    gateT = np.ascontiguousarray(g("L0_attn_b_gate_w").T)
    downT = np.ascontiguousarray(g("L0_attn_b_down_w").T)
    qkgtT = np.ascontiguousarray(np.concatenate(
        [g("L0_attn_b_q_w"), g("L0_attn_b_k_w"), g("L0_attn_b_g_w"),
         g("L0_attn_b_tau_w")], 0).T)
    ig1T = np.ascontiguousarray(g("L0_attn_b_ig_w1").T)
    ig2T = np.ascontiguousarray(g("L0_attn_b_ig_w2").T)
    cwcol = np.ascontiguousarray(
        g("L0_attn_b_conv_w").reshape(16, 128, 4).transpose(1, 0, 2).reshape(128, 64))
    cbcol = np.ascontiguousarray(g("L0_attn_b_conv_b").reshape(16, 128).T)
    igb2col = np.ascontiguousarray(g("L0_attn_b_ig_b2").reshape(16, 128).T)
    gbc = np.zeros((128, 1), np.float32)
    gbc[0:64, 0] = g("L0_attn_b_g_b")
    tbc = np.zeros((128, 1), np.float32)
    tbc[0:64, 0] = g("L0_attn_b_tau_b")
    rwrow = np.tile(g("L0_attn_b_residual_weight")[None, :], (128, 1))
    nwrow = np.tile(g("L0_attn_b_norm_weight")[None, :], (128, 1))
    smask = np.ones((128, 512), np.float32)
    smask[:, ::CB] = 0.0
    ones64 = np.ones((128, 64), np.float32)

    def hmask_c(c):
        hm = np.zeros((128, 8), np.float32)
        if c > 0:
            hm[:, c - 1] = 1.0
        return hm
    per_core = []
    for c in range(NCORE):
        pm = np.zeros((128, 8), np.float32)
        pm[:, :c] = 1.0
        per_core.append({
            "x": np.ascontiguousarray(x[c * TPC:(c + 1) * TPC]),
            "wq": g("L0_attn_a_Wq"),
            "wk": g("L0_attn_a_Wk"),
            "wv": g("L0_attn_a_Wv"),
            "wbeta": g("L0_attn_a_Wbeta"),
            "wo": g("L0_attn_a_Wo"),
            "ident": ident,
            "cvec": cvec,
            "m_slo": m_slo,
            "m_trq": m_trq,
            "idpat": idpat,
            "id4": id4,
            "pmask": pm,
            "pimask": 1.0 - pm,
            "gn_col": gn_col,
            "phi2": phi2,
            "wsh": np.ascontiguousarray(Wff[c * EPC:(c + 1) * EPC]),
            "bsh": np.ascontiguousarray(bff[c * EPC:(c + 1) * EPC]),
            "upT": upT, "gateT": gateT, "downT": downT, "qkgtT": qkgtT,
            "ig1T": ig1T, "ig2T": ig2T, "cwcol": cwcol, "cbcol": cbcol,
            "igb2col": igb2col, "gbc": gbc, "tbc": tbc, "rwrow": rwrow,
            "nwrow": nwrow, "smask": smask, "ones64": ones64, "hmask": hmask_c(c),
        })
    return per_core


def kernel(**inputs):
    per_core = _prep_inputs(inputs)
    nc, _ = build_program()
    res = run_bass_kernel_spmd(nc, per_core, core_ids=list(range(NCORE)))
    out = np.concatenate([np.asarray(r["out"], np.float32) for r in res.results], 0)
    return out.reshape(1, 16384, DM)


# version 05
# speedup vs baseline: 1.2456x; 1.1380x over previous
import numpy as np
import ml_dtypes
import concourse.bass as bass
import concourse.mybir as mybir
from concourse.bass_utils import run_bass_kernel_spmd
from contextlib import ExitStack

F32 = mybir.dt.float32
BF16 = mybir.dt.bfloat16
AF = mybir.ActivationFunctionType
ALU = mybir.AluOpType
AX = mybir.AxisListType
BF = ml_dtypes.bfloat16

NCORE = 8
TPC = 2048
NTILE = TPC // 128
DM = 1024
H = 4
DK = 256
CA = 128
NCH_A = TPC // CA
CB = 64
NCH_B = TPC // CB
DI = 2048
SEG = 512
NSEG = TPC // SEG
NE = 64
EPC = NE // NCORE

RG = [list(range(NCORE))]

CONSTS = [1e-5, -0.5, -8.0, 1.0, -1.0, 0.125, 1.0 / 256, 1.0 / 2048]
CIDX = {v: i for i, v in enumerate(CONSTS)}


class Sem:
    __slots__ = ("h", "v")

    def __init__(self, h):
        self.h = h
        self.v = 0


class Ctx:
    def __init__(self, nc, dbg=()):
        self.nc = nc
        self.dbg = set(dbg)
        self.dbg_outs = {}
        self.top = ExitStack()
        self._semn = 0

    def sem(self, name=None):
        self._semn += 1
        h = self.top.enter_context(self.nc.semaphore(name or f"s{self._semn}"))
        return Sem(h)

    def init_arena(self):
        import concourse.mybir as _mb
        avail = self.nc.sbuf_bytes_remaining - 8192
        base, end = self.nc.bump_sbuf(avail)
        self.abase, self.aend, self.acur = base, end, base
        self.amarks = []
        self.apeak = base

    def salloc(self, name, shape, dtype):
        nb = int(np.prod(shape[1:])) * mybir.dt.size(dtype)
        nb = (nb + 31) & ~31
        t = self.nc.alloc_sbuf_tensor_at(name, shape, dtype, offset=self.acur)
        self.acur += nb
        self.apeak = max(self.apeak, self.acur)
        assert self.acur <= self.aend, (name, self.acur, self.aend)
        return t

    def uname(self, n):
        self._semn += 1
        return f"{n}_{self._semn}"

    def smark(self):
        self.amarks.append(self.acur)

    def srel(self):
        self.acur = self.amarks.pop()

    def cst(self, v):
        return self.cvec[:, CIDX[v]:CIDX[v] + 1]

    def cst64(self, v):
        return self.cvec[0:64, CIDX[v]:CIDX[v] + 1]

    def dump_dram(self, name, dram):
        if name not in self.dbg:
            return
        nc = self.nc
        o = nc.dram_tensor("dbg_" + name, list(dram.shape), dram.dtype,
                           kind="ExternalOutput")
        s = self.sem()
        with nc.Block() as blk:
            @blk.sync
            def _(e):
                e.dma_start(out=o.ap(), in_=dram.ap()).then_inc(s.h, 16)
                e.wait_ge(s.h, 16)
        self.dbg_outs[name] = list(dram.shape)

    def dump(self, name, ap):
        if name not in self.dbg:
            return
        nc = self.nc
        shape = list(ap.shape)
        o = nc.dram_tensor("dbg_" + name, shape, ap.dtype, kind="ExternalOutput")
        s = self.sem()
        with nc.Block() as blk:
            @blk.sync
            def _(e):
                e.dma_start(out=o.ap(), in_=ap).then_inc(s.h, 16)
                e.wait_ge(s.h, 16)
        self.dbg_outs[name] = shape


def inc(instr, sem, n=1):
    instr.then_inc(sem.h, n)
    sem.v += n


class Pipe:

    ENGS = ("tensor", "vector", "scalar", "gpsimd", "sync")

    def __init__(self, ctx):
        self.ctx = ctx
        if not hasattr(ctx, "esems"):
            ctx.esems = {E: ctx.sem() for E in self.ENGS}
            ctx.dsems = {E: ctx.sem() for E in self.ENGS}
        self.steps = []

    def _cur(self):
        return [(E, self.ctx.esems[E].v, self.ctx.dsems[E].v) for E in self.ENGS]

    def add(self, eng, fn, amt=1, is_dma=False, deps="all"):
        ctx = self.ctx
        waits = []
        if deps == "all":
            for (E, cv, dv) in self._cur():
                if cv > 0:
                    waits.append((ctx.esems[E], cv))
                if dv > 0:
                    waits.append((ctx.dsems[E], dv))
        elif deps:
            for d in deps:
                if d is None:
                    continue
                if isinstance(d, list):
                    for (E, cv, dv) in d:
                        if cv > 0:
                            waits.append((ctx.esems[E], cv))
                        if dv > 0:
                            waits.append((ctx.dsems[E], dv))
                else:
                    waits.append(d)
        sem = ctx.dsems[eng] if is_dma else ctx.esems[eng]
        sem.v += amt
        handle = (sem, sem.v)
        self.steps.append((eng, fn, amt, sem, waits))
        return handle

    def t(self, fn, deps="all"):
        return self.add("tensor", fn, deps=deps)

    def v(self, fn, deps="all", drain=True):
        if drain:
            orig = fn

            def fn2(e, orig=orig):
                e.drain()
                return orig(e)
            fn = fn2
        return self.add("vector", fn, deps=deps)

    def s(self, fn, deps="all"):
        return self.add("scalar", fn, deps=deps)

    def g(self, fn, deps="all"):
        return self.add("gpsimd", fn, deps=deps)

    def dma(self, eng, fn, deps="all"):
        return self.add(eng, fn, amt=16, is_dma=True, deps=deps)

    def barrier(self):
        return self._cur()

    def run(self):
        nc = self.ctx.nc
        by_eng = {}
        for (eng, fn, amt, sem, waits) in self.steps:
            by_eng.setdefault(eng, []).append((fn, amt, sem, waits))
        final = self._cur()
        with nc.Block(no_gpsimd_drain=True) as blk:
            for eng, lst in by_eng.items():
                def mk(lst):
                    def body(e):
                        seen = {}
                        for (fn, amt, sem, waits) in lst:
                            for (ws, wv) in waits:
                                if seen.get(id(ws), -1) < wv:
                                    e.wait_ge(ws.h, wv)
                                    seen[id(ws)] = wv
                            o = fn(e)
                            o.then_inc(sem.h, amt)
                    return body
                getattr(blk, eng)(mk(lst))

            @blk.sync
            def _(e):
                for (E, cv, dv) in final:
                    if cv > 0:
                        e.wait_ge(self.ctx.esems[E].h, cv)
                    if dv > 0:
                        e.wait_ge(self.ctx.dsems[E].h, dv)


class SerialPhase:

    def __init__(self, ctx):
        self.ctx = ctx
        self.steps = []

    def add(self, eng, fn, amt=1):
        self.steps.append((eng, fn, amt))

    def t(self, fn):
        self.add("tensor", fn)

    def v(self, fn):
        self.add("vector", fn)

    def s(self, fn):
        self.add("scalar", fn)

    def g(self, fn):
        self.add("gpsimd", fn)

    def dma(self, eng, fn):
        self.add(eng, fn, 16)

    def run(self):
        nc = self.ctx.nc
        sem = self.ctx.sem()
        cum = 0
        pre = []
        for (eng, fn, amt) in self.steps:
            pre.append(cum)
            cum += amt
        by_eng = {}
        for i, (eng, fn, amt) in enumerate(self.steps):
            by_eng.setdefault(eng, []).append((i, fn, amt))
        with nc.Block(no_gpsimd_drain=True) as blk:
            for eng, lst in by_eng.items():
                def mk(lst):
                    def body(e):
                        for (i, fn, amt) in lst:
                            if pre[i] > 0:
                                e.wait_ge(sem.h, pre[i])
                            o = fn(e)
                            o.then_inc(sem.h, amt)
                        return
                    return body
                getattr(blk, eng)(mk(lst))

            @blk.sync
            def _(e):
                e.wait_ge(sem.h, cum)


def emit_ln_transpose(ctx, est, xnT, h_src=None, x_dram=None, store_stats=None):
    nc = ctx.nc
    from_dram = x_dram is not None
    s_ld = ctx.sem()
    s_var = ctx.sem()
    s_rstd = ctx.sem()
    s_xn = ctx.sem()
    s_tp = ctx.sem()
    s_sc = ctx.sem()

    ctx.smark()
    xt = ctx.salloc(ctx.uname("ln_x"), [128, 2 * DM], F32) if from_dram else None
    sq = ctx.salloc(ctx.uname("ln_sq"), [128, 16], F32)
    st = ctx.salloc(ctx.uname("ln_st"), [128, 16], F32)
    xnb = ctx.salloc(ctx.uname("ln_xn"), [128, 2 * DM], BF16)
    pst = est.enter_context(nc.psum_tensor(ctx.uname("ln_ps"), [128, 2 * DM], BF16))

    def src(t):
        return (xt[:, (t % 2) * DM:(t % 2 + 1) * DM] if from_dram
                else h_src[:, t * DM:(t + 1) * DM])

    with nc.Block(no_gpsimd_drain=True) as blk:
        if from_dram:
            @blk.sync
            def _(e):
                for t in range(NTILE):
                    if t >= 2:
                        e.wait_ge(s_xn.h, t - 1)
                    o = e.dma_start(out=xt[:, (t % 2) * DM:(t % 2 + 1) * DM],
                                    in_=x_dram[t * 128:(t + 1) * 128, :])
                    inc(o, s_ld, 16)

        @blk.vector
        def _(e):
            for t in range(NTILE):
                p = t % 2
                if from_dram:
                    e.wait_ge(s_ld.h, 16 * (t + 1))
                e.bn_stats(sq[:, 0:6], src(t)[:, 0:512] if False else
                           bass.AP(src(t).tensor, src(t).offset, [src(t).ap[0], [1, 512]]))
                e.bn_stats(sq[:, 6:12],
                           bass.AP(src(t).tensor, src(t).offset + 512, [src(t).ap[0], [1, 512]]))
                e.drain()
                tt = e.bn_aggr(st[:, p * 8:p * 8 + 2], sq[:, 0:12])
                e.drain()
                inc(tt, s_var)
                e.wait_ge(s_rstd.h, t + 1)
                if t >= 2:
                    e.wait_ge(s_tp.h, t - 1)
                o = e.tensor_scalar(xnb[:, p * DM:(p + 1) * DM], src(t),
                                    st[:, p * 8 + 0:p * 8 + 1], st[:, p * 8 + 3:p * 8 + 4],
                                    op0=ALU.subtract, op1=ALU.mult)
                inc(o, s_xn)
                if store_stats is not None:
                    e.tensor_copy(store_stats[:, 2 * t:2 * t + 1], st[:, p * 8:p * 8 + 1])
                    e.tensor_copy(store_stats[:, 2 * t + 1:2 * t + 2],
                                  st[:, p * 8 + 3:p * 8 + 4])
                if t >= 1:
                    e.wait_ge(s_tp.h, t)
                    sc = e.tensor_copy(
                        bass.AP(xnT, (t - 1) * 128,
                                [[8 * TPC, 128], [TPC, 8], [1, 128]]),
                        bass.AP(pst, ((t - 1) % 2) * DM, [[2 * DM, 128], [128, 8], [1, 128]]))
                    inc(sc, s_sc)
            e.wait_ge(s_tp.h, NTILE)
            sc = e.tensor_copy(
                bass.AP(xnT, (NTILE - 1) * 128, [[8 * TPC, 128], [TPC, 8], [1, 128]]),
                bass.AP(pst, ((NTILE - 1) % 2) * DM, [[2 * DM, 128], [128, 8], [1, 128]]))
            inc(sc, s_sc)

        @blk.scalar
        def _(e):
            for t in range(NTILE):
                p = t % 2
                e.wait_ge(s_var.h, t + 1)
                if t >= 1:
                    e.wait_ge(s_xn.h, t)
                e.activation(st[:, p * 8 + 3:p * 8 + 4], st[:, p * 8 + 1:p * 8 + 2],
                             AF.Ln, bias=ctx.cst(1e-5))
                o = e.activation(st[:, p * 8 + 3:p * 8 + 4], st[:, p * 8 + 3:p * 8 + 4],
                                 AF.Exp, scale=ctx.cst(-0.5))
                inc(o, s_rstd)

        @blk.tensor
        def _(e):
            for t in range(NTILE):
                p = t % 2
                e.wait_ge(s_xn.h, t + 1)
                if t >= 2:
                    e.wait_ge(s_sc.h, t - 1)
                for i in range(8):
                    o = e.transpose(pst[:, p * DM + i * 128:p * DM + (i + 1) * 128],
                                    xnb[:, p * DM + i * 128:p * DM + (i + 1) * 128],
                                    ctx.ident[:, :])
                inc(o, s_tp)
    ctx.srel()


def emit_proj_qkvb(ctx, est, xnT, wq, wk, wv, wbeta, q_tm, k_tm, v_tm, beta_sb):
    nc = ctx.nc
    s_mm = ctx.sem()
    s_ev = ctx.sem()
    s_ss = ctx.sem()
    s_rs = ctx.sem()
    s_ap = ctx.sem()

    ctx.smark()
    ps = est.enter_context(nc.psum_tensor("pj_ps", [128, 3 * DM], F32))
    psb = est.enter_context(nc.psum_tensor("pj_psb", [128, 2 * H], F32))
    qf = ctx.salloc("pj_qf", [128, 2 * DM], BF16)
    kf = ctx.salloc("pj_kf", [128, 2 * DM], BF16)
    sq = ctx.salloc("pj_sq", [128, DM], BF16)
    rs = ctx.salloc("pj_rs", [128, 16], F32)

    with nc.Block(no_gpsimd_drain=True) as blk:
        @blk.tensor
        def _(e):
            for t in range(NTILE):
                if t >= 1:
                    e.wait_ge(s_ev.h, 2 * t)
                for w, off in ((wq, 0), (wk, DM), (wv, 2 * DM)):
                    for half in range(2):
                        for dt in range(8):
                            o = e.matmul(ps[:, off + half * 512:off + (half + 1) * 512],
                                         xnT[:, dt * TPC + t * 128:dt * TPC + (t + 1) * 128],
                                         w[:, dt * DM + half * 512:dt * DM + (half + 1) * 512],
                                         start=(dt == 0), stop=(dt == 7))
                inc(o, s_mm)
                for dt in range(8):
                    o = e.matmul(psb[:, (t % 2) * H:(t % 2 + 1) * H],
                                 xnT[:, dt * TPC + t * 128:dt * TPC + (t + 1) * 128],
                                 wbeta[:, dt * H:(dt + 1) * H],
                                 start=(dt == 0), stop=(dt == 7))
                inc(o, s_mm)

        @blk.scalar
        def _(e):
            for t in range(NTILE):
                p = t % 2
                e.wait_ge(s_mm.h, 2 * t + 1)
                if t >= 2:
                    e.wait_ge(s_ap.h, t - 1)
                e.activation(qf[:, p * DM:(p + 1) * DM], ps[:, 0:DM], AF.Copy, bias=0.0)
                e.activation(kf[:, p * DM:(p + 1) * DM], ps[:, DM:2 * DM], AF.Copy, bias=0.0)
                o = e.activation(v_tm[:, t * DM:(t + 1) * DM], ps[:, 2 * DM:3 * DM],
                                 AF.Copy, bias=0.0)
                inc(o, s_ev)
                e.wait_ge(s_mm.h, 2 * t + 2)
                o = e.activation(beta_sb[:, t * H:(t + 1) * H], psb[:, p * H:(p + 1) * H],
                                 AF.Sigmoid)
                inc(o, s_ev)
                e.wait_ge(s_ss.h, t + 1)
                e.activation(rs[:, p * 8:p * 8 + 8], rs[:, p * 8:p * 8 + 8], AF.Ln)
                o = e.activation(rs[:, p * 8:p * 8 + 8], rs[:, p * 8:p * 8 + 8],
                                 AF.Exp, scale=ctx.cst(-0.5))
                inc(o, s_rs)

        @blk.vector
        def _(e):
            for t in range(NTILE):
                p = t % 2
                e.wait_ge(s_ev.h, 2 * t + 1)
                e.tensor_tensor(sq[:, :], qf[:, p * DM:(p + 1) * DM],
                                qf[:, p * DM:(p + 1) * DM], op=ALU.mult)
                e.tensor_reduce(rs[:, p * 8:p * 8 + 4],
                                sq[:, :].rearrange("a (h d) -> a h d", d=DK),
                                axis=AX.X, op=ALU.add)
                e.tensor_tensor(sq[:, :], kf[:, p * DM:(p + 1) * DM],
                                kf[:, p * DM:(p + 1) * DM], op=ALU.mult)
                o = e.tensor_reduce(rs[:, p * 8 + 4:p * 8 + 8],
                                    sq[:, :].rearrange("a (h d) -> a h d", d=DK),
                                    axis=AX.X, op=ALU.add)
                inc(o, s_ss)
                e.wait_ge(s_rs.h, t + 1)
                e.tensor_tensor(q_tm[:, t * DM:(t + 1) * DM].rearrange("a (h d) -> a h d", d=DK),
                                qf[:, p * DM:(p + 1) * DM].rearrange("a (h d) -> a h d", d=DK),
                                bass.AP(rs, p * 8, [[16, 128], [1, H], [0, DK]]),
                                op=ALU.mult)
                o = e.tensor_tensor(k_tm[:, t * DM:(t + 1) * DM].rearrange("a (h d) -> a h d", d=DK),
                                    kf[:, p * DM:(p + 1) * DM].rearrange("a (h d) -> a h d", d=DK),
                                    bass.AP(rs, p * 8 + 4, [[16, 128], [1, H], [0, DK]]),
                                    op=ALU.mult)
                inc(o, s_ap)
    ctx.srel()


def emit_attn_a_delta(ctx, est, q_tm, k_tm, v_tm, beta_sb, o_tm,
                      gin_bp, gout_bp, mask_slo, mask_trilq, idpat, id4,
                      pmask, pimask):
    nc = ctx.nc
    ctx.smark()
    sbf = lambda n, s, d: ctx.salloc(n, s, d)
    psf = lambda n, s, d: est.enter_context(nc.psum_tensor(n, s, d))

    w_fm = sbf("w_fm", [128, NCH_A * H * 256], BF16)
    u_tm = sbf("u_tm", [128, NCH_A * H * 256], BF16)
    s_sb = sbf("s_sb", [128, 2048], BF16)
    kf = sbf("kf", [128, 1024], BF16)
    e_sb = sbf("e_sb", [128, 1024], BF16)
    ctx.smark()
    bk = sbf("bk", [128, 1024], BF16)
    bv = sbf("bv", [128, 1024], BF16)
    a_sb = sbf("a_sb", [128, 512], BF16)
    at_sb = sbf("at_sb", [128, 512], BF16)
    apow = sbf("apow", [128, 7 * 512], BF16)
    m_sb = sbf("m_sb", [128, 2 * 512], BF16)
    tt_sb = sbf("tt_sb", [128, 512], BF16)

    ps_t = psf("ps_t", [128, 1024], BF16)
    ps_a = psf("ps_a", [128, 512], F32)
    ps_w = psf("ps_w", [128, 1024], F32)
    est2 = ExitStack()
    ps_b = est2.enter_context(nc.psum_tensor("ps_b", [128, 512], F32))
    ps_c = est2.enter_context(nc.psum_tensor("ps_c", [128, 512], F32))
    ps_s = None

    def wfm(n, h, dkt):
        return w_fm[:, ((n * H + h) * 2 + dkt) * 128:((n * H + h) * 2 + dkt) * 128 + 128]

    def utm(n, h):
        return u_tm[:, (n * H + h) * 256:(n * H + h) * 256 + 256]

    def ktm(n, h, dkt):
        return k_tm[:, n * DM + h * 256 + dkt * 128:n * DM + h * 256 + dkt * 128 + 128]

    def ssb(dkt, h):
        return s_sb[:, dkt * 1024 + h * 256:dkt * 1024 + h * 256 + 256]

    def ssb_t(dkt):
        return s_sb[:, dkt * 1024:(dkt + 1) * 1024]

    bcast_beta = lambda n: bass.AP(beta_sb, n * H, [[NTILE * H, 128], [1, H], [0, 256]])

    for n in range(NCH_A):
        ph = SerialPhase(ctx)
        ph.v(lambda e, n=n: e.tensor_tensor(
            bk[:, :].rearrange("a (h d) -> a h d", d=256),
            k_tm[:, n * DM:(n + 1) * DM].rearrange("a (h d) -> a h d", d=256),
            bcast_beta(n), op=ALU.mult))
        ph.v(lambda e, n=n: e.tensor_tensor(
            bv[:, :].rearrange("a (h d) -> a h d", d=256),
            v_tm[:, n * DM:(n + 1) * DM].rearrange("a (h d) -> a h d", d=256),
            bcast_beta(n), op=ALU.mult))

        def tp_k(e, n=n):
            for h in range(H):
                for dkt in range(2):
                    o = e.transpose(ps_t[:, (h * 2 + dkt) * 128:(h * 2 + dkt) * 128 + 128],
                                    ktm(n, h, dkt), ctx.ident[:, :])
            return o
        ph.t(tp_k)
        ph.v(lambda e: e.tensor_copy(kf[:, :], ps_t[:, :]))

        def mm_s(e, n=n):
            for h in range(H):
                for dkt in range(2):
                    o = e.matmul(ps_a[:, h * 128:(h + 1) * 128],
                                 kf[:, (h * 2 + dkt) * 128:(h * 2 + dkt) * 128 + 128],
                                 kf[:, (h * 2 + dkt) * 128:(h * 2 + dkt) * 128 + 128],
                                 start=(dkt == 0), stop=(dkt == 1))
            return o
        ph.t(mm_s)

        def ev_a(e, n=n):
            for h in range(H):
                o = e.scalar_tensor_tensor(
                    a_sb[:, h * 128:(h + 1) * 128], ps_a[:, h * 128:(h + 1) * 128],
                    beta_sb[:, n * H + h:n * H + h + 1], mask_slo[:, :],
                    op0=ALU.mult, op1=ALU.mult)
            return o
        ph.v(ev_a)
        ph.v(lambda e: e.tensor_copy(apow[:, 0:512], a_sb[:, :]))

        def tp_a(e):
            for h in range(H):
                o = e.transpose(ps_t[:, h * 128:(h + 1) * 128],
                                a_sb[:, h * 128:(h + 1) * 128], ctx.ident[:, :])
            return o
        ph.t(tp_a)
        ph.v(lambda e: e.tensor_copy(at_sb[:, :], ps_t[:, 0:512]))

        for j in range(1, 7):
            def mm_pow(e, j=j):
                for h in range(H):
                    sl = slice(h * 128, (h + 1) * 128)
                    e.matmul(ps_b[:, sl], at_sb[:, sl],
                             apow[:, (j - 1) * 512:(j - 1) * 512 + 512][:, sl],
                             start=True, stop=True)
                    o = e.matmul(ps_c[:, sl],
                                 apow[:, (j - 1) * 512:(j - 1) * 512 + 512][:, sl],
                                 at_sb[:, sl], start=True, stop=True)
                return o
            ph.t(mm_pow)
            ph.v(lambda e, j=j: e.tensor_copy(apow[:, j * 512:(j + 1) * 512], ps_b[:, :]))
            if j < 6:
                ph.s(lambda e: e.activation(at_sb[:, :], ps_c[:, :], AF.Copy, bias=0.0))
            else:
                ph.v(lambda e: e.tensor_tensor(m_sb[:, 0:512], ps_c[:, :],
                                               id4[:, :], op=ALU.add))

        mcur = 0
        for j in (5, 4, 3, 2, 1):
            def mm_ch(e, j=j, mcur=mcur):
                for h in range(H):
                    sl = slice(h * 128, (h + 1) * 128)
                    e.matmul(ps_a[:, sl], apow[:, j * 512:(j + 1) * 512][:, sl],
                             m_sb[:, mcur * 512:mcur * 512 + 512][:, sl],
                             start=True, stop=False)
                    o = e.matmul(ps_a[:, sl], ctx.ident[:, :],
                                 m_sb[:, mcur * 512:mcur * 512 + 512][:, sl],
                                 start=False, stop=True)
                return o
            ph.t(mm_ch)
            ph.v(lambda e, mcur=mcur: e.tensor_copy(
                m_sb[:, (1 - mcur) * 512:(1 - mcur) * 512 + 512], ps_a[:, :]))
            mcur = 1 - mcur

        def mm_last(e, mcur=mcur):
            for h in range(H):
                sl = slice(h * 128, (h + 1) * 128)
                o = e.matmul(ps_a[:, sl], apow[:, 0:512][:, sl],
                             m_sb[:, mcur * 512:mcur * 512 + 512][:, sl],
                             start=True, stop=True)
            return o
        ph.t(mm_last)
        ph.v(lambda e, mcur=mcur: e.tensor_tensor(
            tt_sb[:, :], m_sb[:, mcur * 512:mcur * 512 + 512], ps_a[:, :],
            op=ALU.subtract))

        def mm_w(e, n=n):
            for h in range(H):
                for dkt in range(2):
                    o = e.matmul(ps_w[:, (h * 2 + dkt) * 128:(h * 2 + dkt) * 128 + 128],
                                 bk[:, h * 256 + dkt * 128:h * 256 + dkt * 128 + 128],
                                 tt_sb[:, h * 128:(h + 1) * 128],
                                 start=True, stop=True)
            return o
        ph.t(mm_w)
        ph.v(lambda e, n=n: e.tensor_copy(
            w_fm[:, n * H * 256:(n + 1) * H * 256], ps_w[:, :]))

        def mm_u(e, n=n):
            for h in range(H):
                o = e.matmul(ps_w[:, h * 256:(h + 1) * 256],
                             tt_sb[:, h * 128:(h + 1) * 128],
                             bv[:, h * 256:(h + 1) * 256],
                             start=True, stop=True)
            return o
        ph.t(mm_u)
        ph.v(lambda e, n=n: e.tensor_copy(
            u_tm[:, n * H * 256:(n + 1) * H * 256], ps_w[:, :]))
        ph.run()

    est2.close()
    ps_s = psf("ps_s", [128, 2048], F32)
    ctx.srel()
    ctx.smark()
    pt_sb = sbf("pt_sb", [128, 2048], BF16)
    ptm = sbf("ptm", [128, 2048], BF16)
    bpin = sbf("bpin", [128, 2048], BF16)
    qf = sbf("qf", [128, 1024], BF16)
    mt_sb = sbf("mt_sb", [128, 512], BF16)
    ctx.dump("w_fm", w_fm[:, :])
    ctx.dump("u_tm", u_tm[:, :])

    btrace = None
    if "btrace" in ctx.dbg:
        btrace = nc.dram_tensor("dbg_btrace", [NCH_A, 128, 2048], BF16,
                                kind="ExternalOutput")
        ctx.dbg_outs["btrace"] = [NCH_A, 128, 2048]
    for pas in ("B", "P"):
        ph = SerialPhase(ctx)
        if pas == "B":
            ph.v(lambda e: e.memset(s_sb[:, :], 0.0))
        else:
            ph.v(lambda e: e.tensor_copy(s_sb[:, :], idpat[:, :]))
        for n in range(NCH_A):
            def mm_e(e, n=n):
                for h in range(H):
                    for dkt in range(2):
                        o = e.matmul(ps_w[:, h * 256:(h + 1) * 256],
                                     wfm(n, h, dkt), ssb(dkt, h),
                                     start=(dkt == 0), stop=(dkt == 1))
                return o
            ph.t(mm_e)
            if pas == "B":
                ph.v(lambda e, n=n: e.tensor_tensor(
                    e_sb[:, :], u_tm[:, n * H * 256:(n + 1) * H * 256],
                    ps_w[:, :], op=ALU.subtract))
            else:
                ph.s(lambda e: e.activation(e_sb[:, :], ps_w[:, :], AF.Copy,
                                            bias=0.0, scale=ctx.cst(-1.0)))

            def mm_su(e, n=n):
                for dkt in range(2):
                    for h in range(H):
                        o = e.matmul(ps_s[:, dkt * 1024 + h * 256:dkt * 1024 + (h) * 256 + 256],
                                     ktm(n, h, dkt), e_sb[:, h * 256:(h + 1) * 256],
                                     start=(n == 0 and h % 2 == 0),
                                     stop=(n == NCH_A - 1))
                return o
            ph.t(mm_su)
            if pas == "B":
                ph.v(lambda e: e.tensor_copy(ssb_t(0), ps_s[:, 0:1024]))
                ph.v(lambda e: e.tensor_copy(ssb_t(1), ps_s[:, 1024:2048]))
            else:
                ph.v(lambda e: e.tensor_tensor(ssb_t(0), ps_s[:, 0:1024],
                                               idpat[:, 0:1024], op=ALU.add))
                ph.v(lambda e: e.tensor_tensor(ssb_t(1), ps_s[:, 1024:2048],
                                               idpat[:, 1024:2048], op=ALU.add))
            if pas == "B" and btrace is not None:
                ph.dma("sync", lambda e, n=n: e.dma_start(out=btrace[n, :, :],
                                                          in_=s_sb[:, :]))
        if pas == "B":
            ph.dma("sync", lambda e: e.dma_start(out=gin_bp[:, 0:2048], in_=s_sb[:, :]))
        else:
            def tp_p(e):
                for dkt in range(2):
                    for h in range(H):
                        for d2 in range(2):
                            o = e.transpose(
                                ps_t[:, ((dkt * H + h) * 2 + d2) % 8 * 128:
                                     ((dkt * H + h) * 2 + d2) % 8 * 128 + 128],
                                s_sb[:, dkt * 1024 + h * 256 + d2 * 128:
                                     dkt * 1024 + h * 256 + d2 * 128 + 128],
                                ctx.ident[:, :])
                return o
            def tp_p1(e):
                for idx, (dkt, h, d2) in enumerate(
                        [(a, b, c) for a in range(2) for b in range(H) for c in range(2)][:8]):
                    o = e.transpose(ps_t[:, idx * 128:idx * 128 + 128],
                                    s_sb[:, dkt * 1024 + h * 256 + d2 * 128:
                                         dkt * 1024 + h * 256 + d2 * 128 + 128],
                                    ctx.ident[:, :])
                return o

            def ev_p1(e):
                for idx, (dkt, h, d2) in enumerate(
                        [(a, b, c) for a in range(2) for b in range(H) for c in range(2)][:8]):
                    o = e.tensor_copy(pt_sb[:, d2 * 1024 + h * 256 + dkt * 128:
                                            d2 * 1024 + h * 256 + dkt * 128 + 128],
                                      ps_t[:, idx * 128:idx * 128 + 128])
                return o

            def tp_p2(e):
                for idx, (dkt, h, d2) in enumerate(
                        [(a, b, c) for a in range(2) for b in range(H) for c in range(2)][8:]):
                    o = e.transpose(ps_t[:, idx * 128:idx * 128 + 128],
                                    s_sb[:, dkt * 1024 + h * 256 + d2 * 128:
                                         dkt * 1024 + h * 256 + d2 * 128 + 128],
                                    ctx.ident[:, :])
                return o

            def ev_p2(e):
                for idx, (dkt, h, d2) in enumerate(
                        [(a, b, c) for a in range(2) for b in range(H) for c in range(2)][8:]):
                    o = e.tensor_copy(pt_sb[:, d2 * 1024 + h * 256 + dkt * 128:
                                            d2 * 1024 + h * 256 + dkt * 128 + 128],
                                      ps_t[:, idx * 128:idx * 128 + 128])
                return o
            ph.t(tp_p1)
            ph.v(ev_p1)
            ph.t(tp_p2)
            ph.v(ev_p2)
            ph.dma("sync", lambda e: e.dma_start(out=gin_bp[:, 2048:4096], in_=pt_sb[:, :]))
        ph.run()

    ph = SerialPhase(ctx)
    ph.g(lambda e: e.collective_compute(
        "AllGather", ALU.bypass, replica_groups=RG,
        ins=[gin_bp.ap().opt()], outs=[gout_bp.ap().opt()]))
    ph.v(lambda e: e.memset(s_sb[:, :], 0.0))
    for i in range(NCORE - 1):
        ph.dma("sync", lambda e, i=i: e.dma_start(out=bpin[:, :],
                                                  in_=gout_bp[i, :, 2048:4096]))
        ph.v(lambda e, i=i: e.tensor_scalar(
            ptm[:, :], bpin[:, :], pmask[:, i:i + 1], None, op0=ALU.mult))
        ph.v(lambda e, i=i: e.scalar_tensor_tensor(
            ptm[:, :], idpat[:, :], pimask[:, i:i + 1], ptm[:, :],
            op0=ALU.mult, op1=ALU.add))
        ph.dma("sync", lambda e, i=i: e.dma_start(out=bpin[:, :],
                                                  in_=gout_bp[i, :, 0:2048]))

        def mm_s0(e, i=i):
            for dkt in range(2):
                for h in range(H):
                    for dkt2 in range(2):
                        o = e.matmul(
                            ps_s[:, dkt * 1024 + h * 256:dkt * 1024 + h * 256 + 256],
                            ptm[:, dkt2 * 1024 + h * 256 + dkt * 128:
                                dkt2 * 1024 + h * 256 + dkt * 128 + 128],
                            ssb(dkt2, h), start=(dkt2 == 0), stop=(dkt2 == 1))
            return o
        ph.t(mm_s0)
        ph.v(lambda e, i=i: e.scalar_tensor_tensor(
            s_sb[:, :], bpin[:, :], pmask[:, i:i + 1], ps_s[:, :],
            op0=ALU.mult, op1=ALU.add))
    ph.run()
    ctx.dump("s0", s_sb[:, :])

    for n in range(NCH_A):
        ph = SerialPhase(ctx)

        def tp_q(e, n=n):
            for h in range(H):
                for dkt in range(2):
                    o = e.transpose(ps_t[:, (h * 2 + dkt) * 128:(h * 2 + dkt) * 128 + 128],
                                    q_tm[:, n * DM + h * 256 + dkt * 128:
                                         n * DM + h * 256 + dkt * 128 + 128],
                                    ctx.ident[:, :])
            return o
        ph.t(tp_q)
        ph.v(lambda e: e.tensor_copy(qf[:, :], ps_t[:, :]))

        def tp_k2(e, n=n):
            for h in range(H):
                for dkt in range(2):
                    o = e.transpose(ps_t[:, (h * 2 + dkt) * 128:(h * 2 + dkt) * 128 + 128],
                                    ktm(n, h, dkt), ctx.ident[:, :])
            return o
        ph.t(tp_k2)
        ph.v(lambda e: e.tensor_copy(kf[:, :], ps_t[:, :]))

        def mm_e(e, n=n):
            for h in range(H):
                for dkt in range(2):
                    o = e.matmul(ps_w[:, h * 256:(h + 1) * 256],
                                 wfm(n, h, dkt), ssb(dkt, h),
                                 start=(dkt == 0), stop=(dkt == 1))
            return o
        ph.t(mm_e)
        ph.v(lambda e, n=n: e.tensor_tensor(
            e_sb[:, :], u_tm[:, n * H * 256:(n + 1) * H * 256], ps_w[:, :],
            op=ALU.subtract))

        def mm_mt(e):
            for h in range(H):
                for dkt in range(2):
                    o = e.matmul(ps_a[:, h * 128:(h + 1) * 128],
                                 kf[:, (h * 2 + dkt) * 128:(h * 2 + dkt) * 128 + 128],
                                 qf[:, (h * 2 + dkt) * 128:(h * 2 + dkt) * 128 + 128],
                                 start=(dkt == 0), stop=(dkt == 1))
            return o
        ph.t(mm_mt)
        ph.v(lambda e: e.tensor_tensor(
            mt_sb[:, :].rearrange("a (h c) -> a h c", c=128),
            ps_a[:, :].rearrange("a (h c) -> a h c", c=128),
            bass.AP(mask_trilq, 0, [[128, 128], [0, H], [1, 128]]),
            op=ALU.mult))

        def mm_o(e, n=n):
            for h in range(H):
                for dkt in range(2):
                    e.matmul(ps_w[:, h * 256:(h + 1) * 256],
                             qf[:, (h * 2 + dkt) * 128:(h * 2 + dkt) * 128 + 128],
                             ssb(dkt, h), start=(dkt == 0), stop=False)
                o = e.matmul(ps_w[:, h * 256:(h + 1) * 256],
                             mt_sb[:, h * 128:(h + 1) * 128],
                             e_sb[:, h * 256:(h + 1) * 256], start=False, stop=True)
            return o
        ph.t(mm_o)
        ph.v(lambda e, n=n: e.tensor_copy(o_tm[:, n * DM:(n + 1) * DM], ps_w[:, :]))

        def mm_su(e, n=n):
            for dkt in range(2):
                for h in range(H):
                    if n == 0:
                        e.matmul(ps_s[:, dkt * 1024 + h * 256:dkt * 1024 + h * 256 + 256],
                                 ctx.ident[:, :], ssb(dkt, h),
                                 start=(h % 2 == 0), stop=False)
                    o = e.matmul(ps_s[:, dkt * 1024 + h * 256:dkt * 1024 + h * 256 + 256],
                                 ktm(n, h, dkt), e_sb[:, h * 256:(h + 1) * 256],
                                 start=False, stop=(n == NCH_A - 1))
            return o
        ph.t(mm_su)
        ph.s(lambda e: e.activation(ssb_t(0), ps_s[:, 0:1024], AF.Copy, bias=0.0))
        ph.s(lambda e: e.activation(ssb_t(1), ps_s[:, 1024:2048], AF.Copy, bias=0.0))
        ph.run()
    ctx.srel()
    ctx.srel()
    ctx.dump("o_tm", o_tm[:, :])
    ctx.dump_dram("gin", gin_bp)


def emit_attn_b(ctx, est, h_in_d, h_out_d, w, dr):
    nc = ctx.nc
    ctx.smark()
    SW = 512
    NSG = TPC // SW
    CPS = SW // CB

    s_f = ctx.salloc("b_sf", [64, DI], F32)
    s_bf = ctx.salloc("b_sbf", [64, DI], BF16)
    qb_all = ctx.salloc("b_qball", [64, TPC], BF16)
    btot_all = ctx.salloc("b_btot", [64, NCH_B], F32)
    stats = ctx.salloc("b_stats", [128, 2 * NTILE], F32)
    carry = ctx.salloc("b_carry", [128, 16 * 3], BF16)
    cumg = ctx.salloc("b_cumg", [64, NCH_B + 8], F32)

    xn3 = ctx.salloc("b_xn3", [128, 2 * DM], BF16)

    ctx.smark()
    w["upT"] = ctx.salloc("b_upT", [128, 8 * DI], BF16)
    sup = ctx.sem()
    with nc.Block(no_gpsimd_drain=True) as blk:
        @blk.gpsimd
        def _(e):
            o = e.dma_start(out=w["upT"][:, :],
                            in_=bass.AP(dr["upT_d"], 0,
                                        [[DI, 128], [128 * DI, 8], [1, DI]]))
            inc(o, sup, 16)
            e.wait_ge(sup.h, sup.v)
    ln_tmp = ctx.salloc("b_lntmp", [128, DM], F32)
    bnst = ctx.salloc("b_bnst", [128, 16], F32)
    hg = ctx.salloc("b_hg", [128, 8 * 48], F32)

    est_ps = ExitStack()

    def ln_tile_emit(ph, t, dst_fm_tile_fn):
        p = t % 2
        ph.dma("sync", lambda e, t=t, p=p: e.dma_start(
            out=ln_tmp[:, 0:DM],
            in_=h_in_d[t * 128:(t + 1) * 128, :]))

        def stats_f(e, t=t, p=p):
            e.bn_stats(bnst[:, 0:6],
                       bass.AP(ln_tmp, 0, [[DM, 128], [1, 512]]))
            e.bn_stats(bnst[:, 6:12],
                       bass.AP(ln_tmp, 512, [[DM, 128], [1, 512]]))
            e.drain()
            o = e.bn_aggr(stats[:, 2 * t:2 * t + 2], bnst[:, 0:12])
            e.drain()
            return o
        ph.v(stats_f)
        ph.s(lambda e, t=t: e.activation(stats[:, 2 * t + 1:2 * t + 2],
                                         stats[:, 2 * t + 1:2 * t + 2],
                                         AF.Ln, bias=ctx.cst(1e-5)))
        ph.s(lambda e, t=t: e.activation(stats[:, 2 * t + 1:2 * t + 2],
                                         stats[:, 2 * t + 1:2 * t + 2],
                                         AF.Exp, scale=ctx.cst(-0.5)))
        ph.v(lambda e, t=t, p=p: e.tensor_scalar(
            xn3[:, p * DM:(p + 1) * DM], ln_tmp[:, 0:DM],
            stats[:, 2 * t:2 * t + 1], stats[:, 2 * t + 1:2 * t + 2],
            op0=ALU.subtract, op1=ALU.mult))

    ps_h = est_ps.enter_context(nc.psum_tensor("b_psh", [128, 2 * DM], BF16))
    ps_hf = est_ps.enter_context(nc.psum_tensor("b_pshf", [128, 64], F32))
    ph = SerialPhase(ctx)
    ph.dma("sync", lambda e: e.dma_start(out=ln_tmp[0:3, 0:DM],
                                         in_=h_in_d[TPC - 3:TPC, :]))

    def h_stats(e):
        e.bn_stats(bnst[0:3, 0:6], bass.AP(ln_tmp, 0, [[DM, 3], [1, 512]]))
        e.bn_stats(bnst[0:3, 6:12], bass.AP(ln_tmp, 512, [[DM, 3], [1, 512]]))
        e.drain()
        o = e.bn_aggr(bnst[0:3, 12:14], bnst[0:3, 0:12])
        e.drain()
        return o
    ph.v(h_stats)
    ph.s(lambda e: e.activation(bnst[:, 13:14], bnst[:, 13:14], AF.Ln,
                                bias=ctx.cst(1e-5)))
    ph.s(lambda e: e.activation(bnst[:, 13:14], bnst[:, 13:14], AF.Exp,
                                scale=ctx.cst(-0.5)))
    ph.v(lambda e: e.tensor_scalar(
        xn3[0:3, 0:DM], ln_tmp[0:3, 0:DM], bnst[0:3, 12:13], bnst[0:3, 13:14],
        op0=ALU.subtract, op1=ALU.mult))

    def h_tp(e):
        for i in range(8):
            o = e.transpose(ps_h[:, i * 128:i * 128 + 3],
                            xn3[0:3, i * 128:(i + 1) * 128], ctx.ident[0:3, 0:3])
        return o
    ph.t(h_tp)
    ph.v(lambda e: e.tensor_copy(xn3[:, DM:DM + 8 * 3].rearrange(
        "a (d c) -> a d c", c=3),
        bass.AP(ps_h, 0, [[2 * DM, 128], [128, 8], [1, 3]])))

    def h_mm(e):
        for cht in range(16):
            for dt in range(8):
                o = e.matmul(ps_hf[:, cht * 3:cht * 3 + 3],
                             w["upT"][:, dt * DI + cht * 128:dt * DI + cht * 128 + 128],
                             xn3[:, DM + dt * 3:DM + dt * 3 + 3],
                             start=(dt == 0), stop=(dt == 7))
        return o
    ph.t(h_mm)
    ph.v(lambda e: e.tensor_copy(hg[:, 0:48], ps_hf[:, 0:48]))
    ph.dma("sync", lambda e: e.dma_start(out=dr["gin_h"].ap(), in_=hg[:, 0:48]))
    ph.g(lambda e: e.collective_compute(
        "AllGather", ALU.bypass, replica_groups=RG,
        ins=[dr["gin_h"].ap().opt()], outs=[dr["gout_h"].ap().opt()]))
    ph.dma("sync", lambda e: e.dma_start(
        out=hg[:, :], in_=bass.AP(dr["gout_h"], 0, [[48, 128], [128 * 48, 8], [1, 48]])))

    def h_sel(e):
        e.memset(carry[:, :], 0.0)
        e.memset(s_f[:, :], 0.0)
        e.memset(s_bf[:, :], 0.0)
        e.drain()
        for i in range(8):
            o = e.scalar_tensor_tensor(carry[:, :], hg[:, i * 48:(i + 1) * 48],
                                       w["hmask"][:, i:i + 1], carry[:, :],
                                       op0=ALU.mult, op1=ALU.add)
            e.drain()
        return o
    ph.v(h_sel)
    ph.run()
    est_ps.close()
    ctx.dump("carry", carry[:, :])

    xnT_s = ctx.salloc("b_xnTs", [128, 8 * SW], BF16)
    xup = ctx.salloc("b_xup", [128, 16 * SW], BF16)
    shf = ctx.salloc("b_shf", [128, 16 * SW], BF16)
    vfm = ctx.salloc("b_vfm", [128, 16 * SW], BF16)
    sh_tm = ctx.salloc("b_shtm", [128, 2 * DI], BF16)
    xcat = ctx.salloc("b_xcat", [128, 2 * (SW + 3)], BF16)
    acc_a = ctx.salloc("b_acca", [128, 2 * SW], F32)
    acc_b = ctx.salloc("b_accb", [128, 2 * SW], F32)
    qk_t = ctx.salloc("b_qkt", [128, SW], BF16)
    sg_t = ctx.salloc("b_sgt", [128, 2 * SW], BF16)
    sc_b2 = ctx.salloc("b_scb2", [64, SW], F32)
    sc_c2 = ctx.salloc("b_scc2", [64, SW], F32)
    gt_t = ctx.salloc("b_gtt", [128, SW], F32)
    k_sb = ctx.salloc("b_ksb", [64, SW], F32)
    tau_s = ctx.salloc("b_taus", [64, SW], F32)
    gl_s = ctx.salloc("b_gls", [64, SW], F32)
    it_s = ctx.salloc("b_its", [64, SW], F32)
    rt_s = ctx.salloc("b_rts", [64, SW], F32)
    b_s = ctx.salloc("b_bs", [64, SW], F32)
    eb_s = ctx.salloc("b_ebs", [64, SW], BF16)
    sc_a = ctx.salloc("b_sca", [64, SW], F32)
    qs_s = ctx.salloc("b_qss", [64, SW], BF16)
    ks_s = ctx.salloc("b_kss", [64, SW], BF16)
    kd_s = ctx.salloc("b_kds", [64, SW], BF16)
    kit = ctx.salloc("b_kit", [64, SW], BF16)
    r16 = ctx.salloc("b_r16", [16, SW], BF16)
    seg_sc = ctx.salloc("b_segsc", [64, 24], F32)
    rr_f = ctx.salloc("b_rrf", [1, SW], F32)
    r_row = ctx.salloc("b_rrow", [1, SW], BF16)
    k2b = ctx.salloc("b_k2b", [64, SW], BF16)
    r_col = ctx.salloc("b_rcol", [64, 8], F32)
    v_tm = ctx.salloc("b_vtm", [64, 2 * DI], BF16)
    kdt = ctx.salloc("b_kdt", [64, 2 * 64], BF16)
    a_p = ctx.salloc("b_ap", [64, 2 * 64], BF16)
    o_ch = ctx.salloc("b_och", [64, DI], BF16)
    onescol = ctx.salloc("b_ones", [64, 1], BF16)

    est_p1 = ExitStack()
    ps_q = est_p1.enter_context(nc.psum_tensor("b_psq", [128, 1024], F32))
    ps_t = est_p1.enter_context(nc.psum_tensor("b_pst", [128, 2048], BF16))
    ps_o = est_p1.enter_context(nc.psum_tensor("b_pso", [64, 1024], F32))
    ps_ds = est_p1.enter_context(nc.psum_tensor("b_psds", [64, 1024], F32))
    ps_x = ps_q

    for sg in range(NSG):
        ph = Pipe(ctx)
        ph.v(lambda e: e.memset(onescol[:, :], 1.0), deps="all")
        Hd = {}; Hst = {}; Hrs = {}; Hxn = {}; Htp = {}; Hsc = {}
        for tt_ in range(4):
            t = sg * 4 + tt_
            p2 = tt_ % 2
            Hd[tt_] = ph.dma("sync", lambda e, t=t: e.dma_start(
                out=ln_tmp[:, 0:DM],
                in_=h_in_d[t * 128:(t + 1) * 128, :]), deps=[Hxn.get(tt_ - 1)])

            def stats_f(e, t=t):
                e.bn_stats(bnst[:, 0:6],
                           bass.AP(ln_tmp, 0, [[DM, 128], [1, 512]]))
                e.bn_stats(bnst[:, 6:12],
                           bass.AP(ln_tmp, 512, [[DM, 128], [1, 512]]))
                e.drain()
                o = e.bn_aggr(stats[:, 2 * t:2 * t + 2], bnst[:, 0:12])
                e.drain()
                return o
            Hst[tt_] = ph.v(stats_f, deps=[Hd[tt_], Hrs.get(tt_ - 1)])

            def rs_f(e, t=t):
                e.activation(stats[:, 2 * t + 1:2 * t + 2],
                             stats[:, 2 * t + 1:2 * t + 2], AF.Ln, bias=ctx.cst(1e-5))
                return e.activation(stats[:, 2 * t + 1:2 * t + 2],
                                    stats[:, 2 * t + 1:2 * t + 2], AF.Exp,
                                    scale=ctx.cst(-0.5))
            Hrs[tt_] = ph.s(rs_f, deps=[Hst[tt_]])
            Hxn[tt_] = ph.v(lambda e, t=t, p2=p2: e.tensor_scalar(
                xn3[:, p2 * DM:(p2 + 1) * DM], ln_tmp[:, 0:DM],
                stats[:, 2 * t:2 * t + 1], stats[:, 2 * t + 1:2 * t + 2],
                op0=ALU.subtract, op1=ALU.mult),
                deps=[Hrs[tt_], Htp.get(tt_ - 2)])

            def tp_x(e, p2=p2):
                for i in range(8):
                    o = e.transpose(ps_t[:, p2 * 1024 + i * 128:p2 * 1024 + (i + 1) * 128],
                                    xn3[:, p2 * DM + i * 128:p2 * DM + (i + 1) * 128],
                                    ctx.ident[:, :])
                return o
            Htp[tt_] = ph.t(tp_x, deps=[Hxn[tt_], Hsc.get(tt_ - 2)])
            Hsc[tt_] = ph.v(lambda e, tt_=tt_, p2=p2: e.tensor_copy(
                bass.AP(xnT_s, tt_ * 128, [[8 * SW, 128], [SW, 8], [1, 128]]),
                bass.AP(ps_t, p2 * 1024, [[2048, 128], [128, 8], [1, 128]])),
                deps=[Htp[tt_]])

        HmmU = {}; HevX = {}
        for cht in range(16):
            pq = cht % 2

            def mm_up(e, cht=cht, pq=pq):
                for dt in range(8):
                    o = e.matmul(ps_q[:, pq * 512:(pq + 1) * 512],
                                 w["upT"][:, dt * DI + cht * 128:dt * DI + cht * 128 + 128],
                                 xnT_s[:, dt * SW:(dt + 1) * SW],
                                 start=(dt == 0), stop=(dt == 7))
                return o
            HmmU[cht] = ph.t(mm_up, deps=[Hsc[3], HevX.get(cht - 2)])
            HevX[cht] = ph.s(lambda e, cht=cht, pq=pq: e.activation(
                xup[:, cht * SW:(cht + 1) * SW], ps_q[:, pq * 512:(pq + 1) * 512],
                AF.Copy, bias=0.0), deps=[HmmU[cht]])

        Hsil = {}; Ht4 = {}
        for cht in range(16):
            px = cht % 2

            def cpy(e, cht=cht, px=px):
                e.tensor_copy(xcat[:, px * (SW + 3):px * (SW + 3) + 3],
                              carry[:, cht * 3:(cht + 1) * 3])
                e.drain()
                return e.tensor_copy(xcat[:, px * (SW + 3) + 3:(px + 1) * (SW + 3)],
                                     xup[:, cht * SW:(cht + 1) * SW])
            c1 = ph.v(cpy, deps=[HevX[cht], Ht4.get(cht - 2)])

            def taps(e, cht=cht, px=px):
                base = px * (SW + 3)
                pa = px * SW
                e.tensor_scalar(acc_a[:, pa:pa + SW], xcat[:, base:base + SW],
                                w["cwcol"][:, cht * 4:cht * 4 + 1], None, op0=ALU.mult)
                e.drain()
                e.scalar_tensor_tensor(acc_b[:, pa:pa + SW], xcat[:, base + 1:base + 1 + SW],
                                       w["cwcol"][:, cht * 4 + 1:cht * 4 + 2],
                                       acc_a[:, pa:pa + SW], op0=ALU.mult, op1=ALU.add)
                e.drain()
                e.scalar_tensor_tensor(acc_a[:, pa:pa + SW], xcat[:, base + 2:base + 2 + SW],
                                       w["cwcol"][:, cht * 4 + 2:cht * 4 + 3],
                                       acc_b[:, pa:pa + SW], op0=ALU.mult, op1=ALU.add)
                e.drain()
                return e.scalar_tensor_tensor(acc_b[:, pa:pa + SW],
                                              xcat[:, base + 3:base + 3 + SW],
                                              w["cwcol"][:, cht * 4 + 3:cht * 4 + 4],
                                              acc_a[:, pa:pa + SW], op0=ALU.mult, op1=ALU.add)
            Ht4[cht] = ph.v(taps, deps=[Hsil.get(cht - 2)])
            Hsil[cht] = ph.s(lambda e, cht=cht, px=px: e.activation(
                shf[:, cht * SW:(cht + 1) * SW], acc_b[:, px * SW:(px + 1) * SW],
                AF.Silu, bias=w["cbcol"][:, cht:cht + 1]), deps=[Ht4[cht]])
            if sg < NSG - 1:
                ph.v(lambda e, cht=cht: e.tensor_copy(
                    carry[:, cht * 3:(cht + 1) * 3],
                    xup[:, (cht + 1) * SW - 3:(cht + 1) * SW]), deps=[HevX[cht]])

        Hr1 = {}
        for cht in range(16):
            Hr1[cht] = ph.t(lambda e, cht=cht: e.matmul(
                ps_q[0:16, 0:512], w["ig1T"][:, cht * 16:(cht + 1) * 16],
                xup[:, cht * SW:(cht + 1) * SW],
                start=(cht == 0), stop=(cht == 15)),
                deps=[HevX[15]] if cht == 0 else [])
        HevR = ph.v(lambda e: e.tensor_copy(r16[:, :], ps_q[0:16, 0:512]),
                    deps=[Hr1[15]])
        Hsig = {}; Hvml = {}
        for cht in range(16):
            pq = cht % 2
            r2 = ph.t(lambda e, cht=cht, pq=pq: e.matmul(
                ps_q[:, pq * 512:(pq + 1) * 512],
                w["ig2T"][:, cht * 128:(cht + 1) * 128],
                r16[:, :], start=True, stop=True),
                deps=[HevR, Hsig.get(cht - 2)])
            Hsig[cht] = ph.s(lambda e, cht=cht, pq=pq: e.activation(
                sg_t[:, pq * SW:(pq + 1) * SW], ps_q[:, pq * 512:(pq + 1) * 512],
                AF.Sigmoid, bias=w["igb2col"][:, cht:cht + 1]), deps=[r2])
            Hvml[cht] = ph.v(lambda e, cht=cht, pq=pq: e.tensor_tensor(
                vfm[:, cht * SW:(cht + 1) * SW], sg_t[:, pq * SW:(pq + 1) * SW],
                xup[:, cht * SW:(cht + 1) * SW], op=ALU.mult), deps=[Hsig[cht]])

        HevSp = {}; HdmaSp = {}
        for tt_ in range(4):
            def tp_s(e, tt_=tt_):
                for i in range(16):
                    o = e.transpose(ps_t[:, i * 128:(i + 1) * 128],
                                    shf[:, i * SW + tt_ * 128:i * SW + tt_ * 128 + 128],
                                    ctx.ident[:, :])
                return o
            tsp = ph.t(tp_s, deps=[Hsil[15], HevSp.get(tt_ - 1)])
            HevSp[tt_] = ph.v(lambda e, tt_=tt_: e.tensor_copy(
                sh_tm[:, (tt_ % 2) * DI:(tt_ % 2 + 1) * DI], ps_t[:, :]),
                deps=[tsp, HdmaSp.get(tt_ - 2)])
            HdmaSp[tt_] = ph.dma("sync", lambda e, tt_=tt_, sg=sg: e.dma_start(
                out=dr["sh_d"][(sg * 4 + tt_) * 128:(sg * 4 + tt_ + 1) * 128, :],
                in_=sh_tm[:, (tt_ % 2) * DI:(tt_ % 2 + 1) * DI]), deps=[HevSp[tt_]])

        Hqk = {}
        for cht in range(16):
            def mm_qk(e, cht=cht):
                for of in range(2):
                    o = e.matmul(ps_q[:, of * 512:(of + 1) * 512],
                                 w["qkgtT"][:, cht * 256 + of * 128:cht * 256 + of * 128 + 128],
                                 shf[:, cht * SW:(cht + 1) * SW],
                                 start=(cht == 0), stop=(cht == 15))
                return o
            Hqk[cht] = ph.t(mm_qk, deps=[Hsig[15], Hsil[cht]] if cht == 0 else [Hsil[cht]])
        HevQK = ph.v(lambda e: e.tensor_copy(qk_t[:, :], ps_q[:, 0:512]), deps=[Hqk[15]])
        HevGT = ph.s(lambda e: e.activation(gt_t[:, :], ps_q[:, 512:1024], AF.Copy,
                                            bias=0.0), deps=[Hqk[15]])
        HdK = ph.dma("gpsimd", lambda e: e.dma_start(out=k_sb[:, :], in_=qk_t[64:128, :]),
                     deps=[HevQK])

        def sp1(e):
            e.activation(sc_a[:, :], gt_t[0:64, :], AF.Abs, bias=w["gbc"][0:64, 0:1])
            e.activation(sc_a[:, :], sc_a[:, :], AF.Exp, scale=ctx.cst64(-1.0))
            e.activation(sc_a[:, :], sc_a[:, :], AF.Ln, bias=ctx.cst64(1.0))
            return e.activation(gl_s[:, :], gt_t[0:64, :], AF.Relu,
                                bias=w["gbc"][0:64, 0:1])
        h1_ = ph.s(sp1, deps=[HevGT])
        h2_ = ph.v(lambda e: e.tensor_tensor(gl_s[:, :], gl_s[:, :], sc_a[:, :],
                                             op=ALU.add), deps=[h1_])
        HdT = ph.dma("gpsimd", lambda e: e.dma_start(out=tau_s[:, :], in_=gt_t[64:128, :]),
                     deps=[HevGT])
        h3_ = ph.s(lambda e: e.activation(tau_s[:, :], tau_s[:, :], AF.Sigmoid,
                                          bias=w["tbc"][0:64, 0:1]), deps=[HdT])
        h4_ = ph.s(lambda e: e.activation(it_s[:, :], gl_s[:, :], AF.Ln),
                   deps=[h2_, h3_])
        h5_ = ph.v(lambda e: e.tensor_tensor(it_s[:, :], it_s[:, :], tau_s[:, :],
                                             op=ALU.mult), deps=[h4_])
        h6_ = ph.s(lambda e: e.activation(it_s[:, :], it_s[:, :], AF.Exp), deps=[h5_])
        h7_ = ph.v(lambda e: e.scalar_tensor_tensor(
            rt_s[:, :], gl_s[:, :], ctx.cst64(-1.0), tau_s[:, :],
            op0=ALU.mult, op1=ALU.mult), deps=[h3_])
        h8_ = ph.v(lambda e: e.tensor_tensor_scan(
            b_s[:, :], w["smask"][0:64, :], rt_s[:, :], 0.0,
            op0=ALU.mult, op1=ALU.add), deps=[h7_])

        def extr(e):
            e.drain()
            e.tensor_copy(seg_sc[:, 0:8], bass.AP(b_s, 63, [[SW, 64], [64, 8]]))
            e.drain()
            return e.tensor_copy(seg_sc[:, 8:16], bass.AP(b_s, 31, [[SW, 64], [64, 8]]))
        h9_ = ph.v(extr, deps=[h8_])
        h10 = ph.s(lambda e: e.activation(seg_sc[:, 16:24], seg_sc[:, 0:8], AF.Exp),
                   deps=[h9_])
        hbt = ph.v(lambda e, sg=sg: e.tensor_copy(btot_all[:, sg * 8:(sg + 1) * 8],
                                                  seg_sc[:, 0:8]), deps=[h9_])
        heb = ph.s(lambda e: e.activation(eb_s[:, :], b_s[:, :], AF.Exp), deps=[h8_])
        hqb = ph.v(lambda e, sg=sg: e.scalar_tensor_tensor(
            qb_all[:, sg * SW:(sg + 1) * SW], qk_t[0:64, :], ctx.cst64(0.125),
            eb_s[:, :], op0=ALU.mult, op1=ALU.mult), deps=[heb, HevQK])

        def qs_f(e):
            e.tensor_tensor(
                sc_a[:, :].rearrange("a (c x) -> a c x", x=CB),
                b_s[:, :].rearrange("a (c x) -> a c x", x=CB),
                bass.AP(seg_sc, 8, [[24, 64], [1, 8], [0, CB]]), op=ALU.subtract)
            return e.nop()
        hq1 = ph.v(qs_f, deps=[h9_])
        hq2 = ph.s(lambda e: e.activation(sc_a[:, :], sc_a[:, :], AF.Exp), deps=[hq1])
        hq3 = ph.v(lambda e: e.scalar_tensor_tensor(
            qs_s[:, :], qk_t[0:64, :], ctx.cst64(0.125), sc_a[:, :],
            op0=ALU.mult, op1=ALU.mult), deps=[hq2])
        hkit = ph.v(lambda e: e.tensor_tensor(kit[:, :], k_sb[:, :], it_s[:, :],
                                              op=ALU.mult), deps=[HdK, h6_])

        def ks_f(e):
            e.tensor_tensor(
                sc_b2[:, :].rearrange("a (c x) -> a c x", x=CB),
                bass.AP(seg_sc, 8, [[24, 64], [1, 8], [0, CB]]),
                b_s[:, :].rearrange("a (c x) -> a c x", x=CB), op=ALU.subtract)
            return e.nop()
        hk1 = ph.v(ks_f, deps=[h9_])
        hk2 = ph.s(lambda e: e.activation(sc_b2[:, :], sc_b2[:, :], AF.Exp), deps=[hk1])
        hk3 = ph.v(lambda e: e.tensor_tensor(ks_s[:, :], kit[:, :], sc_b2[:, :],
                                             op=ALU.mult), deps=[hk2, hkit])

        def kd_f(e):
            e.tensor_tensor(
                sc_c2[:, :].rearrange("a (c x) -> a c x", x=CB),
                bass.AP(seg_sc, 0, [[24, 64], [1, 8], [0, CB]]),
                b_s[:, :].rearrange("a (c x) -> a c x", x=CB), op=ALU.subtract)
            return e.nop()
        hd1 = ph.v(kd_f, deps=[h9_])
        hd2 = ph.s(lambda e: e.activation(sc_c2[:, :], sc_c2[:, :], AF.Exp), deps=[hd1])
        hd3 = ph.v(lambda e: e.tensor_tensor(kd_s[:, :], kit[:, :], sc_c2[:, :],
                                             op=ALU.mult), deps=[hd2, hkit])

        hr1 = ph.v(lambda e: e.tensor_tensor(k2b[:, :], k_sb[:, :], k_sb[:, :],
                                             op=ALU.mult), deps=[HdK])
        hr2 = ph.t(lambda e: e.matmul(ps_o[0:1, 0:512], onescol[:, :], k2b[:, :],
                                      start=True, stop=True), deps=[hr1])
        hr3 = ph.v(lambda e: e.tensor_scalar(rr_f[:, :], ps_o[0:1, 0:512], 1e-24,
                                             None, op0=ALU.max), deps=[hr2])

        def rex(e):
            e.activation(rr_f[:, :], rr_f[:, :], AF.Ln)
            return e.activation(r_row[:, :], rr_f[:, :], AF.Exp,
                                scale=ctx.cvec[0:1, CIDX[-0.5]:CIDX[-0.5] + 1])
        hr4 = ph.s(rex, deps=[hr3])

        def tp_r(e):
            for cc in range(8):
                o = e.transpose(ps_t[0:64, 2 * cc:2 * cc + 1],
                                r_row[:, cc * 64:(cc + 1) * 64], ctx.ident[0:1, 0:1])
            return o
        hr5 = ph.t(tp_r, deps=[hr4, HdmaSp.get(3)])
        hr6 = ph.v(lambda e: e.tensor_copy(r_col[:, :],
                                           bass.AP(ps_t, 0, [[2048, 64], [2, 8]])),
                   deps=[hr5])

        Htpv = {}; HevV = {}; HevK2 = {}; HevA = {}; Homm = {}; Hdsm = {}
        HevO = {}; Hstt = {}; Hsbf = {}; HdmaO = {}
        for cc in range(CPS):
            n = sg * CPS + cc
            pv = cc % 2

            def tp_v(e, cc=cc, pv=pv):
                for i in range(16):
                    o = e.transpose(ps_t[0:64, i * 128:(i + 1) * 128],
                                    vfm[:, i * SW + cc * CB:i * SW + (cc + 1) * CB],
                                    ctx.ident[:, :])
                return o
            Htpv[cc] = ph.t(tp_v, deps=[Hvml[15] if cc == 0 else None,
                                        HevK2.get(cc - 1), hr6 if cc == 0 else None])
            HevV[cc] = ph.v(lambda e, pv=pv: e.tensor_copy(
                v_tm[:, pv * DI:(pv + 1) * DI], ps_t[0:64, :]),
                deps=[Htpv[cc], Homm.get((cc - 2, 1)), Hdsm.get((cc - 2, 1))])
            tpk = ph.t(lambda e, cc=cc: e.transpose(
                ps_t[0:64, 0:64], kd_s[:, cc * CB:(cc + 1) * CB],
                ctx.ident[0:64, 0:64]),
                deps=[hd3 if cc == 0 else None, HevV[cc]])
            HevK2[cc] = ph.v(lambda e, cc=cc, pv=pv: e.tensor_scalar(
                kdt[:, pv * 64:(pv + 1) * 64], ps_t[0:64, 0:64],
                r_col[:, cc:cc + 1], None, op0=ALU.mult),
                deps=[tpk, hr6 if cc == 0 else None, Hdsm.get((cc - 2, 1))])
            amm = ph.t(lambda e, cc=cc: e.matmul(
                ps_ds[:, 0:64], ks_s[:, cc * CB:(cc + 1) * CB],
                qs_s[:, cc * CB:(cc + 1) * CB], start=True, stop=True),
                deps=[hq3 if cc == 0 else None, hk3 if cc == 0 else None,
                      Hstt.get((cc - 1, 1))])
            HevA[cc] = ph.v(lambda e, cc=cc, pv=pv: e.scalar_tensor_tensor(
                a_p[:, pv * 64:(pv + 1) * 64], ps_ds[:, 0:64], r_col[:, cc:cc + 1],
                w["m64"], op0=ALU.mult, op1=ALU.mult),
                deps=[amm, Homm.get((cc - 1, 1))])
            for vh in range(2):
                def mm_o(e, cc=cc, vh=vh, pv=pv, sg=sg):
                    for j in range(2):
                        e.matmul(ps_o[:, j * 512:(j + 1) * 512],
                                 qb_all[:, sg * SW + cc * CB:sg * SW + (cc + 1) * CB],
                                 s_bf[:, vh * 1024 + j * 512:vh * 1024 + (j + 1) * 512],
                                 start=True, stop=False)
                        o = e.matmul(ps_o[:, j * 512:(j + 1) * 512],
                                     a_p[:, pv * 64:(pv + 1) * 64],
                                     v_tm[:, pv * DI + vh * 1024:pv * DI + (vh + 1) * 1024]
                                     [:, j * 512:(j + 1) * 512],
                                     start=False, stop=True)
                    return o
                Homm[(cc, vh)] = ph.t(mm_o, deps=[
                    HevA[cc], HevV[cc], hqb if cc == 0 else None,
                    Hsbf.get((cc - 1, vh)), HevK2[cc] if vh == 0 else None,
                    HevO.get((cc, 0)) if vh == 1 else HevO.get((cc - 1, 1))])

                def mm_ds(e, cc=cc, vh=vh, pv=pv):
                    for j in range(2):
                        o = e.matmul(ps_ds[:, j * 512:(j + 1) * 512],
                                     kdt[:, pv * 64:(pv + 1) * 64],
                                     v_tm[:, pv * DI + vh * 1024:pv * DI + (vh + 1) * 1024]
                                     [:, j * 512:(j + 1) * 512],
                                     start=True, stop=True)
                    return o
                Hdsm[(cc, vh)] = ph.t(mm_ds, deps=[
                    HevK2[cc], HevV[cc],
                    HevA[cc] if vh == 0 else Hstt.get((cc, 0))])
                HevO[(cc, vh)] = ph.v(lambda e, vh=vh: e.tensor_copy(
                    o_ch[:, vh * 1024:(vh + 1) * 1024],
                    ps_o[:, :]), deps=[Homm[(cc, vh)], HdmaO.get(cc - 1)])
                Hstt[(cc, vh)] = ph.v(lambda e, cc=cc, vh=vh: e.scalar_tensor_tensor(
                    s_f[:, vh * 1024:(vh + 1) * 1024],
                    s_f[:, vh * 1024:(vh + 1) * 1024],
                    seg_sc[:, 16 + cc:17 + cc], ps_ds[:, :],
                    op0=ALU.mult, op1=ALU.add), deps=[Hdsm[(cc, vh)], h10 if cc == 0 else None])
                Hsbf[(cc, vh)] = ph.s(lambda e, vh=vh: e.activation(
                    s_bf[:, vh * 1024:(vh + 1) * 1024],
                    s_f[:, vh * 1024:(vh + 1) * 1024], AF.Copy, bias=0.0),
                    deps=[Hstt[(cc, vh)]])
            HdmaO[cc] = ph.dma("sync", lambda e, n=n: e.dma_start(
                out=dr["o_d"][n * CB:(n + 1) * CB, :],
                in_=o_ch[:, :]), deps=[HevO[(cc, 1)]])
        ph.run()

    ph = SerialPhase(ctx)
    ph.v(lambda e: e.tensor_tensor_scan(cumg[:, 0:NCH_B],
                                        w["ones64"][0:64, 0:NCH_B],
                                        btot_all[:, :], 0.0,
                                        op0=ALU.mult, op1=ALU.add))
    ph.v(lambda e: e.memset(cumg[:, NCH_B:NCH_B + 8], 0.0))
    ph.s(lambda e: e.activation(cumg[:, NCH_B + 1:NCH_B + 2],
                                cumg[:, NCH_B - 1:NCH_B], AF.Exp))
    ph.dma("sync", lambda e: e.dma_start(out=dr["gin_g"][:, 0:DI], in_=s_f[:, :]))
    ph.dma("sync", lambda e: e.dma_start(out=dr["gin_g"][:, DI:DI + 1],
                                         in_=cumg[:, NCH_B + 1:NCH_B + 2],
                                         allow_slow_non_contiguous=True))
    ph.g(lambda e: e.collective_compute(
        "AllGather", ALU.bypass, replica_groups=RG,
        ins=[dr["gin_g"].ap().opt()], outs=[dr["gout_g"].ap().opt()]))
    ph.run()
    ctx.srel()

    ctx.smark()
    gg = ctx.salloc("b_gg", [64, DI + 32], F32)
    gsc = ctx.salloc("b_gsc", [64, 8], F32)
    ph = SerialPhase(ctx)
    ph.v(lambda e: e.memset(s_f[:, :], 0.0))
    for i in range(NCORE - 1):
        ph.dma("sync", lambda e, i=i: e.dma_start(out=gg[:, 0:DI + 1],
                                                  in_=dr["gout_g"][i, :, 0:DI + 1]))
        ph.v(lambda e, i=i: e.tensor_scalar(
            gsc[:, 0:1], gg[:, DI:DI + 1], w["pm"][0:64, i:i + 1],
            w["pim"][0:64, i:i + 1], op0=ALU.mult, op1=ALU.add))
        ph.v(lambda e, i=i: e.tensor_scalar(
            gg[:, 0:DI], gg[:, 0:DI], w["pm"][0:64, i:i + 1], None, op0=ALU.mult))
        ph.v(lambda e, i=i: e.scalar_tensor_tensor(
            s_f[:, :], s_f[:, :], gsc[:, 0:1], gg[:, 0:DI],
            op0=ALU.mult, op1=ALU.add))
    ph.s(lambda e: e.activation(s_bf[:, :], s_f[:, :], AF.Copy, bias=0.0))
    ph.run()
    ctx.dump("s0b", s_f[:, :])

    ctx.smark()
    w["gateT"] = ctx.salloc("b_gateT", [128, 8 * DI], BF16)
    w["downT"] = ctx.salloc("b_downT", [128, 16 * DM], BF16)
    w["rwrow"] = ctx.salloc("b_rw", [128, DI], BF16)
    w["nwrow"] = ctx.salloc("b_nw", [128, DI], BF16)
    ecg = ctx.salloc("b_ecg", [64, NCH_B], F32)
    qbd = ctx.salloc("b_qbd", [64, 2 * 2 * CB], BF16)
    o_t = ctx.salloc("b_ot", [128, DI], BF16)
    sh_t = ctx.salloc("b_sht", [128, DI], BF16)
    g2_t = ctx.salloc("b_g2t", [128, 2 * DI], BF16)
    m1_t = ctx.salloc("b_m1t", [128, DI], BF16)
    xg2_t = ctx.salloc("b_xg2t", [128, DI], BF16)
    xgf_t = ctx.salloc("b_xgft", [128, DI], BF16)
    xnf_t = ctx.salloc("b_xnft", [128, DM], BF16)
    xgT_t = ctx.salloc("b_xgTt", [128, DI], BF16)
    ht2 = ctx.salloc("b_ht2", [128, 2 * DM], F32)
    rms_c = ctx.salloc("b_rmsc", [128, 8], F32)

    ph = SerialPhase(ctx)
    ph.dma("gpsimd", lambda e: e.dma_start(
        out=w["gateT"][:, :],
        in_=bass.AP(dr["gateT_d"], 0, [[DI, 128], [128 * DI, 8], [1, DI]])))
    ph.dma("gpsimd", lambda e: e.dma_start(
        out=w["downT"][:, :],
        in_=bass.AP(dr["downT_d"], 0, [[DM, 128], [128 * DM, 16], [1, DM]])))
    ph.dma("gpsimd", lambda e: e.dma_start(out=w["rwrow"][:, :], in_=dr["rwrow_d"].ap()))
    ph.dma("gpsimd", lambda e: e.dma_start(out=w["nwrow"][:, :], in_=dr["nwrow_d"].ap()))
    ph.v(lambda e: e.memset(ecg[:, 0:1], 0.0))
    ph.v(lambda e: e.tensor_copy(ecg[:, 1:NCH_B], cumg[:, 0:NCH_B - 1]))
    ph.s(lambda e: e.activation(ecg[:, :], ecg[:, :], AF.Exp))
    ph.run()

    est_p1.close()
    ps_big = est.enter_context(nc.psum_tensor("b_psbig", [128, DI], F32))
    ps_tp2 = est.enter_context(nc.psum_tensor("b_pstp2", [128, DI], BF16))
    ps_dn = est.enter_context(nc.psum_tensor("b_psdn", [128, DM], F32))

    ph = Pipe(ctx)
    Hht = {}; Hxn = {}; Htpa = {}; Heva = {}; Hfg = {}; Hsil = {}
    Hqbd = {}; Hoc = {}; Hdo = {}; Hao = {}; Hdsh = {}; Hm1 = {}
    Hxg2 = {}; Hsq = {}; Hexp = {}; Hxgf = {}; Htpd = {}; Hevd = {}
    Htpe = {}; Heve = {}; Hdn = {}; Hra = {}; Hout = {}
    for t in range(NTILE):
        p = t % 2
        Hht[t] = ph.dma("sync", lambda e, t=t, p=p: e.dma_start(
            out=ht2[:, p * DM:(p + 1) * DM], in_=h_in_d[t * 128:(t + 1) * 128, :]),
            deps=[Hout.get(t - 2)])
        Hxn[t] = ph.v(lambda e, t=t, p=p: e.tensor_scalar(
            xn3[:, p * DM:(p + 1) * DM], ht2[:, p * DM:(p + 1) * DM],
            stats[:, 2 * t:2 * t + 1], stats[:, 2 * t + 1:2 * t + 2],
            op0=ALU.subtract, op1=ALU.mult),
            deps=[Hht[t], Htpa.get(t - 2)])

        def tp_xn(e, p=p):
            for i in range(8):
                o = e.transpose(ps_tp2[:, i * 128:(i + 1) * 128],
                                xn3[:, p * DM + i * 128:p * DM + (i + 1) * 128],
                                ctx.ident[:, :])
            return o
        Htpa[t] = ph.t(tp_xn, deps=[Hxn[t], Heve.get(t - 1)])
        Heva[t] = ph.v(lambda e: e.tensor_copy(xnf_t[:, :], ps_tp2[:, 0:1024]),
                       deps=[Htpa[t]])

        def mm_fg(e):
            for q4 in range(4):
                for dt in range(8):
                    o = e.matmul(ps_big[:, q4 * 512:(q4 + 1) * 512],
                                 xnf_t[:, dt * 128:(dt + 1) * 128],
                                 w["gateT"][:, dt * DI + q4 * 512:dt * DI + (q4 + 1) * 512],
                                 start=(dt == 0), stop=(dt == 7))
            return o
        Hfg[t] = ph.t(mm_fg, deps=[Heva[t], Hao.get(t - 1)])
        Hsil[t] = ph.s(lambda e, p=p: e.activation(
            g2_t[:, p * DI:(p + 1) * DI], ps_big[:, :], AF.Silu),
            deps=[Hfg[t], Hxg2.get(t - 2)])

        for half in range(2):
            Hqbd[(t, half)] = ph.v(lambda e, t=t, p=p, half=half: e.tensor_scalar(
                qbd[:, p * 128 + half * CB:p * 128 + (half + 1) * CB],
                qb_all[:, t * 128 + half * CB:t * 128 + (half + 1) * CB],
                ecg[:, 2 * t + half:2 * t + half + 1], None, op0=ALU.mult),
                deps=[Hoc.get(t - 2)])

        def mm_oc(e, t=t, p=p):
            for half in range(2):
                for j in range(4):
                    o = e.matmul(ps_big[0:64, j * 512:(j + 1) * 512] if half == 0
                                 else ps_big[64:128, j * 512:(j + 1) * 512],
                                 qbd[:, p * 128 + half * CB:p * 128 + (half + 1) * CB],
                                 s_bf[:, j * 512:(j + 1) * 512],
                                 start=True, stop=True,
                                 tile_position=(0, 64) if half == 1 else None)
            return o
        Hoc[t] = ph.t(mm_oc, deps=[Hsil[t], Hqbd[(t, 0)], Hqbd[(t, 1)]])
        Hdo[t] = ph.dma("sync", lambda e, t=t: e.dma_start(
            out=o_t[:, :], in_=dr["o_d"][t * 128:(t + 1) * 128, :]),
            deps=[Hm1.get(t - 1)])
        Hao[t] = ph.v(lambda e: e.tensor_tensor(
            o_t[:, :], o_t[:, :],
            ps_big[:, :], op=ALU.add), deps=[Hoc[t], Hdo[t]])

        Hdsh[t] = ph.dma("sync", lambda e, t=t: e.dma_start(
            out=sh_t[:, :], in_=dr["sh_d"][t * 128:(t + 1) * 128, :]),
            deps=[Hm1.get(t - 1)])
        Hm1[t] = ph.v(lambda e: e.tensor_tensor(
            m1_t[:, :], sh_t[:, :], w["rwrow"][:, :],
            op=ALU.mult), deps=[Hdsh[t], Hsq.get(t - 1)])
        ph.v(lambda e: e.tensor_tensor(m1_t[:, :], m1_t[:, :],
                                       o_t[:, :],
                                       op=ALU.add), deps=[Hao[t]])
        Hxg2[t] = ph.v(lambda e, p=p: e.tensor_tensor(
            xg2_t[:, :], m1_t[:, :], g2_t[:, p * DI:(p + 1) * DI],
            op=ALU.mult), deps=[Hsil[t]])
        Hsq[t] = ph.s(lambda e, p=p: e.activation(
            m1_t[:, :], xg2_t[:, :], AF.Square, accum_out=rms_c[:, p * 4:p * 4 + 1]),
            deps=[Hxg2[t], Hxgf.get(t - 2)])
        ph.s(lambda e, p=p: e.activation(rms_c[:, p * 4 + 1:p * 4 + 2],
                                         rms_c[:, p * 4:p * 4 + 1], AF.Ln,
                                         scale=ctx.cst(1.0 / DI), bias=ctx.cst(1e-5)))
        Hexp[t] = ph.s(lambda e, p=p: e.activation(
            rms_c[:, p * 4 + 1:p * 4 + 2], rms_c[:, p * 4 + 1:p * 4 + 2], AF.Exp,
            scale=ctx.cst(-0.5)))
        ph.v(lambda e: e.tensor_tensor(xgf_t[:, :], xg2_t[:, :], w["nwrow"][:, :],
                                       op=ALU.mult), deps=[Hxg2[t], Htpe.get(t - 1)])
        Hxgf[t] = ph.v(lambda e, p=p: e.tensor_scalar(
            xgf_t[:, :], xgf_t[:, :], rms_c[:, p * 4 + 1:p * 4 + 2], None,
            op0=ALU.mult), deps=[Hexp[t]])

        def tp_xg1(e):
            for i in range(8):
                o = e.transpose(ps_tp2[:, i * 128:(i + 1) * 128],
                                xgf_t[:, i * 128:(i + 1) * 128], ctx.ident[:, :])
            return o

        def tp_xg2_(e):
            for i in range(8, 16):
                o = e.transpose(ps_tp2[:, (i - 8) * 128:(i - 7) * 128],
                                xgf_t[:, i * 128:(i + 1) * 128], ctx.ident[:, :])
            return o
        Htpd[t] = ph.t(tp_xg1, deps=[Hxgf[t], Heva[t]])
        Hevd[t] = ph.v(lambda e: e.tensor_copy(xgT_t[:, 0:1024], ps_tp2[:, 0:1024]),
                       deps=[Htpd[t], Hdn.get(t - 1)])
        Htpe[t] = ph.t(tp_xg2_, deps=[Hevd[t]])
        Heve[t] = ph.v(lambda e: e.tensor_copy(xgT_t[:, 1024:2048], ps_tp2[:, 0:1024]),
                       deps=[Htpe[t]])

        def mm_dn(e):
            for half in range(2):
                for cht in range(16):
                    o = e.matmul(ps_dn[:, half * 512:(half + 1) * 512],
                                 xgT_t[:, cht * 128:(cht + 1) * 128],
                                 w["downT"][:, cht * DM + half * 512:cht * DM + (half + 1) * 512],
                                 start=(cht == 0), stop=(cht == 15))
            return o
        Hdn[t] = ph.t(mm_dn, deps=[Heve[t], Hra.get(t - 1)])
        Hra[t] = ph.v(lambda e, p=p: e.tensor_tensor(
            ht2[:, p * DM:(p + 1) * DM], ht2[:, p * DM:(p + 1) * DM],
            ps_dn[:, :], op=ALU.add), deps=[Hdn[t]])
        Hout[t] = ph.dma("sync", lambda e, t=t, p=p: e.dma_start(
            out=h_out_d[t * 128:(t + 1) * 128, :], in_=ht2[:, p * DM:(p + 1) * DM]),
            deps=[Hra[t]])
    ph.run()
    ctx.srel()
    ctx.srel()
    ctx.srel()


def emit_ffn(ctx, est, tag, h_in_d, h_out_d, phi_sb, wsh_d, bsh_d,
             gin_s, gout_s, gin_y, gout_y):
    nc = ctx.nc
    ctx.smark()
    xnT = ctx.salloc(f"f{tag}_xnT", [128, 8 * TPC], BF16)
    s_col = ctx.salloc(f"f{tag}_scol", [128, 8], F32)
    s_bf = ctx.salloc(f"f{tag}_sbf", [128, 8], BF16)
    sg = ctx.salloc(f"f{tag}_sg", [128, 64], F32)
    y_bf = ctx.salloc(f"f{tag}_ybf", [64, DM], BF16)
    est_p = ExitStack()
    emit_ln_transpose(ctx, est_p, xnT, x_dram=h_in_d)
    est_p.close()

    ctx.smark()
    w_e = ctx.salloc(f"f{tag}_we", [128, 2 * 8 * DM], BF16)
    y_loc = ctx.salloc(f"f{tag}_yloc", [1, EPC * DM], F32)
    b_loc = ctx.salloc(f"f{tag}_bloc", [1, EPC * DM], F32)

    est_y = ExitStack()
    ps_y = est_y.enter_context(nc.psum_tensor(f"f{tag}_psy", [1, DM], F32))

    ph = Pipe(ctx)
    ph.v(lambda e: e.tensor_reduce(
        s_col[:, :], xnT[:, :].rearrange("a (d t) -> a d t", t=TPC),
        axis=AX.X, op=ALU.add))
    ph.dma("sync", lambda e: e.dma_start(out=gin_s.ap(), in_=s_col[:, :]))
    ph.g(lambda e: e.collective_compute(
        "AllGather", ALU.bypass, replica_groups=RG,
        ins=[gin_s.ap().opt()], outs=[gout_s.ap().opt()]))
    ph.dma("sync", lambda e: e.dma_start(
        out=sg[:, :], in_=bass.AP(gout_s, 0, [[8, 128], [1024, 8], [1, 8]])))
    ph.dma("sync", lambda e: e.dma_start(
        out=b_loc[:, :], in_=bass.AP(bsh_d, 0, [[EPC * DM, 1], [1, EPC * DM]])),
        deps="all")

    def sum_s(e):
        for i in range(1, 8):
            o = e.tensor_tensor(s_col[:, :], sg[:, 0:8] if i == 1 else s_col[:, :],
                                sg[:, i * 8:(i + 1) * 8], op=ALU.add)
            e.drain()
        return o
    ph.v(sum_s)
    ph.v(lambda e: e.tensor_copy(s_bf[:, :], s_col[:, :]))
    ph.run()
    ctx.dump("s_tot" + tag, s_col[:, :])

    ph = Pipe(ctx)
    h_mm = {}
    h_ev = {}
    h_dw = {}
    for e_i in range(EPC):
        h_dw[e_i] = ph.dma("gpsimd", lambda e, e_i=e_i: e.dma_start(
            out=w_e[:, (e_i % 2) * 8 * DM:(e_i % 2 + 1) * 8 * DM],
            in_=bass.AP(wsh_d, e_i * DM * DM, [[DM, 128], [128 * DM, 8], [1, DM]])),
            deps=[h_mm.get(e_i - 2)])

        def mm_y(e, e_i=e_i):
            for half in range(2):
                for dt in range(8):
                    o = e.matmul(ps_y[:, half * 512:(half + 1) * 512],
                                 s_bf[:, dt:dt + 1],
                                 w_e[:, (e_i % 2) * 8 * DM + dt * DM + half * 512:
                                     (e_i % 2) * 8 * DM + dt * DM + (half + 1) * 512],
                                 start=(dt == 0), stop=(dt == 7))
            return o
        h_mm[e_i] = ph.t(mm_y, deps=[h_dw[e_i], h_ev.get(e_i - 1)])
        h_ev[e_i] = ph.v(lambda e, e_i=e_i: e.tensor_copy(
            y_loc[:, e_i * DM:(e_i + 1) * DM], ps_y[:, :]), deps=[h_mm[e_i]])
    ph.v(lambda e: e.tensor_tensor(y_loc[:, :], y_loc[:, :], b_loc[:, :],
                                   op=ALU.add), deps="all")
    ph.dma("sync", lambda e: e.dma_start(
        out=bass.AP(gin_y, 0, [[EPC * DM, 1], [1, EPC * DM]]), in_=y_loc[:, :]),
        deps="all")
    ph.g(lambda e: e.collective_compute(
        "AllGather", ALU.bypass, replica_groups=RG,
        ins=[gin_y.ap().opt()], outs=[gout_y.ap().opt()]), deps="all")
    ph.dma("gpsimd", lambda e: e.dma_start(
        out=y_bf[:, :], in_=bass.AP(gout_y, 0, [[DM, 64], [1, DM]])), deps="all")
    ph.run()
    est_y.close()
    ctx.srel()
    el = ctx.salloc(f"f{tag}_el", [128, 2 * 2048], BF16)
    zc = ctx.salloc(f"f{tag}_zc", [128, 8], F32)
    craw = ctx.salloc(f"f{tag}_craw", [128, 2 * 64], F32)
    c_bf = ctx.salloc(f"f{tag}_cbf", [128, 2 * 64], BF16)
    cT = ctx.salloc(f"f{tag}_cT", [64, NTILE * 128], BF16)
    ht = ctx.salloc(f"f{tag}_ht", [128, 2 * DM], F32)

    ps_l = est.enter_context(nc.psum_tensor(f"f{tag}_psl", [128, 2048], F32))
    ps_c = est.enter_context(nc.psum_tensor(f"f{tag}_psc", [64, 128], BF16))
    ps_o = est.enter_context(nc.psum_tensor(f"f{tag}_pso", [128, DM], F32))
    ctx.dump("y" + tag, y_bf[:, :])

    ph = Pipe(ctx)
    H_exp = {}
    H_red = {}
    H_rcp = {}
    H_scl = {}
    H_tpc = {}
    H_ecT = {}
    H_add = {}
    H_out = {}
    H_h = {}
    for t in range(NTILE):
        p = t % 2
        H_h[t] = ph.dma("sync", lambda e, t=t, p=p: e.dma_start(
            out=ht[:, p * DM:(p + 1) * DM], in_=h_in_d[t * 128:(t + 1) * 128, :]),
            deps=[H_out.get(t - 2)])

        def mm_l(e, t=t):
            for q4 in range(4):
                for dt in range(8):
                    o = e.matmul(ps_l[:, q4 * 512:(q4 + 1) * 512],
                                 xnT[:, dt * TPC + t * 128:dt * TPC + (t + 1) * 128],
                                 phi_sb[:, dt * 2048 + q4 * 512:dt * 2048 + (q4 + 1) * 512],
                                 start=(dt == 0), stop=(dt == 7))
            return o
        l1 = ph.t(mm_l, deps=[H_exp.get(t - 1)])
        H_exp[t] = ph.s(lambda e, p=p: e.activation(
            el[:, p * 2048:(p + 1) * 2048], ps_l[:, :], AF.Exp,
            bias=ctx.cst(-8.0), accum_out=zc[:, p * 4:p * 4 + 1]),
            deps=[l1, H_scl.get(t - 2)])
        H_red[t] = ph.v(lambda e, p=p: e.tensor_reduce(
            craw[:, p * 64:(p + 1) * 64],
            el[:, p * 2048:(p + 1) * 2048].rearrange("a (n q) -> a n q", q=32),
            axis=AX.X, op=ALU.add), deps=[H_exp[t]])
        H_rcp[t] = ph.v(lambda e, p=p: e.reciprocal(
            zc[:, p * 4 + 1:p * 4 + 2], zc[:, p * 4:p * 4 + 1]), deps=[H_exp[t]])

        def scl(e, p=p):
            e.drain()
            return e.tensor_scalar(c_bf[:, p * 64:(p + 1) * 64],
                                   craw[:, p * 64:(p + 1) * 64],
                                   zc[:, p * 4 + 1:p * 4 + 2], None, op0=ALU.mult)
        H_scl[t] = ph.v(scl, deps=[H_tpc.get(t - 2)])
        H_tpc[t] = ph.t(lambda e, p=p: e.transpose(
            ps_c[:, :], c_bf[:, p * 64:(p + 1) * 64], ctx.ident[:, :]),
            deps=[H_scl[t], H_ecT.get(t - 1)])
        H_ecT[t] = ph.v(lambda e, t=t: e.tensor_copy(
            cT[:, t * 128:(t + 1) * 128], ps_c[:, :]), deps=[H_tpc[t]])

        def mm_o(e, t=t):
            for half in range(2):
                o = e.matmul(ps_o[:, half * 512:(half + 1) * 512],
                             cT[:, t * 128:(t + 1) * 128],
                             y_bf[:, half * 512:(half + 1) * 512],
                             start=True, stop=True)
            return o
        oo = ph.t(mm_o, deps=[H_ecT[t], H_add.get(t - 1)])
        H_add[t] = ph.v(lambda e, p=p: e.tensor_tensor(
            ht[:, p * DM:(p + 1) * DM], ht[:, p * DM:(p + 1) * DM],
            ps_o[:, :], op=ALU.add), deps=[oo, H_h[t]])
        H_out[t] = ph.dma("sync", lambda e, t=t, p=p: e.dma_start(
            out=h_out_d[t * 128:(t + 1) * 128, :], in_=ht[:, p * DM:(p + 1) * DM]),
            deps=[H_add[t]])
    ph.run()
    ctx.srel()


def build_program(dbg=()):
    nc = bass.Bass("TRN2", target_bir_lowering=False)
    ctx = Ctx(nc, dbg)

    x_d = nc.dram_tensor("x", [TPC, DM], F32, kind="ExternalInput")
    wq_d = nc.dram_tensor("wq", [DM, DM], F32, kind="ExternalInput")
    wk_d = nc.dram_tensor("wk", [DM, DM], F32, kind="ExternalInput")
    wv_d = nc.dram_tensor("wv", [DM, DM], F32, kind="ExternalInput")
    wb_d = nc.dram_tensor("wbeta", [DM, H], F32, kind="ExternalInput")
    wo_d = nc.dram_tensor("wo", [DM, DM], F32, kind="ExternalInput")
    ident_d = nc.dram_tensor("ident", [128, 128], BF16, kind="ExternalInput")
    cvec_d = nc.dram_tensor("cvec", [128, len(CONSTS)], F32, kind="ExternalInput")
    m_slo_d = nc.dram_tensor("m_slo", [128, 128], BF16, kind="ExternalInput")
    m_trq_d = nc.dram_tensor("m_trq", [128, 128], BF16, kind="ExternalInput")
    idpat_d = nc.dram_tensor("idpat", [128, 2048], BF16, kind="ExternalInput")
    id4_d = nc.dram_tensor("id4", [128, 512], BF16, kind="ExternalInput")
    pmask_d = nc.dram_tensor("pmask", [128, 8], F32, kind="ExternalInput")
    pimask_d = nc.dram_tensor("pimask", [128, 8], F32, kind="ExternalInput")
    gn_col_d = nc.dram_tensor("gn_col", [128, 8], F32, kind="ExternalInput")
    out_d = nc.dram_tensor("out", [TPC, DM], F32, kind="ExternalOutput")

    phi_d = nc.dram_tensor("phi2", [DM, 2048], F32, kind="ExternalInput")
    wsh_d = nc.dram_tensor("wsh", [EPC, DM, DM], F32, kind="ExternalInput")
    bsh_d = nc.dram_tensor("bsh", [EPC, DM], F32, kind="ExternalInput")

    gin_bp = nc.dram_tensor("gin_bp", [128, 4096], BF16)
    gout_bp = nc.dram_tensor("gout_bp", [8, 128, 4096], BF16)
    h1_d = nc.dram_tensor("h1_d", [TPC, DM], F32)
    h2_d = nc.dram_tensor("h2_d", [TPC, DM], F32)
    gin_s1 = nc.dram_tensor("gin_s1", [128, 8], F32)
    gout_s1 = nc.dram_tensor("gout_s1", [8, 128, 8], F32)
    gin_y1 = nc.dram_tensor("gin_y1", [EPC, DM], F32)
    gout_y1 = nc.dram_tensor("gout_y1", [8, EPC, DM], F32)
    gin_s2 = nc.dram_tensor("gin_s2", [128, 8], F32)
    gout_s2 = nc.dram_tensor("gout_s2", [8, 128, 8], F32)
    gin_y2 = nc.dram_tensor("gin_y2", [EPC, DM], F32)
    gout_y2 = nc.dram_tensor("gout_y2", [8, EPC, DM], F32)

    h3_d = nc.dram_tensor("h3_d", [TPC, DM], F32)
    sh_d = nc.dram_tensor("sh_d", [TPC, DI], BF16)
    o_d = nc.dram_tensor("o_d", [TPC, DI], BF16)
    gin_h = nc.dram_tensor("gin_h", [128, 48], F32)
    gout_h = nc.dram_tensor("gout_h", [8, 128, 48], F32)
    gin_g = nc.dram_tensor("gin_g", [64, DI + 32], F32)
    gout_g = nc.dram_tensor("gout_g", [8, 64, DI + 32], F32)
    upT_d = nc.dram_tensor("upT", [DM, DI], F32, kind="ExternalInput")
    gateT_d = nc.dram_tensor("gateT", [DM, DI], F32, kind="ExternalInput")
    downT_d = nc.dram_tensor("downT", [DI, DM], F32, kind="ExternalInput")
    qkgtT_d = nc.dram_tensor("qkgtT", [DI, 256], F32, kind="ExternalInput")
    ig1T_d = nc.dram_tensor("ig1T", [DI, 16], F32, kind="ExternalInput")
    ig2T_d = nc.dram_tensor("ig2T", [16, DI], F32, kind="ExternalInput")
    cwcol_d = nc.dram_tensor("cwcol", [128, 64], F32, kind="ExternalInput")
    cbcol_d = nc.dram_tensor("cbcol", [128, 16], F32, kind="ExternalInput")
    igb2_d = nc.dram_tensor("igb2col", [128, 16], F32, kind="ExternalInput")
    gbc_d = nc.dram_tensor("gbc", [128, 1], F32, kind="ExternalInput")
    tbc_d = nc.dram_tensor("tbc", [128, 1], F32, kind="ExternalInput")
    rwrow_d = nc.dram_tensor("rwrow", [128, DI], F32, kind="ExternalInput")
    nwrow_d = nc.dram_tensor("nwrow", [128, DI], F32, kind="ExternalInput")
    smask_d = nc.dram_tensor("smask", [128, 512], F32, kind="ExternalInput")
    ones64_d = nc.dram_tensor("ones64", [128, 64], F32, kind="ExternalInput")
    hmask_d = nc.dram_tensor("hmask", [128, 8], F32, kind="ExternalInput")

    top = ctx.top
    ctx.ident = top.enter_context(nc.sbuf_tensor("ident_sb", [128, 128], BF16))
    ctx.cvec = top.enter_context(nc.sbuf_tensor("cvec_sb", [128, len(CONSTS)], F32))
    mask_slo = top.enter_context(nc.sbuf_tensor("m_slo_sb", [128, 128], BF16))
    mask_trq = top.enter_context(nc.sbuf_tensor("m_trq_sb", [128, 128], BF16))
    idpat = top.enter_context(nc.sbuf_tensor("idpat_sb", [128, 2048], BF16))
    id4 = top.enter_context(nc.sbuf_tensor("id4_sb", [128, 512], BF16))
    pmask = top.enter_context(nc.sbuf_tensor("pmask_sb", [128, 8], F32))
    pimask = top.enter_context(nc.sbuf_tensor("pimask_sb", [128, 8], F32))
    gn_col = top.enter_context(nc.sbuf_tensor("gn_col_sb", [128, 8], F32))
    ctx.init_arena()

    with top:
        s0 = ctx.sem()
        with nc.Block(no_gpsimd_drain=True) as blk:
            @blk.sync
            def _(e):
                for dst, srcd in ((ctx.ident, ident_d), (ctx.cvec, cvec_d),
                                  (mask_slo, m_slo_d), (mask_trq, m_trq_d),
                                  (idpat, idpat_d), (id4, id4_d),
                                  (pmask, pmask_d), (pimask, pimask_d),
                                  (gn_col, gn_col_d)):
                    o = e.dma_start(out=dst[:, :], in_=srcd.ap())
                    inc(o, s0, 16)
                e.wait_ge(s0.h, s0.v)

        ctx.smark()
        q_tm = ctx.salloc("q_tm", [128, NTILE * DM], BF16)
        k_tm = ctx.salloc("k_tm", [128, NTILE * DM], BF16)
        beta_sb = ctx.salloc("beta", [128, NTILE * H], F32)
        v_tm = ctx.salloc("v_tm", [128, NTILE * DM], BF16)

        ctx.smark()
        xnT = ctx.salloc("xnT", [128, 8 * TPC], BF16)
        wq_sb = ctx.salloc("wq_sb", [128, 8 * DM], BF16)
        wk_sb = ctx.salloc("wk_sb", [128, 8 * DM], BF16)
        wv_sb = ctx.salloc("wv_sb", [128, 8 * DM], BF16)
        wbe_sb = ctx.salloc("wbe_sb", [128, 8 * H], BF16)
        sw = ctx.sem()
        with nc.Block(no_gpsimd_drain=True) as blk:
            @blk.gpsimd
            def _(e):
                for w_d, w_sb, cols in ((wq_d, wq_sb, DM), (wk_d, wk_sb, DM),
                                        (wv_d, wv_sb, DM), (wb_d, wbe_sb, H)):
                    o = e.dma_start(
                        out=w_sb[:, :],
                        in_=bass.AP(w_d, 0, [[cols, 128], [128 * cols, 8], [1, cols]]))
                    inc(o, sw, 16)
                e.wait_ge(sw.h, sw.v)

        est_p = ExitStack()
        emit_ln_transpose(ctx, est_p, xnT, x_dram=x_d)
        est_p.close()
        ctx.dump("xnT", xnT[:, :])

        est_p = ExitStack()
        emit_proj_qkvb(ctx, est_p, xnT, wq_sb, wk_sb, wv_sb, wbe_sb,
                       q_tm, k_tm, v_tm, beta_sb)
        est_p.close()
        ctx.srel()
        ctx.dump("q_tm", q_tm[:, :])
        ctx.dump("k_tm", k_tm[:, :])
        ctx.dump("v_tm", v_tm[:, :])
        ctx.dump("beta", beta_sb[:, :])

        o_tm = v_tm
        est_d = ExitStack()
        emit_attn_a_delta(ctx, est_d, q_tm, k_tm, v_tm, beta_sb, o_tm,
                          gin_bp, gout_bp, mask_slo, mask_trq, idpat, id4,
                          pmask, pimask)
        est_d.close()

        ctx.smark()
        est_p6 = ExitStack()
        h_sb = ctx.salloc("h_sb", [128, 2 * DM], F32)
        wo_sb = ctx.salloc("wo_sb", [128, 8 * DM], BF16)
        xt6 = ctx.salloc("xt6", [128, 2 * DM], F32)
        onr = ctx.salloc("onr", [128, DM], BF16)
        sq6 = ctx.salloc("sq6", [128, DM], F32)
        rs6 = ctx.salloc("rs6", [128, 4], F32)
        onT = ctx.salloc("onT", [128, DM], BF16)
        pt6 = est_p6.enter_context(nc.psum_tensor("pt6", [128, DM], BF16))
        pr6 = est_p6.enter_context(nc.psum_tensor("pr6", [128, DM], F32))

        ph = SerialPhase(ctx)
        ph.dma("gpsimd", lambda e: e.dma_start(
            out=wo_sb[:, :],
            in_=bass.AP(wo_d, 0, [[DM, 128], [128 * DM, 8], [1, DM]])))

        def scale_wo(e):
            for dt in range(8):
                o = e.tensor_scalar(wo_sb[:, dt * DM:(dt + 1) * DM],
                                    wo_sb[:, dt * DM:(dt + 1) * DM],
                                    gn_col[:, dt:dt + 1], None, op0=ALU.mult)
            return o
        ph.v(scale_wo)
        for t in range(NTILE):
            ph.dma("sync", lambda e, t=t: e.dma_start(
                out=xt6[:, (t % 2) * DM:(t % 2 + 1) * DM],
                in_=x_d[t * 128:(t + 1) * 128, :]))
            ph.v(lambda e, t=t: e.tensor_tensor(
                sq6[:, :], o_tm[:, t * DM:(t + 1) * DM],
                o_tm[:, t * DM:(t + 1) * DM], op=ALU.mult))
            ph.v(lambda e, t=t: e.tensor_reduce(
                rs6[:, :], sq6[:, :].rearrange("a (h d) -> a h d", d=DK),
                axis=AX.X, op=ALU.add))
            ph.s(lambda e: e.activation(rs6[:, :], rs6[:, :], AF.Ln,
                                        scale=ctx.cst(1.0 / 256), bias=ctx.cst(1e-5)))
            ph.s(lambda e: e.activation(rs6[:, :], rs6[:, :], AF.Exp,
                                        scale=ctx.cst(-0.5)))
            ph.v(lambda e, t=t: e.tensor_tensor(
                onr[:, :].rearrange("a (h d) -> a h d", d=DK),
                o_tm[:, t * DM:(t + 1) * DM].rearrange("a (h d) -> a h d", d=DK),
                bass.AP(rs6, 0, [[4, 128], [1, H], [0, DK]]), op=ALU.mult))

            def tp_on(e, t=t):
                for i in range(8):
                    o = e.transpose(pt6[:, i * 128:(i + 1) * 128],
                                    onr[:, i * 128:(i + 1) * 128], ctx.ident[:, :])
                return o
            ph.t(tp_on)
            ph.v(lambda e: e.tensor_copy(onT[:, :], pt6[:, :]))

            def mm_wo(e, t=t):
                for half in range(2):
                    for dt in range(8):
                        o = e.matmul(pr6[:, half * 512:(half + 1) * 512],
                                     onT[:, dt * 128:(dt + 1) * 128],
                                     wo_sb[:, dt * DM + half * 512:dt * DM + (half + 1) * 512],
                                     start=(dt == 0), stop=(dt == 7))
                return o
            ph.t(mm_wo)
            ph.v(lambda e, t=t: e.tensor_tensor(
                h_sb[:, (t % 2) * DM:(t % 2 + 1) * DM],
                xt6[:, (t % 2) * DM:(t % 2 + 1) * DM],
                pr6[:, :], op=ALU.add))
            ph.dma("sync", lambda e, t=t: e.dma_start(
                out=h1_d[t * 128:(t + 1) * 128, :],
                in_=h_sb[:, (t % 2) * DM:(t % 2 + 1) * DM]))
        ph.run()
        est_p6.close()
        ctx.srel()
        ctx.srel()
        ctx.dump_dram("h1", h1_d)

        ctx.smark()
        phi_sb = ctx.salloc("phi_sb", [128, 8 * 2048], BF16)
        sph = ctx.sem()
        with nc.Block(no_gpsimd_drain=True) as blk:
            @blk.gpsimd
            def _(e):
                o = e.dma_start(out=phi_sb[:, :],
                                in_=bass.AP(phi_d, 0, [[2048, 128],
                                                       [128 * 2048, 8], [1, 2048]]))
                inc(o, sph, 16)
                e.wait_ge(sph.h, sph.v)
        est_f1 = ExitStack()
        emit_ffn(ctx, est_f1, "1", h1_d, h2_d, phi_sb, wsh_d, bsh_d,
                 gin_s1, gout_s1, gin_y1, gout_y1)
        est_f1.close()
        ctx.srel()
        ctx.dump_dram("h2", h2_d)

        ctx.smark()
        w = {
            "qkgtT": ctx.salloc("b_qkgtT", [128, 16 * 256], BF16),
            "ig1T": ctx.salloc("b_ig1T", [128, 16 * 16], BF16),
            "ig2T": ctx.salloc("b_ig2T", [16, DI], BF16),
            "cwcol": ctx.salloc("b_cw", [128, 64], F32),
            "cbcol": ctx.salloc("b_cb", [128, 16], F32),
            "igb2col": ctx.salloc("b_igb", [128, 16], F32),
            "gbc": ctx.salloc("b_gbc", [128, 1], F32),
            "tbc": ctx.salloc("b_tbc", [128, 1], F32),
            "smask": ctx.salloc("b_sm", [128, 512], F32),
            "ones64": ctx.salloc("b_o64", [128, 64], F32),
            "hmask": ctx.salloc("b_hm", [128, 8], F32),
            "m64": mask_trq[0:64, 0:64],
            "pm": pmask,
            "pim": pimask,
        }
        swb = ctx.sem()
        with nc.Block(no_gpsimd_drain=True) as blk:
            @blk.gpsimd
            def _(e):
                o = e.dma_start(out=w["qkgtT"][:, :],
                                in_=bass.AP(qkgtT_d, 0, [[256, 128], [128 * 256, 16], [1, 256]]))
                inc(o, swb, 16)
                o = e.dma_start(out=w["ig1T"][:, :],
                                in_=bass.AP(ig1T_d, 0, [[16, 128], [128 * 16, 16], [1, 16]]))
                inc(o, swb, 16)
                o = e.dma_start(out=w["ig2T"][:, :], in_=ig2T_d.ap())
                inc(o, swb, 16)
            @blk.sync
            def _(e):
                for dst, srcd in ((w["cwcol"], cwcol_d), (w["cbcol"], cbcol_d),
                                  (w["igb2col"], igb2_d), (w["gbc"], gbc_d),
                                  (w["tbc"], tbc_d), (w["smask"], smask_d),
                                  (w["ones64"], ones64_d), (w["hmask"], hmask_d)):
                    o = e.dma_start(out=dst[:, :], in_=srcd.ap())
                    inc(o, swb, 16)
                e.wait_ge(swb.h, swb.v)
            @blk.gpsimd
            def _(e):
                e.wait_ge(swb.h, swb.v)
        dr = {"sh_d": sh_d, "o_d": o_d, "gin_h": gin_h, "gout_h": gout_h,
              "gin_g": gin_g, "gout_g": gout_g, "gateT_d": gateT_d,
              "downT_d": downT_d, "upT_d": upT_d, "rwrow_d": rwrow_d,
              "nwrow_d": nwrow_d}
        est_b = ExitStack()
        emit_attn_b(ctx, est_b, h2_d, h3_d, w, dr)
        est_b.close()
        ctx.srel()
        ctx.dump_dram("h3", h3_d)

        ctx.smark()
        phi_sb2 = ctx.salloc("phi_sb2", [128, 8 * 2048], BF16)
        sph2 = ctx.sem()
        with nc.Block(no_gpsimd_drain=True) as blk:
            @blk.gpsimd
            def _(e):
                o = e.dma_start(out=phi_sb2[:, :],
                                in_=bass.AP(phi_d, 0, [[2048, 128],
                                                       [128 * 2048, 8], [1, 2048]]))
                inc(o, sph2, 16)
                e.wait_ge(sph2.h, sph2.v)
        est_f2 = ExitStack()
        emit_ffn(ctx, est_f2, "2", h3_d, out_d, phi_sb2, wsh_d, bsh_d,
                 gin_s2, gout_s2, gin_y2, gout_y2)
        est_f2.close()
        ctx.srel()
        top.pop_all()
    return nc, ctx


def _prep_inputs(inputs):
    g = lambda n: np.asarray(inputs[n], np.float32)
    x = g("L0_attn_a_x").reshape(16384, DM)
    ident = np.eye(128, dtype=BF)
    ii, jj = np.indices((128, 128))
    m_slo = (jj < ii).astype(BF)
    m_trq = (ii <= jj).astype(BF)
    idpat = np.zeros((128, 2048), BF)
    for dkt in range(2):
        for h in range(H):
            for p in range(128):
                idpat[p, dkt * 1024 + h * 256 + dkt * 128 + p] = 1
    id4 = np.tile(np.eye(128, dtype=BF), (1, 4)).reshape(128, 512)
    id4 = np.concatenate([np.eye(128, dtype=BF)] * 4, axis=1)
    gn4 = np.tile(g("L0_attn_a_g_norm"), H)
    gn_col = gn4.reshape(8, 128).T.astype(np.float32).copy()
    cvec = np.tile(np.asarray(CONSTS, np.float32)[None, :], (128, 1))
    phi2 = np.ascontiguousarray(g("L0_ffn_phi").reshape(DM, 2048))
    Wff = g("L0_ffn_W")
    bff = g("L0_ffn_b")
    upT = np.ascontiguousarray(g("L0_attn_b_up_w").T)
    gateT = np.ascontiguousarray(g("L0_attn_b_gate_w").T)
    downT = np.ascontiguousarray(g("L0_attn_b_down_w").T)
    qkgtT = np.ascontiguousarray(np.concatenate(
        [g("L0_attn_b_q_w"), g("L0_attn_b_k_w"), g("L0_attn_b_g_w"),
         g("L0_attn_b_tau_w")], 0).T)
    ig1T = np.ascontiguousarray(g("L0_attn_b_ig_w1").T)
    ig2T = np.ascontiguousarray(g("L0_attn_b_ig_w2").T)
    cwcol = np.ascontiguousarray(
        g("L0_attn_b_conv_w").reshape(16, 128, 4).transpose(1, 0, 2).reshape(128, 64))
    cbcol = np.ascontiguousarray(g("L0_attn_b_conv_b").reshape(16, 128).T)
    igb2col = np.ascontiguousarray(g("L0_attn_b_ig_b2").reshape(16, 128).T)
    gbc = np.zeros((128, 1), np.float32)
    gbc[0:64, 0] = g("L0_attn_b_g_b")
    tbc = np.zeros((128, 1), np.float32)
    tbc[0:64, 0] = g("L0_attn_b_tau_b")
    rwrow = np.tile(g("L0_attn_b_residual_weight")[None, :], (128, 1))
    nwrow = np.tile(g("L0_attn_b_norm_weight")[None, :], (128, 1))
    smask = np.ones((128, 512), np.float32)
    smask[:, ::CB] = 0.0
    ones64 = np.ones((128, 64), np.float32)

    def hmask_c(c):
        hm = np.zeros((128, 8), np.float32)
        if c > 0:
            hm[:, c - 1] = 1.0
        return hm
    per_core = []
    for c in range(NCORE):
        pm = np.zeros((128, 8), np.float32)
        pm[:, :c] = 1.0
        per_core.append({
            "x": np.ascontiguousarray(x[c * TPC:(c + 1) * TPC]),
            "wq": g("L0_attn_a_Wq"),
            "wk": g("L0_attn_a_Wk"),
            "wv": g("L0_attn_a_Wv"),
            "wbeta": g("L0_attn_a_Wbeta"),
            "wo": g("L0_attn_a_Wo"),
            "ident": ident,
            "cvec": cvec,
            "m_slo": m_slo,
            "m_trq": m_trq,
            "idpat": idpat,
            "id4": id4,
            "pmask": pm,
            "pimask": 1.0 - pm,
            "gn_col": gn_col,
            "phi2": phi2,
            "wsh": np.ascontiguousarray(Wff[c * EPC:(c + 1) * EPC]),
            "bsh": np.ascontiguousarray(bff[c * EPC:(c + 1) * EPC]),
            "upT": upT, "gateT": gateT, "downT": downT, "qkgtT": qkgtT,
            "ig1T": ig1T, "ig2T": ig2T, "cwcol": cwcol, "cbcol": cbcol,
            "igb2col": igb2col, "gbc": gbc, "tbc": tbc, "rwrow": rwrow,
            "nwrow": nwrow, "smask": smask, "ones64": ones64, "hmask": hmask_c(c),
        })
    return per_core


def kernel(**inputs):
    per_core = _prep_inputs(inputs)
    nc, _ = build_program()
    res = run_bass_kernel_spmd(nc, per_core, core_ids=list(range(NCORE)))
    out = np.concatenate([np.asarray(r["out"], np.float32) for r in res.results], 0)
    return out.reshape(1, 16384, DM)
